# version 01
# baseline (speedup 1.0000x reference)
import contextlib

import numpy as np
import ml_dtypes

import concourse.bass as bass
import concourse.mybir as mybir
from concourse.bass_utils import run_bass_kernel_spmd

F32 = mybir.dt.float32
BF16 = mybir.dt.bfloat16
AF = mybir.ActivationFunctionType
ALU = mybir.AluOpType
AX = mybir.AxisListType

N_CORES = 8
T = 4096
D = 1024
H = 16
DH = 64
HID = 4096
NRT = 7

NEG = -1.0e30

ENGINES = ("tensor", "vector", "scalar", "gpsimd", "sync")
QUEUES = ("qs", "qg", "qc")
QUEUE_ENGINE = {"qs": "sync", "qg": "gpsimd", "qc": "scalar"}
NSLOT = {"qs": 8, "qg": 8, "qc": 2}
DOMAINS = tuple(ENGINES) + tuple(f"{q}{i}" for q in QUEUES for i in range(NSLOT[q]))


class Step:
    __slots__ = ("eng", "queue", "slot", "emit", "reads", "writes", "sreads",
                 "needs_inc", "inc_val", "waits", "tag")

    def __init__(self, eng, queue, emit, reads, writes, tag="", sreads=()):
        self.eng = eng
        self.queue = queue
        self.slot = None
        self.emit = emit
        self.reads = tuple(reads)
        self.writes = tuple(writes)
        self.sreads = tuple(sreads)
        self.needs_inc = False
        self.inc_val = None
        self.waits = []
        self.tag = tag

    @property
    def domain(self):
        if self.queue is not None:
            return f"{self.queue}{self.slot}"
        return self.eng


class Prog:
    def __init__(self, full_sync=False):
        self.steps = []
        self.full_sync = full_sync
        self._quse = {q: 0 for q in QUEUES}

    def step(self, eng, emit, reads=(), writes=(), tag="", sreads=()):
        self.steps.append(Step(eng, None, emit, reads, writes, tag, sreads))

    def dma(self, queue, emit, reads=(), writes=(), tag=""):
        s = Step(QUEUE_ENGINE[queue], queue, emit, reads, writes, tag)
        s.slot = self._quse[queue] % NSLOT[queue]
        self._quse[queue] += 1
        self.steps.append(s)

    def barrier(self, tag="barrier", clear=True):
        s = Step("barrier", None, None, (), (), tag)
        s.slot = clear
        self.steps.append(s)

    def resolve(self):
        last_write = {}
        readers = {}
        barrier_pending = []
        deps = [None] * len(self.steps)
        for i, s in enumerate(self.steps):
            if s.eng == "barrier":
                deps[i] = list(barrier_pending)
                barrier_pending = [i]
                last_write = {}
                readers = {}
                continue
            barrier_pending.append(i)
            dmap = {}
            for r in s.reads:
                w = last_write.get(r)
                if w is not None:
                    dmap[w] = dmap.get(w, False) or (r in s.sreads)
            for wr in s.writes:
                w = last_write.get(wr)
                if w is not None:
                    dmap.setdefault(w, False)
                for rd in readers.get(wr, ()):
                    dmap.setdefault(rd, False)
            dmap.pop(i, None)
            deps[i] = dmap
            for r in s.reads:
                readers.setdefault(r, []).append(i)
            for wr in s.writes:
                last_write[wr] = i
                readers[wr] = []

        need = [False] * len(self.steps)
        for i, s in enumerate(self.steps):
            me = s.domain if s.eng != "barrier" else None
            dl = deps[i] if isinstance(deps[i], list) else deps[i].items()
            for jj in dl:
                j, always = (jj, False) if isinstance(jj, int) else jj
                p = self.steps[j]
                if p.eng == "barrier":
                    continue
                fs = (self.full_sync is True or
                      (isinstance(self.full_sync, (set, frozenset, tuple)) and
                       (s.eng in self.full_sync if me is not None else False)))
                if me is None or fs or always or p.domain != me or p.queue is not None:
                    need[j] = True
        counters = {}
        for s, nd in zip(self.steps, need):
            if s.eng == "barrier":
                if s.slot:
                    counters = {d: v for d, v in counters.items() if d not in ENGINES}
                continue
            if s.queue is not None:
                d = s.domain
                counters[d] = counters.get(d, 0) + 16
                s.needs_inc = True
                s.inc_val = counters[d]
            elif nd:
                counters[s.eng] = counters.get(s.eng, 0) + 1
                s.needs_inc = True
                s.inc_val = counters[s.eng]

        known = {e: {} for e in ENGINES}
        for i, s in enumerate(self.steps):
            if s.eng == "barrier":
                targets = {}
                for j in deps[i]:
                    p = self.steps[j]
                    if p.eng == "barrier" or not p.needs_inc:
                        continue
                    targets[p.domain] = max(targets.get(p.domain, 0), p.inc_val)
                s.waits = sorted(targets.items())
                if s.slot:
                    for e in ENGINES:
                        for d, v in s.waits:
                            if d not in ENGINES and known[e].get(d, -1) < v:
                                known[e][d] = v
                        for d in ENGINES:
                            known[e].pop(d, None)
                else:
                    for e in ENGINES:
                        for d, v in s.waits:
                            if known[e].get(d, -1) < v:
                                known[e][d] = v
                continue
            kn = known[s.eng]
            targets = {}
            if s.queue is not None and s.inc_val > 16:
                targets[s.domain] = s.inc_val - 16
            dl = deps[i] if isinstance(deps[i], list) else deps[i].items()
            for jj in dl:
                j, always = (jj, False) if isinstance(jj, int) else jj
                p = self.steps[j]
                if p.eng == "barrier" or not p.needs_inc:
                    continue
                d = p.domain
                if (p.queue is None and d == s.eng and not always and not
                        (self.full_sync is True or
                         (isinstance(self.full_sync, (set, frozenset, tuple)) and s.eng in self.full_sync))):
                    continue
                targets[d] = max(targets.get(d, 0), p.inc_val)
            ws = []
            for d, v in sorted(targets.items()):
                if kn.get(d, -1) >= v:
                    continue
                ws.append((d, v))
                kn[d] = v
            s.waits = ws

    def emit_all(self, sems, block):
        per_engine = {e: [] for e in ENGINES}
        nbar = 0
        for s in self.steps:
            if s.eng == "barrier":
                nbar += 1
                for e in ENGINES:
                    per_engine[e].append(s)
            else:
                per_engine[s.eng].append(s)

        bsem = sems["bar"]
        esem = sems["epo"]

        def run_stream(name, eng):
            bar_k = 0
            for s in per_engine[name]:
                for d, v in s.waits:
                    eng.wait_ge(sems[d], v)
                if s.eng == "barrier":
                    if not s.slot:
                        continue
                    bar_k += 1
                    eng.drain()
                    eng.sem_inc(bsem, 1)
                    if name == "gpsimd":
                        eng.wait_ge(bsem, 5 * bar_k)
                        for d in ENGINES:
                            eng.sem_clear(sems[d])
                        eng.sem_inc(esem, 1)
                    eng.wait_ge(esem, bar_k)
                    continue
                inst = s.emit(eng)
                if s.needs_inc:
                    assert inst is not None, f"step {s.tag} returned no instruction"
                    if s.queue is not None:
                        inst.then_inc(sems[s.domain], 16)
                    else:
                        inst.then_inc(sems[s.eng], 1)

        @block.tensor
        def _(eng):
            run_stream("tensor", eng)

        @block.vector
        def _(eng):
            run_stream("vector", eng)

        @block.scalar
        def _(eng):
            run_stream("scalar", eng)

        @block.gpsimd
        def _(eng):
            run_stream("gpsimd", eng)

        @block.sync
        def _(eng):
            run_stream("sync", eng)


def make_consts():
    c = {}
    c["ident_bf"] = np.eye(128, dtype=ml_dtypes.bfloat16)
    c["ident_f32"] = np.eye(128, dtype=np.float32)
    c["ones_bf"] = np.ones((1, 128), dtype=ml_dtypes.bfloat16)
    c["ones_f32"] = np.ones((1, 128), dtype=np.float32)
    c["ones_col"] = np.ones((128, 1), dtype=ml_dtypes.bfloat16)
    ev = np.zeros((128, 2), dtype=np.float32)
    ev[:, 0] = 1e-5
    ev[:, 1] = 1e-30
    c["epsv"] = ev
    qk = np.zeros((128, 2048), dtype=np.float32)
    kq = np.zeros((128, 2048), dtype=np.float32)
    r = np.arange(128)[:, None]
    cc = np.arange(512)[None, :]
    for j in range(4):
        qk[:, j * 512:(j + 1) * 512] = np.where(cc <= j * 128 + r, 0.0, NEG)
        kq[:, j * 512:(j + 1) * 512] = np.where(j * 128 + r <= cc, 0.0, NEG)
    c["amask_qk"] = qk.astype(ml_dtypes.bfloat16)
    c["amask_kq"] = kq.astype(ml_dtypes.bfloat16)
    l = np.arange(128)[:, None]
    j = np.arange(128)[None, :]
    tri = np.zeros((128, 384), dtype=np.float32)
    tri[:, 0:128] = (l <= j)
    tri[:, 128:256] = (j < l)
    tri[:, 256:384] = (j <= l)
    c["tri_bf"] = tri.astype(ml_dtypes.bfloat16)
    c["allones_bf"] = np.ones((128, 128), dtype=ml_dtypes.bfloat16)
    ir = np.zeros((128, 8), dtype=np.float32)
    ir[:, 0:7] = (7.0 - np.arange(7))[None, :]
    c["iota_rev"] = ir
    return c


CONST_SPECS = [
    ("ident_bf", [128, 128], BF16),
    ("ones_bf", [1, 128], BF16),
    ("ones_f32", [1, 128], F32),
    ("ones_col", [128, 1], BF16),
    ("iota_rev", [128, 8], F32),
    ("epsv", [128, 2], F32),
]
PHASE_CONST_SPECS = [
    ("ident_f32", [128, 128], F32),
    ("tri_bf", [128, 384], BF16), ("allones_bf", [128, 128], BF16),
]


class MK:
    def __init__(self, full_sync=("vector", "scalar"), dbg_n=0):
        self.dbg_n = dbg_n
        self.dbg_i = 0
        self.nc = bass.Bass("TRN2", target_bir_lowering=False,
                            detect_race_conditions=(full_sync is True))
        self.es = contextlib.ExitStack()
        self.prog = Prog(full_sync=full_sync)
        nc = self.nc

        if self.dbg_n:
            self.d_dbg = nc.dram_tensor("dbg", [self.dbg_n, 128, 1024], F32,
                                        kind="ExternalOutput").ap()
        self.dx = nc.dram_tensor("x", [T, D], F32, kind="ExternalInput").ap()
        self.dy = nc.dram_tensor("y", [T, D], F32, kind="ExternalOutput").ap()
        dt = lambda n, s: nc.dram_tensor(n, s, F32, kind="ExternalInput").ap()
        self.d_aqkv_w = dt("aqkv_w", [D, 3 * D])
        self.d_aqkv_b = dt("aqkv_b", [3 * D])
        self.d_aproj_w = dt("aproj_w", [D, D])
        self.d_aproj_b = dt("aproj_b", [D])
        self.d_bqkv_w = dt("bqkv_w", [3 * D, D])
        self.d_bg_w = dt("bg_w", [H, D])
        self.d_bg_b = dt("bg_b", [H])
        self.d_bb_w = dt("bb_w", [H, D])
        self.d_bb_b = dt("bb_b", [H])
        self.d_bout_w = dt("bout_w", [D, D])
        self.d_rw = dt("rw", [NRT, D])
        self.d_wg = dt("wg", [NRT, D, HID])
        self.d_wu = dt("wu", [NRT, D, HID])
        self.d_wd = dt("wd", [NRT, HID, D])
        self.d_swg = dt("swg", [1, D, HID])
        self.d_swu = dt("swu", [1, D, HID])
        self.d_swd = dt("swd", [1, HID, D])
        self.d_xres = nc.dram_tensor("x_res", [T, D], F32).ap()

        self.consts = {}
        for nm, shp, dty in CONST_SPECS:
            dap = nc.dram_tensor(nm, shp, dty, kind="ExternalInput").ap()
            sb = self.es.enter_context(nc.sbuf_tensor("c_" + nm, shp, dty))
            self.consts[nm] = sb
            self.prog.dma("qs", (lambda sb=sb, dap=dap: lambda e: e.dma_start(sb[:], dap))(),
                          writes=("c_" + nm,), tag="ldc_" + nm)
        self.d_amask_qk = nc.dram_tensor("amask_qk", [128, 2048], BF16, kind="ExternalInput").ap()
        self.d_amask_kq = nc.dram_tensor("amask_kq", [128, 2048], BF16, kind="ExternalInput").ap()

        self.dconst = {}
        for nm, shp, dty in PHASE_CONST_SPECS:
            self.dconst[nm] = nc.dram_tensor(nm, shp, dty, kind="ExternalInput").ap()
        self.ident_bf = self.consts["ident_bf"]
        self.ones_bf = self.consts["ones_bf"]
        self.iota_rev = self.consts["iota_rev"]

        self.pf = [self.es.enter_context(nc.psum_tensor(f"pf{i}", [128, 512], F32))
                   for i in range(8)]

        self.ln_xc = self.es.enter_context(nc.sbuf_tensor("ln_xc", [128, D], F32))
        self.ln_st = self.es.enter_context(nc.sbuf_tensor("ln_st", [128, 8], F32))

        self.rwT_hi = self.es.enter_context(nc.sbuf_tensor("rwT_hi", [128, 8, 8], BF16))
        self.rwT_lo = self.es.enter_context(nc.sbuf_tensor("rwT_lo", [128, 8, 8], BF16))

    def dump(self, ap, reads, tag="dbg"):
        if not self.dbg_n or self.dbg_i >= self.dbg_n:
            return
        i = self.dbg_i
        self.dbg_i += 1
        w = ap.shape[-1] if len(ap.shape) == 2 else 1024
        np_parts = ap.shape[0]
        self.prog.dma("qg", (lambda ap=ap, i=i, np_parts=np_parts, w=w: lambda e: e.dma_start(
            self.d_dbg[i, 0:np_parts, 0:w], ap))(),
            reads=tuple(reads), writes=(f"dbg{i}",), tag=f"{tag}{i}")

    def pbank(self, i, lo=0, hi=512):
        return self.pf[i][:, lo:hi]

    def pbank_bf(self, i, lo=0, hi=128):
        return self.pf[i][:].bitcast(BF16)[:, lo:hi]

    def ln(self, xt_ap, xt_res, xn_ap, xn_res, xlo_ap=None, xlo_res=None, tag=""):
        p = self.prog
        st = self.ln_st
        xc = self.ln_xc
        p.step("vector", lambda e: e.tensor_reduce(st[:, 0:1], xt_ap, axis=AX.X, op=ALU.add),
               reads=(xt_res,), writes=("ln_st",), tag=tag + "sum")
        p.step("scalar", lambda e: e.mul(st[:, 1:2], st[:, 0:1], -1.0 / D),
               reads=("ln_st",), writes=("ln_st",), tag=tag + "nm")
        p.step("vector", lambda e: e.tensor_scalar_add(xc[:], xt_ap, st[:, 1:2]),
               reads=(xt_res, "ln_st"), writes=("ln_xc",), tag=tag + "xc",
               sreads=("ln_st",))
        p.step("scalar", lambda e: e.activation(
            xn_ap, xc[:], AF.Square, accum_out=st[:, 2:3]),
            reads=("ln_xc",), writes=(xn_res, "ln_st"), tag=tag + "ss")
        p.step("scalar", lambda e: e.activation(st[:, 3:4], st[:, 2:3], AF.Sqrt,
                                                scale=1.0 / D,
                                                bias=self.consts["epsv"][:, 0:1]),
               reads=("ln_st", "c_epsv"), writes=("ln_st",), tag=tag + "std")
        p.step("vector", lambda e: e.reciprocal(st[:, 4:5], st[:, 3:4]),
               reads=("ln_st",), writes=("ln_st",), tag=tag + "rstd")
        p.step("vector", lambda e: e.tensor_scalar_mul(xn_ap, xc[:], st[:, 4:5]),
               reads=("ln_xc", "ln_st"), writes=(xn_res,), tag=tag + "xn",
               sreads=("ln_st",))
        if xlo_ap is not None:
            p.step("vector", lambda e: e.scalar_tensor_tensor(
                xlo_ap, xc[:], st[:, 4:5], xn_ap, op0=ALU.mult, op1=ALU.subtract),
                reads=("ln_xc", "ln_st", xn_res), writes=(xlo_res,), tag=tag + "xlo",
                sreads=("ln_st",))

    def tr128(self, src_ap_fn, src_res, dst_ap_fn, dst_res, n, banks, tag=""):
        p = self.prog
        for c in range(n):
            b = banks[c % len(banks)]
            pap = self.pbank_bf(b)
            p.step("tensor", (lambda c=c, pap=pap: lambda e: e.transpose(
                pap, src_ap_fn(c), self.ident_bf[:]))(),
                reads=(src_res, "c_ident_bf"), writes=(f"pf{b}",), tag=f"{tag}T{c}")
            p.step("scalar", (lambda c=c, pap=pap: lambda e: e.copy(dst_ap_fn(c), pap))(),
                   reads=(f"pf{b}",), writes=(dst_res,), tag=f"{tag}E{c}")

    def prelude_router(self):
        nc, p = self.nc, self.prog
        with contextlib.ExitStack() as ps:
            rwf = ps.enter_context(nc.sbuf_tensor("pr_rwf", [128, D], F32))
            rwh = ps.enter_context(nc.sbuf_tensor("pr_rwh", [128, D], BF16))
            rwl = ps.enter_context(nc.sbuf_tensor("pr_rwl", [128, D], BF16))
            p.step("vector", lambda e: e.memset(rwf[:], 0.0), writes=("pr_rwf",), tag="rw0")
            p.step("vector", lambda e: e.memset(self.rwT_hi[:].rearrange("p a b -> p (a b)"), 0.0),
                   writes=("rwT_hi",), tag="rwh0")
            p.step("vector", lambda e: e.memset(self.rwT_lo[:].rearrange("p a b -> p (a b)"), 0.0),
                   writes=("rwT_lo",), tag="rwl0")
            p.dma("qs", lambda e: e.dma_start(rwf[0:NRT, :], self.d_rw),
                  writes=("pr_rwf",), tag="rwld")
            p.step("vector", lambda e: e.tensor_copy(rwh[:], rwf[:]),
                   reads=("pr_rwf",), writes=("pr_rwh",), tag="rwh")
            p.step("vector", lambda e: e.tensor_tensor(rwl[:], rwf[:], rwh[:], op=ALU.subtract),
                   reads=("pr_rwf", "pr_rwh"), writes=("pr_rwl",), tag="rwl")
            for (src, sres, dst, dres) in [(rwh, "pr_rwh", self.rwT_hi, "rwT_hi"),
                                           (rwl, "pr_rwl", self.rwT_lo, "rwT_lo")]:
                for c in range(8):
                    b = c % 2
                    pap = self.pbank_bf(b)
                    p.step("tensor", (lambda src=src, c=c, pap=pap: lambda e: e.transpose(
                        pap, src[:, c * 128:(c + 1) * 128], self.ident_bf[:]))(),
                        reads=(sres, "c_ident_bf"), writes=(f"pf{b}",), tag=f"rwT{c}")
                    p.step("scalar", (lambda dst=dst, c=c, pap=pap: lambda e: e.copy(
                        dst[:, c, 0:NRT], pap[:, 0:NRT]))(),
                        reads=(f"pf{b}",), writes=(dres,), tag=f"rwTe{c}")
            p.barrier("prelude_router_end")

    def phase_attn_a(self, src_dram, dst_dram):
        nc, p = self.nc, self.prog
        ps = contextlib.ExitStack()
        amq = ps.enter_context(nc.sbuf_tensor("a_amq", [128, 2048], BF16))
        amk = ps.enter_context(nc.sbuf_tensor("a_amk", [128, 2048], BF16))
        idf = ps.enter_context(nc.sbuf_tensor("a_idf", [128, 128], F32))
        p.dma("qs", lambda e: e.dma_start(amq[:], self.d_amask_qk), writes=("a_amq",), tag="a_amq")
        p.dma("qs", lambda e: e.dma_start(amk[:], self.d_amask_kq), writes=("a_amk",), tag="a_amk")
        p.dma("qs", lambda e: e.dma_start(idf[:], self.dconst["ident_f32"]),
              writes=("c_ident_f32",), tag="a_idf")

        yT = ps.enter_context(nc.sbuf_tensor("a_yT", [128, 8, T], BF16))
        kT = [ps.enter_context(nc.sbuf_tensor(f"a_kT{i}", [65, T], BF16)) for i in range(2)]
        qT = [ps.enter_context(nc.sbuf_tensor(f"a_qT{i}", [65, T], BF16)) for i in range(2)]
        vsb = ps.enter_context(nc.sbuf_tensor("a_v", [128, 32, 2, 64], BF16))
        xt = [ps.enter_context(nc.sbuf_tensor(f"a_xt{i}", [128, D], F32)) for i in range(2)]
        xn = ps.enter_context(nc.sbuf_tensor("a_xn", [128, D], BF16))
        wgr = ps.enter_context(nc.sbuf_tensor("a_wgr", [128, 8, 384], BF16))
        bqc = ps.enter_context(nc.sbuf_tensor("a_bqc", [64, 48], F32))
        bqr = ps.enter_context(nc.sbuf_tensor("a_bqr", [1, D], BF16))
        ptile = [ps.enter_context(nc.sbuf_tensor(f"a_pt{i}", [128, 512], BF16)) for i in range(2)]
        mxt = ps.enter_context(nc.sbuf_tensor("a_mxt", [128, 66], F32))
        mtmp = ps.enter_context(nc.sbuf_tensor("a_mtmp", [128, 1], F32))
        linv = ps.enter_context(nc.sbuf_tensor("a_linv", [1, 512], F32))
        ottmp = ps.enter_context(nc.sbuf_tensor("a_ottmp", [128, 512], BF16))
        xnt_stack = contextlib.ExitStack()
        xnt = xnt_stack.enter_context(nc.sbuf_tensor("a_xnt", [128, 8, T], BF16))

        p.dma("qs", lambda e: e.dma_start(bqc[:], self.d_aqkv_b.rearrange("(c p) -> p c", p=64),
                                          allow_slow_non_contiguous=True),
              writes=("a_bqc",), tag="a_bqc")
        p.dma("qg", lambda e: e.dma_start(bqr[:], self.d_aqkv_b[2 * D:3 * D].unsqueeze(0)),
              writes=("a_bqr",), tag="a_bqr")
        p.step("vector", lambda e: e.memset(mxt[:], 0.0), writes=("a_mxt",), tag="a_mxt0")
        for hs in range(2):
            p.step("vector", (lambda hs=hs: lambda e: e.memset(kT[hs][64:65, :], 1.0))(),
                   writes=(f"a_kT{hs}r64",), tag=f"a_k1_{hs}")

        for i in range(32):
            b = i % 2
            p.dma("qs", (lambda i=i, b=b: lambda e: e.dma_start(
                xt[b][:], src_dram[i * 128:(i + 1) * 128, :]))(),
                writes=(f"a_xt{b}",), tag=f"a0ld{i}")
            self.ln(xt[b][:], f"a_xt{b}", xn[:], "a_xn", tag=f"a0ln{i}_")
            self.tr128(lambda c: xn[:, c * 128:(c + 1) * 128], "a_xn",
                       (lambda c, i=i: xnt[:, c, i * 128:(i + 1) * 128]),
                       "a_xnt", 8, [6, 7], tag=f"a0tr{i}_")

        for g in range(8):
            for mat in range(3):
                col0 = mat * D + g * 128
                p.dma("qg", (lambda mat=mat, col0=col0: lambda e: e.dma_start(
                    wgr[:, :, mat * 128:(mat + 1) * 128],
                    self.d_aqkv_w.rearrange("(c p) n -> p c n", p=128)[:, :, col0:col0 + 128]))(),
                    writes=("a_wgr",), tag=f"a1w{g}_{mat}")
            for tb in range(8):
                t0 = tb * 512
                for mat in range(2):
                    for hs in range(2):
                        bk = (tb * 4 + mat * 2 + hs) % 4
                        for c in range(8):
                            p.step("tensor", (lambda mat=mat, hs=hs, c=c, t0=t0, bk=bk:
                                   lambda e: e.matmul(
                                       self.pbank(bk)[0:64, :],
                                       wgr[:, c, mat * 128 + hs * 64:mat * 128 + (hs + 1) * 64],
                                       xnt[:, c, t0:t0 + 512],
                                       start=(c == 0), stop=(c == 7)))(),
                                   reads=("a_wgr", "a_xnt"), writes=(f"pf{bk}",),
                                   tag=f"a1qk{g}_{tb}_{mat}_{hs}")
                        dst = qT if mat == 0 else kT
                        dres = "a_qT" if mat == 0 else "a_kT"
                        bcol = (mat * 1024 + g * 128 + hs * 64) // 64
                        p.step("scalar", (lambda dst=dst, hs=hs, t0=t0, bk=bk, bcol=bcol:
                               lambda e: e.activation(
                                   dst[hs][0:64, t0:t0 + 512],
                                   self.pf[bk][0:64, 0:512],
                                   AF.Identity, bias=bqc[0:64, bcol:bcol + 1]))(),
                               reads=(f"pf{bk}", "a_bqc"), writes=(f"{dres}{hs}",),
                               tag=f"a1qke{g}_{tb}_{mat}_{hs}", sreads=("a_bqc",))
                for ts in range(4):
                    bk = 4 + ts % 2
                    p.step("tensor", (lambda g=g, bk=bk: lambda e: e.matmul(
                        self.pbank(bk, 0, 128), self.ones_bf[0:1, 0:128],
                        bqr[0:1, g * 128:(g + 1) * 128],
                        start=True, stop=False))(),
                        reads=("a_bqr", "c_ones_bf"), writes=(f"pf{bk}",),
                        tag=f"a1vb{g}_{tb}_{ts}")
                    for c in range(8):
                        p.step("tensor", (lambda c=c, t0=t0, ts=ts, bk=bk: lambda e:
                               e.matmul(self.pbank(bk, 0, 128),
                                        xnt[:, c, t0 + ts * 128:t0 + (ts + 1) * 128],
                                        wgr[:, c, 256:384],
                                        start=False, stop=(c == 7)))(),
                               reads=("a_wgr", "a_xnt"), writes=(f"pf{bk}",),
                               tag=f"a1v{g}_{tb}_{ts}_{c}")
                    for hs in range(2):
                        p.step("scalar", (lambda tb=tb, ts=ts, hs=hs, bk=bk: lambda e: e.copy(
                            vsb[:, tb * 4 + ts, hs, :],
                            self.pf[bk][:, hs * 64:(hs + 1) * 64]))(),
                            reads=(f"pf{bk}",), writes=("a_v",), tag=f"a1ve{g}_{tb}_{ts}_{hs}")
            for hs in range(2):
                h = 2 * g + hs
                for qb in range(8):
                    q0 = qb * 512
                    for qt in range(4):
                        for kb in range(qb + 1):
                            bk = (qt + kb) % 2
                            p.step("tensor", (lambda hs=hs, q0=q0, qt=qt, kb=kb, bk=bk:
                                   lambda e: e.matmul(
                                       self.pbank(bk),
                                       qT[hs][0:64, q0 + qt * 128:q0 + (qt + 1) * 128],
                                       kT[hs][0:64, kb * 512:(kb + 1) * 512],
                                       start=True, stop=True))(),
                                   reads=(f"a_qT{hs}", f"a_kT{hs}"), writes=(f"pf{bk}",),
                                   tag=f"aS1_{h}_{qb}_{qt}_{kb}")
                            if kb == qb:
                                p.step("vector", (lambda qt=qt, bk=bk: lambda e: e.tensor_tensor(
                                    self.pbank(bk), self.pbank(bk),
                                    amq[:, qt * 512:(qt + 1) * 512], op=ALU.add))(),
                                    reads=(f"pf{bk}", "a_amq"), writes=(f"pf{bk}",),
                                    tag=f"aS1m_{h}_{qb}_{qt}")
                            if kb == 0:
                                p.step("vector", (lambda bk=bk: lambda e: e.reduce_max(
                                    mxt[:, 64:65], self.pbank(bk), axis=AX.X))(),
                                    reads=(f"pf{bk}",), writes=("a_mxt",),
                                    tag=f"aS1r_{h}_{qb}_{qt}_{kb}")
                            else:
                                p.step("vector", (lambda bk=bk: lambda e: e.reduce_max(
                                    mtmp[:], self.pbank(bk), axis=AX.X))(),
                                    reads=(f"pf{bk}",), writes=("a_mtmp",),
                                    tag=f"aS1r_{h}_{qb}_{qt}_{kb}")
                                p.step("vector", lambda e: e.tensor_tensor(
                                    mxt[:, 64:65], mxt[:, 64:65], mtmp[:], op=ALU.max),
                                    reads=("a_mtmp",), writes=("a_mxt",),
                                    tag=f"aS1x_{h}_{qb}_{qt}_{kb}")
                        p.step("tensor", lambda e: e.transpose(
                            self.pbank(6, 0, 128)[0:65, :], mxt[:, 0:65],
                            idf[:]),
                            reads=("a_mxt", "c_ident_f32"), writes=("pf6",),
                            tag=f"aS1t_{h}_{qb}_{qt}")
                        p.step("scalar", (lambda hs=hs, q0=q0, qt=qt: lambda e: e.mul(
                            qT[hs][64:65, q0 + qt * 128:q0 + (qt + 1) * 128],
                            self.pf[6][64:65, 0:128], -1.0))(),
                            reads=("pf6",), writes=(f"a_qT{hs}r64",),
                            tag=f"aS1w_{h}_{qb}_{qt}")
                    nkt = 4 * (qb + 1)
                    for kt in range(nkt):
                        bk = 2 + kt % 2
                        p.step("tensor", (lambda hs=hs, q0=q0, kt=kt, bk=bk: lambda e: e.matmul(
                            self.pbank(bk), kT[hs][0:65, kt * 128:(kt + 1) * 128],
                            qT[hs][0:65, q0:q0 + 512], start=True, stop=True))(),
                            reads=(f"a_qT{hs}", f"a_qT{hs}r64", f"a_kT{hs}", f"a_kT{hs}r64"),
                            writes=(f"pf{bk}",), tag=f"aS2_{h}_{qb}_{kt}")
                        if kt >= 4 * qb:
                            j = kt - 4 * qb
                            p.step("vector", (lambda j=j, bk=bk: lambda e: e.tensor_tensor(
                                self.pbank(bk), self.pbank(bk),
                                amk[:, j * 512:(j + 1) * 512], op=ALU.add))(),
                                reads=(f"pf{bk}", "a_amk"), writes=(f"pf{bk}",),
                                tag=f"aS2m_{h}_{qb}_{kt}")
                        pt = kt % 2
                        p.step("scalar", (lambda pt=pt, bk=bk: lambda e: e.activation(
                            ptile[pt][:], self.pbank(bk), AF.Exp, scale=0.125))(),
                            reads=(f"pf{bk}",), writes=(f"a_pt{pt}",),
                            tag=f"aS2e_{h}_{qb}_{kt}")
                        p.step("tensor", (lambda kt=kt, pt=pt, nkt=nkt: lambda e: e.matmul(
                            self.pf[4][:, 0:512],
                            vsb[:, kt, :, :].rearrange("p a b -> p (a b)"),
                            ptile[pt][:],
                            start=(kt == 0), stop=(kt == nkt - 1)))(),
                            reads=("a_v", f"a_pt{pt}"), writes=("pf4",),
                            tag=f"aS2o_{h}_{qb}_{kt}")
                        p.step("tensor", (lambda kt=kt, pt=pt, nkt=nkt: lambda e: e.matmul(
                            self.pf[5][0:1, 0:512], self.consts["ones_col"][:], ptile[pt][:],
                            start=(kt == 0), stop=(kt == nkt - 1)))(),
                            reads=("c_ones_col", f"a_pt{pt}"), writes=("pf5",),
                            tag=f"aS2l_{h}_{qb}_{kt}")
                    p.step("vector", lambda e: e.reciprocal(linv[:], self.pf[5][0:1, 0:512]),
                           reads=("pf5",), writes=("a_linv",), tag=f"aL_{h}_{qb}")
                    p.step("tensor", (lambda hs=hs: lambda e: e.matmul(
                        self.pf[5][:, 0:512],
                        self.consts["ones_f32"][0:1, 0:128],
                        linv[:], start=True, stop=True))(),
                        reads=("a_linv", "c_ones_f32"), writes=("pf5",), tag=f"aLb_{h}_{qb}")
                    p.step("scalar", (lambda hs=hs: lambda e: e.copy(
                        ottmp[hs * 64:(hs + 1) * 64, :],
                        self.pf[4][hs * 64:(hs + 1) * 64, 0:512]))(),
                        reads=("pf4",), writes=("a_ottmp",), tag=f"aOe_{h}_{qb}")
                    p.step("vector", (lambda hs=hs, g=g, q0=q0: lambda e: e.tensor_tensor(
                        yT[hs * 64:(hs + 1) * 64, g, q0:q0 + 512],
                        ottmp[hs * 64:(hs + 1) * 64, :],
                        self.pf[5][hs * 64:(hs + 1) * 64, 0:512], op=ALU.mult))(),
                        reads=("a_ottmp", "pf5"), writes=("a_yT",),
                        tag=f"aY_{h}_{qb}")

        p.barrier("a_proj_b", clear=False)
        xnt_stack.close()
        wpj = ps.enter_context(nc.sbuf_tensor("a_wpj", [128, 8, D], BF16))
        bpr = ps.enter_context(nc.sbuf_tensor("a_bpr", [1, D], BF16))
        xo = [ps.enter_context(nc.sbuf_tensor(f"a_xo{i}", [128, D], F32)) for i in range(2)]
        p.dma("qg", lambda e: e.dma_start(
            wpj[:], self.d_aproj_w.rearrange("(c p) n -> p c n", p=128)),
            writes=("a_wpj",), tag="a_wpj")
        p.dma("qg", lambda e: e.dma_start(bpr[:], self.d_aproj_b.unsqueeze(0)),
              writes=("a_bpr",), tag="a_bpr")
        for ti in range(32):
            b = ti % 2
            p.dma("qs", (lambda ti=ti, b=b: lambda e: e.dma_start(
                xt[b][:], src_dram[ti * 128:(ti + 1) * 128, :]))(),
                writes=(f"a_xt{b}",), tag=f"apld{ti}")
            for half in range(2):
                bk = 2 * b + half
                p.step("tensor", (lambda half=half, bk=bk: lambda e: e.matmul(
                    self.pbank(bk), self.ones_bf[0:1, 0:128],
                    bpr[0:1, half * 512:(half + 1) * 512], start=True, stop=False))(),
                    reads=("a_bpr", "c_ones_bf"), writes=(f"pf{bk}",), tag=f"apb{ti}_{half}")
                for c in range(8):
                    p.step("tensor", (lambda ti=ti, half=half, c=c, bk=bk: lambda e: e.matmul(
                        self.pbank(bk), yT[:, c, ti * 128:(ti + 1) * 128],
                        wpj[:, c, half * 512:(half + 1) * 512],
                        start=False, stop=(c == 7)))(),
                        reads=("a_yT", "a_wpj"), writes=(f"pf{bk}",), tag=f"ap{ti}_{half}_{c}")
                p.step("vector", (lambda ti=ti, b=b, half=half, bk=bk: lambda e: e.tensor_tensor(
                    xo[b][:, half * 512:(half + 1) * 512], xt[b][:, half * 512:(half + 1) * 512],
                    self.pbank(bk), op=ALU.add))(),
                    reads=(f"a_xt{b}", f"pf{bk}"), writes=(f"a_xo{b}",), tag=f"apadd{ti}_{half}")
            p.dma("qs", (lambda ti=ti, b=b: lambda e: e.dma_start(
                dst_dram[ti * 128:(ti + 1) * 128, :], xo[b][:]))(),
                reads=(f"a_xo{b}",), writes=(f"dst{ti}",), tag=f"apst{ti}")
        p.barrier("a_end")
        ps.close()

    def phase_ffn(self, src_dram, dst_dram, pfx):
        nc, p = self.nc, self.prog
        ps = contextlib.ExitStack()
        NTB = 4
        TBS = 1024
        xacc = ps.enter_context(nc.sbuf_tensor(pfx + "xacc", [128, 8, 1024], F32))
        xnt = ps.enter_context(nc.sbuf_tensor(pfx + "xnt", [128, 8, 1024], BF16))
        xlt = ps.enter_context(nc.sbuf_tensor(pfx + "xlt", [128, 8, 128], BF16))
        xnb = ps.enter_context(nc.sbuf_tensor(pfx + "xnb", [128, D], BF16))
        xlb = ps.enter_context(nc.sbuf_tensor(pfx + "xlb", [128, D], BF16))
        hsb = ps.enter_context(nc.sbuf_tensor(pfx + "h", [128, 32, 1024], BF16))
        wdsb = ps.enter_context(nc.sbuf_tensor(pfx + "wd", [128, 32, 1024], BF16))
        wgu = [ps.enter_context(nc.sbuf_tensor(f"{pfx}wgu{i}", [128, 8, 256], BF16))
               for i in range(2)]
        sg = [ps.enter_context(nc.sbuf_tensor(f"{pfx}sg{i}", [128, 512], BF16))
              for i in range(4)]
        logits = ps.enter_context(nc.sbuf_tensor(pfx + "lg", [128, 8], F32))
        cmb = ps.enter_context(nc.sbuf_tensor(pfx + "cmb", [128, 8, 8], F32))
        rsc = ps.enter_context(nc.sbuf_tensor(pfx + "rsc", [128, 8], F32))

        for tb in range(NTB):
            r0 = tb * TBS
            p.dma("qs", (lambda r0=r0: lambda e: e.dma_start(
                xacc[:],
                src_dram[r0:r0 + TBS, :].rearrange("(i p) n -> p i n", p=128)))(),
                writes=(pfx + "xacc",), tag=f"{pfx}xld{tb}")
            for ti in range(8):
                self.ln(xacc[:, ti, :], pfx + "xacc", xnb[:], pfx + "xnb",
                        xlb[:], pfx + "xlb", tag=f"{pfx}ln{tb}_{ti}_")
                if tb == 0 and ti == 0:
                    self.dump(xnb[:], (pfx + "xnb",), "d_xnb")
                    self.dump(self.ln_st[:], ("ln_st",), "d_st")
                self.tr128(lambda c: xnb[:, c * 128:(c + 1) * 128], pfx + "xnb",
                           (lambda c, ti=ti: xnt[:, c, ti * 128:(ti + 1) * 128]),
                           pfx + "xnt", 8, [0, 1], tag=f"{pfx}trh{tb}_{ti}_")
                self.tr128(lambda c: xlb[:, c * 128:(c + 1) * 128], pfx + "xlb",
                           (lambda c: xlt[:, c, :]),
                           pfx + "xlt", 8, [0, 1], tag=f"{pfx}trl{tb}_{ti}_")
                bk = 2 + ti % 2
                terms = []
                for (lt, lres, rt, rres) in [(xnt, pfx + "xnt", self.rwT_hi, "rwT_hi"),
                                             (xlt, pfx + "xlt", self.rwT_hi, "rwT_hi"),
                                             (xnt, pfx + "xnt", self.rwT_lo, "rwT_lo")]:
                    for c in range(8):
                        lap = (lt[:, c, ti * 128:(ti + 1) * 128] if lt is xnt
                               else lt[:, c, :])
                        terms.append((lap, lres, rt, rres, c))
                for k, (lap, lres, rt, rres, c) in enumerate(terms):
                    p.step("tensor", (lambda lap=lap, rt=rt, c=c, bk=bk,
                           st=(k == 0), sp=(k == len(terms) - 1):
                           lambda e: e.matmul(self.pbank(bk, 0, 8), lap, rt[:, c, 0:8],
                                              start=st, stop=sp))(),
                           reads=(lres, rres), writes=(f"pf{bk}",),
                           tag=f"{pfx}rt{tb}_{ti}_{k}")
                p.step("scalar", (lambda bk=bk: lambda e: e.copy(logits[:], self.pf[bk][:, 0:8]))(),
                       reads=(f"pf{bk}",), writes=(pfx + "lg",), tag=f"{pfx}rte{tb}_{ti}")
                if tb == 0 and ti == 0:
                    self.dump(logits[:], (pfx + "lg",), "d_lg")
                self._router_select(pfx, tb, ti, logits, rsc, cmb)
                if tb == 0 and ti == 0:
                    self.dump(cmb[:].rearrange("p a b -> p (a b)"), (pfx + "cmb",), "d_cmb")
            for ei in range(8):
                self._expert(pfx, tb, ei, xnt, hsb, wdsb, wgu, sg, xacc, cmb)
                if tb == 0 and ei in (0, 7):
                    self.dump(hsb[:, 0, :], (pfx + "h",), "d_h")
                    self.dump(xacc[:, 0, :], (pfx + "xacc",), "d_xacc")
            p.dma("qs", (lambda r0=r0: lambda e: e.dma_start(
                dst_dram[r0:r0 + TBS, :].rearrange("(i p) n -> p i n", p=128), xacc[:]))(),
                reads=(pfx + "xacc",), writes=(f"{pfx}st{tb}",), tag=f"{pfx}xst{tb}")
        p.barrier(pfx + "end")
        ps.close()

    def _router_select(self, pfx, tb, ti, logits, rsc, cmb):
        p = self.prog
        LG, RS = pfx + "lg", pfx + "rsc"
        p.step("vector", lambda e: e.reduce_max(rsc[:, 0:1], logits[:, 0:NRT], axis=AX.X),
               reads=(LG,), writes=(RS,), tag=f"{pfx}s1{tb}{ti}")
        p.step("vector", lambda e: e.tensor_scalar(
            self._rtmp[:, 0:NRT], logits[:, 0:NRT], rsc[:, 0:1], None, op0=ALU.is_equal),
            reads=(LG, RS), writes=("rtmp",), tag=f"{pfx}s2{tb}{ti}", sreads=(RS,))
        p.step("vector", lambda e: e.tensor_tensor(
            self._rtmp[:, 8:8 + NRT], self._rtmp[:, 0:NRT], self.iota_rev[:, 0:NRT], op=ALU.mult),
            reads=("rtmp", "c_iota_rev"), writes=("rtmp",), tag=f"{pfx}s3{tb}{ti}")
        p.step("vector", lambda e: e.reduce_max(rsc[:, 1:2], self._rtmp[:, 8:8 + NRT], axis=AX.X),
               reads=("rtmp",), writes=(RS,), tag=f"{pfx}s4{tb}{ti}")
        p.step("vector", lambda e: e.tensor_scalar(
            self._rtmp[:, 0:NRT], self.iota_rev[:, 0:NRT], rsc[:, 1:2], None, op0=ALU.is_equal),
            reads=("c_iota_rev", RS), writes=("rtmp",), tag=f"{pfx}s5{tb}{ti}",
            sreads=(RS,))
        p.step("vector", lambda e: e.scalar_tensor_tensor(
            self._rtmp[:, 16:16 + NRT], self._rtmp[:, 0:NRT], NEG, logits[:, 0:NRT],
            op0=ALU.mult, op1=ALU.add),
            reads=("rtmp", LG), writes=("rtmp",), tag=f"{pfx}s6{tb}{ti}")
        p.step("vector", lambda e: e.reduce_max(rsc[:, 2:3], self._rtmp[:, 16:16 + NRT], axis=AX.X),
               reads=("rtmp",), writes=(RS,), tag=f"{pfx}s7{tb}{ti}")
        p.step("vector", lambda e: e.tensor_scalar(
            self._rtmp[:, 24:24 + NRT], self._rtmp[:, 16:16 + NRT], rsc[:, 2:3], None,
            op0=ALU.is_equal),
            reads=("rtmp", RS), writes=("rtmp",), tag=f"{pfx}s8{tb}{ti}",
            sreads=(RS,))
        p.step("vector", lambda e: e.tensor_tensor(
            rsc[:, 3:4], rsc[:, 0:1], rsc[:, 2:3], op=ALU.subtract),
            reads=(RS,), writes=(RS,), tag=f"{pfx}s9{tb}{ti}")
        p.step("scalar", lambda e: e.activation(rsc[:, 4:5], rsc[:, 3:4], AF.Sigmoid),
               reads=(RS,), writes=(RS,), tag=f"{pfx}sa{tb}{ti}")
        p.step("scalar", lambda e: e.activation(rsc[:, 5:6], rsc[:, 3:4], AF.Sigmoid, scale=-1.0),
               reads=(RS,), writes=(RS,), tag=f"{pfx}sb{tb}{ti}")
        p.step("vector", lambda e: e.tensor_scalar_mul(
            self._rtmp[:, 0:NRT], self._rtmp[:, 0:NRT], rsc[:, 4:5]),
            reads=("rtmp", RS), writes=("rtmp",), tag=f"{pfx}sc{tb}{ti}", sreads=(RS,))
        p.step("vector", (lambda ti=ti: lambda e: e.scalar_tensor_tensor(
            cmb[:, ti, 0:NRT], self._rtmp[:, 24:24 + NRT], rsc[:, 5:6], self._rtmp[:, 0:NRT],
            op0=ALU.mult, op1=ALU.add))(),
            reads=("rtmp", RS), writes=(pfx + "cmb",), tag=f"{pfx}sd{tb}{ti}",
            sreads=(RS,))

    def _expert(self, pfx, tb, ei, xnt, hsb, wdsb, wgu, sg, xacc, cmb):
        nc, p = self.nc, self.prog
        shared = (ei == NRT)
        dwg = self.d_swg if shared else self.d_wg
        dwu = self.d_swu if shared else self.d_wu
        dwd = self.d_swd if shared else self.d_wd
        eidx = 0 if shared else ei
        for mg in range(8):
            p.dma("qg", (lambda dwd=dwd, eidx=eidx, mg=mg: lambda e: e.dma_start(
                wdsb[:, mg * 4:(mg + 1) * 4, :],
                dwd[eidx].rearrange("(m p) n -> p m n", p=128)[:, mg * 4:(mg + 1) * 4, :]))(),
                writes=(f"{pfx}wd",), tag=f"{pfx}wd{tb}_{ei}_{mg}")
        for m in range(32):
            sl = m % 2
            p.dma("qg", (lambda dwg=dwg, eidx=eidx, m=m, sl=sl: lambda e: e.dma_start(
                wgu[sl][:, :, 0:128],
                dwg[eidx].rearrange("(c p) n -> p c n", p=128)[:, :, m * 128:(m + 1) * 128]))(),
                writes=(f"{pfx}wgu{sl}g",), tag=f"{pfx}lg{tb}_{ei}_{m}")
            p.dma("qg", (lambda dwu=dwu, eidx=eidx, m=m, sl=sl: lambda e: e.dma_start(
                wgu[sl][:, :, 128:256],
                dwu[eidx].rearrange("(c p) n -> p c n", p=128)[:, :, m * 128:(m + 1) * 128]))(),
                writes=(f"{pfx}wgu{sl}u",), tag=f"{pfx}lu{tb}_{ei}_{m}")
            for th in range(2):
                bg = (m % 2) * 4 + th
                bu = (m % 2) * 4 + 2 + th
                for c in range(8):
                    p.step("tensor", (lambda m=m, sl=sl, th=th, c=c, bg=bg: lambda e: e.matmul(
                        self.pbank(bg), wgu[sl][:, c, 0:128],
                        xnt[:, c, th * 512:(th + 1) * 512],
                        start=(c == 0), stop=(c == 7)))(),
                        reads=(f"{pfx}wgu{sl}g", pfx + "xnt"), writes=(f"pf{bg}",),
                        tag=f"{pfx}g{tb}_{ei}_{m}_{th}")
                for c in range(8):
                    p.step("tensor", (lambda m=m, sl=sl, th=th, c=c, bu=bu: lambda e: e.matmul(
                        self.pbank(bu), wgu[sl][:, c, 128:256],
                        xnt[:, c, th * 512:(th + 1) * 512],
                        start=(c == 0), stop=(c == 7)))(),
                        reads=(f"{pfx}wgu{sl}u", pfx + "xnt"), writes=(f"pf{bu}",),
                        tag=f"{pfx}u{tb}_{ei}_{m}_{th}")
                si = (m % 2) * 2 + th
                p.step("scalar", (lambda si=si, bg=bg: lambda e: e.activation(
                    sg[si][:], self.pbank(bg), AF.Silu))(),
                    reads=(f"pf{bg}",), writes=(f"{pfx}sg{si}",),
                    tag=f"{pfx}si{tb}_{ei}_{m}_{th}")
                p.step("vector", (lambda m=m, th=th, si=si, bu=bu: lambda e: e.tensor_tensor(
                    hsb[:, m, th * 512:(th + 1) * 512], sg[si][:], self.pbank(bu),
                    op=ALU.mult))(),
                    reads=(f"{pfx}sg{si}", f"pf{bu}"), writes=(pfx + "h",),
                    tag=f"{pfx}hm{tb}_{ei}_{m}_{th}")
        for tg in range(4):
            for m in range(32):
                for tt in range(2):
                    for dmh in range(2):
                        bk = (tg % 2) * 4 + tt * 2 + dmh
                        p.step("tensor", (lambda m=m, tg=tg, tt=tt, dmh=dmh, bk=bk:
                               lambda e: e.matmul(
                                   self.pbank(bk),
                                   hsb[:, m, tg * 256 + tt * 128:tg * 256 + (tt + 1) * 128],
                                   wdsb[:, m, dmh * 512:(dmh + 1) * 512],
                                   start=(m == 0), stop=(m == 31)))(),
                               reads=(pfx + "h", f"{pfx}wd"), writes=(f"pf{bk}",),
                               tag=f"{pfx}d{tb}_{ei}_{tg}_{m}_{tt}_{dmh}")
            for tt in range(2):
                ti = tg * 2 + tt
                for dmh in range(2):
                    bk = (tg % 2) * 4 + tt * 2 + dmh
                    if shared:
                        p.step("vector", (lambda ti=ti, dmh=dmh, bk=bk: lambda e: e.tensor_tensor(
                            xacc[:, ti, dmh * 512:(dmh + 1) * 512],
                            xacc[:, ti, dmh * 512:(dmh + 1) * 512],
                            self.pbank(bk), op=ALU.add))(),
                            reads=(f"pf{bk}",), writes=(pfx + "xacc",),
                            tag=f"{pfx}mx{tb}_{ei}_{ti}_{dmh}")
                    else:
                        p.step("vector", (lambda ti=ti, dmh=dmh, bk=bk, ei=ei: lambda e:
                               e.scalar_tensor_tensor(
                                   xacc[:, ti, dmh * 512:(dmh + 1) * 512],
                                   self.pbank(bk), cmb[:, ti, ei:ei + 1],
                                   xacc[:, ti, dmh * 512:(dmh + 1) * 512],
                                   op0=ALU.mult, op1=ALU.add))(),
                               reads=(f"pf{bk}", pfx + "cmb"), writes=(pfx + "xacc",),
                               tag=f"{pfx}mx{tb}_{ei}_{ti}_{dmh}", sreads=(pfx + "cmb",))

    def phase_attn_b(self, src_dram, dst_dram):
        nc, p = self.nc, self.prog
        ps = contextlib.ExitStack()
        tri = ps.enter_context(nc.sbuf_tensor("b_tri", [128, 384], BF16))
        allon = ps.enter_context(nc.sbuf_tensor("b_allones", [128, 128], BF16))
        p.dma("qs", lambda e: e.dma_start(tri[:], self.dconst["tri_bf"]),
              writes=("c_tri_bf",), tag="b_tri")
        p.dma("qs", lambda e: e.dma_start(allon[:], self.dconst["allones_bf"]),
              writes=("c_allones_bf",), tag="b_allon")
        self.UO = tri[:, 0:128]
        self.MSL = tri[:, 128:256]
        self.MLI = tri[:, 256:384]
        self.allones = allon[:]
        wqkvT = ps.enter_context(nc.sbuf_tensor("b_wqkvT", [128, 8, 3 * D], BF16))
        woutT = ps.enter_context(nc.sbuf_tensor("b_woutT", [128, 8, D], BF16))
        wgbT = ps.enter_context(nc.sbuf_tensor("b_wgbT", [128, 8, 32], BF16))
        gbr = ps.enter_context(nc.sbuf_tensor("b_gbr", [1, 32], BF16))
        OT = ps.enter_context(nc.sbuf_tensor("b_OT", [128, 8, T], BF16))
        stg = ps.enter_context(nc.sbuf_tensor("b_stg", [128, D], BF16))
        stgf = ps.enter_context(nc.sbuf_tensor("b_stgf", [128, D], F32))
        xt = ps.enter_context(nc.sbuf_tensor("b_xt", [128, D], F32))
        xn = ps.enter_context(nc.sbuf_tensor("b_xn", [128, D], BF16))
        xnt = ps.enter_context(nc.sbuf_tensor("b_xnt", [128, 8, 128], BF16))
        qkv = ps.enter_context(nc.sbuf_tensor("b_qkv", [128, 3 * D], F32))
        qkn = ps.enter_context(nc.sbuf_tensor("b_qkn", [128, 2048], BF16))
        qtn = ps.enter_context(nc.sbuf_tensor("b_qtn", [128, D], BF16))
        kqT = ps.enter_context(nc.sbuf_tensor("b_kqT", [128, 16, 128], BF16))
        qtT = ps.enter_context(nc.sbuf_tensor("b_qtT", [128, 8, 128], BF16))
        gst = ps.enter_context(nc.sbuf_tensor("b_gst", [128, 16, 16], F32))
        spb = ps.enter_context(nc.sbuf_tensor("b_spb", [128, 16], BF16))
        sqs = ps.enter_context(nc.sbuf_tensor("b_sqs", [128, 2048], F32))
        ssn = ps.enter_context(nc.sbuf_tensor("b_ssn", [128, 32], F32))
        Zf = [ps.enter_context(nc.sbuf_tensor(f"b_Zf{i}", [128, 8, 128], F32)) for i in range(2)]
        Zb = ps.enter_context(nc.sbuf_tensor("b_Zb", [128, 8, 128], BF16))
        U = [dict(
            gma=ps.enter_context(nc.sbuf_tensor(f"b_gma{i}", [128, 128], BF16)),
            gstr=ps.enter_context(nc.sbuf_tensor(f"b_gstr{i}", [128, 128], BF16)),
            gmm=ps.enter_context(nc.sbuf_tensor(f"b_gmm{i}", [128, 128], BF16)),
            gmt=ps.enter_context(nc.sbuf_tensor(f"b_gmt{i}", [128, 128], BF16)),
            dt=ps.enter_context(nc.sbuf_tensor(f"b_dt{i}", [128, 128], BF16)),
            lu=[ps.enter_context(nc.sbuf_tensor(f"b_lu{i}_{j}", [128, 128], BF16)) for j in range(2)],
            lut=[ps.enter_context(nc.sbuf_tensor(f"b_lut{i}_{j}", [128, 128], BF16)) for j in range(2)],
            xs=[ps.enter_context(nc.sbuf_tensor(f"b_xs{i}_{j}", [128, 128], BF16)) for j in range(2)],
            w2=ps.enter_context(nc.sbuf_tensor(f"b_w2{i}", [128, 128], BF16)),
            w3=ps.enter_context(nc.sbuf_tensor(f"b_w3{i}", [128, 64], BF16)),
        ) for i in range(2)]

        p.step("vector", lambda e: e.memset(Zf[0][:].rearrange("p a b -> p (a b)"), 0.0),
               writes=tuple(f"b_Zf0_{h}" for h in range(16)), tag="bZ0")
        p.step("vector", lambda e: e.memset(Zf[1][:].rearrange("p a b -> p (a b)"), 0.0),
               writes=tuple(f"b_Zf1_{h}" for h in range(16)), tag="bZ1")
        p.step("vector", lambda e: e.memset(Zb[:].rearrange("p a b -> p (a b)"), 0.0),
               writes=tuple(f"b_Zb_{h}" for h in range(16)), tag="bZb0")
        p.dma("qg", lambda e: e.dma_start(gbr[0:1, 0:16], self.d_bg_b.unsqueeze(0)),
              writes=("b_gbr",), tag="b_gb1")
        p.dma("qg", lambda e: e.dma_start(gbr[0:1, 16:32], self.d_bb_b.unsqueeze(0)),
              writes=("b_gbr",), tag="b_gb2")

        def load_transposed(dsrc, rows, dst, dres, coloff, tag):
            nrt = (rows + 127) // 128
            for rt in range(nrt):
                r0 = rt * 128
                nr = min(128, rows - r0)
                p.dma("qg", (lambda r0=r0, nr=nr: lambda e: e.dma_start(
                    stg[0:nr, :], dsrc[r0:r0 + nr, :]))(),
                    writes=("b_stg",), tag=f"{tag}ld{rt}")
                for c in range(8):
                    bk = c % 2
                    p.step("tensor", (lambda c=c, bk=bk: lambda e: e.transpose(
                        self.pbank_bf(bk), stg[:, c * 128:(c + 1) * 128], self.ident_bf[:]))(),
                        reads=("b_stg", "c_ident_bf"), writes=(f"pf{bk}",), tag=f"{tag}T{rt}_{c}")
                    p.step("scalar", (lambda c=c, bk=bk, r0=r0, nr=nr, coloff=coloff: lambda e:
                           e.copy(dst[:, c, coloff + r0:coloff + r0 + nr],
                                  self.pbank_bf(bk)[:, 0:nr]))(),
                           reads=(f"pf{bk}",), writes=(dres,), tag=f"{tag}E{rt}_{c}")

        p.step("vector", lambda e: e.memset(stg[:], 0.0), writes=("b_stg",), tag="bstg0")
        load_transposed(self.d_bqkv_w, 3 * D, wqkvT, "b_wqkvT", 0, "bwq")
        load_transposed(self.d_bout_w, D, woutT, "b_woutT", 0, "bwo")
        p.step("vector", lambda e: e.memset(stgf[:], 0.0), writes=("b_stgf",), tag="bstgf0")
        p.dma("qg", lambda e: e.dma_start(stg[0:16, :], self.d_bg_w), writes=("b_stg",), tag="bgw")
        p.dma("qg", lambda e: e.dma_start(stg[16:32, :], self.d_bb_w), writes=("b_stg",), tag="bbw")
        for c in range(8):
            bk = c % 2
            p.step("tensor", (lambda c=c, bk=bk: lambda e: e.transpose(
                self.pbank_bf(bk), stg[:, c * 128:(c + 1) * 128], self.ident_bf[:]))(),
                reads=("b_stg", "c_ident_bf"), writes=(f"pf{bk}",), tag=f"bgT{c}")
            p.step("scalar", (lambda c=c, bk=bk: lambda e: e.copy(
                wgbT[:, c, 0:32], self.pbank_bf(bk)[:, 0:32]))(),
                reads=(f"pf{bk}",), writes=("b_wgbT",), tag=f"bgE{c}")
        p.barrier("b_prelude_end")

        for c in range(32):
            t0 = c * 128
            par = c % 2
            u = U[par]
            zf_in, zf_out = Zf[par], Zf[1 - par]
            p.dma("qs", (lambda t0=t0: lambda e: e.dma_start(
                xt[:], src_dram[t0:t0 + 128, :]))(),
                writes=("b_xt",), tag=f"bld{c}")
            self.ln(xt[:], "b_xt", xn[:], "b_xn", tag=f"bln{c}_")
            self.tr128(lambda cc: xn[:, cc * 128:(cc + 1) * 128], "b_xn",
                       (lambda cc: xnt[:, cc, :]), "b_xnt", 8, [0, 1], tag=f"btr{c}_")
            for j in range(6):
                bk = 2 + j % 3
                for cc in range(8):
                    p.step("tensor", (lambda j=j, cc=cc, bk=bk: lambda e: e.matmul(
                        self.pbank(bk), xnt[:, cc, :], wqkvT[:, cc, j * 512:(j + 1) * 512],
                        start=(cc == 0), stop=(cc == 7)))(),
                        reads=("b_xnt", "b_wqkvT"), writes=(f"pf{bk}",), tag=f"bqkv{c}_{j}")
                p.step("scalar", (lambda j=j, bk=bk: lambda e: e.copy(
                    qkv[:, j * 512:(j + 1) * 512], self.pbank(bk)))(),
                    reads=(f"pf{bk}",), writes=("b_qkv",), tag=f"bqkve{c}_{j}")
            p.step("tensor", lambda e: e.matmul(
                self.pbank(5, 0, 32), self.ones_bf[0:1, 0:128], gbr[0:1, 0:32],
                start=True, stop=False),
                reads=("b_gbr", "c_ones_bf"), writes=("pf5",), tag=f"bgb{c}")
            for cc in range(8):
                p.step("tensor", (lambda cc=cc: lambda e: e.matmul(
                    self.pbank(5, 0, 32), xnt[:, cc, :], wgbT[:, cc, 0:32],
                    start=False, stop=(cc == 7)))(),
                    reads=("b_xnt", "b_wgbT"), writes=("pf5",), tag=f"bg{c}_{cc}")
            p.step("scalar", lambda e: e.activation(
                gst[:, 11, :], self.pf[5][:, 0:16], AF.Sigmoid),
                reads=("pf5",), writes=("b_gsg",), tag=f"bsg{c}")
            p.step("scalar", lambda e: e.activation(
                gst[:, 12, :], gst[:, 11, :], AF.Ln),
                reads=("b_gsg",), writes=("b_gln",), tag=f"blnn{c}")
            p.step("scalar", lambda e: e.mul(
                gst[:, 0, :], gst[:, 12, :], -1.0),
                reads=("b_gln",), writes=("b_gsp",), tag=f"bsp{c}")
            p.step("scalar", lambda e: e.activation(
                gst[:, 1, :], self.pf[5][:, 16:32], AF.Sigmoid),
                reads=("pf5",), writes=("b_gbs",), tag=f"bbs{c}")
            p.step("scalar", lambda e: e.copy(spb[:], gst[:, 0, :]),
                   reads=("b_gsp",), writes=("b_spb",), tag=f"bspb{c}")
            p.step("tensor", lambda e: e.matmul(
                self.pbank(6, 0, 16), self.UO, spb[:], start=True, stop=True),
                reads=("b_spb", "c_tri_bf"), writes=("pf6",), tag=f"bcs{c}")
            p.step("scalar", lambda e: e.copy(gst[:, 2, :], self.pbank(6, 0, 16)),
                   reads=("pf6",), writes=("b_gcs",), tag=f"bcse{c}")
            p.step("scalar", lambda e: e.activation(gst[:, 3, :], gst[:, 2, :], AF.Exp, scale=-1.0),
                   reads=("b_gcs",), writes=("b_ggam",), tag=f"bgam{c}")
            p.step("scalar", lambda e: e.mul(gst[:, 4, :], gst[:, 3, :], -1.0),
                   reads=("b_ggam",), writes=("b_ggamn",), tag=f"bgamn{c}")
            p.step("vector", lambda e: e.tensor_tensor(
                gst[:, 5, :], gst[:, 1, :], gst[:, 3, :], op=ALU.mult),
                reads=("b_gbs", "b_ggam"), writes=("b_gbg",), tag=f"bbg{c}")
            p.step("vector", lambda e: e.tensor_tensor(
                sqs[:], qkv[:, 0:2048], qkv[:, 0:2048], op=ALU.mult),
                reads=("b_qkv",), writes=("b_sqs",), tag=f"bsq{c}")
            p.step("vector", lambda e: e.tensor_reduce(
                ssn[:].unsqueeze(-1), sqs[:].rearrange("p (a b) -> p a b", b=64),
                axis=AX.X, op=ALU.add),
                reads=("b_sqs",), writes=("b_ssn",), tag=f"bssn{c}")
            p.step("scalar", lambda e: e.activation(ssn[:], ssn[:], AF.Sqrt,
                                                    bias=self.consts["epsv"][:, 1:2]),
                   reads=("b_ssn", "c_epsv"), writes=("b_ssn",), tag=f"bsrt{c}")
            p.step("vector", lambda e: e.reciprocal(ssn[:], ssn[:]),
                   reads=("b_ssn",), writes=("b_ssn",), tag=f"brcp{c}")
            p.step("vector", lambda e: e.tensor_tensor(
                gst[:, 10, :], ssn[:, 0:16], gst[:, 3, :], op=ALU.mult),
                reads=("b_ssn", "b_ggam"), writes=("b_grq",), tag=f"brq{c}")
            p.step("vector", lambda e: e.tensor_tensor(
                qkn[:, 0:1024].rearrange("p (a b) -> p a b", b=64),
                qkv[:, 0:1024].rearrange("p (a b) -> p a b", b=64),
                ssn[:, 0:16].unsqueeze(-1).broadcast_to([128, 16, 64]), op=ALU.mult),
                reads=("b_qkv", "b_ssn"), writes=("b_qkn",), tag=f"bqh{c}",
                sreads=("b_ssn",))
            p.step("vector", lambda e: e.tensor_tensor(
                qkn[:, 1024:2048].rearrange("p (a b) -> p a b", b=64),
                qkv[:, 1024:2048].rearrange("p (a b) -> p a b", b=64),
                ssn[:, 16:32].unsqueeze(-1).broadcast_to([128, 16, 64]), op=ALU.mult),
                reads=("b_qkv", "b_ssn"), writes=("b_qkn",), tag=f"bkh{c}",
                sreads=("b_ssn",))
            p.step("vector", lambda e: e.tensor_tensor(
                qtn[:].rearrange("p (a b) -> p a b", b=64),
                qkv[:, 0:1024].rearrange("p (a b) -> p a b", b=64),
                gst[:, 10, :].unsqueeze(-1).broadcast_to([128, 16, 64]), op=ALU.mult),
                reads=("b_qkv", "b_grq"), writes=("b_qtn",), tag=f"bqt{c}",
                sreads=("b_grq",))
            self.tr128(lambda cc: qkn[:, cc * 128:(cc + 1) * 128], "b_qkn",
                       (lambda cc: kqT[:, cc, :]), "b_kqT", 16, [2, 3], tag=f"bkqT{c}_")
            self.tr128(lambda cc: qtn[:, cc * 128:(cc + 1) * 128], "b_qtn",
                       (lambda cc: qtT[:, cc, :]), "b_qtT", 8, [2, 3], tag=f"bqtT{c}_")

            for h in range(16):
                self._unit_b(c, h, par, u, zf_in, zf_out, Zb, qkv, qkn, kqT, qtT,
                             gst, spb, OT)

        p.barrier("b_wout")
        xo = [qkv[:, 0:D], qkv[:, D:2 * D]]
        xti = [xt[:], stgf[:]]
        for ti in range(32):
            b = ti % 2
            p.dma("qs", (lambda ti=ti, b=b: lambda e: e.dma_start(
                xti[b], src_dram[ti * 128:(ti + 1) * 128, :]))(),
                writes=(f"b_xti{b}",), tag=f"bwld{ti}")
            for half in range(2):
                bk = 2 * b + half
                for vc in range(8):
                    p.step("tensor", (lambda ti=ti, half=half, vc=vc, bk=bk: lambda e: e.matmul(
                        self.pbank(bk), OT[:, vc, ti * 128:(ti + 1) * 128],
                        woutT[:, vc, half * 512:(half + 1) * 512],
                        start=(vc == 0), stop=(vc == 7)))(),
                        reads=("b_OT", "b_woutT"), writes=(f"pf{bk}",),
                        tag=f"bw{ti}_{half}_{vc}")
                p.step("vector", (lambda ti=ti, b=b, half=half, bk=bk: lambda e: e.tensor_tensor(
                    xo[b][:, half * 512:(half + 1) * 512],
                    xti[b][:, half * 512:(half + 1) * 512],
                    self.pbank(bk), op=ALU.add))(),
                    reads=(f"b_xti{b}", f"pf{bk}"), writes=(f"b_xo{b}",),
                    tag=f"bwa{ti}_{half}")
            p.dma("qs", (lambda ti=ti, b=b: lambda e: e.dma_start(
                dst_dram[ti * 128:(ti + 1) * 128, :], xo[b]))(),
                reads=(f"b_xo{b}",), writes=(f"bdst{ti}",), tag=f"bwst{ti}")
        p.barrier("b_end")
        ps.close()

    def _unit_b(self, c, h, par, u, zf_in, zf_out, Zb, qkv, qkn, kqT, qtT, gst, spb, OT):
        p = self.prog
        bA, bB, bC, bD = par * 3, par * 3 + 1, par * 3 + 2, 6 + par
        hs, hc = h % 2, h // 2
        lo, hi = hs * 64, (hs + 1) * 64
        kT_h = kqT[lo:hi, 8 + hc, :]
        qT_h = kqT[lo:hi, hc, :]
        qtT_h = qtT[lo:hi, hc, :]
        Zb_h = Zb[lo:hi, hc, lo:hi]
        kres, qres, qtres = "b_kqT", "b_kqT", "b_qtT"
        un = f"u{par}"
        p.step("tensor", (lambda kT_h=kT_h, bA=bA: lambda e: e.matmul(
            self.pbank(bA, 0, 128), kT_h, kT_h, start=True, stop=True))(),
            reads=(kres,), writes=(f"pf{bA}",), tag=f"bG{c}_{h}")
        p.step("vector", (lambda h=h: lambda e: e.tensor_scalar_mul(
            u["gstr"][:], self.UO, gst[:, 0, h:h + 1]))(),
            reads=("b_gsp", "c_tri_bf"), writes=(f"{un}gstr",), tag=f"bB1a{c}_{h}",
            sreads=("b_gsp",))
        p.step("tensor", lambda e: e.matmul(
            self.pbank(bB, 0, 128), self.allones[:], u["gstr"][:], start=True, stop=True),
            reads=(f"{un}gstr", "c_allones_bf"), writes=(f"pf{bB}",), tag=f"bB1{c}_{h}")
        p.step("vector", (lambda h=h, bB=bB: lambda e: e.tensor_scalar(
            u["gma"][:], self.pbank(bB, 0, 128), gst[:, 2, h:h + 1], 0.0,
            op0=ALU.subtract, op1=ALU.min))(),
            reads=(f"pf{bB}", "b_gcs"), writes=(f"{un}gma",), tag=f"barg{c}_{h}",
            sreads=("b_gcs",))
        p.step("vector", (lambda h=h, bB=bB: lambda e: e.tensor_tensor(
            gst[:, 6, h:h + 1], gst[:, 2, h:h + 1], self.pf[bB][:, 127:128],
            op=ALU.subtract))(),
            reads=(f"pf{bB}", "b_gcs"), writes=("b_gecv",), tag=f"becv{c}_{h}")
        p.step("scalar", (lambda h=h: lambda e: e.activation(
            gst[:, 6, h:h + 1], gst[:, 6, h:h + 1], AF.Exp))(),
            reads=("b_gecv",), writes=("b_gecv",), tag=f"becve{c}_{h}")
        p.step("scalar", (lambda h=h, bB=bB: lambda e: e.activation(
            gst[:, 7, h:h + 1], self.pf[bB][:, 127:128], AF.Exp, scale=-1.0))(),
            reads=(f"pf{bB}",), writes=("b_ggcb",), tag=f"bgcb{c}_{h}")
        p.step("scalar", (lambda h=h: lambda e: e.activation(
            u["gma"][:], u["gma"][:], AF.Exp))(),
            reads=(f"{un}gma",), writes=(f"{un}gma",), tag=f"bgexp{c}_{h}")
        p.step("vector", lambda e: e.tensor_tensor(
            u["gstr"][:], u["gma"][:], self.MSL, op=ALU.mult),
            reads=(f"{un}gma", "c_tri_bf"), writes=(f"{un}gstr",), tag=f"bgst{c}_{h}")
        p.step("vector", (lambda h=h, bA=bA: lambda e: e.scalar_tensor_tensor(
            u["lu"][0][:], self.pbank(bA, 0, 128), gst[:, 1, h:h + 1], u["gstr"][:],
            op0=ALU.mult, op1=ALU.mult))(),
            reads=(f"pf{bA}", "b_gbs", f"{un}gstr"), writes=(f"{un}lu0",), tag=f"bLu{c}_{h}",
            sreads=("b_gbs",))
        p.step("tensor", lambda e: e.transpose(
            self.pbank_bf(bC), u["lu"][0][:], self.ident_bf[:]),
            reads=(f"{un}lu0", "c_ident_bf"), writes=(f"pf{bC}",), tag=f"bLuT{c}_{h}")
        p.step("scalar", lambda e: e.copy(u["lut"][0][:], self.pbank_bf(bC)),
               reads=(f"pf{bC}",), writes=(f"{un}lut0",), tag=f"bLuTe{c}_{h}")
        p.step("vector", lambda e: e.tensor_tensor(
            u["gmm"][:], u["gma"][:], self.MLI, op=ALU.mult),
            reads=(f"{un}gma", "c_tri_bf"), writes=(f"{un}gmm",), tag=f"bgmm{c}_{h}")
        p.step("tensor", lambda e: e.transpose(
            self.pbank_bf(bC), u["gmm"][:], self.ident_bf[:]),
            reads=(f"{un}gmm", "c_ident_bf"), writes=(f"pf{bC}",), tag=f"bgmT{c}_{h}")
        p.step("scalar", lambda e: e.copy(u["gmt"][:], self.pbank_bf(bC)),
               reads=(f"pf{bC}",), writes=(f"{un}gmt",), tag=f"bgmTe{c}_{h}")
        p.step("tensor", (lambda kT_h=kT_h, qT_h=qT_h, bA=bA: lambda e: e.matmul(
            self.pbank(bA, 0, 128), kT_h, qT_h, start=True, stop=True))(),
            reads=(kres, qres), writes=(f"pf{bA}",), tag=f"bKQ{c}_{h}")
        p.step("vector", (lambda bA=bA: lambda e: e.tensor_tensor(
            u["dt"][:], self.pbank(bA, 0, 128), u["gmt"][:], op=ALU.mult))(),
            reads=(f"pf{bA}", f"{un}gmt"), writes=(f"{un}dt",), tag=f"bDt{c}_{h}")
        p.step("tensor", (lambda kT_h=kT_h, Zb_h=Zb_h, bB=bB: lambda e: e.matmul(
            self.pbank(bB, 0, 64), kT_h, Zb_h, start=True, stop=True))(),
            reads=(kres, f"b_Zb_{h}"), writes=(f"pf{bB}",), tag=f"bR{c}_{h}")
        p.step("vector", (lambda h=h, bB=bB: lambda e: e.tensor_scalar_mul(
            u["xs"][0][:, 0:64], self.pbank(bB, 0, 64), gst[:, 5, h:h + 1]))(),
            reads=(f"pf{bB}", "b_gbg"), writes=(f"{un}xs0",), tag=f"bX0{c}_{h}",
            sreads=("b_gbg",))
        p.step("vector", (lambda h=h: lambda e: e.tensor_scalar_mul(
            u["xs"][0][:, 64:128], qkv[:, 2048 + h * 64:2048 + (h + 1) * 64],
            gst[:, 1, h:h + 1]))(),
            reads=("b_qkv", "b_gbs"), writes=(f"{un}xs0",), tag=f"bU0{c}_{h}",
            sreads=("b_gbs",))
        for lev in range(7):
            cur, nxt = lev % 2, (lev + 1) % 2
            p.step("tensor", (lambda cur=cur, bD=bD: lambda e: e.matmul(
                self.pbank(bD, 0, 128), u["lut"][cur][:], u["xs"][cur][:],
                start=True, stop=True))(),
                reads=(f"{un}lut{cur}", f"{un}xs{cur}"), writes=(f"pf{bD}",),
                tag=f"bsy{c}_{h}_{lev}")
            p.step("vector", (lambda cur=cur, nxt=nxt, bD=bD: lambda e: e.tensor_tensor(
                u["xs"][nxt][:], u["xs"][cur][:], self.pbank(bD, 0, 128),
                op=ALU.subtract))(),
                reads=(f"{un}xs{cur}", f"pf{bD}"), writes=(f"{un}xs{nxt}",),
                tag=f"bsx{c}_{h}_{lev}")
            if lev < 6:
                p.step("tensor", (lambda cur=cur, bC=bC: lambda e: e.matmul(
                    self.pbank(bC, 0, 128), u["lu"][cur][:], u["lut"][cur][:],
                    start=True, stop=True))(),
                    reads=(f"{un}lu{cur}", f"{un}lut{cur}"), writes=(f"pf{bC}",),
                    tag=f"bsq1{c}_{h}_{lev}")
                p.step("scalar", (lambda nxt=nxt, bC=bC: lambda e: e.copy(
                    u["lut"][nxt][:], self.pbank(bC, 0, 128)))(),
                    reads=(f"pf{bC}",), writes=(f"{un}lut{nxt}",),
                    tag=f"bsq1e{c}_{h}_{lev}")
            if lev < 5:
                p.step("tensor", (lambda cur=cur, bC=bC: lambda e: e.matmul(
                    self.pbank(bC, 0, 128), u["lut"][cur][:], u["lu"][cur][:],
                    start=True, stop=True))(),
                    reads=(f"{un}lu{cur}", f"{un}lut{cur}"), writes=(f"pf{bC}",),
                    tag=f"bsq2{c}_{h}_{lev}")
                p.step("scalar", (lambda nxt=nxt, bC=bC: lambda e: e.copy(
                    u["lu"][nxt][:], self.pbank(bC, 0, 128)))(),
                    reads=(f"pf{bC}",), writes=(f"{un}lu{nxt}",),
                    tag=f"bsq2e{c}_{h}_{lev}")
        fin = 7 % 2
        p.step("vector", (lambda fin=fin, lo=lo, hi=hi: lambda e: e.tensor_tensor(
            u["w2"][:, lo:hi], u["xs"][fin][:, 64:128], u["xs"][fin][:, 0:64],
            op=ALU.subtract))(),
            reads=(f"{un}xs{fin}",), writes=(f"{un}w2",), tag=f"bW2{c}_{h}")
        p.step("vector", (lambda h=h, lo=lo, hi=hi: lambda e: e.tensor_scalar_mul(
            u["w3"][:], u["w2"][:, lo:hi], gst[:, 6, h:h + 1]))(),
            reads=(f"{un}w2", "b_gecv"), writes=(f"{un}w3",), tag=f"bW3{c}_{h}",
            sreads=("b_gecv",))
        p.step("tensor", (lambda hc=hc, qtT_h=qtT_h, lo=lo, hi=hi, bD=bD: lambda e:
               e.matmul(self.pbank(bD, 0, 128), Zb[lo:hi, hc, :], qtT_h,
                        start=True, stop=False))(),
            reads=(f"b_Zb_{h}", qtres), writes=(f"pf{bD}",), tag=f"bO1{c}_{h}")
        p.step("tensor", (lambda bD=bD: lambda e: e.matmul(
            self.pbank(bD, 0, 128), u["w2"][:], u["dt"][:],
            start=False, stop=True))(),
            reads=(f"{un}w2", f"{un}dt"), writes=(f"pf{bD}",), tag=f"bO2{c}_{h}")
        p.step("scalar", (lambda c=c, lo=lo, hi=hi, hc=hc, bD=bD: lambda e: e.copy(
            OT[lo:hi, hc, c * 128:(c + 1) * 128],
            self.pbank(bD, 0, 128)[lo:hi, :]))(),
            reads=(f"pf{bD}",), writes=("b_OT",), tag=f"bOe{c}_{h}")
        p.step("tensor", (lambda hc=hc, bB=bB: lambda e: e.matmul(
            self.pbank(bB, 64, 128), qkn[:, 1024 + hc * 128:1024 + (hc + 1) * 128],
            u["w3"][:], start=True, stop=True))(),
            reads=("b_qkn", f"{un}w3"), writes=(f"pf{bB}",), tag=f"bZm{c}_{h}")
        p.step("vector", (lambda h=h, lo=lo, hi=hi, hc=hc, zf_in=zf_in, zf_out=zf_out, bB=bB:
               lambda e: e.scalar_tensor_tensor(
                   zf_out[lo:hi, hc, lo:hi], zf_in[lo:hi, hc, lo:hi], gst[lo:hi, 7, h:h + 1],
                   self.pbank(bB, 64, 128)[lo:hi, :], op0=ALU.mult, op1=ALU.add))(),
            reads=(f"b_Zf{par}_{h}", "b_ggcb", f"pf{bB}"),
            writes=(f"b_Zf{1 - par}_{h}",), tag=f"bZn{c}_{h}", sreads=("b_ggcb",))
        p.step("scalar", (lambda lo=lo, hi=hi, hc=hc, zf_out=zf_out: lambda e: e.copy(
            Zb[lo:hi, hc, lo:hi], zf_out[lo:hi, hc, lo:hi]))(),
            reads=(f"b_Zf{1 - par}_{h}",), writes=(f"b_Zb_{h}",), tag=f"bZbe{c}_{h}")

    def build(self, phases=("A", "F0", "B", "F1")):
        self._rtmp = self.es.enter_context(self.nc.sbuf_tensor("rtmp", [128, 32], F32))
        self.prelude_router()
        chain = list(phases)
        prev = self.dx
        for i, ph in enumerate(chain):
            dst = self.dy if i == len(chain) - 1 else self.d_xres
            if ph == "A":
                self.phase_attn_a(prev, dst)
            elif ph == "B":
                self.phase_attn_b(prev, dst)
            elif ph in ("F0", "F1"):
                self.phase_ffn(prev, dst, "f0_" if ph == "F0" else "f1_")
            prev = self.d_xres
        self.prog.barrier("final")
        sems = {}
        for d in DOMAINS + ("bar", "epo"):
            sems[d] = self.es.enter_context(self.nc.semaphore("sem_" + d))
        self.prog.resolve()
        with self.nc.Block() as block:
            self.prog.emit_all(sems, block)
        self.es.close()
        return self.nc


def _in_maps(inputs, consts):
    x = np.ascontiguousarray(np.asarray(inputs["L0_attn_a_x"], dtype=np.float32))
    base = dict(
        aqkv_w=np.ascontiguousarray(inputs["L0_attn_a_w_qkv"], dtype=np.float32),
        aqkv_b=np.ascontiguousarray(inputs["L0_attn_a_b_qkv"], dtype=np.float32),
        aproj_w=np.ascontiguousarray(inputs["L0_attn_a_w_proj"], dtype=np.float32),
        aproj_b=np.ascontiguousarray(inputs["L0_attn_a_b_proj"], dtype=np.float32),
        bqkv_w=np.ascontiguousarray(inputs["L0_attn_b_Wqkv"], dtype=np.float32),
        bg_w=np.ascontiguousarray(inputs["L0_attn_b_Wg"], dtype=np.float32),
        bg_b=np.ascontiguousarray(inputs["L0_attn_b_bg"], dtype=np.float32),
        bb_w=np.ascontiguousarray(inputs["L0_attn_b_Wb"], dtype=np.float32),
        bb_b=np.ascontiguousarray(inputs["L0_attn_b_bb"], dtype=np.float32),
        bout_w=np.ascontiguousarray(inputs["L0_attn_b_Wout"], dtype=np.float32),
        rw=np.ascontiguousarray(inputs["L0_ffn_router_w"], dtype=np.float32),
        wg=np.ascontiguousarray(inputs["L0_ffn_w_gate"], dtype=np.float32),
        wu=np.ascontiguousarray(inputs["L0_ffn_w_up"], dtype=np.float32),
        wd=np.ascontiguousarray(inputs["L0_ffn_w_down"], dtype=np.float32),
        swg=np.ascontiguousarray(inputs["L0_ffn_sw_gate"], dtype=np.float32),
        swu=np.ascontiguousarray(inputs["L0_ffn_sw_up"], dtype=np.float32),
        swd=np.ascontiguousarray(inputs["L0_ffn_sw_down"], dtype=np.float32),
        **consts,
    )
    return [dict(base, x=np.ascontiguousarray(x[i])) for i in range(N_CORES)]


def kernel(**inputs):
    consts = make_consts()
    mk = MK()
    nc = mk.build()
    maps = _in_maps(inputs, consts)
    res = run_bass_kernel_spmd(nc, maps, core_ids=list(range(N_CORES)))
    out = np.stack([np.asarray(r["y"], dtype=np.float32) for r in res.results], axis=0)
    return out


# version 02
# speedup vs baseline: 1.1126x; 1.1126x over previous
import contextlib

import numpy as np
import ml_dtypes

import concourse.bass as bass
import concourse.mybir as mybir
from concourse.bass_utils import run_bass_kernel_spmd

F32 = mybir.dt.float32
BF16 = mybir.dt.bfloat16
AF = mybir.ActivationFunctionType
ALU = mybir.AluOpType
AX = mybir.AxisListType

N_CORES = 8
T = 4096
D = 1024
H = 16
DH = 64
HID = 4096
NRT = 7

NEG = -1.0e30

ENGINES = ("tensor", "vector", "scalar", "gpsimd", "sync")
QUEUES = ("qs", "qg", "qc")
QUEUE_ENGINE = {"qs": "sync", "qg": "gpsimd", "qc": "scalar"}
NSLOT = {"qs": 8, "qg": 8, "qc": 2}
DOMAINS = tuple(ENGINES) + tuple(f"{q}{i}" for q in QUEUES for i in range(NSLOT[q]))


class Step:
    __slots__ = ("eng", "queue", "slot", "emit", "reads", "writes", "sreads",
                 "needs_inc", "inc_val", "waits", "tag")

    def __init__(self, eng, queue, emit, reads, writes, tag="", sreads=()):
        self.eng = eng
        self.queue = queue
        self.slot = None
        self.emit = emit
        self.reads = tuple(reads)
        self.writes = tuple(writes)
        self.sreads = tuple(sreads)
        self.needs_inc = False
        self.inc_val = None
        self.waits = []
        self.tag = tag

    @property
    def domain(self):
        if self.queue is not None:
            return f"{self.queue}{self.slot}"
        return self.eng


class Prog:
    def __init__(self, full_sync=False):
        self.steps = []
        self.full_sync = full_sync
        self._quse = {q: 0 for q in QUEUES}

    def step(self, eng, emit, reads=(), writes=(), tag="", sreads=()):
        self.steps.append(Step(eng, None, emit, reads, writes, tag, sreads))

    def dma(self, queue, emit, reads=(), writes=(), tag=""):
        s = Step(QUEUE_ENGINE[queue], queue, emit, reads, writes, tag)
        s.slot = self._quse[queue] % NSLOT[queue]
        self._quse[queue] += 1
        self.steps.append(s)

    def barrier(self, tag="barrier", clear=True):
        s = Step("barrier", None, None, (), (), tag)
        s.slot = clear
        self.steps.append(s)

    def resolve(self):
        last_write = {}
        readers = {}
        barrier_pending = []
        deps = [None] * len(self.steps)
        for i, s in enumerate(self.steps):
            if s.eng == "barrier":
                deps[i] = list(barrier_pending)
                barrier_pending = [i]
                last_write = {}
                readers = {}
                continue
            barrier_pending.append(i)
            dmap = {}
            for r in s.reads:
                w = last_write.get(r)
                if w is not None:
                    dmap[w] = dmap.get(w, False) or (r in s.sreads)
            for wr in s.writes:
                w = last_write.get(wr)
                if w is not None:
                    dmap.setdefault(w, False)
                for rd in readers.get(wr, ()):
                    dmap.setdefault(rd, False)
            dmap.pop(i, None)
            deps[i] = dmap
            for r in s.reads:
                readers.setdefault(r, []).append(i)
            for wr in s.writes:
                last_write[wr] = i
                readers[wr] = []

        need = [False] * len(self.steps)
        for i, s in enumerate(self.steps):
            me = s.domain if s.eng != "barrier" else None
            dl = deps[i] if isinstance(deps[i], list) else deps[i].items()
            for jj in dl:
                j, always = (jj, False) if isinstance(jj, int) else jj
                p = self.steps[j]
                if p.eng == "barrier":
                    continue
                fs = (self.full_sync is True or
                      (isinstance(self.full_sync, (set, frozenset, tuple)) and
                       (s.eng in self.full_sync if me is not None else False)))
                if me is None or fs or always or p.domain != me or p.queue is not None:
                    need[j] = True
        counters = {}
        for s, nd in zip(self.steps, need):
            if s.eng == "barrier":
                if s.slot:
                    counters = {d: v for d, v in counters.items() if d not in ENGINES}
                continue
            if s.queue is not None:
                d = s.domain
                counters[d] = counters.get(d, 0) + 16
                s.needs_inc = True
                s.inc_val = counters[d]
            elif nd:
                counters[s.eng] = counters.get(s.eng, 0) + 1
                s.needs_inc = True
                s.inc_val = counters[s.eng]

        known = {e: {} for e in ENGINES}
        for i, s in enumerate(self.steps):
            if s.eng == "barrier":
                targets = {}
                for j in deps[i]:
                    p = self.steps[j]
                    if p.eng == "barrier" or not p.needs_inc:
                        continue
                    targets[p.domain] = max(targets.get(p.domain, 0), p.inc_val)
                s.waits = sorted(targets.items())
                if s.slot:
                    for e in ENGINES:
                        for d, v in s.waits:
                            if d not in ENGINES and known[e].get(d, -1) < v:
                                known[e][d] = v
                        for d in ENGINES:
                            known[e].pop(d, None)
                else:
                    for e in ENGINES:
                        for d, v in s.waits:
                            if known[e].get(d, -1) < v:
                                known[e][d] = v
                continue
            kn = known[s.eng]
            targets = {}
            if s.queue is not None and s.inc_val > 16:
                targets[s.domain] = s.inc_val - 16
            dl = deps[i] if isinstance(deps[i], list) else deps[i].items()
            for jj in dl:
                j, always = (jj, False) if isinstance(jj, int) else jj
                p = self.steps[j]
                if p.eng == "barrier" or not p.needs_inc:
                    continue
                d = p.domain
                if (p.queue is None and d == s.eng and not always and not
                        (self.full_sync is True or
                         (isinstance(self.full_sync, (set, frozenset, tuple)) and s.eng in self.full_sync))):
                    continue
                targets[d] = max(targets.get(d, 0), p.inc_val)
            ws = []
            for d, v in sorted(targets.items()):
                if kn.get(d, -1) >= v:
                    continue
                ws.append((d, v))
                kn[d] = v
            s.waits = ws

    def emit_all(self, sems, block):
        per_engine = {e: [] for e in ENGINES}
        nbar = 0
        for s in self.steps:
            if s.eng == "barrier":
                nbar += 1
                for e in ENGINES:
                    per_engine[e].append(s)
            else:
                per_engine[s.eng].append(s)

        bsem = sems["bar"]
        esem = sems["epo"]

        def run_stream(name, eng):
            bar_k = 0
            for s in per_engine[name]:
                for d, v in s.waits:
                    eng.wait_ge(sems[d], v)
                if s.eng == "barrier":
                    if not s.slot:
                        continue
                    bar_k += 1
                    eng.drain()
                    eng.sem_inc(bsem, 1)
                    if name == "gpsimd":
                        eng.wait_ge(bsem, 5 * bar_k)
                        for d in ENGINES:
                            eng.sem_clear(sems[d])
                        eng.sem_inc(esem, 1)
                    eng.wait_ge(esem, bar_k)
                    continue
                inst = s.emit(eng)
                if s.needs_inc:
                    assert inst is not None, f"step {s.tag} returned no instruction"
                    if s.queue is not None:
                        inst.then_inc(sems[s.domain], 16)
                    else:
                        inst.then_inc(sems[s.eng], 1)

        @block.tensor
        def _(eng):
            run_stream("tensor", eng)

        @block.vector
        def _(eng):
            run_stream("vector", eng)

        @block.scalar
        def _(eng):
            run_stream("scalar", eng)

        @block.gpsimd
        def _(eng):
            run_stream("gpsimd", eng)

        @block.sync
        def _(eng):
            run_stream("sync", eng)


def make_consts():
    c = {}
    c["ident_bf"] = np.eye(128, dtype=ml_dtypes.bfloat16)
    c["ident_f32"] = np.eye(128, dtype=np.float32)
    c["ones_bf"] = np.ones((1, 128), dtype=ml_dtypes.bfloat16)
    c["ones_f32"] = np.ones((1, 128), dtype=np.float32)
    c["ones_col"] = np.ones((128, 1), dtype=ml_dtypes.bfloat16)
    ev = np.zeros((128, 2), dtype=np.float32)
    ev[:, 0] = 1e-5
    ev[:, 1] = 1e-30
    c["epsv"] = ev
    r = np.arange(128)[:, None]
    y = np.arange(896)[None, :]
    kq = np.where(y - r >= 384, 0.0, NEG).astype(np.float32)
    c["amask_kq"] = kq.astype(ml_dtypes.bfloat16)
    l = np.arange(128)[:, None]
    j = np.arange(128)[None, :]
    tri = np.zeros((128, 384), dtype=np.float32)
    tri[:, 0:128] = (l <= j)
    tri[:, 128:256] = (j < l)
    tri[:, 256:384] = (j <= l)
    c["tri_bf"] = tri.astype(ml_dtypes.bfloat16)
    c["allones_bf"] = np.ones((128, 128), dtype=ml_dtypes.bfloat16)
    ir = np.zeros((128, 8), dtype=np.float32)
    ir[:, 0:7] = (7.0 - np.arange(7))[None, :]
    c["iota_rev"] = ir
    return c


CONST_SPECS = [
    ("ident_bf", [128, 128], BF16),
    ("ones_bf", [1, 128], BF16),
    ("ones_f32", [1, 128], F32),
    ("ones_col", [128, 1], BF16),
    ("iota_rev", [128, 8], F32),
    ("epsv", [128, 2], F32),
]
PHASE_CONST_SPECS = [
    ("ident_f32", [128, 128], F32),
    ("tri_bf", [128, 384], BF16), ("allones_bf", [128, 128], BF16),
]


class MK:
    def __init__(self, full_sync=("vector", "scalar"), dbg_n=0):
        self.dbg_n = dbg_n
        self.dbg_i = 0
        self.nc = bass.Bass("TRN2", target_bir_lowering=False,
                            detect_race_conditions=(full_sync is True))
        self.es = contextlib.ExitStack()
        self.prog = Prog(full_sync=full_sync)
        nc = self.nc

        if self.dbg_n:
            self.d_dbg = nc.dram_tensor("dbg", [self.dbg_n, 128, 1024], F32,
                                        kind="ExternalOutput").ap()
        self.dx = nc.dram_tensor("x", [T, D], F32, kind="ExternalInput").ap()
        self.dy = nc.dram_tensor("y", [T, D], F32, kind="ExternalOutput").ap()
        dt = lambda n, s: nc.dram_tensor(n, s, F32, kind="ExternalInput").ap()
        self.d_aqkv_w = dt("aqkv_w", [D, 3 * D])
        self.d_aqkv_b = dt("aqkv_b", [3 * D])
        self.d_aproj_w = dt("aproj_w", [D, D])
        self.d_aproj_b = dt("aproj_b", [D])
        self.d_bqkv_w = dt("bqkv_w", [3 * D, D])
        self.d_bg_w = dt("bg_w", [H, D])
        self.d_bg_b = dt("bg_b", [H])
        self.d_bb_w = dt("bb_w", [H, D])
        self.d_bb_b = dt("bb_b", [H])
        self.d_bout_w = dt("bout_w", [D, D])
        self.d_rw = dt("rw", [NRT, D])
        self.d_wg = dt("wg", [NRT, D, HID])
        self.d_wu = dt("wu", [NRT, D, HID])
        self.d_wd = dt("wd", [NRT, HID, D])
        self.d_swg = dt("swg", [1, D, HID])
        self.d_swu = dt("swu", [1, D, HID])
        self.d_swd = dt("swd", [1, HID, D])
        self.d_xres = nc.dram_tensor("x_res", [T, D], F32).ap()

        self.consts = {}
        for nm, shp, dty in CONST_SPECS:
            dap = nc.dram_tensor(nm, shp, dty, kind="ExternalInput").ap()
            sb = self.es.enter_context(nc.sbuf_tensor("c_" + nm, shp, dty))
            self.consts[nm] = sb
            self.prog.dma("qs", (lambda sb=sb, dap=dap: lambda e: e.dma_start(sb[:], dap))(),
                          writes=("c_" + nm,), tag="ldc_" + nm)
        self.d_amask_kq = nc.dram_tensor("amask_kq", [128, 896], BF16, kind="ExternalInput").ap()

        self.dconst = {}
        for nm, shp, dty in PHASE_CONST_SPECS:
            self.dconst[nm] = nc.dram_tensor(nm, shp, dty, kind="ExternalInput").ap()
        self.ident_bf = self.consts["ident_bf"]
        self.ones_bf = self.consts["ones_bf"]
        self.iota_rev = self.consts["iota_rev"]

        self.pf = [self.es.enter_context(nc.psum_tensor(f"pf{i}", [128, 512], F32))
                   for i in range(8)]

        self.ln_xc = self.es.enter_context(nc.sbuf_tensor("ln_xc", [128, D], F32))
        self.ln_st = self.es.enter_context(nc.sbuf_tensor("ln_st", [128, 8], F32))

        self.rwT_hi = self.es.enter_context(nc.sbuf_tensor("rwT_hi", [128, 8, 8], BF16))
        self.rwT_lo = self.es.enter_context(nc.sbuf_tensor("rwT_lo", [128, 8, 8], BF16))

    def dump(self, ap, reads, tag="dbg"):
        if not self.dbg_n or self.dbg_i >= self.dbg_n:
            return
        i = self.dbg_i
        self.dbg_i += 1
        w = ap.shape[-1] if len(ap.shape) == 2 else 1024
        np_parts = ap.shape[0]
        self.prog.dma("qg", (lambda ap=ap, i=i, np_parts=np_parts, w=w: lambda e: e.dma_start(
            self.d_dbg[i, 0:np_parts, 0:w], ap))(),
            reads=tuple(reads), writes=(f"dbg{i}",), tag=f"{tag}{i}")

    def pbank(self, i, lo=0, hi=512):
        return self.pf[i][:, lo:hi]

    def pbank_bf(self, i, lo=0, hi=128):
        return self.pf[i][:].bitcast(BF16)[:, lo:hi]

    def ln(self, xt_ap, xt_res, xn_ap, xn_res, xlo_ap=None, xlo_res=None, tag=""):
        p = self.prog
        st = self.ln_st
        xc = self.ln_xc
        p.step("vector", lambda e: e.tensor_reduce(st[:, 0:1], xt_ap, axis=AX.X, op=ALU.add),
               reads=(xt_res,), writes=("ln_st",), tag=tag + "sum")
        p.step("scalar", lambda e: e.mul(st[:, 1:2], st[:, 0:1], -1.0 / D),
               reads=("ln_st",), writes=("ln_st",), tag=tag + "nm")
        p.step("vector", lambda e: e.tensor_scalar_add(xc[:], xt_ap, st[:, 1:2]),
               reads=(xt_res, "ln_st"), writes=("ln_xc",), tag=tag + "xc",
               sreads=("ln_st",))
        p.step("scalar", lambda e: e.activation(
            xn_ap, xc[:], AF.Square, accum_out=st[:, 2:3]),
            reads=("ln_xc",), writes=(xn_res, "ln_st"), tag=tag + "ss")
        p.step("scalar", lambda e: e.activation(st[:, 3:4], st[:, 2:3], AF.Sqrt,
                                                scale=1.0 / D,
                                                bias=self.consts["epsv"][:, 0:1]),
               reads=("ln_st", "c_epsv"), writes=("ln_st",), tag=tag + "std")
        p.step("vector", lambda e: e.reciprocal(st[:, 4:5], st[:, 3:4]),
               reads=("ln_st",), writes=("ln_st",), tag=tag + "rstd")
        p.step("vector", lambda e: e.tensor_scalar_mul(xn_ap, xc[:], st[:, 4:5]),
               reads=("ln_xc", "ln_st"), writes=(xn_res,), tag=tag + "xn",
               sreads=("ln_st",))
        if xlo_ap is not None:
            p.step("vector", lambda e: e.scalar_tensor_tensor(
                xlo_ap, xc[:], st[:, 4:5], xn_ap, op0=ALU.mult, op1=ALU.subtract),
                reads=("ln_xc", "ln_st", xn_res), writes=(xlo_res,), tag=tag + "xlo",
                sreads=("ln_st",))

    def tr128(self, src_ap_fn, src_res, dst_ap_fn, dst_res, n, banks, tag=""):
        p = self.prog
        for c in range(n):
            b = banks[c % len(banks)]
            pap = self.pbank_bf(b)
            p.step("tensor", (lambda c=c, pap=pap: lambda e: e.transpose(
                pap, src_ap_fn(c), self.ident_bf[:]))(),
                reads=(src_res, "c_ident_bf"), writes=(f"pf{b}",), tag=f"{tag}T{c}")
            p.step("scalar", (lambda c=c, pap=pap: lambda e: e.copy(dst_ap_fn(c), pap))(),
                   reads=(f"pf{b}",), writes=(dst_res,), tag=f"{tag}E{c}")

    def prelude_router(self):
        nc, p = self.nc, self.prog
        with contextlib.ExitStack() as ps:
            rwf = ps.enter_context(nc.sbuf_tensor("pr_rwf", [128, D], F32))
            rwh = ps.enter_context(nc.sbuf_tensor("pr_rwh", [128, D], BF16))
            rwl = ps.enter_context(nc.sbuf_tensor("pr_rwl", [128, D], BF16))
            p.step("vector", lambda e: e.memset(rwf[:], 0.0), writes=("pr_rwf",), tag="rw0")
            p.step("vector", lambda e: e.memset(self.rwT_hi[:].rearrange("p a b -> p (a b)"), 0.0),
                   writes=("rwT_hi",), tag="rwh0")
            p.step("vector", lambda e: e.memset(self.rwT_lo[:].rearrange("p a b -> p (a b)"), 0.0),
                   writes=("rwT_lo",), tag="rwl0")
            p.dma("qs", lambda e: e.dma_start(rwf[0:NRT, :], self.d_rw),
                  writes=("pr_rwf",), tag="rwld")
            p.step("vector", lambda e: e.tensor_copy(rwh[:], rwf[:]),
                   reads=("pr_rwf",), writes=("pr_rwh",), tag="rwh")
            p.step("vector", lambda e: e.tensor_tensor(rwl[:], rwf[:], rwh[:], op=ALU.subtract),
                   reads=("pr_rwf", "pr_rwh"), writes=("pr_rwl",), tag="rwl")
            for (src, sres, dst, dres) in [(rwh, "pr_rwh", self.rwT_hi, "rwT_hi"),
                                           (rwl, "pr_rwl", self.rwT_lo, "rwT_lo")]:
                for c in range(8):
                    b = c % 2
                    pap = self.pbank_bf(b)
                    p.step("tensor", (lambda src=src, c=c, pap=pap: lambda e: e.transpose(
                        pap, src[:, c * 128:(c + 1) * 128], self.ident_bf[:]))(),
                        reads=(sres, "c_ident_bf"), writes=(f"pf{b}",), tag=f"rwT{c}")
                    p.step("scalar", (lambda dst=dst, c=c, pap=pap: lambda e: e.copy(
                        dst[:, c, 0:NRT], pap[:, 0:NRT]))(),
                        reads=(f"pf{b}",), writes=(dres,), tag=f"rwTe{c}")
            p.barrier("prelude_router_end")

    def phase_attn_a(self, src_dram, dst_dram):
        nc, p = self.nc, self.prog
        ps = contextlib.ExitStack()
        amk = ps.enter_context(nc.sbuf_tensor("a_amk", [128, 896], BF16))
        p.dma("qs", lambda e: e.dma_start(amk[:], self.d_amask_kq), writes=("a_amk",), tag="a_amk")

        yT = ps.enter_context(nc.sbuf_tensor("a_yT", [128, 8, T], BF16))
        kT = [ps.enter_context(nc.sbuf_tensor(f"a_kT{i}", [65, T], BF16)) for i in range(2)]
        qT = [ps.enter_context(nc.sbuf_tensor(f"a_qT{i}", [65, T], BF16)) for i in range(2)]
        vsb = ps.enter_context(nc.sbuf_tensor("a_v", [128, 32, 192], BF16))
        xt = [ps.enter_context(nc.sbuf_tensor(f"a_xt{i}", [128, D], F32)) for i in range(2)]
        xn = ps.enter_context(nc.sbuf_tensor("a_xn", [128, D], BF16))
        wgr = ps.enter_context(nc.sbuf_tensor("a_wgr", [128, 8, 384], BF16))
        bqc = ps.enter_context(nc.sbuf_tensor("a_bqc", [64, 48], F32))
        bqr = ps.enter_context(nc.sbuf_tensor("a_bqr", [1, D], BF16))
        ptile = [ps.enter_context(nc.sbuf_tensor(f"a_pt{i}", [128, 512], BF16)) for i in range(3)]
        sqq = ps.enter_context(nc.sbuf_tensor("a_sqq", [64, 512], BF16))
        nbr = ps.enter_context(nc.sbuf_tensor("a_nbr", [1, 512], BF16))
        mk1 = ps.enter_context(nc.sbuf_tensor("a_mk1", [1, 2], F32))
        mrb = [ps.enter_context(nc.sbuf_tensor(f"a_mrb{i}", [1, 512], BF16)) for i in range(2)]
        linv = ps.enter_context(nc.sbuf_tensor("a_linv", [1, 512], F32))
        ottmp = ps.enter_context(nc.sbuf_tensor("a_ottmp", [128, 512], BF16))
        xnt_stack = contextlib.ExitStack()
        xnt = xnt_stack.enter_context(nc.sbuf_tensor("a_xnt", [128, 8, T], BF16))

        p.dma("qs", lambda e: e.dma_start(bqc[:], self.d_aqkv_b.rearrange("(c p) -> p c", p=64),
                                          allow_slow_non_contiguous=True),
              writes=("a_bqc",), tag="a_bqc")
        p.dma("qg", lambda e: e.dma_start(bqr[:], self.d_aqkv_b[2 * D:3 * D].unsqueeze(0)),
              writes=("a_bqr",), tag="a_bqr")
        p.step("vector", lambda e: e.memset(vsb[:].rearrange("p a b -> p (a b)"), 0.0),
               writes=("a_v",), tag="a_v0")
        p.step("vector", lambda e: e.memset(vsb[:, :, 96:97], 1.0), writes=("a_v",), tag="a_v1a")
        for hs in range(2):
            p.step("vector", (lambda hs=hs: lambda e: e.memset(kT[hs][64:65, :], 1.0))(),
                   writes=(f"a_kT{hs}r64",), tag=f"a_k1_{hs}")

        for i in range(32):
            b = i % 2
            p.dma("qs", (lambda i=i, b=b: lambda e: e.dma_start(
                xt[b][:], src_dram[i * 128:(i + 1) * 128, :]))(),
                writes=(f"a_xt{b}",), tag=f"a0ld{i}")
            self.ln(xt[b][:], f"a_xt{b}", xn[:], "a_xn", tag=f"a0ln{i}_")
            self.tr128(lambda c: xn[:, c * 128:(c + 1) * 128], "a_xn",
                       (lambda c, i=i: xnt[:, c, i * 128:(i + 1) * 128]),
                       "a_xnt", 8, [6, 7], tag=f"a0tr{i}_")

        for g in range(8):
            for mat in range(3):
                col0 = mat * D + g * 128
                p.dma("qg", (lambda mat=mat, col0=col0: lambda e: e.dma_start(
                    wgr[:, :, mat * 128:(mat + 1) * 128],
                    self.d_aqkv_w.rearrange("(c p) n -> p c n", p=128)[:, :, col0:col0 + 128]))(),
                    writes=("a_wgr",), tag=f"a1w{g}_{mat}")
            for tb in range(8):
                t0 = tb * 512
                for mat in range(2):
                    for hs in range(2):
                        bk = (tb * 4 + mat * 2 + hs) % 4
                        for c in range(8):
                            p.step("tensor", (lambda mat=mat, hs=hs, c=c, t0=t0, bk=bk:
                                   lambda e: e.matmul(
                                       self.pbank(bk)[0:64, :],
                                       wgr[:, c, mat * 128 + hs * 64:mat * 128 + (hs + 1) * 64],
                                       xnt[:, c, t0:t0 + 512],
                                       start=(c == 0), stop=(c == 7)))(),
                                   reads=("a_wgr", "a_xnt"), writes=(f"pf{bk}",),
                                   tag=f"a1qk{g}_{tb}_{mat}_{hs}")
                        dst = qT if mat == 0 else kT
                        dres = "a_qT" if mat == 0 else "a_kT"
                        bcol = (mat * 1024 + g * 128 + hs * 64) // 64
                        p.step("scalar", (lambda dst=dst, hs=hs, t0=t0, bk=bk, bcol=bcol:
                               lambda e: e.activation(
                                   dst[hs][0:64, t0:t0 + 512],
                                   self.pf[bk][0:64, 0:512],
                                   AF.Identity, bias=bqc[0:64, bcol:bcol + 1]))(),
                               reads=(f"pf{bk}", "a_bqc"), writes=(f"{dres}{hs}",),
                               tag=f"a1qke{g}_{tb}_{mat}_{hs}", sreads=("a_bqc",))
                for ts in range(4):
                    bk = 4 + ts % 2
                    p.step("tensor", (lambda g=g, bk=bk: lambda e: e.matmul(
                        self.pbank(bk, 0, 128), self.ones_bf[0:1, 0:128],
                        bqr[0:1, g * 128:(g + 1) * 128],
                        start=True, stop=False))(),
                        reads=("a_bqr", "c_ones_bf"), writes=(f"pf{bk}",),
                        tag=f"a1vb{g}_{tb}_{ts}")
                    for c in range(8):
                        p.step("tensor", (lambda c=c, t0=t0, ts=ts, bk=bk: lambda e:
                               e.matmul(self.pbank(bk, 0, 128),
                                        xnt[:, c, t0 + ts * 128:t0 + (ts + 1) * 128],
                                        wgr[:, c, 256:384],
                                        start=False, stop=(c == 7)))(),
                               reads=("a_wgr", "a_xnt"), writes=(f"pf{bk}",),
                               tag=f"a1v{g}_{tb}_{ts}_{c}")
                    for hs in range(2):
                        c0 = 0 if hs == 0 else 128
                        p.step("scalar", (lambda tb=tb, ts=ts, hs=hs, bk=bk, c0=c0: lambda e:
                               e.copy(vsb[:, tb * 4 + ts, c0:c0 + 64],
                                      self.pf[bk][:, hs * 64:(hs + 1) * 64]))(),
                            reads=(f"pf{bk}",), writes=("a_v",), tag=f"a1ve{g}_{tb}_{ts}_{hs}")
            for hs in range(2):
                h = 2 * g + hs
                for mat in (1, 0):
                    srcT = qT[hs] if mat == 0 else kT[hs]
                    sres = f"a_qT{hs}" if mat == 0 else f"a_kT{hs}"
                    for j in range(8):
                        bk = j % 2
                        p.step("scalar", (lambda srcT=srcT, j=j: lambda e: e.activation(
                            sqq[:], srcT[0:64, j * 512:(j + 1) * 512], AF.Square))(),
                            reads=(sres,), writes=("a_sqq",), tag=f"aN2_{h}_{mat}_{j}")
                        p.step("tensor", (lambda bk=bk: lambda e: e.matmul(
                            self.pf[bk][0:1, 0:512], self.consts["ones_col"][0:64, :],
                            sqq[:], start=True, stop=True))(),
                            reads=("a_sqq", "c_ones_col"), writes=(f"pf{bk}",),
                            tag=f"aNm_{h}_{mat}_{j}")
                        p.step("scalar", (lambda bk=bk: lambda e:
                               e.activation(nbr[:], self.pf[bk][0:1, 0:512], AF.Sqrt))(),
                               reads=(f"pf{bk}",), writes=("a_nbr",), tag=f"aNe_{h}_{mat}_{j}")
                        if mat == 1:
                            p.step("vector", lambda e: e.reduce_max(
                                mk1[:, 1:2], nbr[:], axis=AX.X),
                                reads=("a_nbr",), writes=("a_mk1b",), tag=f"aNb_{h}_{j}")
                            if j == 0:
                                p.step("vector", lambda e: e.tensor_copy(
                                    mk1[:, 0:1], mk1[:, 1:2]),
                                    reads=("a_mk1b",), writes=("a_mk1",), tag=f"aNc_{h}_{j}")
                            else:
                                p.step("vector", lambda e: e.tensor_tensor(
                                    mk1[:, 0:1], mk1[:, 0:1], mk1[:, 1:2], op=ALU.max),
                                    reads=("a_mk1b", "a_mk1"), writes=("a_mk1",),
                                    tag=f"aNc_{h}_{j}")
                        else:
                            mb = j % 2
                            p.step("vector", (lambda mb=mb: lambda e: e.tensor_scalar(
                                mrb[mb][:], nbr[:], mk1[:, 0:1],
                                -1.0, op0=ALU.mult, op1=ALU.mult))(),
                                reads=("a_nbr", "a_mk1"), writes=(f"a_mrb{mb}",),
                                tag=f"aNr_{h}_{j}", sreads=("a_mk1",))
                            p.dma("qs", (lambda hs=hs, j=j, mb=mb: lambda e: e.dma_start(
                                qT[hs][64:65, j * 512:(j + 1) * 512], mrb[mb][:]))(),
                                reads=(f"a_mrb{mb}",), writes=(f"a_qT{hs}r64",),
                                tag=f"aNw_{h}_{j}")
                for qb in range(8):
                    q0 = qb * 512
                    nkt = 4 * (qb + 1)
                    lrow = 96 if hs == 0 else 32
                    vlo = 0 if hs == 0 else 64
                    for kt in range(nkt):
                        bk = kt % 3
                        pt = kt % 3
                        p.step("tensor", (lambda hs=hs, q0=q0, kt=kt, bk=bk: lambda e: e.matmul(
                            self.pbank(bk), kT[hs][0:65, kt * 128:(kt + 1) * 128],
                            qT[hs][0:65, q0:q0 + 512], start=True, stop=True))(),
                            reads=(f"a_qT{hs}", f"a_qT{hs}r64", f"a_kT{hs}", f"a_kT{hs}r64"),
                            writes=(f"pf{bk}",), tag=f"aS2_{h}_{qb}_{kt}")
                        if kt >= 4 * qb:
                            j = kt - 4 * qb
                            p.step("vector", (lambda j=j, bk=bk: lambda e: e.tensor_tensor(
                                self.pbank(bk), self.pbank(bk),
                                amk[:, 384 - 128 * j:896 - 128 * j], op=ALU.add))(),
                                reads=(f"pf{bk}", "a_amk"), writes=(f"pf{bk}",),
                                tag=f"aS2m_{h}_{qb}_{kt}")
                        p.step("scalar", (lambda pt=pt, bk=bk: lambda e: e.activation(
                            ptile[pt][:], self.pbank(bk), AF.Exp, scale=0.125))(),
                            reads=(f"pf{bk}",), writes=(f"a_pt{pt}",),
                            tag=f"aS2e_{h}_{qb}_{kt}")
                        p.step("tensor", (lambda kt=kt, pt=pt, nkt=nkt, vlo=vlo: lambda e:
                               e.matmul(
                                   self.pf[4][:, 0:512],
                                   vsb[:, kt, vlo:vlo + 128],
                                   ptile[pt][:],
                                   start=(kt == 0), stop=(kt == nkt - 1)))(),
                            reads=("a_v", f"a_pt{pt}"), writes=("pf4",),
                            tag=f"aS2o_{h}_{qb}_{kt}")
                    p.step("vector", (lambda lrow=lrow: lambda e: e.reciprocal(
                        linv[:], self.pf[4][lrow:lrow + 1, 0:512]))(),
                        reads=("pf4",), writes=("a_linv",), tag=f"aL_{h}_{qb}")
                    p.step("tensor", lambda e: e.matmul(
                        self.pf[5][:, 0:512],
                        self.consts["ones_f32"][0:1, 0:128],
                        linv[:], start=True, stop=True),
                        reads=("a_linv", "c_ones_f32"), writes=("pf5",), tag=f"aLb_{h}_{qb}")
                    p.step("scalar", (lambda hs=hs: lambda e: e.copy(
                        ottmp[hs * 64:(hs + 1) * 64, :],
                        self.pf[4][hs * 64:(hs + 1) * 64, 0:512]))(),
                        reads=("pf4",), writes=("a_ottmp",), tag=f"aOe_{h}_{qb}")
                    p.step("vector", (lambda hs=hs, g=g, q0=q0: lambda e: e.tensor_tensor(
                        yT[hs * 64:(hs + 1) * 64, g, q0:q0 + 512],
                        ottmp[hs * 64:(hs + 1) * 64, :],
                        self.pf[5][hs * 64:(hs + 1) * 64, 0:512], op=ALU.mult))(),
                        reads=("a_ottmp", "pf5"), writes=("a_yT",),
                        tag=f"aY_{h}_{qb}")

        p.barrier("a_proj_b", clear=False)
        xnt_stack.close()
        wpj = ps.enter_context(nc.sbuf_tensor("a_wpj", [128, 8, D], BF16))
        bpr = ps.enter_context(nc.sbuf_tensor("a_bpr", [1, D], BF16))
        xo = [ps.enter_context(nc.sbuf_tensor(f"a_xo{i}", [128, D], F32)) for i in range(2)]
        p.dma("qg", lambda e: e.dma_start(
            wpj[:], self.d_aproj_w.rearrange("(c p) n -> p c n", p=128)),
            writes=("a_wpj",), tag="a_wpj")
        p.dma("qg", lambda e: e.dma_start(bpr[:], self.d_aproj_b.unsqueeze(0)),
              writes=("a_bpr",), tag="a_bpr")
        for ti in range(32):
            b = ti % 2
            p.dma("qs", (lambda ti=ti, b=b: lambda e: e.dma_start(
                xt[b][:], src_dram[ti * 128:(ti + 1) * 128, :]))(),
                writes=(f"a_xt{b}",), tag=f"apld{ti}")
            for half in range(2):
                bk = 2 * b + half
                p.step("tensor", (lambda half=half, bk=bk: lambda e: e.matmul(
                    self.pbank(bk), self.ones_bf[0:1, 0:128],
                    bpr[0:1, half * 512:(half + 1) * 512], start=True, stop=False))(),
                    reads=("a_bpr", "c_ones_bf"), writes=(f"pf{bk}",), tag=f"apb{ti}_{half}")
                for c in range(8):
                    p.step("tensor", (lambda ti=ti, half=half, c=c, bk=bk: lambda e: e.matmul(
                        self.pbank(bk), yT[:, c, ti * 128:(ti + 1) * 128],
                        wpj[:, c, half * 512:(half + 1) * 512],
                        start=False, stop=(c == 7)))(),
                        reads=("a_yT", "a_wpj"), writes=(f"pf{bk}",), tag=f"ap{ti}_{half}_{c}")
                p.step("vector", (lambda ti=ti, b=b, half=half, bk=bk: lambda e: e.tensor_tensor(
                    xo[b][:, half * 512:(half + 1) * 512], xt[b][:, half * 512:(half + 1) * 512],
                    self.pbank(bk), op=ALU.add))(),
                    reads=(f"a_xt{b}", f"pf{bk}"), writes=(f"a_xo{b}",), tag=f"apadd{ti}_{half}")
            p.dma("qs", (lambda ti=ti, b=b: lambda e: e.dma_start(
                dst_dram[ti * 128:(ti + 1) * 128, :], xo[b][:]))(),
                reads=(f"a_xo{b}",), writes=(f"dst{ti}",), tag=f"apst{ti}")
        p.barrier("a_end")
        ps.close()

    def phase_ffn(self, src_dram, dst_dram, pfx):
        nc, p = self.nc, self.prog
        ps = contextlib.ExitStack()
        NTB = 4
        TBS = 1024
        xacc = ps.enter_context(nc.sbuf_tensor(pfx + "xacc", [128, 8, 1024], F32))
        xnt = ps.enter_context(nc.sbuf_tensor(pfx + "xnt", [128, 8, 1024], BF16))
        xlt = ps.enter_context(nc.sbuf_tensor(pfx + "xlt", [128, 8, 128], BF16))
        xnb = ps.enter_context(nc.sbuf_tensor(pfx + "xnb", [128, D], BF16))
        xlb = ps.enter_context(nc.sbuf_tensor(pfx + "xlb", [128, D], BF16))
        hsb = ps.enter_context(nc.sbuf_tensor(pfx + "h", [128, 32, 1024], BF16))
        wdsb = ps.enter_context(nc.sbuf_tensor(pfx + "wd", [128, 32, 1024], BF16))
        wgu = [ps.enter_context(nc.sbuf_tensor(f"{pfx}wgu{i}", [128, 8, 256], BF16))
               for i in range(2)]
        sg = [ps.enter_context(nc.sbuf_tensor(f"{pfx}sg{i}", [128, 512], BF16))
              for i in range(4)]
        logits = ps.enter_context(nc.sbuf_tensor(pfx + "lg", [128, 8], F32))
        cmb = ps.enter_context(nc.sbuf_tensor(pfx + "cmb", [128, 8, 8], F32))
        rsc = ps.enter_context(nc.sbuf_tensor(pfx + "rsc", [128, 8], F32))

        for tb in range(NTB):
            r0 = tb * TBS
            p.dma("qs", (lambda r0=r0: lambda e: e.dma_start(
                xacc[:],
                src_dram[r0:r0 + TBS, :].rearrange("(i p) n -> p i n", p=128)))(),
                writes=(pfx + "xacc",), tag=f"{pfx}xld{tb}")
            for ti in range(8):
                self.ln(xacc[:, ti, :], pfx + "xacc", xnb[:], pfx + "xnb",
                        xlb[:], pfx + "xlb", tag=f"{pfx}ln{tb}_{ti}_")
                if tb == 0 and ti == 0:
                    self.dump(xnb[:], (pfx + "xnb",), "d_xnb")
                    self.dump(self.ln_st[:], ("ln_st",), "d_st")
                self.tr128(lambda c: xnb[:, c * 128:(c + 1) * 128], pfx + "xnb",
                           (lambda c, ti=ti: xnt[:, c, ti * 128:(ti + 1) * 128]),
                           pfx + "xnt", 8, [0, 1], tag=f"{pfx}trh{tb}_{ti}_")
                self.tr128(lambda c: xlb[:, c * 128:(c + 1) * 128], pfx + "xlb",
                           (lambda c: xlt[:, c, :]),
                           pfx + "xlt", 8, [0, 1], tag=f"{pfx}trl{tb}_{ti}_")
                bk = 2 + ti % 2
                terms = []
                for (lt, lres, rt, rres) in [(xnt, pfx + "xnt", self.rwT_hi, "rwT_hi"),
                                             (xlt, pfx + "xlt", self.rwT_hi, "rwT_hi"),
                                             (xnt, pfx + "xnt", self.rwT_lo, "rwT_lo")]:
                    for c in range(8):
                        lap = (lt[:, c, ti * 128:(ti + 1) * 128] if lt is xnt
                               else lt[:, c, :])
                        terms.append((lap, lres, rt, rres, c))
                for k, (lap, lres, rt, rres, c) in enumerate(terms):
                    p.step("tensor", (lambda lap=lap, rt=rt, c=c, bk=bk,
                           st=(k == 0), sp=(k == len(terms) - 1):
                           lambda e: e.matmul(self.pbank(bk, 0, 8), lap, rt[:, c, 0:8],
                                              start=st, stop=sp))(),
                           reads=(lres, rres), writes=(f"pf{bk}",),
                           tag=f"{pfx}rt{tb}_{ti}_{k}")
                p.step("scalar", (lambda bk=bk: lambda e: e.copy(logits[:], self.pf[bk][:, 0:8]))(),
                       reads=(f"pf{bk}",), writes=(pfx + "lg",), tag=f"{pfx}rte{tb}_{ti}")
                if tb == 0 and ti == 0:
                    self.dump(logits[:], (pfx + "lg",), "d_lg")
                self._router_select(pfx, tb, ti, logits, rsc, cmb)
                if tb == 0 and ti == 0:
                    self.dump(cmb[:].rearrange("p a b -> p (a b)"), (pfx + "cmb",), "d_cmb")
            for ei in range(8):
                self._expert(pfx, tb, ei, xnt, hsb, wdsb, wgu, sg, xacc, cmb)
                if tb == 0 and ei in (0, 7):
                    self.dump(hsb[:, 0, :], (pfx + "h",), "d_h")
                    self.dump(xacc[:, 0, :], (pfx + "xacc",), "d_xacc")
            p.dma("qs", (lambda r0=r0: lambda e: e.dma_start(
                dst_dram[r0:r0 + TBS, :].rearrange("(i p) n -> p i n", p=128), xacc[:]))(),
                reads=(pfx + "xacc",), writes=(f"{pfx}st{tb}",), tag=f"{pfx}xst{tb}")
        p.barrier(pfx + "end")
        ps.close()

    def _router_select(self, pfx, tb, ti, logits, rsc, cmb):
        p = self.prog
        LG, RS = pfx + "lg", pfx + "rsc"
        p.step("vector", lambda e: e.reduce_max(rsc[:, 0:1], logits[:, 0:NRT], axis=AX.X),
               reads=(LG,), writes=(RS,), tag=f"{pfx}s1{tb}{ti}")
        p.step("vector", lambda e: e.tensor_scalar(
            self._rtmp[:, 0:NRT], logits[:, 0:NRT], rsc[:, 0:1], None, op0=ALU.is_equal),
            reads=(LG, RS), writes=("rtmp",), tag=f"{pfx}s2{tb}{ti}", sreads=(RS,))
        p.step("vector", lambda e: e.tensor_tensor(
            self._rtmp[:, 8:8 + NRT], self._rtmp[:, 0:NRT], self.iota_rev[:, 0:NRT], op=ALU.mult),
            reads=("rtmp", "c_iota_rev"), writes=("rtmp",), tag=f"{pfx}s3{tb}{ti}")
        p.step("vector", lambda e: e.reduce_max(rsc[:, 1:2], self._rtmp[:, 8:8 + NRT], axis=AX.X),
               reads=("rtmp",), writes=(RS,), tag=f"{pfx}s4{tb}{ti}")
        p.step("vector", lambda e: e.tensor_scalar(
            self._rtmp[:, 0:NRT], self.iota_rev[:, 0:NRT], rsc[:, 1:2], None, op0=ALU.is_equal),
            reads=("c_iota_rev", RS), writes=("rtmp",), tag=f"{pfx}s5{tb}{ti}",
            sreads=(RS,))
        p.step("vector", lambda e: e.scalar_tensor_tensor(
            self._rtmp[:, 16:16 + NRT], self._rtmp[:, 0:NRT], NEG, logits[:, 0:NRT],
            op0=ALU.mult, op1=ALU.add),
            reads=("rtmp", LG), writes=("rtmp",), tag=f"{pfx}s6{tb}{ti}")
        p.step("vector", lambda e: e.reduce_max(rsc[:, 2:3], self._rtmp[:, 16:16 + NRT], axis=AX.X),
               reads=("rtmp",), writes=(RS,), tag=f"{pfx}s7{tb}{ti}")
        p.step("vector", lambda e: e.tensor_scalar(
            self._rtmp[:, 24:24 + NRT], self._rtmp[:, 16:16 + NRT], rsc[:, 2:3], None,
            op0=ALU.is_equal),
            reads=("rtmp", RS), writes=("rtmp",), tag=f"{pfx}s8{tb}{ti}",
            sreads=(RS,))
        p.step("vector", lambda e: e.tensor_tensor(
            rsc[:, 3:4], rsc[:, 0:1], rsc[:, 2:3], op=ALU.subtract),
            reads=(RS,), writes=(RS,), tag=f"{pfx}s9{tb}{ti}")
        p.step("scalar", lambda e: e.activation(rsc[:, 4:5], rsc[:, 3:4], AF.Sigmoid),
               reads=(RS,), writes=(RS,), tag=f"{pfx}sa{tb}{ti}")
        p.step("scalar", lambda e: e.activation(rsc[:, 5:6], rsc[:, 3:4], AF.Sigmoid, scale=-1.0),
               reads=(RS,), writes=(RS,), tag=f"{pfx}sb{tb}{ti}")
        p.step("vector", lambda e: e.tensor_scalar_mul(
            self._rtmp[:, 0:NRT], self._rtmp[:, 0:NRT], rsc[:, 4:5]),
            reads=("rtmp", RS), writes=("rtmp",), tag=f"{pfx}sc{tb}{ti}", sreads=(RS,))
        p.step("vector", (lambda ti=ti: lambda e: e.scalar_tensor_tensor(
            cmb[:, ti, 0:NRT], self._rtmp[:, 24:24 + NRT], rsc[:, 5:6], self._rtmp[:, 0:NRT],
            op0=ALU.mult, op1=ALU.add))(),
            reads=("rtmp", RS), writes=(pfx + "cmb",), tag=f"{pfx}sd{tb}{ti}",
            sreads=(RS,))

    def _expert(self, pfx, tb, ei, xnt, hsb, wdsb, wgu, sg, xacc, cmb):
        nc, p = self.nc, self.prog
        shared = (ei == NRT)
        dwg = self.d_swg if shared else self.d_wg
        dwu = self.d_swu if shared else self.d_wu
        dwd = self.d_swd if shared else self.d_wd
        eidx = 0 if shared else ei
        for mg in range(8):
            p.dma("qg", (lambda dwd=dwd, eidx=eidx, mg=mg: lambda e: e.dma_start(
                wdsb[:, mg * 4:(mg + 1) * 4, :],
                dwd[eidx].rearrange("(m p) n -> p m n", p=128)[:, mg * 4:(mg + 1) * 4, :]))(),
                writes=(f"{pfx}wd",), tag=f"{pfx}wd{tb}_{ei}_{mg}")
        for m in range(32):
            sl = m % 2
            p.dma("qg", (lambda dwg=dwg, eidx=eidx, m=m, sl=sl: lambda e: e.dma_start(
                wgu[sl][:, :, 0:128],
                dwg[eidx].rearrange("(c p) n -> p c n", p=128)[:, :, m * 128:(m + 1) * 128]))(),
                writes=(f"{pfx}wgu{sl}g",), tag=f"{pfx}lg{tb}_{ei}_{m}")
            p.dma("qg", (lambda dwu=dwu, eidx=eidx, m=m, sl=sl: lambda e: e.dma_start(
                wgu[sl][:, :, 128:256],
                dwu[eidx].rearrange("(c p) n -> p c n", p=128)[:, :, m * 128:(m + 1) * 128]))(),
                writes=(f"{pfx}wgu{sl}u",), tag=f"{pfx}lu{tb}_{ei}_{m}")
            for th in range(2):
                bg = (m % 2) * 4 + th
                bu = (m % 2) * 4 + 2 + th
                for c in range(8):
                    p.step("tensor", (lambda m=m, sl=sl, th=th, c=c, bg=bg: lambda e: e.matmul(
                        self.pbank(bg), wgu[sl][:, c, 0:128],
                        xnt[:, c, th * 512:(th + 1) * 512],
                        start=(c == 0), stop=(c == 7)))(),
                        reads=(f"{pfx}wgu{sl}g", pfx + "xnt"), writes=(f"pf{bg}",),
                        tag=f"{pfx}g{tb}_{ei}_{m}_{th}")
                for c in range(8):
                    p.step("tensor", (lambda m=m, sl=sl, th=th, c=c, bu=bu: lambda e: e.matmul(
                        self.pbank(bu), wgu[sl][:, c, 128:256],
                        xnt[:, c, th * 512:(th + 1) * 512],
                        start=(c == 0), stop=(c == 7)))(),
                        reads=(f"{pfx}wgu{sl}u", pfx + "xnt"), writes=(f"pf{bu}",),
                        tag=f"{pfx}u{tb}_{ei}_{m}_{th}")
                si = (m % 2) * 2 + th
                p.step("scalar", (lambda si=si, bg=bg: lambda e: e.activation(
                    sg[si][:], self.pbank(bg), AF.Silu))(),
                    reads=(f"pf{bg}",), writes=(f"{pfx}sg{si}",),
                    tag=f"{pfx}si{tb}_{ei}_{m}_{th}")
                p.step("vector", (lambda m=m, th=th, si=si, bu=bu: lambda e: e.tensor_tensor(
                    hsb[:, m, th * 512:(th + 1) * 512], sg[si][:], self.pbank(bu),
                    op=ALU.mult))(),
                    reads=(f"{pfx}sg{si}", f"pf{bu}"), writes=(pfx + "h",),
                    tag=f"{pfx}hm{tb}_{ei}_{m}_{th}")
        for tg in range(4):
            for m in range(32):
                for tt in range(2):
                    for dmh in range(2):
                        bk = (tg % 2) * 4 + tt * 2 + dmh
                        p.step("tensor", (lambda m=m, tg=tg, tt=tt, dmh=dmh, bk=bk:
                               lambda e: e.matmul(
                                   self.pbank(bk),
                                   hsb[:, m, tg * 256 + tt * 128:tg * 256 + (tt + 1) * 128],
                                   wdsb[:, m, dmh * 512:(dmh + 1) * 512],
                                   start=(m == 0), stop=(m == 31)))(),
                               reads=(pfx + "h", f"{pfx}wd"), writes=(f"pf{bk}",),
                               tag=f"{pfx}d{tb}_{ei}_{tg}_{m}_{tt}_{dmh}")
            for tt in range(2):
                ti = tg * 2 + tt
                for dmh in range(2):
                    bk = (tg % 2) * 4 + tt * 2 + dmh
                    if shared:
                        p.step("vector", (lambda ti=ti, dmh=dmh, bk=bk: lambda e: e.tensor_tensor(
                            xacc[:, ti, dmh * 512:(dmh + 1) * 512],
                            xacc[:, ti, dmh * 512:(dmh + 1) * 512],
                            self.pbank(bk), op=ALU.add))(),
                            reads=(f"pf{bk}",), writes=(pfx + "xacc",),
                            tag=f"{pfx}mx{tb}_{ei}_{ti}_{dmh}")
                    else:
                        p.step("vector", (lambda ti=ti, dmh=dmh, bk=bk, ei=ei: lambda e:
                               e.scalar_tensor_tensor(
                                   xacc[:, ti, dmh * 512:(dmh + 1) * 512],
                                   self.pbank(bk), cmb[:, ti, ei:ei + 1],
                                   xacc[:, ti, dmh * 512:(dmh + 1) * 512],
                                   op0=ALU.mult, op1=ALU.add))(),
                               reads=(f"pf{bk}", pfx + "cmb"), writes=(pfx + "xacc",),
                               tag=f"{pfx}mx{tb}_{ei}_{ti}_{dmh}", sreads=(pfx + "cmb",))

    def phase_attn_b(self, src_dram, dst_dram):
        nc, p = self.nc, self.prog
        ps = contextlib.ExitStack()
        tri = ps.enter_context(nc.sbuf_tensor("b_tri", [128, 384], BF16))
        allon = ps.enter_context(nc.sbuf_tensor("b_allones", [128, 128], BF16))
        p.dma("qs", lambda e: e.dma_start(tri[:], self.dconst["tri_bf"]),
              writes=("c_tri_bf",), tag="b_tri")
        p.dma("qs", lambda e: e.dma_start(allon[:], self.dconst["allones_bf"]),
              writes=("c_allones_bf",), tag="b_allon")
        self.UO = tri[:, 0:128]
        self.MSL = tri[:, 128:256]
        self.MLI = tri[:, 256:384]
        self.allones = allon[:]
        wqkvT = ps.enter_context(nc.sbuf_tensor("b_wqkvT", [128, 8, 3 * D], BF16))
        woutT = ps.enter_context(nc.sbuf_tensor("b_woutT", [128, 8, D], BF16))
        wgbT = ps.enter_context(nc.sbuf_tensor("b_wgbT", [128, 8, 32], BF16))
        gbr = ps.enter_context(nc.sbuf_tensor("b_gbr", [1, 32], BF16))
        OT = ps.enter_context(nc.sbuf_tensor("b_OT", [128, 8, T], BF16))
        stg = ps.enter_context(nc.sbuf_tensor("b_stg", [128, D], BF16))
        stgf = ps.enter_context(nc.sbuf_tensor("b_stgf", [128, D], F32))
        xt = ps.enter_context(nc.sbuf_tensor("b_xt", [128, D], F32))
        xn = ps.enter_context(nc.sbuf_tensor("b_xn", [128, D], BF16))
        xnt = ps.enter_context(nc.sbuf_tensor("b_xnt", [128, 8, 128], BF16))
        qkv = ps.enter_context(nc.sbuf_tensor("b_qkv", [128, 3 * D], F32))
        qkn = ps.enter_context(nc.sbuf_tensor("b_qkn", [128, 2048], BF16))
        qtn = ps.enter_context(nc.sbuf_tensor("b_qtn", [128, D], BF16))
        kqT = ps.enter_context(nc.sbuf_tensor("b_kqT", [128, 16, 128], BF16))
        qtT = ps.enter_context(nc.sbuf_tensor("b_qtT", [128, 8, 128], BF16))
        gst = ps.enter_context(nc.sbuf_tensor("b_gst", [128, 16, 16], F32))
        spb = ps.enter_context(nc.sbuf_tensor("b_spb", [128, 16], BF16))
        sqs = ps.enter_context(nc.sbuf_tensor("b_sqs", [128, 2048], F32))
        ssn = ps.enter_context(nc.sbuf_tensor("b_ssn", [128, 32], F32))
        Zf = [ps.enter_context(nc.sbuf_tensor(f"b_Zf{i}", [128, 8, 128], F32)) for i in range(2)]
        Zb = ps.enter_context(nc.sbuf_tensor("b_Zb", [128, 8, 128], BF16))
        U = [dict(
            gma=ps.enter_context(nc.sbuf_tensor(f"b_gma{i}", [128, 128], BF16)),
            gstr=ps.enter_context(nc.sbuf_tensor(f"b_gstr{i}", [128, 128], BF16)),
            gmm=ps.enter_context(nc.sbuf_tensor(f"b_gmm{i}", [128, 128], BF16)),
            gmt=ps.enter_context(nc.sbuf_tensor(f"b_gmt{i}", [128, 128], BF16)),
            dt=ps.enter_context(nc.sbuf_tensor(f"b_dt{i}", [128, 128], BF16)),
            lu=[ps.enter_context(nc.sbuf_tensor(f"b_lu{i}_{j}", [128, 128], BF16)) for j in range(2)],
            lut=[ps.enter_context(nc.sbuf_tensor(f"b_lut{i}_{j}", [128, 128], BF16)) for j in range(2)],
            xs=[ps.enter_context(nc.sbuf_tensor(f"b_xs{i}_{j}", [128, 128], BF16)) for j in range(2)],
            w2=ps.enter_context(nc.sbuf_tensor(f"b_w2{i}", [128, 128], BF16)),
            w3=ps.enter_context(nc.sbuf_tensor(f"b_w3{i}", [128, 64], BF16)),
        ) for i in range(2)]

        p.step("vector", lambda e: e.memset(Zf[0][:].rearrange("p a b -> p (a b)"), 0.0),
               writes=tuple(f"b_Zf0_{h}" for h in range(16)), tag="bZ0")
        p.step("vector", lambda e: e.memset(Zf[1][:].rearrange("p a b -> p (a b)"), 0.0),
               writes=tuple(f"b_Zf1_{h}" for h in range(16)), tag="bZ1")
        p.step("vector", lambda e: e.memset(Zb[:].rearrange("p a b -> p (a b)"), 0.0),
               writes=tuple(f"b_Zb_{h}" for h in range(16)), tag="bZb0")
        p.dma("qg", lambda e: e.dma_start(gbr[0:1, 0:16], self.d_bg_b.unsqueeze(0)),
              writes=("b_gbr",), tag="b_gb1")
        p.dma("qg", lambda e: e.dma_start(gbr[0:1, 16:32], self.d_bb_b.unsqueeze(0)),
              writes=("b_gbr",), tag="b_gb2")

        def load_transposed(dsrc, rows, dst, dres, coloff, tag):
            nrt = (rows + 127) // 128
            for rt in range(nrt):
                r0 = rt * 128
                nr = min(128, rows - r0)
                p.dma("qg", (lambda r0=r0, nr=nr: lambda e: e.dma_start(
                    stg[0:nr, :], dsrc[r0:r0 + nr, :]))(),
                    writes=("b_stg",), tag=f"{tag}ld{rt}")
                for c in range(8):
                    bk = c % 2
                    p.step("tensor", (lambda c=c, bk=bk: lambda e: e.transpose(
                        self.pbank_bf(bk), stg[:, c * 128:(c + 1) * 128], self.ident_bf[:]))(),
                        reads=("b_stg", "c_ident_bf"), writes=(f"pf{bk}",), tag=f"{tag}T{rt}_{c}")
                    p.step("scalar", (lambda c=c, bk=bk, r0=r0, nr=nr, coloff=coloff: lambda e:
                           e.copy(dst[:, c, coloff + r0:coloff + r0 + nr],
                                  self.pbank_bf(bk)[:, 0:nr]))(),
                           reads=(f"pf{bk}",), writes=(dres,), tag=f"{tag}E{rt}_{c}")

        p.step("vector", lambda e: e.memset(stg[:], 0.0), writes=("b_stg",), tag="bstg0")
        load_transposed(self.d_bqkv_w, 3 * D, wqkvT, "b_wqkvT", 0, "bwq")
        load_transposed(self.d_bout_w, D, woutT, "b_woutT", 0, "bwo")
        p.step("vector", lambda e: e.memset(stgf[:], 0.0), writes=("b_stgf",), tag="bstgf0")
        p.dma("qg", lambda e: e.dma_start(stg[0:16, :], self.d_bg_w), writes=("b_stg",), tag="bgw")
        p.dma("qg", lambda e: e.dma_start(stg[16:32, :], self.d_bb_w), writes=("b_stg",), tag="bbw")
        for c in range(8):
            bk = c % 2
            p.step("tensor", (lambda c=c, bk=bk: lambda e: e.transpose(
                self.pbank_bf(bk), stg[:, c * 128:(c + 1) * 128], self.ident_bf[:]))(),
                reads=("b_stg", "c_ident_bf"), writes=(f"pf{bk}",), tag=f"bgT{c}")
            p.step("scalar", (lambda c=c, bk=bk: lambda e: e.copy(
                wgbT[:, c, 0:32], self.pbank_bf(bk)[:, 0:32]))(),
                reads=(f"pf{bk}",), writes=("b_wgbT",), tag=f"bgE{c}")
        p.barrier("b_prelude_end")

        for c in range(32):
            t0 = c * 128
            par = c % 2
            u = U[par]
            zf_in, zf_out = Zf[par], Zf[1 - par]
            p.dma("qs", (lambda t0=t0: lambda e: e.dma_start(
                xt[:], src_dram[t0:t0 + 128, :]))(),
                writes=("b_xt",), tag=f"bld{c}")
            self.ln(xt[:], "b_xt", xn[:], "b_xn", tag=f"bln{c}_")
            self.tr128(lambda cc: xn[:, cc * 128:(cc + 1) * 128], "b_xn",
                       (lambda cc: xnt[:, cc, :]), "b_xnt", 8, [0, 1], tag=f"btr{c}_")
            for j in range(6):
                bk = 2 + j % 3
                for cc in range(8):
                    p.step("tensor", (lambda j=j, cc=cc, bk=bk: lambda e: e.matmul(
                        self.pbank(bk), xnt[:, cc, :], wqkvT[:, cc, j * 512:(j + 1) * 512],
                        start=(cc == 0), stop=(cc == 7)))(),
                        reads=("b_xnt", "b_wqkvT"), writes=(f"pf{bk}",), tag=f"bqkv{c}_{j}")
                p.step("scalar", (lambda j=j, bk=bk: lambda e: e.copy(
                    qkv[:, j * 512:(j + 1) * 512], self.pbank(bk)))(),
                    reads=(f"pf{bk}",), writes=("b_qkv",), tag=f"bqkve{c}_{j}")
            p.step("tensor", lambda e: e.matmul(
                self.pbank(5, 0, 32), self.ones_bf[0:1, 0:128], gbr[0:1, 0:32],
                start=True, stop=False),
                reads=("b_gbr", "c_ones_bf"), writes=("pf5",), tag=f"bgb{c}")
            for cc in range(8):
                p.step("tensor", (lambda cc=cc: lambda e: e.matmul(
                    self.pbank(5, 0, 32), xnt[:, cc, :], wgbT[:, cc, 0:32],
                    start=False, stop=(cc == 7)))(),
                    reads=("b_xnt", "b_wgbT"), writes=("pf5",), tag=f"bg{c}_{cc}")
            p.step("scalar", lambda e: e.activation(
                gst[:, 11, :], self.pf[5][:, 0:16], AF.Sigmoid),
                reads=("pf5",), writes=("b_gsg",), tag=f"bsg{c}")
            p.step("scalar", lambda e: e.activation(
                gst[:, 12, :], gst[:, 11, :], AF.Ln),
                reads=("b_gsg",), writes=("b_gln",), tag=f"blnn{c}")
            p.step("scalar", lambda e: e.mul(
                gst[:, 0, :], gst[:, 12, :], -1.0),
                reads=("b_gln",), writes=("b_gsp",), tag=f"bsp{c}")
            p.step("scalar", lambda e: e.activation(
                gst[:, 1, :], self.pf[5][:, 16:32], AF.Sigmoid),
                reads=("pf5",), writes=("b_gbs",), tag=f"bbs{c}")
            p.step("scalar", lambda e: e.copy(spb[:], gst[:, 0, :]),
                   reads=("b_gsp",), writes=("b_spb",), tag=f"bspb{c}")
            p.step("tensor", lambda e: e.matmul(
                self.pbank(6, 0, 16), self.UO, spb[:], start=True, stop=True),
                reads=("b_spb", "c_tri_bf"), writes=("pf6",), tag=f"bcs{c}")
            p.step("scalar", lambda e: e.copy(gst[:, 2, :], self.pbank(6, 0, 16)),
                   reads=("pf6",), writes=("b_gcs",), tag=f"bcse{c}")
            p.step("scalar", lambda e: e.activation(gst[:, 3, :], gst[:, 2, :], AF.Exp, scale=-1.0),
                   reads=("b_gcs",), writes=("b_ggam",), tag=f"bgam{c}")
            p.step("scalar", lambda e: e.mul(gst[:, 4, :], gst[:, 3, :], -1.0),
                   reads=("b_ggam",), writes=("b_ggamn",), tag=f"bgamn{c}")
            p.step("vector", lambda e: e.tensor_tensor(
                gst[:, 5, :], gst[:, 1, :], gst[:, 3, :], op=ALU.mult),
                reads=("b_gbs", "b_ggam"), writes=("b_gbg",), tag=f"bbg{c}")
            p.step("vector", lambda e: e.tensor_tensor(
                sqs[:], qkv[:, 0:2048], qkv[:, 0:2048], op=ALU.mult),
                reads=("b_qkv",), writes=("b_sqs",), tag=f"bsq{c}")
            p.step("vector", lambda e: e.tensor_reduce(
                ssn[:].unsqueeze(-1), sqs[:].rearrange("p (a b) -> p a b", b=64),
                axis=AX.X, op=ALU.add),
                reads=("b_sqs",), writes=("b_ssn",), tag=f"bssn{c}")
            p.step("scalar", lambda e: e.activation(ssn[:], ssn[:], AF.Sqrt,
                                                    bias=self.consts["epsv"][:, 1:2]),
                   reads=("b_ssn", "c_epsv"), writes=("b_ssn",), tag=f"bsrt{c}")
            p.step("vector", lambda e: e.reciprocal(ssn[:], ssn[:]),
                   reads=("b_ssn",), writes=("b_ssn",), tag=f"brcp{c}")
            p.step("vector", lambda e: e.tensor_tensor(
                gst[:, 10, :], ssn[:, 0:16], gst[:, 3, :], op=ALU.mult),
                reads=("b_ssn", "b_ggam"), writes=("b_grq",), tag=f"brq{c}")
            p.step("vector", lambda e: e.tensor_tensor(
                qkn[:, 0:1024].rearrange("p (a b) -> p a b", b=64),
                qkv[:, 0:1024].rearrange("p (a b) -> p a b", b=64),
                ssn[:, 0:16].unsqueeze(-1).broadcast_to([128, 16, 64]), op=ALU.mult),
                reads=("b_qkv", "b_ssn"), writes=("b_qkn",), tag=f"bqh{c}",
                sreads=("b_ssn",))
            p.step("vector", lambda e: e.tensor_tensor(
                qkn[:, 1024:2048].rearrange("p (a b) -> p a b", b=64),
                qkv[:, 1024:2048].rearrange("p (a b) -> p a b", b=64),
                ssn[:, 16:32].unsqueeze(-1).broadcast_to([128, 16, 64]), op=ALU.mult),
                reads=("b_qkv", "b_ssn"), writes=("b_qkn",), tag=f"bkh{c}",
                sreads=("b_ssn",))
            p.step("vector", lambda e: e.tensor_tensor(
                qtn[:].rearrange("p (a b) -> p a b", b=64),
                qkv[:, 0:1024].rearrange("p (a b) -> p a b", b=64),
                gst[:, 10, :].unsqueeze(-1).broadcast_to([128, 16, 64]), op=ALU.mult),
                reads=("b_qkv", "b_grq"), writes=("b_qtn",), tag=f"bqt{c}",
                sreads=("b_grq",))
            self.tr128(lambda cc: qkn[:, cc * 128:(cc + 1) * 128], "b_qkn",
                       (lambda cc: kqT[:, cc, :]), "b_kqT", 16, [2, 3], tag=f"bkqT{c}_")
            self.tr128(lambda cc: qtn[:, cc * 128:(cc + 1) * 128], "b_qtn",
                       (lambda cc: qtT[:, cc, :]), "b_qtT", 8, [2, 3], tag=f"bqtT{c}_")

            for h in range(16):
                self._unit_b(c, h, par, u, zf_in, zf_out, Zb, qkv, qkn, kqT, qtT,
                             gst, spb, OT)

        p.barrier("b_wout")
        xo = [qkv[:, 0:D], qkv[:, D:2 * D]]
        xti = [xt[:], stgf[:]]
        for ti in range(32):
            b = ti % 2
            p.dma("qs", (lambda ti=ti, b=b: lambda e: e.dma_start(
                xti[b], src_dram[ti * 128:(ti + 1) * 128, :]))(),
                writes=(f"b_xti{b}",), tag=f"bwld{ti}")
            for half in range(2):
                bk = 2 * b + half
                for vc in range(8):
                    p.step("tensor", (lambda ti=ti, half=half, vc=vc, bk=bk: lambda e: e.matmul(
                        self.pbank(bk), OT[:, vc, ti * 128:(ti + 1) * 128],
                        woutT[:, vc, half * 512:(half + 1) * 512],
                        start=(vc == 0), stop=(vc == 7)))(),
                        reads=("b_OT", "b_woutT"), writes=(f"pf{bk}",),
                        tag=f"bw{ti}_{half}_{vc}")
                p.step("vector", (lambda ti=ti, b=b, half=half, bk=bk: lambda e: e.tensor_tensor(
                    xo[b][:, half * 512:(half + 1) * 512],
                    xti[b][:, half * 512:(half + 1) * 512],
                    self.pbank(bk), op=ALU.add))(),
                    reads=(f"b_xti{b}", f"pf{bk}"), writes=(f"b_xo{b}",),
                    tag=f"bwa{ti}_{half}")
            p.dma("qs", (lambda ti=ti, b=b: lambda e: e.dma_start(
                dst_dram[ti * 128:(ti + 1) * 128, :], xo[b]))(),
                reads=(f"b_xo{b}",), writes=(f"bdst{ti}",), tag=f"bwst{ti}")
        p.barrier("b_end")
        ps.close()

    def _unit_b(self, c, h, par, u, zf_in, zf_out, Zb, qkv, qkn, kqT, qtT, gst, spb, OT):
        p = self.prog
        bA, bB, bC, bD = par * 3, par * 3 + 1, par * 3 + 2, 6 + par
        hs, hc = h % 2, h // 2
        lo, hi = hs * 64, (hs + 1) * 64
        kT_h = kqT[lo:hi, 8 + hc, :]
        qT_h = kqT[lo:hi, hc, :]
        qtT_h = qtT[lo:hi, hc, :]
        Zb_h = Zb[lo:hi, hc, lo:hi]
        kres, qres, qtres = "b_kqT", "b_kqT", "b_qtT"
        un = f"u{par}"
        p.step("tensor", (lambda kT_h=kT_h, bA=bA: lambda e: e.matmul(
            self.pbank(bA, 0, 128), kT_h, kT_h, start=True, stop=True))(),
            reads=(kres,), writes=(f"pf{bA}",), tag=f"bG{c}_{h}")
        p.step("vector", (lambda h=h: lambda e: e.tensor_scalar_mul(
            u["gstr"][:], self.UO, gst[:, 0, h:h + 1]))(),
            reads=("b_gsp", "c_tri_bf"), writes=(f"{un}gstr",), tag=f"bB1a{c}_{h}",
            sreads=("b_gsp",))
        p.step("tensor", lambda e: e.matmul(
            self.pbank(bB, 0, 128), self.allones[:], u["gstr"][:], start=True, stop=True),
            reads=(f"{un}gstr", "c_allones_bf"), writes=(f"pf{bB}",), tag=f"bB1{c}_{h}")
        p.step("vector", (lambda h=h, bB=bB: lambda e: e.tensor_scalar(
            u["gma"][:], self.pbank(bB, 0, 128), gst[:, 2, h:h + 1], 0.0,
            op0=ALU.subtract, op1=ALU.min))(),
            reads=(f"pf{bB}", "b_gcs"), writes=(f"{un}gma",), tag=f"barg{c}_{h}",
            sreads=("b_gcs",))
        p.step("vector", (lambda h=h, bB=bB: lambda e: e.tensor_tensor(
            gst[:, 6, h:h + 1], gst[:, 2, h:h + 1], self.pf[bB][:, 127:128],
            op=ALU.subtract))(),
            reads=(f"pf{bB}", "b_gcs"), writes=("b_gecv",), tag=f"becv{c}_{h}")
        p.step("scalar", (lambda h=h: lambda e: e.activation(
            gst[:, 6, h:h + 1], gst[:, 6, h:h + 1], AF.Exp))(),
            reads=("b_gecv",), writes=("b_gecv",), tag=f"becve{c}_{h}")
        p.step("scalar", (lambda h=h, bB=bB: lambda e: e.activation(
            gst[:, 7, h:h + 1], self.pf[bB][:, 127:128], AF.Exp, scale=-1.0))(),
            reads=(f"pf{bB}",), writes=("b_ggcb",), tag=f"bgcb{c}_{h}")
        p.step("scalar", (lambda h=h: lambda e: e.activation(
            u["gma"][:], u["gma"][:], AF.Exp))(),
            reads=(f"{un}gma",), writes=(f"{un}gma",), tag=f"bgexp{c}_{h}")
        p.step("vector", lambda e: e.tensor_tensor(
            u["gstr"][:], u["gma"][:], self.MSL, op=ALU.mult),
            reads=(f"{un}gma", "c_tri_bf"), writes=(f"{un}gstr",), tag=f"bgst{c}_{h}")
        p.step("vector", (lambda h=h, bA=bA: lambda e: e.scalar_tensor_tensor(
            u["lu"][0][:], self.pbank(bA, 0, 128), gst[:, 1, h:h + 1], u["gstr"][:],
            op0=ALU.mult, op1=ALU.mult))(),
            reads=(f"pf{bA}", "b_gbs", f"{un}gstr"), writes=(f"{un}lu0",), tag=f"bLu{c}_{h}",
            sreads=("b_gbs",))
        p.step("tensor", lambda e: e.transpose(
            self.pbank_bf(bC), u["lu"][0][:], self.ident_bf[:]),
            reads=(f"{un}lu0", "c_ident_bf"), writes=(f"pf{bC}",), tag=f"bLuT{c}_{h}")
        p.step("scalar", lambda e: e.copy(u["lut"][0][:], self.pbank_bf(bC)),
               reads=(f"pf{bC}",), writes=(f"{un}lut0",), tag=f"bLuTe{c}_{h}")
        p.step("vector", lambda e: e.tensor_tensor(
            u["gmm"][:], u["gma"][:], self.MLI, op=ALU.mult),
            reads=(f"{un}gma", "c_tri_bf"), writes=(f"{un}gmm",), tag=f"bgmm{c}_{h}")
        p.step("tensor", lambda e: e.transpose(
            self.pbank_bf(bC), u["gmm"][:], self.ident_bf[:]),
            reads=(f"{un}gmm", "c_ident_bf"), writes=(f"pf{bC}",), tag=f"bgmT{c}_{h}")
        p.step("scalar", lambda e: e.copy(u["gmt"][:], self.pbank_bf(bC)),
               reads=(f"pf{bC}",), writes=(f"{un}gmt",), tag=f"bgmTe{c}_{h}")
        p.step("tensor", (lambda kT_h=kT_h, qT_h=qT_h, bA=bA: lambda e: e.matmul(
            self.pbank(bA, 0, 128), kT_h, qT_h, start=True, stop=True))(),
            reads=(kres, qres), writes=(f"pf{bA}",), tag=f"bKQ{c}_{h}")
        p.step("vector", (lambda bA=bA: lambda e: e.tensor_tensor(
            u["dt"][:], self.pbank(bA, 0, 128), u["gmt"][:], op=ALU.mult))(),
            reads=(f"pf{bA}", f"{un}gmt"), writes=(f"{un}dt",), tag=f"bDt{c}_{h}")
        p.step("tensor", (lambda kT_h=kT_h, Zb_h=Zb_h, bB=bB: lambda e: e.matmul(
            self.pbank(bB, 0, 64), kT_h, Zb_h, start=True, stop=True))(),
            reads=(kres, f"b_Zb_{h}"), writes=(f"pf{bB}",), tag=f"bR{c}_{h}")
        p.step("vector", (lambda h=h, bB=bB: lambda e: e.tensor_scalar_mul(
            u["xs"][0][:, 0:64], self.pbank(bB, 0, 64), gst[:, 5, h:h + 1]))(),
            reads=(f"pf{bB}", "b_gbg"), writes=(f"{un}xs0",), tag=f"bX0{c}_{h}",
            sreads=("b_gbg",))
        p.step("vector", (lambda h=h: lambda e: e.tensor_scalar_mul(
            u["xs"][0][:, 64:128], qkv[:, 2048 + h * 64:2048 + (h + 1) * 64],
            gst[:, 1, h:h + 1]))(),
            reads=("b_qkv", "b_gbs"), writes=(f"{un}xs0",), tag=f"bU0{c}_{h}",
            sreads=("b_gbs",))
        for lev in range(7):
            cur, nxt = lev % 2, (lev + 1) % 2
            p.step("tensor", (lambda cur=cur, bD=bD: lambda e: e.matmul(
                self.pbank(bD, 0, 128), u["lut"][cur][:], u["xs"][cur][:],
                start=True, stop=True))(),
                reads=(f"{un}lut{cur}", f"{un}xs{cur}"), writes=(f"pf{bD}",),
                tag=f"bsy{c}_{h}_{lev}")
            p.step("vector", (lambda cur=cur, nxt=nxt, bD=bD: lambda e: e.tensor_tensor(
                u["xs"][nxt][:], u["xs"][cur][:], self.pbank(bD, 0, 128),
                op=ALU.subtract))(),
                reads=(f"{un}xs{cur}", f"pf{bD}"), writes=(f"{un}xs{nxt}",),
                tag=f"bsx{c}_{h}_{lev}")
            if lev < 6:
                p.step("tensor", (lambda cur=cur, bC=bC: lambda e: e.matmul(
                    self.pbank(bC, 0, 128), u["lu"][cur][:], u["lut"][cur][:],
                    start=True, stop=True))(),
                    reads=(f"{un}lu{cur}", f"{un}lut{cur}"), writes=(f"pf{bC}",),
                    tag=f"bsq1{c}_{h}_{lev}")
                p.step("scalar", (lambda nxt=nxt, bC=bC: lambda e: e.copy(
                    u["lut"][nxt][:], self.pbank(bC, 0, 128)))(),
                    reads=(f"pf{bC}",), writes=(f"{un}lut{nxt}",),
                    tag=f"bsq1e{c}_{h}_{lev}")
            if lev < 5:
                p.step("tensor", (lambda cur=cur, bC=bC: lambda e: e.matmul(
                    self.pbank(bC, 0, 128), u["lut"][cur][:], u["lu"][cur][:],
                    start=True, stop=True))(),
                    reads=(f"{un}lu{cur}", f"{un}lut{cur}"), writes=(f"pf{bC}",),
                    tag=f"bsq2{c}_{h}_{lev}")
                p.step("scalar", (lambda nxt=nxt, bC=bC: lambda e: e.copy(
                    u["lu"][nxt][:], self.pbank(bC, 0, 128)))(),
                    reads=(f"pf{bC}",), writes=(f"{un}lu{nxt}",),
                    tag=f"bsq2e{c}_{h}_{lev}")
        fin = 7 % 2
        p.step("vector", (lambda fin=fin, lo=lo, hi=hi: lambda e: e.tensor_tensor(
            u["w2"][:, lo:hi], u["xs"][fin][:, 64:128], u["xs"][fin][:, 0:64],
            op=ALU.subtract))(),
            reads=(f"{un}xs{fin}",), writes=(f"{un}w2",), tag=f"bW2{c}_{h}")
        p.step("vector", (lambda h=h, lo=lo, hi=hi: lambda e: e.tensor_scalar_mul(
            u["w3"][:], u["w2"][:, lo:hi], gst[:, 6, h:h + 1]))(),
            reads=(f"{un}w2", "b_gecv"), writes=(f"{un}w3",), tag=f"bW3{c}_{h}",
            sreads=("b_gecv",))
        p.step("tensor", (lambda hc=hc, qtT_h=qtT_h, lo=lo, hi=hi, bD=bD: lambda e:
               e.matmul(self.pbank(bD, 0, 128), Zb[lo:hi, hc, :], qtT_h,
                        start=True, stop=False))(),
            reads=(f"b_Zb_{h}", qtres), writes=(f"pf{bD}",), tag=f"bO1{c}_{h}")
        p.step("tensor", (lambda bD=bD: lambda e: e.matmul(
            self.pbank(bD, 0, 128), u["w2"][:], u["dt"][:],
            start=False, stop=True))(),
            reads=(f"{un}w2", f"{un}dt"), writes=(f"pf{bD}",), tag=f"bO2{c}_{h}")
        p.step("scalar", (lambda c=c, lo=lo, hi=hi, hc=hc, bD=bD: lambda e: e.copy(
            OT[lo:hi, hc, c * 128:(c + 1) * 128],
            self.pbank(bD, 0, 128)[lo:hi, :]))(),
            reads=(f"pf{bD}",), writes=("b_OT",), tag=f"bOe{c}_{h}")
        p.step("tensor", (lambda hc=hc, bB=bB: lambda e: e.matmul(
            self.pbank(bB, 64, 128), qkn[:, 1024 + hc * 128:1024 + (hc + 1) * 128],
            u["w3"][:], start=True, stop=True))(),
            reads=("b_qkn", f"{un}w3"), writes=(f"pf{bB}",), tag=f"bZm{c}_{h}")
        p.step("vector", (lambda h=h, lo=lo, hi=hi, hc=hc, zf_in=zf_in, zf_out=zf_out, bB=bB:
               lambda e: e.scalar_tensor_tensor(
                   zf_out[lo:hi, hc, lo:hi], zf_in[lo:hi, hc, lo:hi], gst[lo:hi, 7, h:h + 1],
                   self.pbank(bB, 64, 128)[lo:hi, :], op0=ALU.mult, op1=ALU.add))(),
            reads=(f"b_Zf{par}_{h}", "b_ggcb", f"pf{bB}"),
            writes=(f"b_Zf{1 - par}_{h}",), tag=f"bZn{c}_{h}", sreads=("b_ggcb",))
        p.step("scalar", (lambda lo=lo, hi=hi, hc=hc, zf_out=zf_out: lambda e: e.copy(
            Zb[lo:hi, hc, lo:hi], zf_out[lo:hi, hc, lo:hi]))(),
            reads=(f"b_Zf{1 - par}_{h}",), writes=(f"b_Zb_{h}",), tag=f"bZbe{c}_{h}")

    def build(self, phases=("A", "F0", "B", "F1")):
        self._rtmp = self.es.enter_context(self.nc.sbuf_tensor("rtmp", [128, 32], F32))
        self.prelude_router()
        chain = list(phases)
        prev = self.dx
        for i, ph in enumerate(chain):
            dst = self.dy if i == len(chain) - 1 else self.d_xres
            if ph == "A":
                self.phase_attn_a(prev, dst)
            elif ph == "B":
                self.phase_attn_b(prev, dst)
            elif ph in ("F0", "F1"):
                self.phase_ffn(prev, dst, "f0_" if ph == "F0" else "f1_")
            prev = self.d_xres
        self.prog.barrier("final")
        sems = {}
        for d in DOMAINS + ("bar", "epo"):
            sems[d] = self.es.enter_context(self.nc.semaphore("sem_" + d))
        self.prog.resolve()
        with self.nc.Block() as block:
            self.prog.emit_all(sems, block)
        self.es.close()
        return self.nc


def _in_maps(inputs, consts):
    x = np.ascontiguousarray(np.asarray(inputs["L0_attn_a_x"], dtype=np.float32))
    base = dict(
        aqkv_w=np.ascontiguousarray(inputs["L0_attn_a_w_qkv"], dtype=np.float32),
        aqkv_b=np.ascontiguousarray(inputs["L0_attn_a_b_qkv"], dtype=np.float32),
        aproj_w=np.ascontiguousarray(inputs["L0_attn_a_w_proj"], dtype=np.float32),
        aproj_b=np.ascontiguousarray(inputs["L0_attn_a_b_proj"], dtype=np.float32),
        bqkv_w=np.ascontiguousarray(inputs["L0_attn_b_Wqkv"], dtype=np.float32),
        bg_w=np.ascontiguousarray(inputs["L0_attn_b_Wg"], dtype=np.float32),
        bg_b=np.ascontiguousarray(inputs["L0_attn_b_bg"], dtype=np.float32),
        bb_w=np.ascontiguousarray(inputs["L0_attn_b_Wb"], dtype=np.float32),
        bb_b=np.ascontiguousarray(inputs["L0_attn_b_bb"], dtype=np.float32),
        bout_w=np.ascontiguousarray(inputs["L0_attn_b_Wout"], dtype=np.float32),
        rw=np.ascontiguousarray(inputs["L0_ffn_router_w"], dtype=np.float32),
        wg=np.ascontiguousarray(inputs["L0_ffn_w_gate"], dtype=np.float32),
        wu=np.ascontiguousarray(inputs["L0_ffn_w_up"], dtype=np.float32),
        wd=np.ascontiguousarray(inputs["L0_ffn_w_down"], dtype=np.float32),
        swg=np.ascontiguousarray(inputs["L0_ffn_sw_gate"], dtype=np.float32),
        swu=np.ascontiguousarray(inputs["L0_ffn_sw_up"], dtype=np.float32),
        swd=np.ascontiguousarray(inputs["L0_ffn_sw_down"], dtype=np.float32),
        **consts,
    )
    return [dict(base, x=np.ascontiguousarray(x[i])) for i in range(N_CORES)]


def kernel(**inputs):
    consts = make_consts()
    mk = MK()
    nc = mk.build()
    maps = _in_maps(inputs, consts)
    res = run_bass_kernel_spmd(nc, maps, core_ids=list(range(N_CORES)))
    out = np.stack([np.asarray(r["y"], dtype=np.float32) for r in res.results], axis=0)
    return out


# version 04
# speedup vs baseline: 1.3415x; 1.2057x over previous
import contextlib

import numpy as np
import ml_dtypes

import concourse.bass as bass
import concourse.mybir as mybir
from concourse.bass_utils import run_bass_kernel_spmd

F32 = mybir.dt.float32
BF16 = mybir.dt.bfloat16
AF = mybir.ActivationFunctionType
ALU = mybir.AluOpType
AX = mybir.AxisListType

N_CORES = 8
T = 4096
D = 1024
H = 16
DH = 64
HID = 4096
NRT = 7

NEG = -1.0e30

ENGINES = ("tensor", "vector", "scalar", "gpsimd", "sync")
QUEUES = ("qs", "qg", "qc")
QUEUE_ENGINE = {"qs": "sync", "qg": "gpsimd", "qc": "scalar"}
NSLOT = {"qs": 8, "qg": 8, "qc": 2}
DOMAINS = tuple(ENGINES) + tuple(f"{q}{i}" for q in QUEUES for i in range(NSLOT[q]))


class Step:
    __slots__ = ("eng", "queue", "slot", "emit", "reads", "writes", "sreads",
                 "needs_inc", "inc_val", "waits", "tag")

    def __init__(self, eng, queue, emit, reads, writes, tag="", sreads=()):
        self.eng = eng
        self.queue = queue
        self.slot = None
        self.emit = emit
        self.reads = tuple(reads)
        self.writes = tuple(writes)
        self.sreads = tuple(sreads)
        self.needs_inc = False
        self.inc_val = None
        self.waits = []
        self.tag = tag

    @property
    def domain(self):
        if self.queue is not None:
            return f"{self.queue}{self.slot}"
        return self.eng


class Prog:
    def __init__(self, full_sync=False):
        self.steps = []
        self.full_sync = full_sync
        self._quse = {q: 0 for q in QUEUES}

    def step(self, eng, emit, reads=(), writes=(), tag="", sreads=()):
        self.steps.append(Step(eng, None, emit, reads, writes, tag, sreads))

    def dma(self, queue, emit, reads=(), writes=(), tag=""):
        s = Step(QUEUE_ENGINE[queue], queue, emit, reads, writes, tag)
        s.slot = self._quse[queue] % NSLOT[queue]
        self._quse[queue] += 1
        self.steps.append(s)

    def barrier(self, tag="barrier", clear=True):
        s = Step("barrier", None, None, (), (), tag)
        s.slot = clear
        self.steps.append(s)

    def resolve(self):
        last_write = {}
        readers = {}
        barrier_pending = []
        deps = [None] * len(self.steps)
        for i, s in enumerate(self.steps):
            if s.eng == "barrier":
                deps[i] = list(barrier_pending)
                barrier_pending = [i]
                last_write = {}
                readers = {}
                continue
            barrier_pending.append(i)
            dmap = {}
            for r in s.reads:
                w = last_write.get(r)
                if w is not None:
                    dmap[w] = dmap.get(w, False) or (r in s.sreads)
            for wr in s.writes:
                w = last_write.get(wr)
                if w is not None:
                    dmap.setdefault(w, False)
                for rd in readers.get(wr, ()):
                    dmap.setdefault(rd, False)
            dmap.pop(i, None)
            deps[i] = dmap
            for r in s.reads:
                readers.setdefault(r, []).append(i)
            for wr in s.writes:
                last_write[wr] = i
                readers[wr] = []

        need = [False] * len(self.steps)
        for i, s in enumerate(self.steps):
            me = s.domain if s.eng != "barrier" else None
            dl = deps[i] if isinstance(deps[i], list) else deps[i].items()
            for jj in dl:
                j, always = (jj, False) if isinstance(jj, int) else jj
                p = self.steps[j]
                if p.eng == "barrier":
                    continue
                fs = (self.full_sync is True or
                      (isinstance(self.full_sync, (set, frozenset, tuple)) and
                       (s.eng in self.full_sync if me is not None else False)))
                if me is None or fs or always or p.domain != me or p.queue is not None:
                    need[j] = True
        counters = {}
        for s, nd in zip(self.steps, need):
            if s.eng == "barrier":
                if s.slot:
                    counters = {d: v for d, v in counters.items() if d not in ENGINES}
                continue
            if s.queue is not None:
                d = s.domain
                counters[d] = counters.get(d, 0) + 16
                s.needs_inc = True
                s.inc_val = counters[d]
            elif nd:
                counters[s.eng] = counters.get(s.eng, 0) + 1
                s.needs_inc = True
                s.inc_val = counters[s.eng]

        known = {e: {} for e in ENGINES}
        for i, s in enumerate(self.steps):
            if s.eng == "barrier":
                targets = {}
                for j in deps[i]:
                    p = self.steps[j]
                    if p.eng == "barrier" or not p.needs_inc:
                        continue
                    targets[p.domain] = max(targets.get(p.domain, 0), p.inc_val)
                s.waits = sorted(targets.items())
                if s.slot:
                    for e in ENGINES:
                        for d, v in s.waits:
                            if d not in ENGINES and known[e].get(d, -1) < v:
                                known[e][d] = v
                        for d in ENGINES:
                            known[e].pop(d, None)
                else:
                    for e in ENGINES:
                        for d, v in s.waits:
                            if known[e].get(d, -1) < v:
                                known[e][d] = v
                continue
            kn = known[s.eng]
            targets = {}
            if s.queue is not None and s.inc_val > 16:
                targets[s.domain] = s.inc_val - 16
            dl = deps[i] if isinstance(deps[i], list) else deps[i].items()
            for jj in dl:
                j, always = (jj, False) if isinstance(jj, int) else jj
                p = self.steps[j]
                if p.eng == "barrier" or not p.needs_inc:
                    continue
                d = p.domain
                if (p.queue is None and d == s.eng and not always and not
                        (self.full_sync is True or
                         (isinstance(self.full_sync, (set, frozenset, tuple)) and s.eng in self.full_sync))):
                    continue
                targets[d] = max(targets.get(d, 0), p.inc_val)
            ws = []
            for d, v in sorted(targets.items()):
                if kn.get(d, -1) >= v:
                    continue
                ws.append((d, v))
                kn[d] = v
            s.waits = ws

    def emit_all(self, sems, block):
        per_engine = {e: [] for e in ENGINES}
        nbar = 0
        for s in self.steps:
            if s.eng == "barrier":
                nbar += 1
                for e in ENGINES:
                    per_engine[e].append(s)
            else:
                per_engine[s.eng].append(s)

        bsem = sems["bar"]
        esem = sems["epo"]

        def run_stream(name, eng):
            bar_k = 0
            for s in per_engine[name]:
                for d, v in s.waits:
                    eng.wait_ge(sems[d], v)
                if s.eng == "barrier":
                    if not s.slot:
                        continue
                    bar_k += 1
                    eng.drain()
                    eng.sem_inc(bsem, 1)
                    if name == "gpsimd":
                        eng.wait_ge(bsem, 5 * bar_k)
                        for d in ENGINES:
                            eng.sem_clear(sems[d])
                        eng.sem_inc(esem, 1)
                    eng.wait_ge(esem, bar_k)
                    continue
                inst = s.emit(eng)
                if s.needs_inc:
                    assert inst is not None, f"step {s.tag} returned no instruction"
                    if s.queue is not None:
                        inst.then_inc(sems[s.domain], 16)
                    else:
                        inst.then_inc(sems[s.eng], 1)

        @block.tensor
        def _(eng):
            run_stream("tensor", eng)

        @block.vector
        def _(eng):
            run_stream("vector", eng)

        @block.scalar
        def _(eng):
            run_stream("scalar", eng)

        @block.gpsimd
        def _(eng):
            run_stream("gpsimd", eng)

        @block.sync
        def _(eng):
            run_stream("sync", eng)


def make_consts():
    c = {}
    c["ident_bf"] = np.eye(128, dtype=ml_dtypes.bfloat16)
    c["ident_f32"] = np.eye(128, dtype=np.float32)
    c["ones_bf"] = np.ones((1, 128), dtype=ml_dtypes.bfloat16)
    c["ones_f32"] = np.ones((1, 128), dtype=np.float32)
    c["ones_col"] = np.ones((128, 1), dtype=ml_dtypes.bfloat16)
    ev = np.zeros((128, 2), dtype=np.float32)
    ev[:, 0] = 1e-5
    ev[:, 1] = 1e-30
    c["epsv"] = ev
    r = np.arange(128)[:, None]
    y = np.arange(896)[None, :]
    kq = np.where(y - r >= 384, 0.0, NEG).astype(np.float32)
    c["amask_kq"] = kq.astype(ml_dtypes.bfloat16)
    l = np.arange(128)[:, None]
    j = np.arange(128)[None, :]
    tri = np.zeros((128, 384), dtype=np.float32)
    tri[:, 0:128] = (l <= j)
    tri[:, 128:256] = (j < l)
    tri[:, 256:384] = (j <= l)
    c["tri_bf"] = tri.astype(ml_dtypes.bfloat16)
    c["allones_bf"] = np.ones((128, 128), dtype=ml_dtypes.bfloat16)
    ir = np.zeros((128, 8), dtype=np.float32)
    ir[:, 0:7] = (7.0 - np.arange(7))[None, :]
    c["iota_rev"] = ir
    return c


CONST_SPECS = [
    ("ident_bf", [128, 128], BF16),
    ("ones_bf", [1, 128], BF16),
    ("ones_f32", [1, 128], F32),
    ("ones_col", [128, 1], BF16),
    ("iota_rev", [128, 8], F32),
    ("epsv", [128, 2], F32),
]
PHASE_CONST_SPECS = [
    ("ident_f32", [128, 128], F32),
    ("tri_bf", [128, 384], BF16), ("allones_bf", [128, 128], BF16),
]


class MK:
    def __init__(self, full_sync=("vector", "scalar"), dbg_n=0):
        self.dbg_n = dbg_n
        self.dbg_i = 0
        self.nc = bass.Bass("TRN2", target_bir_lowering=False,
                            detect_race_conditions=(full_sync is True))
        self.es = contextlib.ExitStack()
        self.prog = Prog(full_sync=full_sync)
        nc = self.nc

        if self.dbg_n:
            self.d_dbg = nc.dram_tensor("dbg", [self.dbg_n, 128, 1024], F32,
                                        kind="ExternalOutput").ap()
        self.dx = nc.dram_tensor("x", [T, D], F32, kind="ExternalInput").ap()
        self.dy = nc.dram_tensor("y", [T, D], F32, kind="ExternalOutput").ap()
        dt = lambda n, s: nc.dram_tensor(n, s, F32, kind="ExternalInput").ap()
        self.d_aqkv_w = dt("aqkv_w", [D, 3 * D])
        self.d_aqkv_b = dt("aqkv_b", [3 * D])
        self.d_aproj_w = dt("aproj_w", [D, D])
        self.d_aproj_b = dt("aproj_b", [D])
        self.d_bqkv_w = dt("bqkv_w", [3 * D, D])
        self.d_bg_w = dt("bg_w", [H, D])
        self.d_bg_b = dt("bg_b", [H])
        self.d_bb_w = dt("bb_w", [H, D])
        self.d_bb_b = dt("bb_b", [H])
        self.d_bout_w = dt("bout_w", [D, D])
        self.d_rw = dt("rw", [NRT, D])
        self.d_wg = dt("wg", [NRT, D, HID])
        self.d_wu = dt("wu", [NRT, D, HID])
        self.d_wd = dt("wd", [NRT, HID, D])
        self.d_swg = dt("swg", [1, D, HID])
        self.d_swu = dt("swu", [1, D, HID])
        self.d_swd = dt("swd", [1, HID, D])
        self.d_xres = nc.dram_tensor("x_res", [T, D], F32).ap()

        self.consts = {}
        for nm, shp, dty in CONST_SPECS:
            dap = nc.dram_tensor(nm, shp, dty, kind="ExternalInput").ap()
            sb = self.es.enter_context(nc.sbuf_tensor("c_" + nm, shp, dty))
            self.consts[nm] = sb
            self.prog.dma("qs", (lambda sb=sb, dap=dap: lambda e: e.dma_start(sb[:], dap))(),
                          writes=("c_" + nm,), tag="ldc_" + nm)
        self.d_amask_kq = nc.dram_tensor("amask_kq", [128, 896], BF16, kind="ExternalInput").ap()

        self.dconst = {}
        for nm, shp, dty in PHASE_CONST_SPECS:
            self.dconst[nm] = nc.dram_tensor(nm, shp, dty, kind="ExternalInput").ap()
        self.ident_bf = self.consts["ident_bf"]
        self.ones_bf = self.consts["ones_bf"]
        self.iota_rev = self.consts["iota_rev"]

        self.pf = [self.es.enter_context(nc.psum_tensor(f"pf{i}", [128, 512], F32))
                   for i in range(8)]

        self.ln_xc = self.es.enter_context(nc.sbuf_tensor("ln_xc", [128, D], F32))
        self.ln_st = self.es.enter_context(nc.sbuf_tensor("ln_st", [128, 8], F32))

        self.rwT_hi = self.es.enter_context(nc.sbuf_tensor("rwT_hi", [128, 8, 8], BF16))
        self.rwT_lo = self.es.enter_context(nc.sbuf_tensor("rwT_lo", [128, 8, 8], BF16))

    def dump(self, ap, reads, tag="dbg"):
        if not self.dbg_n or self.dbg_i >= self.dbg_n:
            return
        i = self.dbg_i
        self.dbg_i += 1
        w = ap.shape[-1] if len(ap.shape) == 2 else 1024
        np_parts = ap.shape[0]
        self.prog.dma("qg", (lambda ap=ap, i=i, np_parts=np_parts, w=w: lambda e: e.dma_start(
            self.d_dbg[i, 0:np_parts, 0:w], ap))(),
            reads=tuple(reads), writes=(f"dbg{i}",), tag=f"{tag}{i}")

    def pbank(self, i, lo=0, hi=512):
        return self.pf[i][:, lo:hi]

    def pbank_bf(self, i, lo=0, hi=128):
        return self.pf[i][:].bitcast(BF16)[:, lo:hi]

    def ln(self, xt_ap, xt_res, xn_ap, xn_res, xlo_ap=None, xlo_res=None, tag=""):
        p = self.prog
        st = self.ln_st
        xc = self.ln_xc
        p.step("vector", lambda e: e.tensor_reduce(st[:, 0:1], xt_ap, axis=AX.X, op=ALU.add),
               reads=(xt_res,), writes=("ln_st",), tag=tag + "sum")
        p.step("scalar", lambda e: e.mul(st[:, 1:2], st[:, 0:1], -1.0 / D),
               reads=("ln_st",), writes=("ln_st",), tag=tag + "nm")
        p.step("vector", lambda e: e.tensor_scalar_add(xc[:], xt_ap, st[:, 1:2]),
               reads=(xt_res, "ln_st"), writes=("ln_xc",), tag=tag + "xc",
               sreads=("ln_st",))
        p.step("scalar", lambda e: e.activation(
            xn_ap, xc[:], AF.Square, accum_out=st[:, 2:3]),
            reads=("ln_xc",), writes=(xn_res, "ln_st"), tag=tag + "ss")
        p.step("scalar", lambda e: e.activation(st[:, 3:4], st[:, 2:3], AF.Ln,
                                                scale=1.0 / D,
                                                bias=self.consts["epsv"][:, 0:1]),
               reads=("ln_st", "c_epsv"), writes=("ln_st",), tag=tag + "std")
        p.step("scalar", lambda e: e.activation(st[:, 4:5], st[:, 3:4], AF.Exp, scale=-0.5),
               reads=("ln_st",), writes=("ln_st",), tag=tag + "rstd")
        p.step("vector", lambda e: e.tensor_scalar_mul(xn_ap, xc[:], st[:, 4:5]),
               reads=("ln_xc", "ln_st"), writes=(xn_res,), tag=tag + "xn",
               sreads=("ln_st",))
        if xlo_ap is not None:
            p.step("vector", lambda e: e.scalar_tensor_tensor(
                xlo_ap, xc[:], st[:, 4:5], xn_ap, op0=ALU.mult, op1=ALU.subtract),
                reads=("ln_xc", "ln_st", xn_res), writes=(xlo_res,), tag=tag + "xlo",
                sreads=("ln_st",))

    def tr128(self, src_ap_fn, src_res, dst_ap_fn, dst_res, n, banks, tag=""):
        p = self.prog
        for c in range(n):
            b = banks[c % len(banks)]
            pap = self.pbank_bf(b)
            p.step("tensor", (lambda c=c, pap=pap: lambda e: e.transpose(
                pap, src_ap_fn(c), self.ident_bf[:]))(),
                reads=(src_res, "c_ident_bf"), writes=(f"pf{b}",), tag=f"{tag}T{c}")
            p.step("scalar", (lambda c=c, pap=pap: lambda e: e.copy(dst_ap_fn(c), pap))(),
                   reads=(f"pf{b}",), writes=(dst_res,), tag=f"{tag}E{c}")

    def prelude_router(self):
        nc, p = self.nc, self.prog
        with contextlib.ExitStack() as ps:
            rwf = ps.enter_context(nc.sbuf_tensor("pr_rwf", [128, D], F32))
            rwh = ps.enter_context(nc.sbuf_tensor("pr_rwh", [128, D], BF16))
            rwl = ps.enter_context(nc.sbuf_tensor("pr_rwl", [128, D], BF16))
            p.step("vector", lambda e: e.memset(rwf[:], 0.0), writes=("pr_rwf",), tag="rw0")
            p.step("vector", lambda e: e.memset(self.rwT_hi[:].rearrange("p a b -> p (a b)"), 0.0),
                   writes=("rwT_hi",), tag="rwh0")
            p.step("vector", lambda e: e.memset(self.rwT_lo[:].rearrange("p a b -> p (a b)"), 0.0),
                   writes=("rwT_lo",), tag="rwl0")
            p.dma("qs", lambda e: e.dma_start(rwf[0:NRT, :], self.d_rw),
                  writes=("pr_rwf",), tag="rwld")
            p.step("vector", lambda e: e.tensor_copy(rwh[:], rwf[:]),
                   reads=("pr_rwf",), writes=("pr_rwh",), tag="rwh")
            p.step("vector", lambda e: e.tensor_tensor(rwl[:], rwf[:], rwh[:], op=ALU.subtract),
                   reads=("pr_rwf", "pr_rwh"), writes=("pr_rwl",), tag="rwl")
            for (src, sres, dst, dres) in [(rwh, "pr_rwh", self.rwT_hi, "rwT_hi"),
                                           (rwl, "pr_rwl", self.rwT_lo, "rwT_lo")]:
                for c in range(8):
                    b = c % 2
                    pap = self.pbank_bf(b)
                    p.step("tensor", (lambda src=src, c=c, pap=pap: lambda e: e.transpose(
                        pap, src[:, c * 128:(c + 1) * 128], self.ident_bf[:]))(),
                        reads=(sres, "c_ident_bf"), writes=(f"pf{b}",), tag=f"rwT{c}")
                    p.step("scalar", (lambda dst=dst, c=c, pap=pap: lambda e: e.copy(
                        dst[:, c, 0:NRT], pap[:, 0:NRT]))(),
                        reads=(f"pf{b}",), writes=(dres,), tag=f"rwTe{c}")
            p.barrier("prelude_router_end")

    def phase_attn_a(self, src_dram, dst_dram):
        nc, p = self.nc, self.prog
        ps = contextlib.ExitStack()
        amk = ps.enter_context(nc.sbuf_tensor("a_amk", [128, 896], BF16))
        p.dma("qs", lambda e: e.dma_start(amk[:], self.d_amask_kq), writes=("a_amk",), tag="a_amk")

        yT = ps.enter_context(nc.sbuf_tensor("a_yT", [128, 8, T], BF16))
        kT = [ps.enter_context(nc.sbuf_tensor(f"a_kT{i}", [65, T], BF16)) for i in range(2)]
        qT = [ps.enter_context(nc.sbuf_tensor(f"a_qT{i}", [65, T], BF16)) for i in range(2)]
        vsb = ps.enter_context(nc.sbuf_tensor("a_v", [128, 32, 192], BF16))
        xt = [ps.enter_context(nc.sbuf_tensor(f"a_xt{i}", [128, D], F32)) for i in range(2)]
        xn = ps.enter_context(nc.sbuf_tensor("a_xn", [128, D], BF16))
        wgr = ps.enter_context(nc.sbuf_tensor("a_wgr", [128, 8, 384], BF16))
        bqc = ps.enter_context(nc.sbuf_tensor("a_bqc", [64, 48], F32))
        bqr = ps.enter_context(nc.sbuf_tensor("a_bqr", [1, D], BF16))
        ptile = [ps.enter_context(nc.sbuf_tensor(f"a_pt{i}", [128, 512], BF16)) for i in range(3)]
        sqq = ps.enter_context(nc.sbuf_tensor("a_sqq", [64, 512], BF16))
        nbr = ps.enter_context(nc.sbuf_tensor("a_nbr", [1, 512], BF16))
        mk1 = ps.enter_context(nc.sbuf_tensor("a_mk1", [1, 2], F32))
        mrb = [ps.enter_context(nc.sbuf_tensor(f"a_mrb{i}", [1, 512], BF16)) for i in range(2)]
        linv = ps.enter_context(nc.sbuf_tensor("a_linv", [1, 512], F32))
        ottmp = ps.enter_context(nc.sbuf_tensor("a_ottmp", [128, 512], BF16))
        xnt_stack = contextlib.ExitStack()
        xnt = xnt_stack.enter_context(nc.sbuf_tensor("a_xnt", [128, 8, T], BF16))

        p.dma("qs", lambda e: e.dma_start(bqc[:], self.d_aqkv_b.rearrange("(c p) -> p c", p=64),
                                          allow_slow_non_contiguous=True),
              writes=("a_bqc",), tag="a_bqc")
        p.dma("qg", lambda e: e.dma_start(bqr[:], self.d_aqkv_b[2 * D:3 * D].unsqueeze(0)),
              writes=("a_bqr",), tag="a_bqr")
        p.step("vector", lambda e: e.memset(vsb[:].rearrange("p a b -> p (a b)"), 0.0),
               writes=("a_v",), tag="a_v0")
        p.step("vector", lambda e: e.memset(vsb[:, :, 96:97], 1.0), writes=("a_v",), tag="a_v1a")
        for hs in range(2):
            p.step("vector", (lambda hs=hs: lambda e: e.memset(kT[hs][64:65, :], 1.0))(),
                   writes=(f"a_kT{hs}r64",), tag=f"a_k1_{hs}")

        for i in range(32):
            b = i % 2
            p.dma("qs", (lambda i=i, b=b: lambda e: e.dma_start(
                xt[b][:], src_dram[i * 128:(i + 1) * 128, :]))(),
                writes=(f"a_xt{b}",), tag=f"a0ld{i}")
            self.ln(xt[b][:], f"a_xt{b}", xn[:], "a_xn", tag=f"a0ln{i}_")
            self.tr128(lambda c: xn[:, c * 128:(c + 1) * 128], "a_xn",
                       (lambda c, i=i: xnt[:, c, i * 128:(i + 1) * 128]),
                       "a_xnt", 8, [6, 7], tag=f"a0tr{i}_")

        for g in range(8):
            for mat in range(3):
                col0 = mat * D + g * 128
                p.dma("qg", (lambda mat=mat, col0=col0: lambda e: e.dma_start(
                    wgr[:, :, mat * 128:(mat + 1) * 128],
                    self.d_aqkv_w.rearrange("(c p) n -> p c n", p=128)[:, :, col0:col0 + 128]))(),
                    writes=("a_wgr",), tag=f"a1w{g}_{mat}")
            for tb in range(8):
                t0 = tb * 512
                for mat in range(2):
                    for hs in range(2):
                        bk = (tb * 4 + mat * 2 + hs) % 4
                        for c in range(8):
                            p.step("tensor", (lambda mat=mat, hs=hs, c=c, t0=t0, bk=bk:
                                   lambda e: e.matmul(
                                       self.pbank(bk)[0:64, :],
                                       wgr[:, c, mat * 128 + hs * 64:mat * 128 + (hs + 1) * 64],
                                       xnt[:, c, t0:t0 + 512],
                                       start=(c == 0), stop=(c == 7)))(),
                                   reads=("a_wgr", "a_xnt"), writes=(f"pf{bk}",),
                                   tag=f"a1qk{g}_{tb}_{mat}_{hs}")
                        dst = qT if mat == 0 else kT
                        dres = "a_qT" if mat == 0 else "a_kT"
                        bcol = (mat * 1024 + g * 128 + hs * 64) // 64
                        p.step("scalar", (lambda dst=dst, hs=hs, t0=t0, bk=bk, bcol=bcol:
                               lambda e: e.activation(
                                   dst[hs][0:64, t0:t0 + 512],
                                   self.pf[bk][0:64, 0:512],
                                   AF.Identity, bias=bqc[0:64, bcol:bcol + 1]))(),
                               reads=(f"pf{bk}", "a_bqc"), writes=(f"{dres}{hs}",),
                               tag=f"a1qke{g}_{tb}_{mat}_{hs}", sreads=("a_bqc",))
                for ts in range(4):
                    bk = 4 + ts % 2
                    p.step("tensor", (lambda g=g, bk=bk: lambda e: e.matmul(
                        self.pbank(bk, 0, 128), self.ones_bf[0:1, 0:128],
                        bqr[0:1, g * 128:(g + 1) * 128],
                        start=True, stop=False))(),
                        reads=("a_bqr", "c_ones_bf"), writes=(f"pf{bk}",),
                        tag=f"a1vb{g}_{tb}_{ts}")
                    for c in range(8):
                        p.step("tensor", (lambda c=c, t0=t0, ts=ts, bk=bk: lambda e:
                               e.matmul(self.pbank(bk, 0, 128),
                                        xnt[:, c, t0 + ts * 128:t0 + (ts + 1) * 128],
                                        wgr[:, c, 256:384],
                                        start=False, stop=(c == 7)))(),
                               reads=("a_wgr", "a_xnt"), writes=(f"pf{bk}",),
                               tag=f"a1v{g}_{tb}_{ts}_{c}")
                    for hs in range(2):
                        c0 = 0 if hs == 0 else 128
                        p.step("scalar", (lambda tb=tb, ts=ts, hs=hs, bk=bk, c0=c0: lambda e:
                               e.copy(vsb[:, tb * 4 + ts, c0:c0 + 64],
                                      self.pf[bk][:, hs * 64:(hs + 1) * 64]))(),
                            reads=(f"pf{bk}",), writes=("a_v",), tag=f"a1ve{g}_{tb}_{ts}_{hs}")
            for hs in range(2):
                h = 2 * g + hs
                for mat in (1, 0):
                    srcT = qT[hs] if mat == 0 else kT[hs]
                    sres = f"a_qT{hs}" if mat == 0 else f"a_kT{hs}"
                    for j in range(8):
                        bk = j % 2
                        p.step("scalar", (lambda srcT=srcT, j=j: lambda e: e.activation(
                            sqq[:], srcT[0:64, j * 512:(j + 1) * 512], AF.Square))(),
                            reads=(sres,), writes=("a_sqq",), tag=f"aN2_{h}_{mat}_{j}")
                        p.step("tensor", (lambda bk=bk: lambda e: e.matmul(
                            self.pf[bk][0:1, 0:512], self.consts["ones_col"][0:64, :],
                            sqq[:], start=True, stop=True))(),
                            reads=("a_sqq", "c_ones_col"), writes=(f"pf{bk}",),
                            tag=f"aNm_{h}_{mat}_{j}")
                        p.step("scalar", (lambda bk=bk: lambda e:
                               e.activation(nbr[:], self.pf[bk][0:1, 0:512], AF.Ln))(),
                               reads=(f"pf{bk}",), writes=("a_nbr",), tag=f"aNe_{h}_{mat}_{j}")
                        if mat == 1:
                            p.step("vector", lambda e: e.reduce_max(
                                mk1[:, 1:2], nbr[:], axis=AX.X),
                                reads=("a_nbr",), writes=("a_mk1b",), tag=f"aNb_{h}_{j}")
                            if j == 0:
                                p.step("vector", lambda e: e.tensor_copy(
                                    mk1[:, 0:1], mk1[:, 1:2]),
                                    reads=("a_mk1b",), writes=("a_mk1",), tag=f"aNc_{h}_{j}")
                            else:
                                p.step("vector", lambda e: e.tensor_tensor(
                                    mk1[:, 0:1], mk1[:, 0:1], mk1[:, 1:2], op=ALU.max),
                                    reads=("a_mk1b", "a_mk1"), writes=("a_mk1",),
                                    tag=f"aNc_{h}_{j}")
                        else:
                            mb = j % 2
                            p.step("vector", (lambda mb=mb: lambda e: e.tensor_scalar_add(
                                mrb[mb][:], nbr[:], mk1[:, 0:1]))(),
                                reads=("a_nbr", "a_mk1"), writes=(f"a_mrb{mb}",),
                                tag=f"aNr_{h}_{j}", sreads=("a_mk1",))
                            p.step("scalar", (lambda mb=mb: lambda e: e.activation(
                                mrb[mb][:], mrb[mb][:], AF.Exp, scale=0.5))(),
                                reads=(f"a_mrb{mb}",), writes=(f"a_mrb{mb}",),
                                tag=f"aNx_{h}_{j}")
                            p.step("vector", (lambda mb=mb: lambda e: e.tensor_scalar_mul(
                                mrb[mb][:], mrb[mb][:], -1.0))(),
                                reads=(f"a_mrb{mb}",), writes=(f"a_mrb{mb}",),
                                tag=f"aNn_{h}_{j}")
                            p.dma("qs", (lambda hs=hs, j=j, mb=mb: lambda e: e.dma_start(
                                qT[hs][64:65, j * 512:(j + 1) * 512], mrb[mb][:]))(),
                                reads=(f"a_mrb{mb}",), writes=(f"a_qT{hs}r64",),
                                tag=f"aNw_{h}_{j}")
                for qb in range(8):
                    q0 = qb * 512
                    nkt = 4 * (qb + 1)
                    lrow = 96 if hs == 0 else 32
                    vlo = 0 if hs == 0 else 64
                    for kt in range(nkt):
                        bk = kt % 3
                        pt = kt % 3
                        p.step("tensor", (lambda hs=hs, q0=q0, kt=kt, bk=bk: lambda e: e.matmul(
                            self.pbank(bk), kT[hs][0:65, kt * 128:(kt + 1) * 128],
                            qT[hs][0:65, q0:q0 + 512], start=True, stop=True))(),
                            reads=(f"a_qT{hs}", f"a_qT{hs}r64", f"a_kT{hs}", f"a_kT{hs}r64"),
                            writes=(f"pf{bk}",), tag=f"aS2_{h}_{qb}_{kt}")
                        if kt >= 4 * qb:
                            j = kt - 4 * qb
                            p.step("vector", (lambda j=j, bk=bk: lambda e: e.tensor_tensor(
                                self.pbank(bk), self.pbank(bk),
                                amk[:, 384 - 128 * j:896 - 128 * j], op=ALU.add))(),
                                reads=(f"pf{bk}", "a_amk"), writes=(f"pf{bk}",),
                                tag=f"aS2m_{h}_{qb}_{kt}")
                        p.step("scalar", (lambda pt=pt, bk=bk: lambda e: e.activation(
                            ptile[pt][:], self.pbank(bk), AF.Exp, scale=0.125))(),
                            reads=(f"pf{bk}",), writes=(f"a_pt{pt}",),
                            tag=f"aS2e_{h}_{qb}_{kt}")
                        p.step("tensor", (lambda kt=kt, pt=pt, nkt=nkt, vlo=vlo: lambda e:
                               e.matmul(
                                   self.pf[4][:, 0:512],
                                   vsb[:, kt, vlo:vlo + 128],
                                   ptile[pt][:],
                                   start=(kt == 0), stop=(kt == nkt - 1)))(),
                            reads=("a_v", f"a_pt{pt}"), writes=("pf4",),
                            tag=f"aS2o_{h}_{qb}_{kt}")
                    p.step("vector", (lambda lrow=lrow: lambda e: e.reciprocal(
                        linv[:], self.pf[4][lrow:lrow + 1, 0:512]))(),
                        reads=("pf4",), writes=("a_linv",), tag=f"aL_{h}_{qb}")
                    p.step("tensor", lambda e: e.matmul(
                        self.pf[5][:, 0:512],
                        self.consts["ones_f32"][0:1, 0:128],
                        linv[:], start=True, stop=True),
                        reads=("a_linv", "c_ones_f32"), writes=("pf5",), tag=f"aLb_{h}_{qb}")
                    p.step("scalar", (lambda hs=hs: lambda e: e.copy(
                        ottmp[hs * 64:(hs + 1) * 64, :],
                        self.pf[4][hs * 64:(hs + 1) * 64, 0:512]))(),
                        reads=("pf4",), writes=("a_ottmp",), tag=f"aOe_{h}_{qb}")
                    p.step("vector", (lambda hs=hs, g=g, q0=q0: lambda e: e.tensor_tensor(
                        yT[hs * 64:(hs + 1) * 64, g, q0:q0 + 512],
                        ottmp[hs * 64:(hs + 1) * 64, :],
                        self.pf[5][hs * 64:(hs + 1) * 64, 0:512], op=ALU.mult))(),
                        reads=("a_ottmp", "pf5"), writes=("a_yT",),
                        tag=f"aY_{h}_{qb}")

        p.barrier("a_proj_b", clear=False)
        xnt_stack.close()
        wpj = ps.enter_context(nc.sbuf_tensor("a_wpj", [128, 8, D], BF16))
        bpr = ps.enter_context(nc.sbuf_tensor("a_bpr", [1, D], BF16))
        xo = [ps.enter_context(nc.sbuf_tensor(f"a_xo{i}", [128, D], F32)) for i in range(2)]
        p.dma("qg", lambda e: e.dma_start(
            wpj[:], self.d_aproj_w.rearrange("(c p) n -> p c n", p=128)),
            writes=("a_wpj",), tag="a_wpj")
        p.dma("qg", lambda e: e.dma_start(bpr[:], self.d_aproj_b.unsqueeze(0)),
              writes=("a_bpr",), tag="a_bpr")
        for ti in range(32):
            b = ti % 2
            p.dma("qs", (lambda ti=ti, b=b: lambda e: e.dma_start(
                xt[b][:], src_dram[ti * 128:(ti + 1) * 128, :]))(),
                writes=(f"a_xt{b}",), tag=f"apld{ti}")
            for half in range(2):
                bk = 2 * b + half
                p.step("tensor", (lambda half=half, bk=bk: lambda e: e.matmul(
                    self.pbank(bk), self.ones_bf[0:1, 0:128],
                    bpr[0:1, half * 512:(half + 1) * 512], start=True, stop=False))(),
                    reads=("a_bpr", "c_ones_bf"), writes=(f"pf{bk}",), tag=f"apb{ti}_{half}")
                for c in range(8):
                    p.step("tensor", (lambda ti=ti, half=half, c=c, bk=bk: lambda e: e.matmul(
                        self.pbank(bk), yT[:, c, ti * 128:(ti + 1) * 128],
                        wpj[:, c, half * 512:(half + 1) * 512],
                        start=False, stop=(c == 7)))(),
                        reads=("a_yT", "a_wpj"), writes=(f"pf{bk}",), tag=f"ap{ti}_{half}_{c}")
                p.step("vector", (lambda ti=ti, b=b, half=half, bk=bk: lambda e: e.tensor_tensor(
                    xo[b][:, half * 512:(half + 1) * 512], xt[b][:, half * 512:(half + 1) * 512],
                    self.pbank(bk), op=ALU.add))(),
                    reads=(f"a_xt{b}", f"pf{bk}"), writes=(f"a_xo{b}",), tag=f"apadd{ti}_{half}")
            p.dma("qs", (lambda ti=ti, b=b: lambda e: e.dma_start(
                dst_dram[ti * 128:(ti + 1) * 128, :], xo[b][:]))(),
                reads=(f"a_xo{b}",), writes=(f"dst{ti}",), tag=f"apst{ti}")
        p.barrier("a_end")
        ps.close()

    def phase_ffn(self, src_dram, dst_dram, pfx):
        nc, p = self.nc, self.prog
        ps = contextlib.ExitStack()
        NTB = 4
        TBS = 1024
        xacc = ps.enter_context(nc.sbuf_tensor(pfx + "xacc", [128, 8, 1024], F32))
        xnt = ps.enter_context(nc.sbuf_tensor(pfx + "xnt", [128, 8, 1024], BF16))
        xlt = ps.enter_context(nc.sbuf_tensor(pfx + "xlt", [128, 8, 128], BF16))
        xnb = ps.enter_context(nc.sbuf_tensor(pfx + "xnb", [128, D], BF16))
        xlb = ps.enter_context(nc.sbuf_tensor(pfx + "xlb", [128, D], BF16))
        hsb = ps.enter_context(nc.sbuf_tensor(pfx + "h", [128, 32, 1024], BF16))
        wdsb = ps.enter_context(nc.sbuf_tensor(pfx + "wd", [128, 32, 1024], BF16))
        wgu = [ps.enter_context(nc.sbuf_tensor(f"{pfx}wgu{i}", [128, 8, 256], BF16))
               for i in range(2)]
        sg = [ps.enter_context(nc.sbuf_tensor(f"{pfx}sg{i}", [128, 512], BF16))
              for i in range(4)]
        logits = ps.enter_context(nc.sbuf_tensor(pfx + "lg", [128, 8], F32))
        cmb = ps.enter_context(nc.sbuf_tensor(pfx + "cmb", [128, 8, 8], F32))
        rsc = ps.enter_context(nc.sbuf_tensor(pfx + "rsc", [128, 8], F32))

        for tb in range(NTB):
            r0 = tb * TBS
            p.dma("qs", (lambda r0=r0: lambda e: e.dma_start(
                xacc[:],
                src_dram[r0:r0 + TBS, :].rearrange("(i p) n -> p i n", p=128)))(),
                writes=(pfx + "xacc",), tag=f"{pfx}xld{tb}")
            for ti in range(8):
                self.ln(xacc[:, ti, :], pfx + "xacc", xnb[:], pfx + "xnb",
                        xlb[:], pfx + "xlb", tag=f"{pfx}ln{tb}_{ti}_")
                if tb == 0 and ti == 0:
                    self.dump(xnb[:], (pfx + "xnb",), "d_xnb")
                    self.dump(self.ln_st[:], ("ln_st",), "d_st")
                self.tr128(lambda c: xnb[:, c * 128:(c + 1) * 128], pfx + "xnb",
                           (lambda c, ti=ti: xnt[:, c, ti * 128:(ti + 1) * 128]),
                           pfx + "xnt", 8, [0, 1], tag=f"{pfx}trh{tb}_{ti}_")
                self.tr128(lambda c: xlb[:, c * 128:(c + 1) * 128], pfx + "xlb",
                           (lambda c: xlt[:, c, :]),
                           pfx + "xlt", 8, [0, 1], tag=f"{pfx}trl{tb}_{ti}_")
                bk = 2 + ti % 2
                terms = []
                for (lt, lres, rt, rres) in [(xnt, pfx + "xnt", self.rwT_hi, "rwT_hi"),
                                             (xlt, pfx + "xlt", self.rwT_hi, "rwT_hi"),
                                             (xnt, pfx + "xnt", self.rwT_lo, "rwT_lo")]:
                    for c in range(8):
                        lap = (lt[:, c, ti * 128:(ti + 1) * 128] if lt is xnt
                               else lt[:, c, :])
                        terms.append((lap, lres, rt, rres, c))
                for k, (lap, lres, rt, rres, c) in enumerate(terms):
                    p.step("tensor", (lambda lap=lap, rt=rt, c=c, bk=bk,
                           st=(k == 0), sp=(k == len(terms) - 1):
                           lambda e: e.matmul(self.pbank(bk, 0, 8), lap, rt[:, c, 0:8],
                                              start=st, stop=sp))(),
                           reads=(lres, rres), writes=(f"pf{bk}",),
                           tag=f"{pfx}rt{tb}_{ti}_{k}")
                p.step("scalar", (lambda bk=bk: lambda e: e.copy(logits[:], self.pf[bk][:, 0:8]))(),
                       reads=(f"pf{bk}",), writes=(pfx + "lg",), tag=f"{pfx}rte{tb}_{ti}")
                if tb == 0 and ti == 0:
                    self.dump(logits[:], (pfx + "lg",), "d_lg")
                self._router_select(pfx, tb, ti, logits, rsc, cmb)
                if tb == 0 and ti == 0:
                    self.dump(cmb[:].rearrange("p a b -> p (a b)"), (pfx + "cmb",), "d_cmb")
            for ei in range(8):
                self._expert(pfx, tb, ei, xnt, hsb, wdsb, wgu, sg, xacc, cmb)
                if tb == 0 and ei in (0, 7):
                    self.dump(hsb[:, 0, :], (pfx + "h",), "d_h")
                    self.dump(xacc[:, 0, :], (pfx + "xacc",), "d_xacc")
            p.dma("qs", (lambda r0=r0: lambda e: e.dma_start(
                dst_dram[r0:r0 + TBS, :].rearrange("(i p) n -> p i n", p=128), xacc[:]))(),
                reads=(pfx + "xacc",), writes=(f"{pfx}st{tb}",), tag=f"{pfx}xst{tb}")
        p.barrier(pfx + "end")
        ps.close()

    def _router_select(self, pfx, tb, ti, logits, rsc, cmb):
        p = self.prog
        LG, RS = pfx + "lg", pfx + "rsc"
        p.step("vector", lambda e: e.reduce_max(rsc[:, 0:1], logits[:, 0:NRT], axis=AX.X),
               reads=(LG,), writes=(RS,), tag=f"{pfx}s1{tb}{ti}")
        p.step("vector", lambda e: e.tensor_scalar(
            self._rtmp[:, 0:NRT], logits[:, 0:NRT], rsc[:, 0:1], None, op0=ALU.is_equal),
            reads=(LG, RS), writes=("rtmp",), tag=f"{pfx}s2{tb}{ti}", sreads=(RS,))
        p.step("vector", lambda e: e.tensor_tensor(
            self._rtmp[:, 8:8 + NRT], self._rtmp[:, 0:NRT], self.iota_rev[:, 0:NRT], op=ALU.mult),
            reads=("rtmp", "c_iota_rev"), writes=("rtmp",), tag=f"{pfx}s3{tb}{ti}")
        p.step("vector", lambda e: e.reduce_max(rsc[:, 1:2], self._rtmp[:, 8:8 + NRT], axis=AX.X),
               reads=("rtmp",), writes=(RS,), tag=f"{pfx}s4{tb}{ti}")
        p.step("vector", lambda e: e.tensor_scalar(
            self._rtmp[:, 0:NRT], self.iota_rev[:, 0:NRT], rsc[:, 1:2], None, op0=ALU.is_equal),
            reads=("c_iota_rev", RS), writes=("rtmp",), tag=f"{pfx}s5{tb}{ti}",
            sreads=(RS,))
        p.step("vector", lambda e: e.scalar_tensor_tensor(
            self._rtmp[:, 16:16 + NRT], self._rtmp[:, 0:NRT], NEG, logits[:, 0:NRT],
            op0=ALU.mult, op1=ALU.add),
            reads=("rtmp", LG), writes=("rtmp",), tag=f"{pfx}s6{tb}{ti}")
        p.step("vector", lambda e: e.reduce_max(rsc[:, 2:3], self._rtmp[:, 16:16 + NRT], axis=AX.X),
               reads=("rtmp",), writes=(RS,), tag=f"{pfx}s7{tb}{ti}")
        p.step("vector", lambda e: e.tensor_scalar(
            self._rtmp[:, 24:24 + NRT], self._rtmp[:, 16:16 + NRT], rsc[:, 2:3], None,
            op0=ALU.is_equal),
            reads=("rtmp", RS), writes=("rtmp",), tag=f"{pfx}s8{tb}{ti}",
            sreads=(RS,))
        p.step("vector", lambda e: e.tensor_tensor(
            rsc[:, 3:4], rsc[:, 0:1], rsc[:, 2:3], op=ALU.subtract),
            reads=(RS,), writes=(RS,), tag=f"{pfx}s9{tb}{ti}")
        p.step("scalar", lambda e: e.activation(rsc[:, 4:5], rsc[:, 3:4], AF.Sigmoid),
               reads=(RS,), writes=(RS,), tag=f"{pfx}sa{tb}{ti}")
        p.step("scalar", lambda e: e.activation(rsc[:, 5:6], rsc[:, 3:4], AF.Sigmoid, scale=-1.0),
               reads=(RS,), writes=(RS,), tag=f"{pfx}sb{tb}{ti}")
        p.step("vector", lambda e: e.tensor_scalar_mul(
            self._rtmp[:, 0:NRT], self._rtmp[:, 0:NRT], rsc[:, 4:5]),
            reads=("rtmp", RS), writes=("rtmp",), tag=f"{pfx}sc{tb}{ti}", sreads=(RS,))
        p.step("vector", (lambda ti=ti: lambda e: e.scalar_tensor_tensor(
            cmb[:, ti, 0:NRT], self._rtmp[:, 24:24 + NRT], rsc[:, 5:6], self._rtmp[:, 0:NRT],
            op0=ALU.mult, op1=ALU.add))(),
            reads=("rtmp", RS), writes=(pfx + "cmb",), tag=f"{pfx}sd{tb}{ti}",
            sreads=(RS,))

    def _expert(self, pfx, tb, ei, xnt, hsb, wdsb, wgu, sg, xacc, cmb):
        nc, p = self.nc, self.prog
        shared = (ei == NRT)
        dwg = self.d_swg if shared else self.d_wg
        dwu = self.d_swu if shared else self.d_wu
        dwd = self.d_swd if shared else self.d_wd
        eidx = 0 if shared else ei
        for m in range(32):
            if 4 <= m < 20 and m % 2 == 0:
                mg = (m - 4) // 2
                p.dma("qg", (lambda dwd=dwd, eidx=eidx, mg=mg: lambda e: e.dma_start(
                    wdsb[:, mg * 4:(mg + 1) * 4, :],
                    dwd[eidx].rearrange("(m p) n -> p m n", p=128)[:, mg * 4:(mg + 1) * 4, :]))(),
                    writes=(f"{pfx}wd",), tag=f"{pfx}wd{tb}_{ei}_{mg}")
            sl = m % 2
            p.dma("qg", (lambda dwg=dwg, eidx=eidx, m=m, sl=sl: lambda e: e.dma_start(
                wgu[sl][:, :, 0:128],
                dwg[eidx].rearrange("(c p) n -> p c n", p=128)[:, :, m * 128:(m + 1) * 128]))(),
                writes=(f"{pfx}wgu{sl}g",), tag=f"{pfx}lg{tb}_{ei}_{m}")
            p.dma("qg", (lambda dwu=dwu, eidx=eidx, m=m, sl=sl: lambda e: e.dma_start(
                wgu[sl][:, :, 128:256],
                dwu[eidx].rearrange("(c p) n -> p c n", p=128)[:, :, m * 128:(m + 1) * 128]))(),
                writes=(f"{pfx}wgu{sl}u",), tag=f"{pfx}lu{tb}_{ei}_{m}")
            for th in range(2):
                bg = (m % 2) * 4 + th
                bu = (m % 2) * 4 + 2 + th
                for c in range(8):
                    p.step("tensor", (lambda m=m, sl=sl, th=th, c=c, bg=bg: lambda e: e.matmul(
                        self.pbank(bg), wgu[sl][:, c, 0:128],
                        xnt[:, c, th * 512:(th + 1) * 512],
                        start=(c == 0), stop=(c == 7)))(),
                        reads=(f"{pfx}wgu{sl}g", pfx + "xnt"), writes=(f"pf{bg}",),
                        tag=f"{pfx}g{tb}_{ei}_{m}_{th}")
                for c in range(8):
                    p.step("tensor", (lambda m=m, sl=sl, th=th, c=c, bu=bu: lambda e: e.matmul(
                        self.pbank(bu), wgu[sl][:, c, 128:256],
                        xnt[:, c, th * 512:(th + 1) * 512],
                        start=(c == 0), stop=(c == 7)))(),
                        reads=(f"{pfx}wgu{sl}u", pfx + "xnt"), writes=(f"pf{bu}",),
                        tag=f"{pfx}u{tb}_{ei}_{m}_{th}")
                si = (m % 2) * 2 + th
                p.step("scalar", (lambda si=si, bg=bg: lambda e: e.activation(
                    sg[si][:], self.pbank(bg), AF.Silu))(),
                    reads=(f"pf{bg}",), writes=(f"{pfx}sg{si}",),
                    tag=f"{pfx}si{tb}_{ei}_{m}_{th}")
                p.step("vector", (lambda m=m, th=th, si=si, bu=bu: lambda e: e.tensor_tensor(
                    hsb[:, m, th * 512:(th + 1) * 512], sg[si][:], self.pbank(bu),
                    op=ALU.mult))(),
                    reads=(f"{pfx}sg{si}", f"pf{bu}"), writes=(pfx + "h",),
                    tag=f"{pfx}hm{tb}_{ei}_{m}_{th}")
        for tg in range(4):
            for m in range(32):
                for tt in range(2):
                    for dmh in range(2):
                        bk = (tg % 2) * 4 + tt * 2 + dmh
                        p.step("tensor", (lambda m=m, tg=tg, tt=tt, dmh=dmh, bk=bk:
                               lambda e: e.matmul(
                                   self.pbank(bk),
                                   hsb[:, m, tg * 256 + tt * 128:tg * 256 + (tt + 1) * 128],
                                   wdsb[:, m, dmh * 512:(dmh + 1) * 512],
                                   start=(m == 0), stop=(m == 31)))(),
                               reads=(pfx + "h", f"{pfx}wd"), writes=(f"pf{bk}",),
                               tag=f"{pfx}d{tb}_{ei}_{tg}_{m}_{tt}_{dmh}")
            for tt in range(2):
                ti = tg * 2 + tt
                for dmh in range(2):
                    bk = (tg % 2) * 4 + tt * 2 + dmh
                    if shared:
                        p.step("vector", (lambda ti=ti, dmh=dmh, bk=bk: lambda e: e.tensor_tensor(
                            xacc[:, ti, dmh * 512:(dmh + 1) * 512],
                            xacc[:, ti, dmh * 512:(dmh + 1) * 512],
                            self.pbank(bk), op=ALU.add))(),
                            reads=(f"pf{bk}",), writes=(pfx + "xacc",),
                            tag=f"{pfx}mx{tb}_{ei}_{ti}_{dmh}")
                    else:
                        p.step("vector", (lambda ti=ti, dmh=dmh, bk=bk, ei=ei: lambda e:
                               e.scalar_tensor_tensor(
                                   xacc[:, ti, dmh * 512:(dmh + 1) * 512],
                                   self.pbank(bk), cmb[:, ti, ei:ei + 1],
                                   xacc[:, ti, dmh * 512:(dmh + 1) * 512],
                                   op0=ALU.mult, op1=ALU.add))(),
                               reads=(f"pf{bk}", pfx + "cmb"), writes=(pfx + "xacc",),
                               tag=f"{pfx}mx{tb}_{ei}_{ti}_{dmh}", sreads=(pfx + "cmb",))

    def phase_attn_b(self, src_dram, dst_dram):
        nc, p = self.nc, self.prog
        ps = contextlib.ExitStack()
        tri = ps.enter_context(nc.sbuf_tensor("b_tri", [128, 384], BF16))
        allon = ps.enter_context(nc.sbuf_tensor("b_allones", [128, 128], BF16))
        p.dma("qs", lambda e: e.dma_start(tri[:], self.dconst["tri_bf"]),
              writes=("c_tri_bf",), tag="b_tri")
        p.dma("qs", lambda e: e.dma_start(allon[:], self.dconst["allones_bf"]),
              writes=("c_allones_bf",), tag="b_allon")
        self.UO = tri[:, 0:128]
        self.MSL = tri[:, 128:256]
        self.MLI = tri[:, 256:384]
        self.allones = allon[:]
        wqkvT = ps.enter_context(nc.sbuf_tensor("b_wqkvT", [128, 8, 3 * D], BF16))
        woutT = ps.enter_context(nc.sbuf_tensor("b_woutT", [128, 8, D], BF16))
        wgbT = ps.enter_context(nc.sbuf_tensor("b_wgbT", [128, 8, 32], BF16))
        gbr = ps.enter_context(nc.sbuf_tensor("b_gbr", [1, 32], BF16))
        OT = ps.enter_context(nc.sbuf_tensor("b_OT", [128, 8, T], BF16))
        stg = ps.enter_context(nc.sbuf_tensor("b_stg", [128, D], BF16))
        stgf = ps.enter_context(nc.sbuf_tensor("b_stgf", [128, D], F32))
        xt = ps.enter_context(nc.sbuf_tensor("b_xt", [128, D], F32))
        xn = ps.enter_context(nc.sbuf_tensor("b_xn", [128, D], BF16))
        xnt = ps.enter_context(nc.sbuf_tensor("b_xnt", [128, 8, 128], BF16))
        qkv = ps.enter_context(nc.sbuf_tensor("b_qkv", [128, 3 * D], F32))
        qkn = ps.enter_context(nc.sbuf_tensor("b_qkn", [128, 2048], BF16))
        qtn = ps.enter_context(nc.sbuf_tensor("b_qtn", [128, D], BF16))
        kqT = ps.enter_context(nc.sbuf_tensor("b_kqT", [128, 16, 128], BF16))
        qtT = ps.enter_context(nc.sbuf_tensor("b_qtT", [128, 8, 128], BF16))
        gst = ps.enter_context(nc.sbuf_tensor("b_gst", [128, 16, 16], F32))
        spb = ps.enter_context(nc.sbuf_tensor("b_spb", [128, 16], BF16))
        sqs = ps.enter_context(nc.sbuf_tensor("b_sqs", [128, 2048], F32))
        ssn = ps.enter_context(nc.sbuf_tensor("b_ssn", [128, 32], F32))
        Zf = [ps.enter_context(nc.sbuf_tensor(f"b_Zf{i}", [128, 8, 128], F32)) for i in range(2)]
        Zb = ps.enter_context(nc.sbuf_tensor("b_Zb", [128, 8, 128], BF16))
        U = [dict(
            gma=ps.enter_context(nc.sbuf_tensor(f"b_gma{i}", [128, 128], BF16)),
            gstr=ps.enter_context(nc.sbuf_tensor(f"b_gstr{i}", [128, 128], BF16)),
            gmm=ps.enter_context(nc.sbuf_tensor(f"b_gmm{i}", [128, 128], BF16)),
            gmt=ps.enter_context(nc.sbuf_tensor(f"b_gmt{i}", [128, 128], BF16)),
            dt=ps.enter_context(nc.sbuf_tensor(f"b_dt{i}", [128, 128], BF16)),
            lu=[ps.enter_context(nc.sbuf_tensor(f"b_lu{i}_{j}", [128, 128], BF16)) for j in range(2)],
            lut=[ps.enter_context(nc.sbuf_tensor(f"b_lut{i}_{j}", [128, 128], BF16)) for j in range(2)],
            xs=[ps.enter_context(nc.sbuf_tensor(f"b_xs{i}_{j}", [128, 128], BF16)) for j in range(2)],
            w2=ps.enter_context(nc.sbuf_tensor(f"b_w2{i}", [128, 128], BF16)),
            w3=ps.enter_context(nc.sbuf_tensor(f"b_w3{i}", [128, 64], BF16)),
        ) for i in range(4)]

        p.step("vector", lambda e: e.memset(Zf[0][:].rearrange("p a b -> p (a b)"), 0.0),
               writes=tuple(f"b_Zf0_{h}" for h in range(16)), tag="bZ0")
        p.step("vector", lambda e: e.memset(Zf[1][:].rearrange("p a b -> p (a b)"), 0.0),
               writes=tuple(f"b_Zf1_{h}" for h in range(16)), tag="bZ1")
        p.step("vector", lambda e: e.memset(Zb[:].rearrange("p a b -> p (a b)"), 0.0),
               writes=tuple(f"b_Zb_{h}" for h in range(16)), tag="bZb0")
        p.dma("qg", lambda e: e.dma_start(gbr[0:1, 0:16], self.d_bg_b.unsqueeze(0)),
              writes=("b_gbr",), tag="b_gb1")
        p.dma("qg", lambda e: e.dma_start(gbr[0:1, 16:32], self.d_bb_b.unsqueeze(0)),
              writes=("b_gbr",), tag="b_gb2")

        def load_transposed(dsrc, rows, dst, dres, coloff, tag):
            nrt = (rows + 127) // 128
            for rt in range(nrt):
                r0 = rt * 128
                nr = min(128, rows - r0)
                p.dma("qg", (lambda r0=r0, nr=nr: lambda e: e.dma_start(
                    stg[0:nr, :], dsrc[r0:r0 + nr, :]))(),
                    writes=("b_stg",), tag=f"{tag}ld{rt}")
                for c in range(8):
                    bk = c % 2
                    p.step("tensor", (lambda c=c, bk=bk: lambda e: e.transpose(
                        self.pbank_bf(bk), stg[:, c * 128:(c + 1) * 128], self.ident_bf[:]))(),
                        reads=("b_stg", "c_ident_bf"), writes=(f"pf{bk}",), tag=f"{tag}T{rt}_{c}")
                    p.step("scalar", (lambda c=c, bk=bk, r0=r0, nr=nr, coloff=coloff: lambda e:
                           e.copy(dst[:, c, coloff + r0:coloff + r0 + nr],
                                  self.pbank_bf(bk)[:, 0:nr]))(),
                           reads=(f"pf{bk}",), writes=(dres,), tag=f"{tag}E{rt}_{c}")

        p.step("vector", lambda e: e.memset(stg[:], 0.0), writes=("b_stg",), tag="bstg0")
        load_transposed(self.d_bqkv_w, 3 * D, wqkvT, "b_wqkvT", 0, "bwq")
        load_transposed(self.d_bout_w, D, woutT, "b_woutT", 0, "bwo")
        p.step("vector", lambda e: e.memset(stgf[:], 0.0), writes=("b_stgf",), tag="bstgf0")
        p.dma("qg", lambda e: e.dma_start(stg[0:16, :], self.d_bg_w), writes=("b_stg",), tag="bgw")
        p.dma("qg", lambda e: e.dma_start(stg[16:32, :], self.d_bb_w), writes=("b_stg",), tag="bbw")
        for c in range(8):
            bk = c % 2
            p.step("tensor", (lambda c=c, bk=bk: lambda e: e.transpose(
                self.pbank_bf(bk), stg[:, c * 128:(c + 1) * 128], self.ident_bf[:]))(),
                reads=("b_stg", "c_ident_bf"), writes=(f"pf{bk}",), tag=f"bgT{c}")
            p.step("scalar", (lambda c=c, bk=bk: lambda e: e.copy(
                wgbT[:, c, 0:32], self.pbank_bf(bk)[:, 0:32]))(),
                reads=(f"pf{bk}",), writes=("b_wgbT",), tag=f"bgE{c}")
        p.barrier("b_prelude_end")

        for c in range(32):
            t0 = c * 128
            par = c % 2
            u = U[par]
            zf_in, zf_out = Zf[par], Zf[1 - par]
            p.dma("qs", (lambda t0=t0: lambda e: e.dma_start(
                xt[:], src_dram[t0:t0 + 128, :]))(),
                writes=("b_xt",), tag=f"bld{c}")
            self.ln(xt[:], "b_xt", xn[:], "b_xn", tag=f"bln{c}_")
            self.tr128(lambda cc: xn[:, cc * 128:(cc + 1) * 128], "b_xn",
                       (lambda cc: xnt[:, cc, :]), "b_xnt", 8, [4, 5], tag=f"btr{c}_")
            for j in range(6):
                bk = 4 + j % 3
                for cc in range(8):
                    p.step("tensor", (lambda j=j, cc=cc, bk=bk: lambda e: e.matmul(
                        self.pbank(bk), xnt[:, cc, :], wqkvT[:, cc, j * 512:(j + 1) * 512],
                        start=(cc == 0), stop=(cc == 7)))(),
                        reads=("b_xnt", "b_wqkvT"), writes=(f"pf{bk}",), tag=f"bqkv{c}_{j}")
                p.step("scalar", (lambda j=j, bk=bk: lambda e: e.copy(
                    qkv[:, j * 512:(j + 1) * 512], self.pbank(bk)))(),
                    reads=(f"pf{bk}",), writes=("b_qkv",), tag=f"bqkve{c}_{j}")
            p.step("tensor", lambda e: e.matmul(
                self.pbank(7, 0, 32), self.ones_bf[0:1, 0:128], gbr[0:1, 0:32],
                start=True, stop=False),
                reads=("b_gbr", "c_ones_bf"), writes=("pf7",), tag=f"bgb{c}")
            for cc in range(8):
                p.step("tensor", (lambda cc=cc: lambda e: e.matmul(
                    self.pbank(7, 0, 32), xnt[:, cc, :], wgbT[:, cc, 0:32],
                    start=False, stop=(cc == 7)))(),
                    reads=("b_xnt", "b_wgbT"), writes=("pf7",), tag=f"bg{c}_{cc}")
            p.step("scalar", lambda e: e.activation(
                gst[:, 11:13, :].rearrange("p a b -> p (a b)"), self.pf[7][:, 0:32],
                AF.Exp, scale=-1.0),
                reads=("pf7",), writes=("b_gex",), tag=f"bge{c}")
            p.step("vector", lambda e: e.tensor_scalar_add(
                gst[:, 13:15, :].rearrange("p a b -> p (a b)"),
                gst[:, 11:13, :].rearrange("p a b -> p (a b)"), 1.0),
                reads=("b_gex",), writes=("b_g1p",), tag=f"bg1{c}")
            p.step("scalar", lambda e: e.activation(gst[:, 0, :], gst[:, 13, :], AF.Ln),
                   reads=("b_g1p",), writes=("b_gsp",), tag=f"bsp{c}")
            p.step("vector", lambda e: e.reciprocal(gst[:, 1, :], gst[:, 14, :]),
                   reads=("b_g1p",), writes=("b_gbs",), tag=f"bbs{c}")
            p.step("scalar", lambda e: e.copy(spb[:], gst[:, 0, :]),
                   reads=("b_gsp",), writes=("b_spb",), tag=f"bspb{c}")
            p.step("tensor", lambda e: e.matmul(
                self.pbank(6, 0, 16), self.UO, spb[:], start=True, stop=True),
                reads=("b_spb", "c_tri_bf"), writes=("pf6",), tag=f"bcs{c}")
            p.step("scalar", lambda e: e.copy(gst[:, 2, :], self.pbank(6, 0, 16)),
                   reads=("pf6",), writes=("b_gcs",), tag=f"bcse{c}")
            p.step("scalar", lambda e: e.activation(gst[:, 3, :], gst[:, 2, :], AF.Exp, scale=-1.0),
                   reads=("b_gcs",), writes=("b_ggam",), tag=f"bgam{c}")
            p.step("vector", lambda e: e.tensor_tensor(
                gst[:, 5, :], gst[:, 1, :], gst[:, 3, :], op=ALU.mult),
                reads=("b_gbs", "b_ggam"), writes=("b_gbg",), tag=f"bbg{c}")
            p.step("vector", lambda e: e.tensor_tensor(
                sqs[:], qkv[:, 0:2048], qkv[:, 0:2048], op=ALU.mult),
                reads=("b_qkv",), writes=("b_sqs",), tag=f"bsq{c}")
            p.step("vector", lambda e: e.tensor_reduce(
                ssn[:].unsqueeze(-1), sqs[:].rearrange("p (a b) -> p a b", b=64),
                axis=AX.X, op=ALU.add),
                reads=("b_sqs",), writes=("b_ssn",), tag=f"bssn{c}")
            p.step("scalar", lambda e: e.activation(ssn[:], ssn[:], AF.Ln,
                                                    bias=self.consts["epsv"][:, 1:2]),
                   reads=("b_ssn", "c_epsv"), writes=("b_ssn",), tag=f"bsrt{c}")
            p.step("scalar", lambda e: e.activation(ssn[:], ssn[:], AF.Exp, scale=-0.5),
                   reads=("b_ssn",), writes=("b_ssn",), tag=f"brcp{c}")
            p.step("vector", lambda e: e.tensor_tensor(
                gst[:, 10, :], ssn[:, 0:16], gst[:, 3, :], op=ALU.mult),
                reads=("b_ssn", "b_ggam"), writes=("b_grq",), tag=f"brq{c}")
            p.step("vector", lambda e: e.tensor_tensor(
                qkn[:, 0:1024].rearrange("p (a b) -> p a b", b=64),
                qkv[:, 0:1024].rearrange("p (a b) -> p a b", b=64),
                ssn[:, 0:16].unsqueeze(-1).broadcast_to([128, 16, 64]), op=ALU.mult),
                reads=("b_qkv", "b_ssn"), writes=("b_qkn",), tag=f"bqh{c}",
                sreads=("b_ssn",))
            p.step("vector", lambda e: e.tensor_tensor(
                qkn[:, 1024:2048].rearrange("p (a b) -> p a b", b=64),
                qkv[:, 1024:2048].rearrange("p (a b) -> p a b", b=64),
                ssn[:, 16:32].unsqueeze(-1).broadcast_to([128, 16, 64]), op=ALU.mult),
                reads=("b_qkv", "b_ssn"), writes=("b_qkn",), tag=f"bkh{c}",
                sreads=("b_ssn",))
            p.step("vector", lambda e: e.tensor_tensor(
                qtn[:].rearrange("p (a b) -> p a b", b=64),
                qkv[:, 0:1024].rearrange("p (a b) -> p a b", b=64),
                gst[:, 10, :].unsqueeze(-1).broadcast_to([128, 16, 64]), op=ALU.mult),
                reads=("b_qkv", "b_grq"), writes=("b_qtn",), tag=f"bqt{c}",
                sreads=("b_grq",))
            self.tr128(lambda cc: qkn[:, cc * 128:(cc + 1) * 128], "b_qkn",
                       (lambda cc: kqT[:, cc, :]), "b_kqT", 16, [6, 7], tag=f"bkqT{c}_")
            self.tr128(lambda cc: qtn[:, cc * 128:(cc + 1) * 128], "b_qtn",
                       (lambda cc: qtT[:, cc, :]), "b_qtT", 8, [6, 7], tag=f"bqtT{c}_")

            for quad in range(4):
                plans = [self._unit_plan(c, quad * 4 + s, s, U[s], zf_in, zf_out,
                                         Zb, qkv, qkn, kqT, qtT, gst, spb, OT)
                         for s in range(4)]
                mx = max(len(pl) for pl in plans)
                for i in range(mx):
                    for pl in plans:
                        if i < len(pl):
                            eng, emit, reads, writes, sreads = pl[i]
                            p.step(eng, emit, reads=reads, writes=writes,
                                   sreads=sreads, tag=f"bu{c}_{quad}_{i}")

        p.barrier("b_wout")
        xo = [qkv[:, 0:D], qkv[:, D:2 * D]]
        xti = [xt[:], stgf[:]]
        for ti in range(32):
            b = ti % 2
            p.dma("qs", (lambda ti=ti, b=b: lambda e: e.dma_start(
                xti[b], src_dram[ti * 128:(ti + 1) * 128, :]))(),
                writes=(f"b_xti{b}",), tag=f"bwld{ti}")
            for half in range(2):
                bk = 2 * b + half
                for vc in range(8):
                    p.step("tensor", (lambda ti=ti, half=half, vc=vc, bk=bk: lambda e: e.matmul(
                        self.pbank(bk), OT[:, vc, ti * 128:(ti + 1) * 128],
                        woutT[:, vc, half * 512:(half + 1) * 512],
                        start=(vc == 0), stop=(vc == 7)))(),
                        reads=("b_OT", "b_woutT"), writes=(f"pf{bk}",),
                        tag=f"bw{ti}_{half}_{vc}")
                p.step("vector", (lambda ti=ti, b=b, half=half, bk=bk: lambda e: e.tensor_tensor(
                    xo[b][:, half * 512:(half + 1) * 512],
                    xti[b][:, half * 512:(half + 1) * 512],
                    self.pbank(bk), op=ALU.add))(),
                    reads=(f"b_xti{b}", f"pf{bk}"), writes=(f"b_xo{b}",),
                    tag=f"bwa{ti}_{half}")
            p.dma("qs", (lambda ti=ti, b=b: lambda e: e.dma_start(
                dst_dram[ti * 128:(ti + 1) * 128, :], xo[b]))(),
                reads=(f"b_xo{b}",), writes=(f"bdst{ti}",), tag=f"bwst{ti}")
        p.barrier("b_end")
        ps.close()

    def _unit_plan(self, c, h, s, u, zf_in, zf_out, Zb, qkv, qkn, kqT, qtT,
                   gst, spb, OT):
        S = []

        def st(eng, emit, reads=(), writes=(), sreads=()):
            S.append((eng, emit, tuple(reads), tuple(writes), tuple(sreads)))

        bk = s
        pA = self.pf[bk][:, 0:128]
        pX = self.pf[bk][:, 128:256]
        pO = self.pf[bk][:, 256:384]
        pR = self.pf[bk][:, 384:448]
        pZ = self.pf[bk][:, 448:512]
        PK = f"pf{bk}"
        hs, hc = h % 2, h // 2
        lo, hi = hs * 64, (hs + 1) * 64
        kT_h = kqT[lo:hi, 8 + hc, :]
        qT_h = kqT[lo:hi, hc, :]
        qtT_h = qtT[lo:hi, hc, :]
        Zb_h = Zb[lo:hi, hc, lo:hi]
        un = f"u{s}"
        par = c % 2

        st("vector", (lambda: lambda e: e.tensor_scalar_mul(
            u["gstr"][:], self.UO, gst[:, 0, h:h + 1]))(),
            ("b_gsp", "c_tri_bf"), (f"{un}gstr",), ("b_gsp",))
        st("tensor", (lambda: lambda e: e.matmul(
            pA, self.allones[:], u["gstr"][:], start=True, stop=True))(),
            (f"{un}gstr", "c_allones_bf"), (PK,))
        st("vector", (lambda: lambda e: e.tensor_scalar(
            u["gma"][:], pA, gst[:, 2, h:h + 1], 0.0,
            op0=ALU.subtract, op1=ALU.min))(),
            (PK, "b_gcs"), (f"{un}gma",), ("b_gcs",))
        st("vector", (lambda: lambda e: e.tensor_tensor(
            gst[:, 6, h:h + 1], gst[:, 2, h:h + 1], self.pf[bk][:, 127:128],
            op=ALU.subtract))(),
            (PK, "b_gcs"), (f"b_gecv_{h}",))
        st("scalar", (lambda: lambda e: e.activation(
            gst[:, 6, h:h + 1], gst[:, 6, h:h + 1], AF.Exp))(),
            (f"b_gecv_{h}",), (f"b_gecv_{h}",))
        st("scalar", (lambda: lambda e: e.activation(
            gst[:, 7, h:h + 1], self.pf[bk][:, 127:128], AF.Exp, scale=-1.0))(),
            (PK,), (f"b_ggcb_{h}",))
        st("scalar", (lambda: lambda e: e.activation(
            u["gma"][:], u["gma"][:], AF.Exp))(),
            (f"{un}gma",), (f"{un}gma",))
        st("tensor", (lambda: lambda e: e.matmul(
            pA, kT_h, kT_h, start=True, stop=True))(),
            ("b_kqT",), (PK,))
        st("vector", (lambda: lambda e: e.tensor_tensor(
            u["gstr"][:], u["gma"][:], self.MSL, op=ALU.mult))(),
            (f"{un}gma", "c_tri_bf"), (f"{un}gstr",))
        st("vector", (lambda: lambda e: e.scalar_tensor_tensor(
            u["lu"][0][:], pA, gst[:, 1, h:h + 1], u["gstr"][:],
            op0=ALU.mult, op1=ALU.mult))(),
            (PK, "b_gbs", f"{un}gstr"), (f"{un}lu0",), ("b_gbs",))
        st("tensor", (lambda: lambda e: e.transpose(
            self.pf[bk][:].bitcast(BF16)[:, 0:128], u["lu"][0][:], self.ident_bf[:]))(),
            (f"{un}lu0", "c_ident_bf"), (PK,))
        st("scalar", (lambda: lambda e: e.copy(
            u["lut"][0][:], self.pf[bk][:].bitcast(BF16)[:, 0:128]))(),
            (PK,), (f"{un}lut0",))
        st("vector", (lambda: lambda e: e.tensor_tensor(
            u["gmm"][:], u["gma"][:], self.MLI, op=ALU.mult))(),
            (f"{un}gma", "c_tri_bf"), (f"{un}gmm",))
        st("tensor", (lambda: lambda e: e.transpose(
            self.pf[bk][:].bitcast(BF16)[:, 0:128], u["gmm"][:], self.ident_bf[:]))(),
            (f"{un}gmm", "c_ident_bf"), (PK,))
        st("scalar", (lambda: lambda e: e.copy(
            u["gmt"][:], self.pf[bk][:].bitcast(BF16)[:, 0:128]))(),
            (PK,), (f"{un}gmt",))
        st("tensor", (lambda: lambda e: e.matmul(
            pA, kT_h, qT_h, start=True, stop=True))(),
            ("b_kqT",), (PK,))
        st("vector", (lambda: lambda e: e.tensor_tensor(
            u["dt"][:], pA, u["gmt"][:], op=ALU.mult))(),
            (PK, f"{un}gmt"), (f"{un}dt",))
        st("tensor", (lambda: lambda e: e.matmul(
            pR, kT_h, Zb_h, start=True, stop=True))(),
            ("b_kqT", f"b_Zb_{h}"), (PK,))
        st("vector", (lambda: lambda e: e.tensor_scalar_mul(
            u["xs"][0][:, 0:64], pR, gst[:, 5, h:h + 1]))(),
            (PK, "b_gbg"), (f"{un}xs0",), ("b_gbg",))
        st("vector", (lambda: lambda e: e.tensor_scalar_mul(
            u["xs"][0][:, 64:128], qkv[:, 2048 + h * 64:2048 + (h + 1) * 64],
            gst[:, 1, h:h + 1]))(),
            ("b_qkv", "b_gbs"), (f"{un}xs0",), ("b_gbs",))
        for lev in range(7):
            cur, nxt = lev % 2, (lev + 1) % 2
            st("tensor", (lambda cur=cur: lambda e: e.matmul(
                pX, u["lut"][cur][:], u["xs"][cur][:], start=True, stop=True))(),
                (f"{un}lut{cur}", f"{un}xs{cur}"), (PK,))
            st("vector", (lambda cur=cur, nxt=nxt: lambda e: e.tensor_tensor(
                u["xs"][nxt][:], u["xs"][cur][:], pX, op=ALU.subtract))(),
                (f"{un}xs{cur}", PK), (f"{un}xs{nxt}",))
            if lev < 6:
                st("tensor", (lambda cur=cur: lambda e: e.matmul(
                    pA, u["lu"][cur][:], u["lut"][cur][:], start=True, stop=True))(),
                    (f"{un}lu{cur}", f"{un}lut{cur}"), (PK,))
                st("scalar", (lambda nxt=nxt: lambda e: e.copy(u["lut"][nxt][:], pA))(),
                    (PK,), (f"{un}lut{nxt}",))
            if lev < 5:
                st("tensor", (lambda cur=cur: lambda e: e.matmul(
                    pA, u["lut"][cur][:], u["lu"][cur][:], start=True, stop=True))(),
                    (f"{un}lu{cur}", f"{un}lut{cur}"), (PK,))
                st("vector", (lambda nxt=nxt: lambda e: e.tensor_copy(u["lu"][nxt][:], pA))(),
                    (PK,), (f"{un}lu{nxt}",))
        fin = 1
        st("vector", (lambda: lambda e: e.tensor_tensor(
            u["w2"][:, lo:hi], u["xs"][fin][:, 64:128], u["xs"][fin][:, 0:64],
            op=ALU.subtract))(),
            (f"{un}xs{fin}",), (f"{un}w2",))
        st("vector", (lambda: lambda e: e.tensor_scalar_mul(
            u["w3"][:], u["w2"][:, lo:hi], gst[:, 6, h:h + 1]))(),
            (f"{un}w2", f"b_gecv_{h}"), (f"{un}w3",), (f"b_gecv_{h}",))
        st("tensor", (lambda: lambda e: e.matmul(
            pO, Zb[lo:hi, hc, :], qtT_h, start=True, stop=False))(),
            (f"b_Zb_{h}", "b_qtT"), (PK,))
        st("tensor", (lambda: lambda e: e.matmul(
            pO, u["w2"][:], u["dt"][:], start=False, stop=True))(),
            (f"{un}w2", f"{un}dt"), (PK,))
        st("scalar", (lambda: lambda e: e.copy(
            OT[lo:hi, hc, c * 128:(c + 1) * 128], pO[lo:hi, :]))(),
            (PK,), ("b_OT",))
        st("tensor", (lambda: lambda e: e.matmul(
            pZ, qkn[:, 1024 + hc * 128:1024 + (hc + 1) * 128],
            u["w3"][:], start=True, stop=True))(),
            ("b_qkn", f"{un}w3"), (PK,))
        st("vector", (lambda: lambda e: e.scalar_tensor_tensor(
            zf_out[lo:hi, hc, lo:hi], zf_in[lo:hi, hc, lo:hi], gst[lo:hi, 7, h:h + 1],
            pZ[lo:hi, :], op0=ALU.mult, op1=ALU.add))(),
            (f"b_Zf{par}_{h}", f"b_ggcb_{h}", PK), (f"b_Zf{1 - par}_{h}",),
            (f"b_ggcb_{h}",))
        st("scalar", (lambda: lambda e: e.copy(
            Zb[lo:hi, hc, lo:hi], zf_out[lo:hi, hc, lo:hi]))(),
            (f"b_Zf{1 - par}_{h}",), (f"b_Zb_{h}",))
        return S

    def build(self, phases=("A", "F0", "B", "F1")):
        self._rtmp = self.es.enter_context(self.nc.sbuf_tensor("rtmp", [128, 32], F32))
        self.prelude_router()
        chain = list(phases)
        prev = self.dx
        for i, ph in enumerate(chain):
            dst = self.dy if i == len(chain) - 1 else self.d_xres
            if ph == "A":
                self.phase_attn_a(prev, dst)
            elif ph == "B":
                self.phase_attn_b(prev, dst)
            elif ph in ("F0", "F1"):
                self.phase_ffn(prev, dst, "f0_" if ph == "F0" else "f1_")
            prev = self.d_xres
        self.prog.barrier("final")
        sems = {}
        for d in DOMAINS + ("bar", "epo"):
            sems[d] = self.es.enter_context(self.nc.semaphore("sem_" + d))
        self.prog.resolve()
        with self.nc.Block() as block:
            self.prog.emit_all(sems, block)
        self.es.close()
        return self.nc


def _in_maps(inputs, consts):
    x = np.ascontiguousarray(np.asarray(inputs["L0_attn_a_x"], dtype=np.float32))
    base = dict(
        aqkv_w=np.ascontiguousarray(inputs["L0_attn_a_w_qkv"], dtype=np.float32),
        aqkv_b=np.ascontiguousarray(inputs["L0_attn_a_b_qkv"], dtype=np.float32),
        aproj_w=np.ascontiguousarray(inputs["L0_attn_a_w_proj"], dtype=np.float32),
        aproj_b=np.ascontiguousarray(inputs["L0_attn_a_b_proj"], dtype=np.float32),
        bqkv_w=np.ascontiguousarray(inputs["L0_attn_b_Wqkv"], dtype=np.float32),
        bg_w=np.ascontiguousarray(inputs["L0_attn_b_Wg"], dtype=np.float32),
        bg_b=np.ascontiguousarray(inputs["L0_attn_b_bg"], dtype=np.float32),
        bb_w=np.ascontiguousarray(inputs["L0_attn_b_Wb"], dtype=np.float32),
        bb_b=np.ascontiguousarray(inputs["L0_attn_b_bb"], dtype=np.float32),
        bout_w=np.ascontiguousarray(inputs["L0_attn_b_Wout"], dtype=np.float32),
        rw=np.ascontiguousarray(inputs["L0_ffn_router_w"], dtype=np.float32),
        wg=np.ascontiguousarray(inputs["L0_ffn_w_gate"], dtype=np.float32),
        wu=np.ascontiguousarray(inputs["L0_ffn_w_up"], dtype=np.float32),
        wd=np.ascontiguousarray(inputs["L0_ffn_w_down"], dtype=np.float32),
        swg=np.ascontiguousarray(inputs["L0_ffn_sw_gate"], dtype=np.float32),
        swu=np.ascontiguousarray(inputs["L0_ffn_sw_up"], dtype=np.float32),
        swd=np.ascontiguousarray(inputs["L0_ffn_sw_down"], dtype=np.float32),
        **consts,
    )
    return [dict(base, x=np.ascontiguousarray(x[i])) for i in range(N_CORES)]


def kernel(**inputs):
    consts = make_consts()
    mk = MK()
    nc = mk.build()
    maps = _in_maps(inputs, consts)
    res = run_bass_kernel_spmd(nc, maps, core_ids=list(range(N_CORES)))
    out = np.stack([np.asarray(r["y"], dtype=np.float32) for r in res.results], axis=0)
    return out


# version 05
# speedup vs baseline: 1.4156x; 1.0553x over previous
import contextlib

import numpy as np
import ml_dtypes

import concourse.bass as bass
import concourse.mybir as mybir
from concourse.bass_utils import run_bass_kernel_spmd

F32 = mybir.dt.float32
BF16 = mybir.dt.bfloat16
AF = mybir.ActivationFunctionType
ALU = mybir.AluOpType
AX = mybir.AxisListType

N_CORES = 8
T = 4096
D = 1024
H = 16
DH = 64
HID = 4096
NRT = 7

NEG = -1.0e30

ENGINES = ("tensor", "vector", "scalar", "gpsimd", "sync")
QUEUES = ("qs", "qg", "qc")
QUEUE_ENGINE = {"qs": "sync", "qg": "gpsimd", "qc": "scalar"}
NSLOT = {"qs": 8, "qg": 8, "qc": 2}
DOMAINS = tuple(ENGINES) + tuple(f"{q}{i}" for q in QUEUES for i in range(NSLOT[q]))


class Step:
    __slots__ = ("eng", "queue", "slot", "emit", "reads", "writes", "sreads",
                 "needs_inc", "inc_val", "waits", "tag")

    def __init__(self, eng, queue, emit, reads, writes, tag="", sreads=()):
        self.eng = eng
        self.queue = queue
        self.slot = None
        self.emit = emit
        self.reads = tuple(reads)
        self.writes = tuple(writes)
        self.sreads = tuple(sreads)
        self.needs_inc = False
        self.inc_val = None
        self.waits = []
        self.tag = tag

    @property
    def domain(self):
        if self.queue is not None:
            return f"{self.queue}{self.slot}"
        return self.eng


class Prog:
    def __init__(self, full_sync=False):
        self.steps = []
        self.full_sync = full_sync
        self._quse = {q: 0 for q in QUEUES}

    def step(self, eng, emit, reads=(), writes=(), tag="", sreads=()):
        self.steps.append(Step(eng, None, emit, reads, writes, tag, sreads))

    def dma(self, queue, emit, reads=(), writes=(), tag=""):
        s = Step(QUEUE_ENGINE[queue], queue, emit, reads, writes, tag)
        s.slot = self._quse[queue] % NSLOT[queue]
        self._quse[queue] += 1
        self.steps.append(s)

    def barrier(self, tag="barrier", clear=True):
        s = Step("barrier", None, None, (), (), tag)
        s.slot = clear
        self.steps.append(s)

    def resolve(self):
        last_write = {}
        readers = {}
        barrier_pending = []
        deps = [None] * len(self.steps)
        for i, s in enumerate(self.steps):
            if s.eng == "barrier":
                deps[i] = list(barrier_pending)
                barrier_pending = [i]
                last_write = {}
                readers = {}
                continue
            barrier_pending.append(i)
            dmap = {}
            for r in s.reads:
                w = last_write.get(r)
                if w is not None:
                    dmap[w] = dmap.get(w, False) or (r in s.sreads)
            for wr in s.writes:
                w = last_write.get(wr)
                if w is not None:
                    dmap.setdefault(w, False)
                for rd in readers.get(wr, ()):
                    dmap.setdefault(rd, False)
            dmap.pop(i, None)
            deps[i] = dmap
            for r in s.reads:
                readers.setdefault(r, []).append(i)
            for wr in s.writes:
                last_write[wr] = i
                readers[wr] = []

        need = [False] * len(self.steps)
        for i, s in enumerate(self.steps):
            me = s.domain if s.eng != "barrier" else None
            dl = deps[i] if isinstance(deps[i], list) else deps[i].items()
            for jj in dl:
                j, always = (jj, False) if isinstance(jj, int) else jj
                p = self.steps[j]
                if p.eng == "barrier":
                    continue
                fs = (self.full_sync is True or
                      (isinstance(self.full_sync, (set, frozenset, tuple)) and
                       (s.eng in self.full_sync if me is not None else False)))
                if me is None or fs or always or p.domain != me or p.queue is not None:
                    need[j] = True
        counters = {}
        for s, nd in zip(self.steps, need):
            if s.eng == "barrier":
                if s.slot:
                    counters = {d: v for d, v in counters.items() if d not in ENGINES}
                continue
            if s.queue is not None:
                d = s.domain
                counters[d] = counters.get(d, 0) + 16
                s.needs_inc = True
                s.inc_val = counters[d]
            elif nd:
                counters[s.eng] = counters.get(s.eng, 0) + 1
                s.needs_inc = True
                s.inc_val = counters[s.eng]

        known = {e: {} for e in ENGINES}
        for i, s in enumerate(self.steps):
            if s.eng == "barrier":
                targets = {}
                for j in deps[i]:
                    p = self.steps[j]
                    if p.eng == "barrier" or not p.needs_inc:
                        continue
                    targets[p.domain] = max(targets.get(p.domain, 0), p.inc_val)
                s.waits = sorted(targets.items())
                if s.slot:
                    for e in ENGINES:
                        for d, v in s.waits:
                            if d not in ENGINES and known[e].get(d, -1) < v:
                                known[e][d] = v
                        for d in ENGINES:
                            known[e].pop(d, None)
                else:
                    for e in ENGINES:
                        for d, v in s.waits:
                            if known[e].get(d, -1) < v:
                                known[e][d] = v
                continue
            kn = known[s.eng]
            targets = {}
            if s.queue is not None and s.inc_val > 16:
                targets[s.domain] = s.inc_val - 16
            dl = deps[i] if isinstance(deps[i], list) else deps[i].items()
            for jj in dl:
                j, always = (jj, False) if isinstance(jj, int) else jj
                p = self.steps[j]
                if p.eng == "barrier" or not p.needs_inc:
                    continue
                d = p.domain
                if (p.queue is None and d == s.eng and not always and not
                        (self.full_sync is True or
                         (isinstance(self.full_sync, (set, frozenset, tuple)) and s.eng in self.full_sync))):
                    continue
                targets[d] = max(targets.get(d, 0), p.inc_val)
            ws = []
            for d, v in sorted(targets.items()):
                if kn.get(d, -1) >= v:
                    continue
                ws.append((d, v))
                kn[d] = v
            s.waits = ws

    def emit_all(self, sems, block):
        per_engine = {e: [] for e in ENGINES}
        nbar = 0
        for s in self.steps:
            if s.eng == "barrier":
                nbar += 1
                for e in ENGINES:
                    per_engine[e].append(s)
            else:
                per_engine[s.eng].append(s)

        bsem = sems["bar"]
        esem = sems["epo"]

        def run_stream(name, eng):
            bar_k = 0
            for s in per_engine[name]:
                for d, v in s.waits:
                    eng.wait_ge(sems[d], v)
                if s.eng == "barrier":
                    if not s.slot:
                        continue
                    bar_k += 1
                    eng.drain()
                    eng.sem_inc(bsem, 1)
                    if name == "gpsimd":
                        eng.wait_ge(bsem, 5 * bar_k)
                        for d in ENGINES:
                            eng.sem_clear(sems[d])
                        eng.sem_inc(esem, 1)
                    eng.wait_ge(esem, bar_k)
                    continue
                inst = s.emit(eng)
                if s.needs_inc:
                    assert inst is not None, f"step {s.tag} returned no instruction"
                    if s.queue is not None:
                        inst.then_inc(sems[s.domain], 16)
                    else:
                        inst.then_inc(sems[s.eng], 1)

        @block.tensor
        def _(eng):
            run_stream("tensor", eng)

        @block.vector
        def _(eng):
            run_stream("vector", eng)

        @block.scalar
        def _(eng):
            run_stream("scalar", eng)

        @block.gpsimd
        def _(eng):
            run_stream("gpsimd", eng)

        @block.sync
        def _(eng):
            run_stream("sync", eng)


def make_consts():
    c = {}
    c["ident_bf"] = np.eye(128, dtype=ml_dtypes.bfloat16)
    c["ident_f32"] = np.eye(128, dtype=np.float32)
    c["ones_bf"] = np.ones((1, 128), dtype=ml_dtypes.bfloat16)
    c["ones_f32"] = np.ones((1, 128), dtype=np.float32)
    c["ones_col"] = np.ones((128, 1), dtype=ml_dtypes.bfloat16)
    ev = np.zeros((128, 2), dtype=np.float32)
    ev[:, 0] = 1e-5
    ev[:, 1] = 1e-30
    c["epsv"] = ev
    r = np.arange(128)[:, None]
    y = np.arange(896)[None, :]
    kq = np.where(y - r >= 384, 0.0, NEG).astype(np.float32)
    c["amask_kq"] = kq.astype(ml_dtypes.bfloat16)
    l = np.arange(128)[:, None]
    j = np.arange(128)[None, :]
    tri = np.zeros((128, 384), dtype=np.float32)
    tri[:, 0:128] = (l <= j)
    tri[:, 128:256] = (j < l)
    tri[:, 256:384] = (j <= l)
    c["tri_bf"] = tri.astype(ml_dtypes.bfloat16)
    c["allones_bf"] = np.ones((128, 128), dtype=ml_dtypes.bfloat16)
    ir = np.zeros((128, 8), dtype=np.float32)
    ir[:, 0:7] = (7.0 - np.arange(7))[None, :]
    c["iota_rev"] = ir
    return c


CONST_SPECS = [
    ("ident_bf", [128, 128], BF16),
    ("ones_bf", [1, 128], BF16),
    ("ones_f32", [1, 128], F32),
    ("ones_col", [128, 1], BF16),
    ("iota_rev", [128, 8], F32),
    ("epsv", [128, 2], F32),
]
PHASE_CONST_SPECS = [
    ("ident_f32", [128, 128], F32),
    ("tri_bf", [128, 384], BF16), ("allones_bf", [128, 128], BF16),
]


class MK:
    def __init__(self, full_sync=("vector", "scalar"), dbg_n=0):
        self.dbg_n = dbg_n
        self.dbg_i = 0
        self.nc = bass.Bass("TRN2", target_bir_lowering=False,
                            detect_race_conditions=(full_sync is True))
        self.es = contextlib.ExitStack()
        self.prog = Prog(full_sync=full_sync)
        nc = self.nc

        if self.dbg_n:
            self.d_dbg = nc.dram_tensor("dbg", [self.dbg_n, 128, 1024], F32,
                                        kind="ExternalOutput").ap()
        self.dx = nc.dram_tensor("x", [T, D], F32, kind="ExternalInput").ap()
        self.dy = nc.dram_tensor("y", [T, D], F32, kind="ExternalOutput").ap()
        dt = lambda n, s: nc.dram_tensor(n, s, F32, kind="ExternalInput").ap()
        self.d_aqkv_w = dt("aqkv_w", [D, 3 * D])
        self.d_aqkv_b = dt("aqkv_b", [3 * D])
        self.d_aproj_w = dt("aproj_w", [D, D])
        self.d_aproj_b = dt("aproj_b", [D])
        self.d_bqkv_w = dt("bqkv_w", [3 * D, D])
        self.d_bg_w = dt("bg_w", [H, D])
        self.d_bg_b = dt("bg_b", [H])
        self.d_bb_w = dt("bb_w", [H, D])
        self.d_bb_b = dt("bb_b", [H])
        self.d_bout_w = dt("bout_w", [D, D])
        self.d_rw = dt("rw", [NRT, D])
        self.d_wg = dt("wg", [NRT, D, HID])
        self.d_wu = dt("wu", [NRT, D, HID])
        self.d_wd = dt("wd", [NRT, HID, D])
        self.d_swg = dt("swg", [1, D, HID])
        self.d_swu = dt("swu", [1, D, HID])
        self.d_swd = dt("swd", [1, HID, D])
        self.d_xres = nc.dram_tensor("x_res", [T, D], F32).ap()

        self.consts = {}
        for nm, shp, dty in CONST_SPECS:
            dap = nc.dram_tensor(nm, shp, dty, kind="ExternalInput").ap()
            sb = self.es.enter_context(nc.sbuf_tensor("c_" + nm, shp, dty))
            self.consts[nm] = sb
            self.prog.dma("qs", (lambda sb=sb, dap=dap: lambda e: e.dma_start(sb[:], dap))(),
                          writes=("c_" + nm,), tag="ldc_" + nm)
        self.d_amask_kq = nc.dram_tensor("amask_kq", [128, 896], BF16, kind="ExternalInput").ap()

        self.dconst = {}
        for nm, shp, dty in PHASE_CONST_SPECS:
            self.dconst[nm] = nc.dram_tensor(nm, shp, dty, kind="ExternalInput").ap()
        self.ident_bf = self.consts["ident_bf"]
        self.ones_bf = self.consts["ones_bf"]
        self.iota_rev = self.consts["iota_rev"]

        self.pf = [self.es.enter_context(nc.psum_tensor(f"pf{i}", [128, 512], F32))
                   for i in range(8)]

        self.ln_xc = self.es.enter_context(nc.sbuf_tensor("ln_xc", [128, D], F32))
        self.ln_st = self.es.enter_context(nc.sbuf_tensor("ln_st", [128, 8], F32))

        self.rwT_hi = self.es.enter_context(nc.sbuf_tensor("rwT_hi", [128, 8, 8], BF16))
        self.rwT_lo = self.es.enter_context(nc.sbuf_tensor("rwT_lo", [128, 8, 8], BF16))

    def dump(self, ap, reads, tag="dbg"):
        if not self.dbg_n or self.dbg_i >= self.dbg_n:
            return
        i = self.dbg_i
        self.dbg_i += 1
        w = ap.shape[-1] if len(ap.shape) == 2 else 1024
        np_parts = ap.shape[0]
        self.prog.dma("qg", (lambda ap=ap, i=i, np_parts=np_parts, w=w: lambda e: e.dma_start(
            self.d_dbg[i, 0:np_parts, 0:w], ap))(),
            reads=tuple(reads), writes=(f"dbg{i}",), tag=f"{tag}{i}")

    def pbank(self, i, lo=0, hi=512):
        return self.pf[i][:, lo:hi]

    def pbank_bf(self, i, lo=0, hi=128):
        return self.pf[i][:].bitcast(BF16)[:, lo:hi]

    def ln(self, xt_ap, xt_res, xn_ap, xn_res, xlo_ap=None, xlo_res=None, tag=""):
        p = self.prog
        st = self.ln_st
        xc = self.ln_xc
        p.step("vector", lambda e: e.tensor_reduce(st[:, 0:1], xt_ap, axis=AX.X, op=ALU.add),
               reads=(xt_res,), writes=("ln_st",), tag=tag + "sum")
        p.step("scalar", lambda e: e.mul(st[:, 1:2], st[:, 0:1], -1.0 / D),
               reads=("ln_st",), writes=("ln_st",), tag=tag + "nm")
        p.step("vector", lambda e: e.tensor_scalar_add(xc[:], xt_ap, st[:, 1:2]),
               reads=(xt_res, "ln_st"), writes=("ln_xc",), tag=tag + "xc",
               sreads=("ln_st",))
        p.step("scalar", lambda e: e.activation(
            xn_ap, xc[:], AF.Square, accum_out=st[:, 2:3]),
            reads=("ln_xc",), writes=(xn_res, "ln_st"), tag=tag + "ss")
        p.step("scalar", lambda e: e.activation(st[:, 3:4], st[:, 2:3], AF.Ln,
                                                scale=1.0 / D,
                                                bias=self.consts["epsv"][:, 0:1]),
               reads=("ln_st", "c_epsv"), writes=("ln_st",), tag=tag + "std")
        p.step("scalar", lambda e: e.activation(st[:, 4:5], st[:, 3:4], AF.Exp, scale=-0.5),
               reads=("ln_st",), writes=("ln_st",), tag=tag + "rstd")
        p.step("vector", lambda e: e.tensor_scalar_mul(xn_ap, xc[:], st[:, 4:5]),
               reads=("ln_xc", "ln_st"), writes=(xn_res,), tag=tag + "xn",
               sreads=("ln_st",))
        if xlo_ap is not None:
            p.step("vector", lambda e: e.scalar_tensor_tensor(
                xlo_ap, xc[:], st[:, 4:5], xn_ap, op0=ALU.mult, op1=ALU.subtract),
                reads=("ln_xc", "ln_st", xn_res), writes=(xlo_res,), tag=tag + "xlo",
                sreads=("ln_st",))

    def tr128(self, src_ap_fn, src_res, dst_ap_fn, dst_res, n, banks, tag=""):
        p = self.prog
        for c in range(n):
            b = banks[c % len(banks)]
            pap = self.pbank_bf(b)
            p.step("tensor", (lambda c=c, pap=pap: lambda e: e.transpose(
                pap, src_ap_fn(c), self.ident_bf[:]))(),
                reads=(src_res, "c_ident_bf"), writes=(f"pf{b}",), tag=f"{tag}T{c}")
            p.step("scalar", (lambda c=c, pap=pap: lambda e: e.copy(dst_ap_fn(c), pap))(),
                   reads=(f"pf{b}",), writes=(dst_res,), tag=f"{tag}E{c}")

    def prelude_router(self):
        nc, p = self.nc, self.prog
        with contextlib.ExitStack() as ps:
            rwf = ps.enter_context(nc.sbuf_tensor("pr_rwf", [128, D], F32))
            rwh = ps.enter_context(nc.sbuf_tensor("pr_rwh", [128, D], BF16))
            rwl = ps.enter_context(nc.sbuf_tensor("pr_rwl", [128, D], BF16))
            p.step("vector", lambda e: e.memset(rwf[:], 0.0), writes=("pr_rwf",), tag="rw0")
            p.step("vector", lambda e: e.memset(self.rwT_hi[:].rearrange("p a b -> p (a b)"), 0.0),
                   writes=("rwT_hi",), tag="rwh0")
            p.step("vector", lambda e: e.memset(self.rwT_lo[:].rearrange("p a b -> p (a b)"), 0.0),
                   writes=("rwT_lo",), tag="rwl0")
            p.dma("qs", lambda e: e.dma_start(rwf[0:NRT, :], self.d_rw),
                  writes=("pr_rwf",), tag="rwld")
            p.step("vector", lambda e: e.tensor_copy(rwh[:], rwf[:]),
                   reads=("pr_rwf",), writes=("pr_rwh",), tag="rwh")
            p.step("vector", lambda e: e.tensor_tensor(rwl[:], rwf[:], rwh[:], op=ALU.subtract),
                   reads=("pr_rwf", "pr_rwh"), writes=("pr_rwl",), tag="rwl")
            for (src, sres, dst, dres) in [(rwh, "pr_rwh", self.rwT_hi, "rwT_hi"),
                                           (rwl, "pr_rwl", self.rwT_lo, "rwT_lo")]:
                for c in range(8):
                    b = c % 2
                    pap = self.pbank_bf(b)
                    p.step("tensor", (lambda src=src, c=c, pap=pap: lambda e: e.transpose(
                        pap, src[:, c * 128:(c + 1) * 128], self.ident_bf[:]))(),
                        reads=(sres, "c_ident_bf"), writes=(f"pf{b}",), tag=f"rwT{c}")
                    p.step("scalar", (lambda dst=dst, c=c, pap=pap: lambda e: e.copy(
                        dst[:, c, 0:NRT], pap[:, 0:NRT]))(),
                        reads=(f"pf{b}",), writes=(dres,), tag=f"rwTe{c}")
            p.barrier("prelude_router_end")

    def phase_attn_a(self, src_dram, dst_dram):
        nc, p = self.nc, self.prog
        ps = contextlib.ExitStack()
        amk = ps.enter_context(nc.sbuf_tensor("a_amk", [128, 896], BF16))
        p.dma("qs", lambda e: e.dma_start(amk[:], self.d_amask_kq), writes=("a_amk",), tag="a_amk")

        yT = ps.enter_context(nc.sbuf_tensor("a_yT", [128, 8, T], BF16))
        kT = [ps.enter_context(nc.sbuf_tensor(f"a_kT{i}", [65, T], BF16)) for i in range(2)]
        qT = [ps.enter_context(nc.sbuf_tensor(f"a_qT{i}", [65, T], BF16)) for i in range(2)]
        vsb = ps.enter_context(nc.sbuf_tensor("a_v", [128, 32, 192], BF16))
        _xt1 = ps.enter_context(nc.sbuf_tensor("a_xt0", [128, D], F32))
        xt = [_xt1, _xt1]
        xn = ps.enter_context(nc.sbuf_tensor("a_xn", [128, D], BF16))
        wgr = ps.enter_context(nc.sbuf_tensor("a_wgr", [128, 8, 384], BF16))
        bqc = ps.enter_context(nc.sbuf_tensor("a_bqc", [64, 48], F32))
        bqr = ps.enter_context(nc.sbuf_tensor("a_bqr", [1, 128], BF16))
        ptile = [ps.enter_context(nc.sbuf_tensor(f"a_pt{i}", [128, 512], BF16)) for i in range(4)]
        sqq = [ps.enter_context(nc.sbuf_tensor(f"a_sqq{i}", [64, 512], BF16)) for i in range(2)]
        nbr = [ps.enter_context(nc.sbuf_tensor(f"a_nbr{i}", [1, 512], BF16)) for i in range(2)]
        mk1 = [ps.enter_context(nc.sbuf_tensor(f"a_mk1{i}", [1, 2], F32)) for i in range(2)]
        _mrb = [ps.enter_context(nc.sbuf_tensor(f"a_mrb{i}", [1, 512], BF16)) for i in range(2)]
        mrb = [_mrb[0], _mrb[0], _mrb[1], _mrb[1]]
        linv = [ps.enter_context(nc.sbuf_tensor(f"a_linv{i}", [1, 512], F32)) for i in range(2)]
        ottmp = ps.enter_context(nc.sbuf_tensor("a_ottmp", [128, 512], BF16))
        xnt_stack = contextlib.ExitStack()
        xnt = xnt_stack.enter_context(nc.sbuf_tensor("a_xnt", [128, 8, T], BF16))

        p.dma("qs", lambda e: e.dma_start(bqc[:], self.d_aqkv_b.rearrange("(c p) -> p c", p=64),
                                          allow_slow_non_contiguous=True),
              writes=("a_bqc",), tag="a_bqc")

        p.step("vector", lambda e: e.memset(vsb[:].rearrange("p a b -> p (a b)"), 0.0),
               writes=("a_v",), tag="a_v0")
        p.step("vector", lambda e: e.memset(vsb[:, :, 96:97], 1.0), writes=("a_v",), tag="a_v1a")
        for hs in range(2):
            p.step("vector", (lambda hs=hs: lambda e: e.memset(kT[hs][64:65, :], 1.0))(),
                   writes=(f"a_kT{hs}r64",), tag=f"a_k1_{hs}")

        for i in range(32):
            b = i % 2
            p.dma("qs", (lambda i=i: lambda e: e.dma_start(
                xt[0][:], src_dram[i * 128:(i + 1) * 128, :]))(),
                writes=("a_xt0",), tag=f"a0ld{i}")
            self.ln(xt[0][:], "a_xt0", xn[:], "a_xn", tag=f"a0ln{i}_")
            self.tr128(lambda c: xn[:, c * 128:(c + 1) * 128], "a_xn",
                       (lambda c, i=i: xnt[:, c, i * 128:(i + 1) * 128]),
                       "a_xnt", 8, [6, 7], tag=f"a0tr{i}_")

        for g in range(8):
            for mat in range(3):
                col0 = mat * D + g * 128
                p.dma("qg", (lambda mat=mat, col0=col0: lambda e: e.dma_start(
                    wgr[:, :, mat * 128:(mat + 1) * 128],
                    self.d_aqkv_w.rearrange("(c p) n -> p c n", p=128)[:, :, col0:col0 + 128]))(),
                    writes=("a_wgr",), tag=f"a1w{g}_{mat}")
            p.dma("qg", (lambda g=g: lambda e: e.dma_start(
                bqr[:], self.d_aqkv_b[2 * D + g * 128:2 * D + (g + 1) * 128].unsqueeze(0)))(),
                writes=("a_bqr",), tag=f"a1bv{g}")
            for tb in range(8):
                t0 = tb * 512
                for mat in range(2):
                    for hs in range(2):
                        bk = (tb * 4 + mat * 2 + hs) % 4
                        for c in range(8):
                            p.step("tensor", (lambda mat=mat, hs=hs, c=c, t0=t0, bk=bk:
                                   lambda e: e.matmul(
                                       self.pbank(bk)[0:64, :],
                                       wgr[:, c, mat * 128 + hs * 64:mat * 128 + (hs + 1) * 64],
                                       xnt[:, c, t0:t0 + 512],
                                       start=(c == 0), stop=(c == 7)))(),
                                   reads=("a_wgr", "a_xnt"), writes=(f"pf{bk}",),
                                   tag=f"a1qk{g}_{tb}_{mat}_{hs}")
                        dst = qT if mat == 0 else kT
                        dres = "a_qT" if mat == 0 else "a_kT"
                        bcol = (mat * 1024 + g * 128 + hs * 64) // 64
                        p.step("scalar", (lambda dst=dst, hs=hs, t0=t0, bk=bk, bcol=bcol:
                               lambda e: e.activation(
                                   dst[hs][0:64, t0:t0 + 512],
                                   self.pf[bk][0:64, 0:512],
                                   AF.Identity, bias=bqc[0:64, bcol:bcol + 1]))(),
                               reads=(f"pf{bk}", "a_bqc"), writes=(f"{dres}{hs}",),
                               tag=f"a1qke{g}_{tb}_{mat}_{hs}", sreads=("a_bqc",))
                for ts in range(4):
                    bk = 4 + ts % 2
                    p.step("tensor", (lambda g=g, bk=bk: lambda e: e.matmul(
                        self.pbank(bk, 0, 128), self.ones_bf[0:1, 0:128],
                        bqr[0:1, 0:128],
                        start=True, stop=False))(),
                        reads=("a_bqr", "c_ones_bf"), writes=(f"pf{bk}",),
                        tag=f"a1vb{g}_{tb}_{ts}")
                    for c in range(8):
                        p.step("tensor", (lambda c=c, t0=t0, ts=ts, bk=bk: lambda e:
                               e.matmul(self.pbank(bk, 0, 128),
                                        xnt[:, c, t0 + ts * 128:t0 + (ts + 1) * 128],
                                        wgr[:, c, 256:384],
                                        start=False, stop=(c == 7)))(),
                               reads=("a_wgr", "a_xnt"), writes=(f"pf{bk}",),
                               tag=f"a1v{g}_{tb}_{ts}_{c}")
                    for hs in range(2):
                        c0 = 0 if hs == 0 else 128
                        p.step("scalar", (lambda tb=tb, ts=ts, hs=hs, bk=bk, c0=c0: lambda e:
                               e.copy(vsb[:, tb * 4 + ts, c0:c0 + 64],
                                      self.pf[bk][:, hs * 64:(hs + 1) * 64]))(),
                            reads=(f"pf{bk}",), writes=("a_v",), tag=f"a1ve{g}_{tb}_{ts}_{hs}")
            def head_plan(hs):
                S = []

                def st(eng, emit, reads=(), writes=(), sreads=()):
                    S.append((eng, emit, tuple(reads), tuple(writes), tuple(sreads)))

                h = 2 * g + hs
                for mat in (1, 0):
                    srcT = qT[hs] if mat == 0 else kT[hs]
                    sres = f"a_qT{hs}" if mat == 0 else f"a_kT{hs}"
                    for j in range(8):
                        bk = 2 * hs + j % 2
                        st("scalar", (lambda srcT=srcT, j=j: lambda e: e.activation(
                            sqq[hs][:], srcT[0:64, j * 512:(j + 1) * 512], AF.Square))(),
                            (sres,), (f"a_sqq{hs}",))
                        st("tensor", (lambda bk=bk: lambda e: e.matmul(
                            self.pf[bk][0:1, 0:512], self.consts["ones_col"][0:64, :],
                            sqq[hs][:], start=True, stop=True))(),
                            (f"a_sqq{hs}", "c_ones_col"), (f"pf{bk}",))
                        st("scalar", (lambda bk=bk: lambda e:
                               e.activation(nbr[hs][:], self.pf[bk][0:1, 0:512], AF.Ln))(),
                               (f"pf{bk}",), (f"a_nbr{hs}",))
                        if mat == 1:
                            st("vector", (lambda: lambda e: e.reduce_max(
                                mk1[hs][:, 1:2], nbr[hs][:], axis=AX.X))(),
                                ((f"a_nbr{hs}"),), (f"a_mk1b{hs}",))
                            if j == 0:
                                st("vector", (lambda: lambda e: e.tensor_copy(
                                    mk1[hs][:, 0:1], mk1[hs][:, 1:2]))(),
                                    (f"a_mk1b{hs}",), (f"a_mk1{hs}",))
                            else:
                                st("vector", (lambda: lambda e: e.tensor_tensor(
                                    mk1[hs][:, 0:1], mk1[hs][:, 0:1], mk1[hs][:, 1:2],
                                    op=ALU.max))(),
                                    (f"a_mk1b{hs}", f"a_mk1{hs}"), (f"a_mk1{hs}",))
                        else:
                            mb = 2 * hs + j % 2
                            st("vector", (lambda mb=mb: lambda e: e.tensor_scalar_add(
                                mrb[mb][:], nbr[hs][:], mk1[hs][:, 0:1]))(),
                                ((f"a_nbr{hs}"), f"a_mk1{hs}"), (f"a_mrb{mb // 2}",),
                                (f"a_mk1{hs}",))
                            st("scalar", (lambda mb=mb: lambda e: e.activation(
                                mrb[mb][:], mrb[mb][:], AF.Exp, scale=0.5))(),
                                (f"a_mrb{mb // 2}",), (f"a_mrb{mb // 2}",))
                            st("vector", (lambda mb=mb: lambda e: e.tensor_scalar_mul(
                                mrb[mb][:], mrb[mb][:], -1.0))(),
                                (f"a_mrb{mb // 2}",), (f"a_mrb{mb // 2}",))
                            st("qsdma", (lambda hs=hs, j=j, mb=mb: lambda e: e.dma_start(
                                qT[hs][64:65, j * 512:(j + 1) * 512], mrb[mb][:]))(),
                                (f"a_mrb{mb // 2}",), (f"a_qT{hs}r64",))
                for qb in range(8):
                    q0 = qb * 512
                    nkt = 4 * (qb + 1)
                    lrow = 96 if hs == 0 else 32
                    vlo = 0 if hs == 0 else 64
                    ob = 4 + hs
                    lb = 6 + hs
                    for kt in range(nkt):
                        bk = 2 * hs + kt % 2
                        pt = 2 * hs + kt % 2
                        st("tensor", (lambda hs=hs, q0=q0, kt=kt, bk=bk: lambda e: e.matmul(
                            self.pbank(bk), kT[hs][0:65, kt * 128:(kt + 1) * 128],
                            qT[hs][0:65, q0:q0 + 512], start=True, stop=True))(),
                            (f"a_qT{hs}", f"a_qT{hs}r64", f"a_kT{hs}", f"a_kT{hs}r64"),
                            (f"pf{bk}",))
                        if kt >= 4 * qb:
                            j = kt - 4 * qb
                            st("vector", (lambda j=j, bk=bk: lambda e: e.tensor_tensor(
                                self.pbank(bk), self.pbank(bk),
                                amk[:, 384 - 128 * j:896 - 128 * j], op=ALU.add))(),
                                (f"pf{bk}", "a_amk"), (f"pf{bk}",))
                        st("scalar", (lambda pt=pt, bk=bk: lambda e: e.activation(
                            ptile[pt][:], self.pbank(bk), AF.Exp, scale=0.125))(),
                            (f"pf{bk}",), (f"a_pt{pt}",))
                        st("tensor", (lambda kt=kt, pt=pt, nkt=nkt, vlo=vlo, ob=ob: lambda e:
                               e.matmul(
                                   self.pf[ob][:, 0:512],
                                   vsb[:, kt, vlo:vlo + 128],
                                   ptile[pt][:],
                                   start=(kt == 0), stop=(kt == nkt - 1)))(),
                            ("a_v", f"a_pt{pt}"), (f"pf{ob}",))
                    st("vector", (lambda lrow=lrow, hs=hs, ob=ob: lambda e: e.reciprocal(
                        linv[hs][:], self.pf[ob][lrow:lrow + 1, 0:512]))(),
                        (f"pf{ob}",), (f"a_linv{hs}",))
                    st("tensor", (lambda hs=hs, lb=lb: lambda e: e.matmul(
                        self.pf[lb][:, 0:512],
                        self.consts["ones_f32"][0:1, 0:128],
                        linv[hs][:], start=True, stop=True))(),
                        ((f"a_linv{hs}"), "c_ones_f32"), (f"pf{lb}",))
                    st("scalar", (lambda hs=hs, ob=ob: lambda e: e.copy(
                        ottmp[hs * 64:(hs + 1) * 64, :],
                        self.pf[ob][hs * 64:(hs + 1) * 64, 0:512]))(),
                        (f"pf{ob}",), (f"a_ottmp{hs}",))
                    st("vector", (lambda hs=hs, g=g, q0=q0, lb=lb: lambda e: e.tensor_tensor(
                        yT[hs * 64:(hs + 1) * 64, g, q0:q0 + 512],
                        ottmp[hs * 64:(hs + 1) * 64, :],
                        self.pf[lb][hs * 64:(hs + 1) * 64, 0:512], op=ALU.mult))(),
                        (f"a_ottmp{hs}", f"pf{lb}"), ("a_yT",))
                return S

            plans = [head_plan(0), head_plan(1)]
            mx = max(len(pl) for pl in plans)
            for i in range(mx):
                for pl in plans:
                    if i < len(pl):
                        eng, emit, reads, writes, sreads = pl[i]
                        if eng == "qsdma":
                            p.dma("qs", emit, reads=reads, writes=writes,
                                  tag=f"aat{g}_{i}")
                        else:
                            p.step(eng, emit, reads=reads, writes=writes,
                                   sreads=sreads, tag=f"aat{g}_{i}")

        p.barrier("a_proj_b", clear=False)
        xnt_stack.close()
        wpj = ps.enter_context(nc.sbuf_tensor("a_wpj", [128, 8, D], BF16))
        bpr = ps.enter_context(nc.sbuf_tensor("a_bpr", [1, D], BF16))
        xo = [ps.enter_context(nc.sbuf_tensor(f"a_xo{i}", [128, D], F32)) for i in range(2)]
        p.dma("qg", lambda e: e.dma_start(
            wpj[:], self.d_aproj_w.rearrange("(c p) n -> p c n", p=128)),
            writes=("a_wpj",), tag="a_wpj")
        p.dma("qg", lambda e: e.dma_start(bpr[:], self.d_aproj_b.unsqueeze(0)),
              writes=("a_bpr",), tag="a_bpr")
        for ti in range(32):
            b = ti % 2
            p.dma("qs", (lambda ti=ti: lambda e: e.dma_start(
                xt[0][:], src_dram[ti * 128:(ti + 1) * 128, :]))(),
                writes=("a_xt0",), tag=f"apld{ti}")
            for half in range(2):
                bk = 2 * b + half
                p.step("tensor", (lambda half=half, bk=bk: lambda e: e.matmul(
                    self.pbank(bk), self.ones_bf[0:1, 0:128],
                    bpr[0:1, half * 512:(half + 1) * 512], start=True, stop=False))(),
                    reads=("a_bpr", "c_ones_bf"), writes=(f"pf{bk}",), tag=f"apb{ti}_{half}")
                for c in range(8):
                    p.step("tensor", (lambda ti=ti, half=half, c=c, bk=bk: lambda e: e.matmul(
                        self.pbank(bk), yT[:, c, ti * 128:(ti + 1) * 128],
                        wpj[:, c, half * 512:(half + 1) * 512],
                        start=False, stop=(c == 7)))(),
                        reads=("a_yT", "a_wpj"), writes=(f"pf{bk}",), tag=f"ap{ti}_{half}_{c}")
                p.step("vector", (lambda ti=ti, b=b, half=half, bk=bk: lambda e: e.tensor_tensor(
                    xo[b][:, half * 512:(half + 1) * 512], xt[0][:, half * 512:(half + 1) * 512],
                    self.pbank(bk), op=ALU.add))(),
                    reads=("a_xt0", f"pf{bk}"), writes=(f"a_xo{b}",), tag=f"apadd{ti}_{half}")
            p.dma("qs", (lambda ti=ti, b=b: lambda e: e.dma_start(
                dst_dram[ti * 128:(ti + 1) * 128, :], xo[b][:]))(),
                reads=(f"a_xo{b}",), writes=(f"dst{ti}",), tag=f"apst{ti}")
        p.barrier("a_end")
        ps.close()

    def phase_ffn(self, src_dram, dst_dram, pfx):
        nc, p = self.nc, self.prog
        ps = contextlib.ExitStack()
        NTB = 4
        TBS = 1024
        xacc = ps.enter_context(nc.sbuf_tensor(pfx + "xacc", [128, 8, 1024], F32))
        xnt = ps.enter_context(nc.sbuf_tensor(pfx + "xnt", [128, 8, 1024], BF16))
        xlt = ps.enter_context(nc.sbuf_tensor(pfx + "xlt", [128, 8, 128], BF16))
        xnb = ps.enter_context(nc.sbuf_tensor(pfx + "xnb", [128, D], BF16))
        xlb = ps.enter_context(nc.sbuf_tensor(pfx + "xlb", [128, D], BF16))
        hsb = ps.enter_context(nc.sbuf_tensor(pfx + "h", [128, 32, 1024], BF16))
        wdsb = ps.enter_context(nc.sbuf_tensor(pfx + "wd", [128, 32, 1024], BF16))
        wgu = [ps.enter_context(nc.sbuf_tensor(f"{pfx}wgu{i}", [128, 8, 256], BF16))
               for i in range(2)]
        sg = [ps.enter_context(nc.sbuf_tensor(f"{pfx}sg{i}", [128, 512], BF16))
              for i in range(4)]
        logits = ps.enter_context(nc.sbuf_tensor(pfx + "lg", [128, 8], F32))
        cmb = ps.enter_context(nc.sbuf_tensor(pfx + "cmb", [128, 8, 8], F32))
        rsc = ps.enter_context(nc.sbuf_tensor(pfx + "rsc", [128, 8], F32))

        for tb in range(NTB):
            r0 = tb * TBS
            p.dma("qs", (lambda r0=r0: lambda e: e.dma_start(
                xacc[:],
                src_dram[r0:r0 + TBS, :].rearrange("(i p) n -> p i n", p=128)))(),
                writes=(pfx + "xacc",), tag=f"{pfx}xld{tb}")
            for ti in range(8):
                self.ln(xacc[:, ti, :], pfx + "xacc", xnb[:], pfx + "xnb",
                        xlb[:], pfx + "xlb", tag=f"{pfx}ln{tb}_{ti}_")
                if tb == 0 and ti == 0:
                    self.dump(xnb[:], (pfx + "xnb",), "d_xnb")
                    self.dump(self.ln_st[:], ("ln_st",), "d_st")
                self.tr128(lambda c: xnb[:, c * 128:(c + 1) * 128], pfx + "xnb",
                           (lambda c, ti=ti: xnt[:, c, ti * 128:(ti + 1) * 128]),
                           pfx + "xnt", 8, [0, 1], tag=f"{pfx}trh{tb}_{ti}_")
                self.tr128(lambda c: xlb[:, c * 128:(c + 1) * 128], pfx + "xlb",
                           (lambda c: xlt[:, c, :]),
                           pfx + "xlt", 8, [0, 1], tag=f"{pfx}trl{tb}_{ti}_")
                bk = 2 + ti % 2
                terms = []
                for (lt, lres, rt, rres) in [(xnt, pfx + "xnt", self.rwT_hi, "rwT_hi"),
                                             (xlt, pfx + "xlt", self.rwT_hi, "rwT_hi"),
                                             (xnt, pfx + "xnt", self.rwT_lo, "rwT_lo")]:
                    for c in range(8):
                        lap = (lt[:, c, ti * 128:(ti + 1) * 128] if lt is xnt
                               else lt[:, c, :])
                        terms.append((lap, lres, rt, rres, c))
                for k, (lap, lres, rt, rres, c) in enumerate(terms):
                    p.step("tensor", (lambda lap=lap, rt=rt, c=c, bk=bk,
                           st=(k == 0), sp=(k == len(terms) - 1):
                           lambda e: e.matmul(self.pbank(bk, 0, 8), lap, rt[:, c, 0:8],
                                              start=st, stop=sp))(),
                           reads=(lres, rres), writes=(f"pf{bk}",),
                           tag=f"{pfx}rt{tb}_{ti}_{k}")
                p.step("scalar", (lambda bk=bk: lambda e: e.copy(logits[:], self.pf[bk][:, 0:8]))(),
                       reads=(f"pf{bk}",), writes=(pfx + "lg",), tag=f"{pfx}rte{tb}_{ti}")
                if tb == 0 and ti == 0:
                    self.dump(logits[:], (pfx + "lg",), "d_lg")
                self._router_select(pfx, tb, ti, logits, rsc, cmb)
                if tb == 0 and ti == 0:
                    self.dump(cmb[:].rearrange("p a b -> p (a b)"), (pfx + "cmb",), "d_cmb")
            for ei in range(8):
                self._expert(pfx, tb, ei, xnt, hsb, wdsb, wgu, sg, xacc, cmb)
                if tb == 0 and ei in (0, 7):
                    self.dump(hsb[:, 0, :], (pfx + "h",), "d_h")
                    self.dump(xacc[:, 0, :], (pfx + "xacc",), "d_xacc")
            p.dma("qs", (lambda r0=r0: lambda e: e.dma_start(
                dst_dram[r0:r0 + TBS, :].rearrange("(i p) n -> p i n", p=128), xacc[:]))(),
                reads=(pfx + "xacc",), writes=(f"{pfx}st{tb}",), tag=f"{pfx}xst{tb}")
        p.barrier(pfx + "end")
        ps.close()

    def _router_select(self, pfx, tb, ti, logits, rsc, cmb):
        p = self.prog
        LG, RS = pfx + "lg", pfx + "rsc"
        p.step("vector", lambda e: e.reduce_max(rsc[:, 0:1], logits[:, 0:NRT], axis=AX.X),
               reads=(LG,), writes=(RS,), tag=f"{pfx}s1{tb}{ti}")
        p.step("vector", lambda e: e.tensor_scalar(
            self._rtmp[:, 0:NRT], logits[:, 0:NRT], rsc[:, 0:1], None, op0=ALU.is_equal),
            reads=(LG, RS), writes=("rtmp",), tag=f"{pfx}s2{tb}{ti}", sreads=(RS,))
        p.step("vector", lambda e: e.tensor_tensor(
            self._rtmp[:, 8:8 + NRT], self._rtmp[:, 0:NRT], self.iota_rev[:, 0:NRT], op=ALU.mult),
            reads=("rtmp", "c_iota_rev"), writes=("rtmp",), tag=f"{pfx}s3{tb}{ti}")
        p.step("vector", lambda e: e.reduce_max(rsc[:, 1:2], self._rtmp[:, 8:8 + NRT], axis=AX.X),
               reads=("rtmp",), writes=(RS,), tag=f"{pfx}s4{tb}{ti}")
        p.step("vector", lambda e: e.tensor_scalar(
            self._rtmp[:, 0:NRT], self.iota_rev[:, 0:NRT], rsc[:, 1:2], None, op0=ALU.is_equal),
            reads=("c_iota_rev", RS), writes=("rtmp",), tag=f"{pfx}s5{tb}{ti}",
            sreads=(RS,))
        p.step("vector", lambda e: e.scalar_tensor_tensor(
            self._rtmp[:, 16:16 + NRT], self._rtmp[:, 0:NRT], NEG, logits[:, 0:NRT],
            op0=ALU.mult, op1=ALU.add),
            reads=("rtmp", LG), writes=("rtmp",), tag=f"{pfx}s6{tb}{ti}")
        p.step("vector", lambda e: e.reduce_max(rsc[:, 2:3], self._rtmp[:, 16:16 + NRT], axis=AX.X),
               reads=("rtmp",), writes=(RS,), tag=f"{pfx}s7{tb}{ti}")
        p.step("vector", lambda e: e.tensor_scalar(
            self._rtmp[:, 24:24 + NRT], self._rtmp[:, 16:16 + NRT], rsc[:, 2:3], None,
            op0=ALU.is_equal),
            reads=("rtmp", RS), writes=("rtmp",), tag=f"{pfx}s8{tb}{ti}",
            sreads=(RS,))
        p.step("vector", lambda e: e.tensor_tensor(
            rsc[:, 3:4], rsc[:, 0:1], rsc[:, 2:3], op=ALU.subtract),
            reads=(RS,), writes=(RS,), tag=f"{pfx}s9{tb}{ti}")
        p.step("scalar", lambda e: e.activation(rsc[:, 4:5], rsc[:, 3:4], AF.Sigmoid),
               reads=(RS,), writes=(RS,), tag=f"{pfx}sa{tb}{ti}")
        p.step("scalar", lambda e: e.activation(rsc[:, 5:6], rsc[:, 3:4], AF.Sigmoid, scale=-1.0),
               reads=(RS,), writes=(RS,), tag=f"{pfx}sb{tb}{ti}")
        p.step("vector", lambda e: e.tensor_scalar_mul(
            self._rtmp[:, 0:NRT], self._rtmp[:, 0:NRT], rsc[:, 4:5]),
            reads=("rtmp", RS), writes=("rtmp",), tag=f"{pfx}sc{tb}{ti}", sreads=(RS,))
        p.step("vector", (lambda ti=ti: lambda e: e.scalar_tensor_tensor(
            cmb[:, ti, 0:NRT], self._rtmp[:, 24:24 + NRT], rsc[:, 5:6], self._rtmp[:, 0:NRT],
            op0=ALU.mult, op1=ALU.add))(),
            reads=("rtmp", RS), writes=(pfx + "cmb",), tag=f"{pfx}sd{tb}{ti}",
            sreads=(RS,))

    def _expert(self, pfx, tb, ei, xnt, hsb, wdsb, wgu, sg, xacc, cmb):
        nc, p = self.nc, self.prog
        shared = (ei == NRT)
        dwg = self.d_swg if shared else self.d_wg
        dwu = self.d_swu if shared else self.d_wu
        dwd = self.d_swd if shared else self.d_wd
        eidx = 0 if shared else ei
        for m in range(32):
            if 4 <= m < 20 and m % 2 == 0:
                mg = (m - 4) // 2
                p.dma("qg", (lambda dwd=dwd, eidx=eidx, mg=mg: lambda e: e.dma_start(
                    wdsb[:, mg * 4:(mg + 1) * 4, :],
                    dwd[eidx].rearrange("(m p) n -> p m n", p=128)[:, mg * 4:(mg + 1) * 4, :]))(),
                    writes=(f"{pfx}wd",), tag=f"{pfx}wd{tb}_{ei}_{mg}")
            sl = m % 2
            p.dma("qg", (lambda dwg=dwg, eidx=eidx, m=m, sl=sl: lambda e: e.dma_start(
                wgu[sl][:, :, 0:128],
                dwg[eidx].rearrange("(c p) n -> p c n", p=128)[:, :, m * 128:(m + 1) * 128]))(),
                writes=(f"{pfx}wgu{sl}g",), tag=f"{pfx}lg{tb}_{ei}_{m}")
            p.dma("qg", (lambda dwu=dwu, eidx=eidx, m=m, sl=sl: lambda e: e.dma_start(
                wgu[sl][:, :, 128:256],
                dwu[eidx].rearrange("(c p) n -> p c n", p=128)[:, :, m * 128:(m + 1) * 128]))(),
                writes=(f"{pfx}wgu{sl}u",), tag=f"{pfx}lu{tb}_{ei}_{m}")
            for th in range(2):
                bg = (m % 2) * 4 + th
                bu = (m % 2) * 4 + 2 + th
                for c in range(8):
                    p.step("tensor", (lambda m=m, sl=sl, th=th, c=c, bg=bg: lambda e: e.matmul(
                        self.pbank(bg), wgu[sl][:, c, 0:128],
                        xnt[:, c, th * 512:(th + 1) * 512],
                        start=(c == 0), stop=(c == 7)))(),
                        reads=(f"{pfx}wgu{sl}g", pfx + "xnt"), writes=(f"pf{bg}",),
                        tag=f"{pfx}g{tb}_{ei}_{m}_{th}")
                for c in range(8):
                    p.step("tensor", (lambda m=m, sl=sl, th=th, c=c, bu=bu: lambda e: e.matmul(
                        self.pbank(bu), wgu[sl][:, c, 128:256],
                        xnt[:, c, th * 512:(th + 1) * 512],
                        start=(c == 0), stop=(c == 7)))(),
                        reads=(f"{pfx}wgu{sl}u", pfx + "xnt"), writes=(f"pf{bu}",),
                        tag=f"{pfx}u{tb}_{ei}_{m}_{th}")
                si = (m % 2) * 2 + th
                p.step("scalar", (lambda si=si, bg=bg: lambda e: e.activation(
                    sg[si][:], self.pbank(bg), AF.Silu))(),
                    reads=(f"pf{bg}",), writes=(f"{pfx}sg{si}",),
                    tag=f"{pfx}si{tb}_{ei}_{m}_{th}")
                p.step("vector", (lambda m=m, th=th, si=si, bu=bu: lambda e: e.tensor_tensor(
                    hsb[:, m, th * 512:(th + 1) * 512], sg[si][:], self.pbank(bu),
                    op=ALU.mult))(),
                    reads=(f"{pfx}sg{si}", f"pf{bu}"), writes=(pfx + "h",),
                    tag=f"{pfx}hm{tb}_{ei}_{m}_{th}")
        for tg in range(4):
            for m in range(32):
                for tt in range(2):
                    for dmh in range(2):
                        bk = (tg % 2) * 4 + tt * 2 + dmh
                        p.step("tensor", (lambda m=m, tg=tg, tt=tt, dmh=dmh, bk=bk:
                               lambda e: e.matmul(
                                   self.pbank(bk),
                                   hsb[:, m, tg * 256 + tt * 128:tg * 256 + (tt + 1) * 128],
                                   wdsb[:, m, dmh * 512:(dmh + 1) * 512],
                                   start=(m == 0), stop=(m == 31)))(),
                               reads=(pfx + "h", f"{pfx}wd"), writes=(f"pf{bk}",),
                               tag=f"{pfx}d{tb}_{ei}_{tg}_{m}_{tt}_{dmh}")
            for tt in range(2):
                ti = tg * 2 + tt
                for dmh in range(2):
                    bk = (tg % 2) * 4 + tt * 2 + dmh
                    if shared:
                        p.step("vector", (lambda ti=ti, dmh=dmh, bk=bk: lambda e: e.tensor_tensor(
                            xacc[:, ti, dmh * 512:(dmh + 1) * 512],
                            xacc[:, ti, dmh * 512:(dmh + 1) * 512],
                            self.pbank(bk), op=ALU.add))(),
                            reads=(f"pf{bk}",), writes=(pfx + "xacc",),
                            tag=f"{pfx}mx{tb}_{ei}_{ti}_{dmh}")
                    else:
                        p.step("vector", (lambda ti=ti, dmh=dmh, bk=bk, ei=ei: lambda e:
                               e.scalar_tensor_tensor(
                                   xacc[:, ti, dmh * 512:(dmh + 1) * 512],
                                   self.pbank(bk), cmb[:, ti, ei:ei + 1],
                                   xacc[:, ti, dmh * 512:(dmh + 1) * 512],
                                   op0=ALU.mult, op1=ALU.add))(),
                               reads=(f"pf{bk}", pfx + "cmb"), writes=(pfx + "xacc",),
                               tag=f"{pfx}mx{tb}_{ei}_{ti}_{dmh}", sreads=(pfx + "cmb",))

    def phase_attn_b(self, src_dram, dst_dram):
        nc, p = self.nc, self.prog
        ps = contextlib.ExitStack()
        tri = ps.enter_context(nc.sbuf_tensor("b_tri", [128, 384], BF16))
        allon = ps.enter_context(nc.sbuf_tensor("b_allones", [128, 128], BF16))
        p.dma("qs", lambda e: e.dma_start(tri[:], self.dconst["tri_bf"]),
              writes=("c_tri_bf",), tag="b_tri")
        p.dma("qs", lambda e: e.dma_start(allon[:], self.dconst["allones_bf"]),
              writes=("c_allones_bf",), tag="b_allon")
        self.UO = tri[:, 0:128]
        self.MSL = tri[:, 128:256]
        self.MLI = tri[:, 256:384]
        self.allones = allon[:]
        wqkvT = ps.enter_context(nc.sbuf_tensor("b_wqkvT", [128, 8, 3 * D], BF16))
        woutT = ps.enter_context(nc.sbuf_tensor("b_woutT", [128, 8, D], BF16))
        wgbT = ps.enter_context(nc.sbuf_tensor("b_wgbT", [128, 8, 32], BF16))
        gbr = ps.enter_context(nc.sbuf_tensor("b_gbr", [1, 32], BF16))
        OT = ps.enter_context(nc.sbuf_tensor("b_OT", [128, 8, T], BF16))
        stg = ps.enter_context(nc.sbuf_tensor("b_stg", [128, D], BF16))
        stgf = ps.enter_context(nc.sbuf_tensor("b_stgf", [128, D], F32))
        xt = ps.enter_context(nc.sbuf_tensor("b_xt", [128, D], F32))
        xn = ps.enter_context(nc.sbuf_tensor("b_xn", [128, D], BF16))
        xnt = ps.enter_context(nc.sbuf_tensor("b_xnt", [128, 8, 128], BF16))
        qkv = ps.enter_context(nc.sbuf_tensor("b_qkv", [128, 3 * D], F32))
        qkn = ps.enter_context(nc.sbuf_tensor("b_qkn", [128, 2048], BF16))
        qtn = ps.enter_context(nc.sbuf_tensor("b_qtn", [128, D], BF16))
        kqT = ps.enter_context(nc.sbuf_tensor("b_kqT", [128, 16, 128], BF16))
        qtT = ps.enter_context(nc.sbuf_tensor("b_qtT", [128, 8, 128], BF16))
        gst = ps.enter_context(nc.sbuf_tensor("b_gst", [128, 16, 16], F32))
        spb = ps.enter_context(nc.sbuf_tensor("b_spb", [128, 16], BF16))
        sqs = ps.enter_context(nc.sbuf_tensor("b_sqs", [128, 2048], F32))
        ssn = ps.enter_context(nc.sbuf_tensor("b_ssn", [128, 32], F32))
        Zf = [ps.enter_context(nc.sbuf_tensor(f"b_Zf{i}", [128, 8, 128], F32)) for i in range(2)]
        Zb = ps.enter_context(nc.sbuf_tensor("b_Zb", [128, 8, 128], BF16))
        U = [dict(
            gma=ps.enter_context(nc.sbuf_tensor(f"b_gma{i}", [128, 128], BF16)),
            gstr=ps.enter_context(nc.sbuf_tensor(f"b_gstr{i}", [128, 128], BF16)),
            gmm=ps.enter_context(nc.sbuf_tensor(f"b_gmm{i}", [128, 128], BF16)),
            gmt=ps.enter_context(nc.sbuf_tensor(f"b_gmt{i}", [128, 128], BF16)),
            dt=ps.enter_context(nc.sbuf_tensor(f"b_dt{i}", [128, 128], BF16)),
            lu=[ps.enter_context(nc.sbuf_tensor(f"b_lu{i}_{j}", [128, 128], BF16)) for j in range(2)],
            lut=[ps.enter_context(nc.sbuf_tensor(f"b_lut{i}_{j}", [128, 128], BF16)) for j in range(2)],
            xs=[ps.enter_context(nc.sbuf_tensor(f"b_xs{i}_{j}", [128, 128], BF16)) for j in range(2)],
            w2=ps.enter_context(nc.sbuf_tensor(f"b_w2{i}", [128, 128], BF16)),
            w3=ps.enter_context(nc.sbuf_tensor(f"b_w3{i}", [128, 64], BF16)),
        ) for i in range(4)]

        p.step("vector", lambda e: e.memset(Zf[0][:].rearrange("p a b -> p (a b)"), 0.0),
               writes=tuple(f"b_Zf0_{h}" for h in range(16)), tag="bZ0")
        p.step("vector", lambda e: e.memset(Zf[1][:].rearrange("p a b -> p (a b)"), 0.0),
               writes=tuple(f"b_Zf1_{h}" for h in range(16)), tag="bZ1")
        p.step("vector", lambda e: e.memset(Zb[:].rearrange("p a b -> p (a b)"), 0.0),
               writes=tuple(f"b_Zb_{h}" for h in range(16)), tag="bZb0")
        p.dma("qg", lambda e: e.dma_start(gbr[0:1, 0:16], self.d_bg_b.unsqueeze(0)),
              writes=("b_gbr",), tag="b_gb1")
        p.dma("qg", lambda e: e.dma_start(gbr[0:1, 16:32], self.d_bb_b.unsqueeze(0)),
              writes=("b_gbr",), tag="b_gb2")

        def load_transposed(dsrc, rows, dst, dres, coloff, tag):
            nrt = (rows + 127) // 128
            for rt in range(nrt):
                r0 = rt * 128
                nr = min(128, rows - r0)
                p.dma("qg", (lambda r0=r0, nr=nr: lambda e: e.dma_start(
                    stg[0:nr, :], dsrc[r0:r0 + nr, :]))(),
                    writes=("b_stg",), tag=f"{tag}ld{rt}")
                for c in range(8):
                    bk = c % 2
                    p.step("tensor", (lambda c=c, bk=bk: lambda e: e.transpose(
                        self.pbank_bf(bk), stg[:, c * 128:(c + 1) * 128], self.ident_bf[:]))(),
                        reads=("b_stg", "c_ident_bf"), writes=(f"pf{bk}",), tag=f"{tag}T{rt}_{c}")
                    p.step("scalar", (lambda c=c, bk=bk, r0=r0, nr=nr, coloff=coloff: lambda e:
                           e.copy(dst[:, c, coloff + r0:coloff + r0 + nr],
                                  self.pbank_bf(bk)[:, 0:nr]))(),
                           reads=(f"pf{bk}",), writes=(dres,), tag=f"{tag}E{rt}_{c}")

        p.step("vector", lambda e: e.memset(stg[:], 0.0), writes=("b_stg",), tag="bstg0")
        load_transposed(self.d_bqkv_w, 3 * D, wqkvT, "b_wqkvT", 0, "bwq")
        load_transposed(self.d_bout_w, D, woutT, "b_woutT", 0, "bwo")
        p.step("vector", lambda e: e.memset(stgf[:], 0.0), writes=("b_stgf",), tag="bstgf0")
        p.dma("qg", lambda e: e.dma_start(stg[0:16, :], self.d_bg_w), writes=("b_stg",), tag="bgw")
        p.dma("qg", lambda e: e.dma_start(stg[16:32, :], self.d_bb_w), writes=("b_stg",), tag="bbw")
        for c in range(8):
            bk = c % 2
            p.step("tensor", (lambda c=c, bk=bk: lambda e: e.transpose(
                self.pbank_bf(bk), stg[:, c * 128:(c + 1) * 128], self.ident_bf[:]))(),
                reads=("b_stg", "c_ident_bf"), writes=(f"pf{bk}",), tag=f"bgT{c}")
            p.step("scalar", (lambda c=c, bk=bk: lambda e: e.copy(
                wgbT[:, c, 0:32], self.pbank_bf(bk)[:, 0:32]))(),
                reads=(f"pf{bk}",), writes=("b_wgbT",), tag=f"bgE{c}")
        p.barrier("b_prelude_end")

        for c in range(32):
            t0 = c * 128
            par = c % 2
            u = U[par]
            zf_in, zf_out = Zf[par], Zf[1 - par]
            p.dma("qs", (lambda t0=t0: lambda e: e.dma_start(
                xt[:], src_dram[t0:t0 + 128, :]))(),
                writes=("b_xt",), tag=f"bld{c}")
            self.ln(xt[:], "b_xt", xn[:], "b_xn", tag=f"bln{c}_")
            self.tr128(lambda cc: xn[:, cc * 128:(cc + 1) * 128], "b_xn",
                       (lambda cc: xnt[:, cc, :]), "b_xnt", 8, [4, 5], tag=f"btr{c}_")
            for j in range(6):
                bk = 4 + j % 3
                for cc in range(8):
                    p.step("tensor", (lambda j=j, cc=cc, bk=bk: lambda e: e.matmul(
                        self.pbank(bk), xnt[:, cc, :], wqkvT[:, cc, j * 512:(j + 1) * 512],
                        start=(cc == 0), stop=(cc == 7)))(),
                        reads=("b_xnt", "b_wqkvT"), writes=(f"pf{bk}",), tag=f"bqkv{c}_{j}")
                p.step("scalar", (lambda j=j, bk=bk: lambda e: e.copy(
                    qkv[:, j * 512:(j + 1) * 512], self.pbank(bk)))(),
                    reads=(f"pf{bk}",), writes=("b_qkv",), tag=f"bqkve{c}_{j}")
            p.step("tensor", lambda e: e.matmul(
                self.pbank(7, 0, 32), self.ones_bf[0:1, 0:128], gbr[0:1, 0:32],
                start=True, stop=False),
                reads=("b_gbr", "c_ones_bf"), writes=("pf7",), tag=f"bgb{c}")
            for cc in range(8):
                p.step("tensor", (lambda cc=cc: lambda e: e.matmul(
                    self.pbank(7, 0, 32), xnt[:, cc, :], wgbT[:, cc, 0:32],
                    start=False, stop=(cc == 7)))(),
                    reads=("b_xnt", "b_wgbT"), writes=("pf7",), tag=f"bg{c}_{cc}")
            p.step("scalar", lambda e: e.activation(
                gst[:, 11:13, :].rearrange("p a b -> p (a b)"), self.pf[7][:, 0:32],
                AF.Exp, scale=-1.0),
                reads=("pf7",), writes=("b_gex",), tag=f"bge{c}")
            p.step("vector", lambda e: e.tensor_scalar_add(
                gst[:, 13:15, :].rearrange("p a b -> p (a b)"),
                gst[:, 11:13, :].rearrange("p a b -> p (a b)"), 1.0),
                reads=("b_gex",), writes=("b_g1p",), tag=f"bg1{c}")
            p.step("scalar", lambda e: e.activation(gst[:, 0, :], gst[:, 13, :], AF.Ln),
                   reads=("b_g1p",), writes=("b_gsp",), tag=f"bsp{c}")
            p.step("vector", lambda e: e.reciprocal(gst[:, 1, :], gst[:, 14, :]),
                   reads=("b_g1p",), writes=("b_gbs",), tag=f"bbs{c}")
            p.step("scalar", lambda e: e.copy(spb[:], gst[:, 0, :]),
                   reads=("b_gsp",), writes=("b_spb",), tag=f"bspb{c}")
            p.step("tensor", lambda e: e.matmul(
                self.pbank(6, 0, 16), self.UO, spb[:], start=True, stop=True),
                reads=("b_spb", "c_tri_bf"), writes=("pf6",), tag=f"bcs{c}")
            p.step("scalar", lambda e: e.copy(gst[:, 2, :], self.pbank(6, 0, 16)),
                   reads=("pf6",), writes=("b_gcs",), tag=f"bcse{c}")
            p.step("scalar", lambda e: e.activation(gst[:, 3, :], gst[:, 2, :], AF.Exp, scale=-1.0),
                   reads=("b_gcs",), writes=("b_ggam",), tag=f"bgam{c}")
            p.step("vector", lambda e: e.tensor_tensor(
                gst[:, 5, :], gst[:, 1, :], gst[:, 3, :], op=ALU.mult),
                reads=("b_gbs", "b_ggam"), writes=("b_gbg",), tag=f"bbg{c}")
            p.step("vector", lambda e: e.tensor_tensor(
                sqs[:], qkv[:, 0:2048], qkv[:, 0:2048], op=ALU.mult),
                reads=("b_qkv",), writes=("b_sqs",), tag=f"bsq{c}")
            p.step("vector", lambda e: e.tensor_reduce(
                ssn[:].unsqueeze(-1), sqs[:].rearrange("p (a b) -> p a b", b=64),
                axis=AX.X, op=ALU.add),
                reads=("b_sqs",), writes=("b_ssn",), tag=f"bssn{c}")
            p.step("scalar", lambda e: e.activation(ssn[:], ssn[:], AF.Ln,
                                                    bias=self.consts["epsv"][:, 1:2]),
                   reads=("b_ssn", "c_epsv"), writes=("b_ssn",), tag=f"bsrt{c}")
            p.step("scalar", lambda e: e.activation(ssn[:], ssn[:], AF.Exp, scale=-0.5),
                   reads=("b_ssn",), writes=("b_ssn",), tag=f"brcp{c}")
            p.step("vector", lambda e: e.tensor_tensor(
                gst[:, 10, :], ssn[:, 0:16], gst[:, 3, :], op=ALU.mult),
                reads=("b_ssn", "b_ggam"), writes=("b_grq",), tag=f"brq{c}")
            p.step("vector", lambda e: e.tensor_tensor(
                qkn[:, 0:1024].rearrange("p (a b) -> p a b", b=64),
                qkv[:, 0:1024].rearrange("p (a b) -> p a b", b=64),
                ssn[:, 0:16].unsqueeze(-1).broadcast_to([128, 16, 64]), op=ALU.mult),
                reads=("b_qkv", "b_ssn"), writes=("b_qkn",), tag=f"bqh{c}",
                sreads=("b_ssn",))
            p.step("vector", lambda e: e.tensor_tensor(
                qkn[:, 1024:2048].rearrange("p (a b) -> p a b", b=64),
                qkv[:, 1024:2048].rearrange("p (a b) -> p a b", b=64),
                ssn[:, 16:32].unsqueeze(-1).broadcast_to([128, 16, 64]), op=ALU.mult),
                reads=("b_qkv", "b_ssn"), writes=("b_qkn",), tag=f"bkh{c}",
                sreads=("b_ssn",))
            p.step("vector", lambda e: e.tensor_tensor(
                qtn[:].rearrange("p (a b) -> p a b", b=64),
                qkv[:, 0:1024].rearrange("p (a b) -> p a b", b=64),
                gst[:, 10, :].unsqueeze(-1).broadcast_to([128, 16, 64]), op=ALU.mult),
                reads=("b_qkv", "b_grq"), writes=("b_qtn",), tag=f"bqt{c}",
                sreads=("b_grq",))
            self.tr128(lambda cc: qkn[:, cc * 128:(cc + 1) * 128], "b_qkn",
                       (lambda cc: kqT[:, cc, :]), "b_kqT", 16, [6, 7], tag=f"bkqT{c}_")
            self.tr128(lambda cc: qtn[:, cc * 128:(cc + 1) * 128], "b_qtn",
                       (lambda cc: qtT[:, cc, :]), "b_qtT", 8, [6, 7], tag=f"bqtT{c}_")

            for quad in range(4):
                plans = [self._unit_plan(c, quad * 4 + s, s, U[s], zf_in, zf_out,
                                         Zb, qkv, qkn, kqT, qtT, gst, spb, OT)
                         for s in range(4)]
                mx = max(len(pl) for pl in plans)
                for i in range(mx):
                    for pl in plans:
                        if i < len(pl):
                            eng, emit, reads, writes, sreads = pl[i]
                            p.step(eng, emit, reads=reads, writes=writes,
                                   sreads=sreads, tag=f"bu{c}_{quad}_{i}")

        p.barrier("b_wout")
        xo = [qkv[:, 0:D], qkv[:, D:2 * D]]
        xti = [xt[:], stgf[:]]
        for ti in range(32):
            b = ti % 2
            p.dma("qs", (lambda ti=ti, b=b: lambda e: e.dma_start(
                xti[b], src_dram[ti * 128:(ti + 1) * 128, :]))(),
                writes=(f"b_xti{b}",), tag=f"bwld{ti}")
            for half in range(2):
                bk = 2 * b + half
                for vc in range(8):
                    p.step("tensor", (lambda ti=ti, half=half, vc=vc, bk=bk: lambda e: e.matmul(
                        self.pbank(bk), OT[:, vc, ti * 128:(ti + 1) * 128],
                        woutT[:, vc, half * 512:(half + 1) * 512],
                        start=(vc == 0), stop=(vc == 7)))(),
                        reads=("b_OT", "b_woutT"), writes=(f"pf{bk}",),
                        tag=f"bw{ti}_{half}_{vc}")
                p.step("vector", (lambda ti=ti, b=b, half=half, bk=bk: lambda e: e.tensor_tensor(
                    xo[b][:, half * 512:(half + 1) * 512],
                    xti[b][:, half * 512:(half + 1) * 512],
                    self.pbank(bk), op=ALU.add))(),
                    reads=(f"b_xti{b}", f"pf{bk}"), writes=(f"b_xo{b}",),
                    tag=f"bwa{ti}_{half}")
            p.dma("qs", (lambda ti=ti, b=b: lambda e: e.dma_start(
                dst_dram[ti * 128:(ti + 1) * 128, :], xo[b]))(),
                reads=(f"b_xo{b}",), writes=(f"bdst{ti}",), tag=f"bwst{ti}")
        p.barrier("b_end")
        ps.close()

    def _unit_plan(self, c, h, s, u, zf_in, zf_out, Zb, qkv, qkn, kqT, qtT,
                   gst, spb, OT):
        S = []

        def st(eng, emit, reads=(), writes=(), sreads=()):
            S.append((eng, emit, tuple(reads), tuple(writes), tuple(sreads)))

        bk = s
        pA = self.pf[bk][:, 0:128]
        pX = self.pf[bk][:, 128:256]
        pO = self.pf[bk][:, 256:384]
        pR = self.pf[bk][:, 384:448]
        pZ = self.pf[bk][:, 448:512]
        PK = f"pf{bk}"
        hs, hc = h % 2, h // 2
        lo, hi = hs * 64, (hs + 1) * 64
        kT_h = kqT[lo:hi, 8 + hc, :]
        qT_h = kqT[lo:hi, hc, :]
        qtT_h = qtT[lo:hi, hc, :]
        Zb_h = Zb[lo:hi, hc, lo:hi]
        un = f"u{s}"
        par = c % 2

        st("vector", (lambda: lambda e: e.tensor_scalar_mul(
            u["gstr"][:], self.UO, gst[:, 0, h:h + 1]))(),
            ("b_gsp", "c_tri_bf"), (f"{un}gstr",), ("b_gsp",))
        st("tensor", (lambda: lambda e: e.matmul(
            pA, self.allones[:], u["gstr"][:], start=True, stop=True))(),
            (f"{un}gstr", "c_allones_bf"), (PK,))
        st("vector", (lambda: lambda e: e.tensor_scalar(
            u["gma"][:], pA, gst[:, 2, h:h + 1], 0.0,
            op0=ALU.subtract, op1=ALU.min))(),
            (PK, "b_gcs"), (f"{un}gma",), ("b_gcs",))
        st("vector", (lambda: lambda e: e.tensor_tensor(
            gst[:, 6, h:h + 1], gst[:, 2, h:h + 1], self.pf[bk][:, 127:128],
            op=ALU.subtract))(),
            (PK, "b_gcs"), (f"b_gecv_{h}",))
        st("scalar", (lambda: lambda e: e.activation(
            gst[:, 6, h:h + 1], gst[:, 6, h:h + 1], AF.Exp))(),
            (f"b_gecv_{h}",), (f"b_gecv_{h}",))
        st("scalar", (lambda: lambda e: e.activation(
            gst[:, 7, h:h + 1], self.pf[bk][:, 127:128], AF.Exp, scale=-1.0))(),
            (PK,), (f"b_ggcb_{h}",))
        st("scalar", (lambda: lambda e: e.activation(
            u["gma"][:], u["gma"][:], AF.Exp))(),
            (f"{un}gma",), (f"{un}gma",))
        st("tensor", (lambda: lambda e: e.matmul(
            pA, kT_h, kT_h, start=True, stop=True))(),
            ("b_kqT",), (PK,))
        st("vector", (lambda: lambda e: e.tensor_tensor(
            u["gstr"][:], u["gma"][:], self.MSL, op=ALU.mult))(),
            (f"{un}gma", "c_tri_bf"), (f"{un}gstr",))
        st("vector", (lambda: lambda e: e.scalar_tensor_tensor(
            u["lu"][0][:], pA, gst[:, 1, h:h + 1], u["gstr"][:],
            op0=ALU.mult, op1=ALU.mult))(),
            (PK, "b_gbs", f"{un}gstr"), (f"{un}lu0",), ("b_gbs",))
        st("tensor", (lambda: lambda e: e.transpose(
            self.pf[bk][:].bitcast(BF16)[:, 0:128], u["lu"][0][:], self.ident_bf[:]))(),
            (f"{un}lu0", "c_ident_bf"), (PK,))
        st("scalar", (lambda: lambda e: e.copy(
            u["lut"][0][:], self.pf[bk][:].bitcast(BF16)[:, 0:128]))(),
            (PK,), (f"{un}lut0",))
        st("vector", (lambda: lambda e: e.tensor_tensor(
            u["gmm"][:], u["gma"][:], self.MLI, op=ALU.mult))(),
            (f"{un}gma", "c_tri_bf"), (f"{un}gmm",))
        st("tensor", (lambda: lambda e: e.transpose(
            self.pf[bk][:].bitcast(BF16)[:, 0:128], u["gmm"][:], self.ident_bf[:]))(),
            (f"{un}gmm", "c_ident_bf"), (PK,))
        st("scalar", (lambda: lambda e: e.copy(
            u["gmt"][:], self.pf[bk][:].bitcast(BF16)[:, 0:128]))(),
            (PK,), (f"{un}gmt",))
        st("tensor", (lambda: lambda e: e.matmul(
            pA, kT_h, qT_h, start=True, stop=True))(),
            ("b_kqT",), (PK,))
        st("vector", (lambda: lambda e: e.tensor_tensor(
            u["dt"][:], pA, u["gmt"][:], op=ALU.mult))(),
            (PK, f"{un}gmt"), (f"{un}dt",))
        st("tensor", (lambda: lambda e: e.matmul(
            pR, kT_h, Zb_h, start=True, stop=True))(),
            ("b_kqT", f"b_Zb_{h}"), (PK,))
        st("vector", (lambda: lambda e: e.tensor_scalar_mul(
            u["xs"][0][:, 0:64], pR, gst[:, 5, h:h + 1]))(),
            (PK, "b_gbg"), (f"{un}xs0",), ("b_gbg",))
        st("vector", (lambda: lambda e: e.tensor_scalar_mul(
            u["xs"][0][:, 64:128], qkv[:, 2048 + h * 64:2048 + (h + 1) * 64],
            gst[:, 1, h:h + 1]))(),
            ("b_qkv", "b_gbs"), (f"{un}xs0",), ("b_gbs",))
        for lev in range(7):
            cur, nxt = lev % 2, (lev + 1) % 2
            st("tensor", (lambda cur=cur: lambda e: e.matmul(
                pX, u["lut"][cur][:], u["xs"][cur][:], start=True, stop=True))(),
                (f"{un}lut{cur}", f"{un}xs{cur}"), (PK,))
            st("vector", (lambda cur=cur, nxt=nxt: lambda e: e.tensor_tensor(
                u["xs"][nxt][:], u["xs"][cur][:], pX, op=ALU.subtract))(),
                (f"{un}xs{cur}", PK), (f"{un}xs{nxt}",))
            if lev < 6:
                st("tensor", (lambda cur=cur: lambda e: e.matmul(
                    pA, u["lu"][cur][:], u["lut"][cur][:], start=True, stop=True))(),
                    (f"{un}lu{cur}", f"{un}lut{cur}"), (PK,))
                st("scalar", (lambda nxt=nxt: lambda e: e.copy(u["lut"][nxt][:], pA))(),
                    (PK,), (f"{un}lut{nxt}",))
            if lev < 5:
                st("tensor", (lambda cur=cur: lambda e: e.matmul(
                    pA, u["lut"][cur][:], u["lu"][cur][:], start=True, stop=True))(),
                    (f"{un}lu{cur}", f"{un}lut{cur}"), (PK,))
                st("vector", (lambda nxt=nxt: lambda e: e.tensor_copy(u["lu"][nxt][:], pA))(),
                    (PK,), (f"{un}lu{nxt}",))
        fin = 1
        st("vector", (lambda: lambda e: e.tensor_tensor(
            u["w2"][:, lo:hi], u["xs"][fin][:, 64:128], u["xs"][fin][:, 0:64],
            op=ALU.subtract))(),
            (f"{un}xs{fin}",), (f"{un}w2",))
        st("vector", (lambda: lambda e: e.tensor_scalar_mul(
            u["w3"][:], u["w2"][:, lo:hi], gst[:, 6, h:h + 1]))(),
            (f"{un}w2", f"b_gecv_{h}"), (f"{un}w3",), (f"b_gecv_{h}",))
        st("tensor", (lambda: lambda e: e.matmul(
            pO, Zb[lo:hi, hc, :], qtT_h, start=True, stop=False))(),
            (f"b_Zb_{h}", "b_qtT"), (PK,))
        st("tensor", (lambda: lambda e: e.matmul(
            pO, u["w2"][:], u["dt"][:], start=False, stop=True))(),
            (f"{un}w2", f"{un}dt"), (PK,))
        st("scalar", (lambda: lambda e: e.copy(
            OT[lo:hi, hc, c * 128:(c + 1) * 128], pO[lo:hi, :]))(),
            (PK,), ("b_OT",))
        st("tensor", (lambda: lambda e: e.matmul(
            pZ, qkn[:, 1024 + hc * 128:1024 + (hc + 1) * 128],
            u["w3"][:], start=True, stop=True))(),
            ("b_qkn", f"{un}w3"), (PK,))
        st("vector", (lambda: lambda e: e.scalar_tensor_tensor(
            zf_out[lo:hi, hc, lo:hi], zf_in[lo:hi, hc, lo:hi], gst[lo:hi, 7, h:h + 1],
            pZ[lo:hi, :], op0=ALU.mult, op1=ALU.add))(),
            (f"b_Zf{par}_{h}", f"b_ggcb_{h}", PK), (f"b_Zf{1 - par}_{h}",),
            (f"b_ggcb_{h}",))
        st("scalar", (lambda: lambda e: e.copy(
            Zb[lo:hi, hc, lo:hi], zf_out[lo:hi, hc, lo:hi]))(),
            (f"b_Zf{1 - par}_{h}",), (f"b_Zb_{h}",))
        return S

    def build(self, phases=("A", "F0", "B", "F1")):
        self._rtmp = self.es.enter_context(self.nc.sbuf_tensor("rtmp", [128, 32], F32))
        self.prelude_router()
        chain = list(phases)
        prev = self.dx
        for i, ph in enumerate(chain):
            dst = self.dy if i == len(chain) - 1 else self.d_xres
            if ph == "A":
                self.phase_attn_a(prev, dst)
            elif ph == "B":
                self.phase_attn_b(prev, dst)
            elif ph in ("F0", "F1"):
                self.phase_ffn(prev, dst, "f0_" if ph == "F0" else "f1_")
            prev = self.d_xres
        self.prog.barrier("final")
        sems = {}
        for d in DOMAINS + ("bar", "epo"):
            sems[d] = self.es.enter_context(self.nc.semaphore("sem_" + d))
        self.prog.resolve()
        with self.nc.Block() as block:
            self.prog.emit_all(sems, block)
        self.es.close()
        return self.nc


def _in_maps(inputs, consts):
    x = np.ascontiguousarray(np.asarray(inputs["L0_attn_a_x"], dtype=np.float32))
    base = dict(
        aqkv_w=np.ascontiguousarray(inputs["L0_attn_a_w_qkv"], dtype=np.float32),
        aqkv_b=np.ascontiguousarray(inputs["L0_attn_a_b_qkv"], dtype=np.float32),
        aproj_w=np.ascontiguousarray(inputs["L0_attn_a_w_proj"], dtype=np.float32),
        aproj_b=np.ascontiguousarray(inputs["L0_attn_a_b_proj"], dtype=np.float32),
        bqkv_w=np.ascontiguousarray(inputs["L0_attn_b_Wqkv"], dtype=np.float32),
        bg_w=np.ascontiguousarray(inputs["L0_attn_b_Wg"], dtype=np.float32),
        bg_b=np.ascontiguousarray(inputs["L0_attn_b_bg"], dtype=np.float32),
        bb_w=np.ascontiguousarray(inputs["L0_attn_b_Wb"], dtype=np.float32),
        bb_b=np.ascontiguousarray(inputs["L0_attn_b_bb"], dtype=np.float32),
        bout_w=np.ascontiguousarray(inputs["L0_attn_b_Wout"], dtype=np.float32),
        rw=np.ascontiguousarray(inputs["L0_ffn_router_w"], dtype=np.float32),
        wg=np.ascontiguousarray(inputs["L0_ffn_w_gate"], dtype=np.float32),
        wu=np.ascontiguousarray(inputs["L0_ffn_w_up"], dtype=np.float32),
        wd=np.ascontiguousarray(inputs["L0_ffn_w_down"], dtype=np.float32),
        swg=np.ascontiguousarray(inputs["L0_ffn_sw_gate"], dtype=np.float32),
        swu=np.ascontiguousarray(inputs["L0_ffn_sw_up"], dtype=np.float32),
        swd=np.ascontiguousarray(inputs["L0_ffn_sw_down"], dtype=np.float32),
        **consts,
    )
    return [dict(base, x=np.ascontiguousarray(x[i])) for i in range(N_CORES)]


def kernel(**inputs):
    consts = make_consts()
    mk = MK()
    nc = mk.build()
    maps = _in_maps(inputs, consts)
    res = run_bass_kernel_spmd(nc, maps, core_ids=list(range(N_CORES)))
    out = np.stack([np.asarray(r["y"], dtype=np.float32) for r in res.results], axis=0)
    return out


# version 06 (final)
# speedup vs baseline: 1.4612x; 1.0322x over previous
import contextlib

import numpy as np
import ml_dtypes

import concourse.bass as bass
import concourse.mybir as mybir
from concourse.bass_utils import run_bass_kernel_spmd

F32 = mybir.dt.float32
BF16 = mybir.dt.bfloat16
AF = mybir.ActivationFunctionType
ALU = mybir.AluOpType
AX = mybir.AxisListType

N_CORES = 8
T = 4096
D = 1024
H = 16
DH = 64
HID = 4096
NRT = 7

NEG = -1.0e30

ENGINES = ("tensor", "vector", "scalar", "gpsimd", "sync")
QUEUES = ("qs", "qg", "qc")
QUEUE_ENGINE = {"qs": "sync", "qg": "gpsimd", "qc": "scalar"}
NSLOT = {"qs": 8, "qg": 8, "qc": 2}
DOMAINS = tuple(ENGINES) + tuple(f"{q}{i}" for q in QUEUES for i in range(NSLOT[q]))


class Step:
    __slots__ = ("eng", "queue", "slot", "emit", "reads", "writes", "sreads",
                 "needs_inc", "inc_val", "waits", "tag")

    def __init__(self, eng, queue, emit, reads, writes, tag="", sreads=()):
        self.eng = eng
        self.queue = queue
        self.slot = None
        self.emit = emit
        self.reads = tuple(reads)
        self.writes = tuple(writes)
        self.sreads = tuple(sreads)
        self.needs_inc = False
        self.inc_val = None
        self.waits = []
        self.tag = tag

    @property
    def domain(self):
        if self.queue is not None:
            return f"{self.queue}{self.slot}"
        return self.eng


class Prog:
    def __init__(self, full_sync=False):
        self.steps = []
        self.full_sync = full_sync
        self._quse = {q: 0 for q in QUEUES}

    def step(self, eng, emit, reads=(), writes=(), tag="", sreads=()):
        self.steps.append(Step(eng, None, emit, reads, writes, tag, sreads))

    def dma(self, queue, emit, reads=(), writes=(), tag=""):
        s = Step(QUEUE_ENGINE[queue], queue, emit, reads, writes, tag)
        s.slot = self._quse[queue] % NSLOT[queue]
        self._quse[queue] += 1
        self.steps.append(s)

    def barrier(self, tag="barrier", clear=True):
        s = Step("barrier", None, None, (), (), tag)
        s.slot = clear
        self.steps.append(s)

    def resolve(self):
        last_write = {}
        readers = {}
        barrier_pending = []
        deps = [None] * len(self.steps)
        for i, s in enumerate(self.steps):
            if s.eng == "barrier":
                deps[i] = list(barrier_pending)
                barrier_pending = [i]
                last_write = {}
                readers = {}
                continue
            barrier_pending.append(i)
            dmap = {}
            for r in s.reads:
                w = last_write.get(r)
                if w is not None:
                    dmap[w] = dmap.get(w, False) or (r in s.sreads)
            for wr in s.writes:
                w = last_write.get(wr)
                if w is not None:
                    dmap.setdefault(w, False)
                for rd in readers.get(wr, ()):
                    dmap.setdefault(rd, False)
            dmap.pop(i, None)
            deps[i] = dmap
            for r in s.reads:
                readers.setdefault(r, []).append(i)
            for wr in s.writes:
                last_write[wr] = i
                readers[wr] = []

        need = [False] * len(self.steps)
        for i, s in enumerate(self.steps):
            me = s.domain if s.eng != "barrier" else None
            dl = deps[i] if isinstance(deps[i], list) else deps[i].items()
            for jj in dl:
                j, always = (jj, False) if isinstance(jj, int) else jj
                p = self.steps[j]
                if p.eng == "barrier":
                    continue
                fs = (self.full_sync is True or
                      (isinstance(self.full_sync, (set, frozenset, tuple)) and
                       (s.eng in self.full_sync if me is not None else False)))
                if me is None or fs or always or p.domain != me or p.queue is not None:
                    need[j] = True
        counters = {}
        for s, nd in zip(self.steps, need):
            if s.eng == "barrier":
                if s.slot:
                    counters = {d: v for d, v in counters.items() if d not in ENGINES}
                continue
            if s.queue is not None:
                d = s.domain
                counters[d] = counters.get(d, 0) + 16
                s.needs_inc = True
                s.inc_val = counters[d]
            elif nd:
                counters[s.eng] = counters.get(s.eng, 0) + 1
                s.needs_inc = True
                s.inc_val = counters[s.eng]

        known = {e: {} for e in ENGINES}
        for i, s in enumerate(self.steps):
            if s.eng == "barrier":
                targets = {}
                for j in deps[i]:
                    p = self.steps[j]
                    if p.eng == "barrier" or not p.needs_inc:
                        continue
                    targets[p.domain] = max(targets.get(p.domain, 0), p.inc_val)
                s.waits = sorted(targets.items())
                if s.slot:
                    for e in ENGINES:
                        for d, v in s.waits:
                            if d not in ENGINES and known[e].get(d, -1) < v:
                                known[e][d] = v
                        for d in ENGINES:
                            known[e].pop(d, None)
                else:
                    for e in ENGINES:
                        for d, v in s.waits:
                            if known[e].get(d, -1) < v:
                                known[e][d] = v
                continue
            kn = known[s.eng]
            targets = {}
            if s.queue is not None and s.inc_val > 16:
                targets[s.domain] = s.inc_val - 16
            dl = deps[i] if isinstance(deps[i], list) else deps[i].items()
            for jj in dl:
                j, always = (jj, False) if isinstance(jj, int) else jj
                p = self.steps[j]
                if p.eng == "barrier" or not p.needs_inc:
                    continue
                d = p.domain
                if (p.queue is None and d == s.eng and not always and not
                        (self.full_sync is True or
                         (isinstance(self.full_sync, (set, frozenset, tuple)) and s.eng in self.full_sync))):
                    continue
                targets[d] = max(targets.get(d, 0), p.inc_val)
            ws = []
            for d, v in sorted(targets.items()):
                if kn.get(d, -1) >= v:
                    continue
                ws.append((d, v))
                kn[d] = v
            s.waits = ws

    def emit_all(self, sems, block):
        per_engine = {e: [] for e in ENGINES}
        nbar = 0
        for s in self.steps:
            if s.eng == "barrier":
                nbar += 1
                for e in ENGINES:
                    per_engine[e].append(s)
            else:
                per_engine[s.eng].append(s)

        bsem = sems["bar"]
        esem = sems["epo"]

        def run_stream(name, eng):
            bar_k = 0
            for s in per_engine[name]:
                for d, v in s.waits:
                    eng.wait_ge(sems[d], v)
                if s.eng == "barrier":
                    if not s.slot:
                        continue
                    bar_k += 1
                    eng.drain()
                    eng.sem_inc(bsem, 1)
                    if name == "gpsimd":
                        eng.wait_ge(bsem, 5 * bar_k)
                        for d in ENGINES:
                            eng.sem_clear(sems[d])
                        eng.sem_inc(esem, 1)
                    eng.wait_ge(esem, bar_k)
                    continue
                inst = s.emit(eng)
                if s.needs_inc:
                    assert inst is not None, f"step {s.tag} returned no instruction"
                    if s.queue is not None:
                        inst.then_inc(sems[s.domain], 16)
                    else:
                        inst.then_inc(sems[s.eng], 1)

        @block.tensor
        def _(eng):
            run_stream("tensor", eng)

        @block.vector
        def _(eng):
            run_stream("vector", eng)

        @block.scalar
        def _(eng):
            run_stream("scalar", eng)

        @block.gpsimd
        def _(eng):
            run_stream("gpsimd", eng)

        @block.sync
        def _(eng):
            run_stream("sync", eng)


def make_consts():
    c = {}
    c["ident_bf"] = np.eye(128, dtype=ml_dtypes.bfloat16)
    c["ident_f32"] = np.eye(128, dtype=np.float32)
    c["ones_bf"] = np.ones((1, 128), dtype=ml_dtypes.bfloat16)
    c["ones_f32"] = np.ones((1, 128), dtype=np.float32)
    c["ones_col"] = np.ones((128, 1), dtype=ml_dtypes.bfloat16)
    ev = np.zeros((128, 2), dtype=np.float32)
    ev[:, 0] = 1e-5
    ev[:, 1] = 1e-30
    c["epsv"] = ev
    r = np.arange(128)[:, None]
    y = np.arange(896)[None, :]
    kq = np.where(y - r >= 384, 0.0, NEG).astype(np.float32)
    c["amask_kq"] = kq.astype(ml_dtypes.bfloat16)
    l = np.arange(128)[:, None]
    j = np.arange(128)[None, :]
    tri = np.zeros((128, 384), dtype=np.float32)
    tri[:, 0:128] = (l <= j)
    tri[:, 128:256] = (j < l)
    tri[:, 256:384] = (j <= l)
    c["tri_bf"] = tri.astype(ml_dtypes.bfloat16)
    c["allones_bf"] = np.ones((128, 128), dtype=ml_dtypes.bfloat16)
    ir = np.zeros((128, 8), dtype=np.float32)
    ir[:, 0:7] = (7.0 - np.arange(7))[None, :]
    c["iota_rev"] = ir
    return c


CONST_SPECS = [
    ("ident_bf", [128, 128], BF16),
    ("ones_bf", [1, 128], BF16),
    ("ones_f32", [1, 128], F32),
    ("ones_col", [128, 1], BF16),
    ("iota_rev", [128, 8], F32),
    ("epsv", [128, 2], F32),
]
PHASE_CONST_SPECS = [
    ("ident_f32", [128, 128], F32),
    ("tri_bf", [128, 384], BF16), ("allones_bf", [128, 128], BF16),
]


class MK:
    def __init__(self, full_sync=("vector", "scalar"), dbg_n=0):
        self.dbg_n = dbg_n
        self.dbg_i = 0
        self.nc = bass.Bass("TRN2", target_bir_lowering=False,
                            detect_race_conditions=(full_sync is True))
        self.es = contextlib.ExitStack()
        self.prog = Prog(full_sync=full_sync)
        nc = self.nc

        if self.dbg_n:
            self.d_dbg = nc.dram_tensor("dbg", [self.dbg_n, 128, 1024], F32,
                                        kind="ExternalOutput").ap()
        self.dx = nc.dram_tensor("x", [T, D], F32, kind="ExternalInput").ap()
        self.dy = nc.dram_tensor("y", [T, D], F32, kind="ExternalOutput").ap()
        dt = lambda n, s: nc.dram_tensor(n, s, F32, kind="ExternalInput").ap()
        self.d_aqkv_w = dt("aqkv_w", [D, 3 * D])
        self.d_aqkv_b = dt("aqkv_b", [3 * D])
        self.d_aproj_w = dt("aproj_w", [D, D])
        self.d_aproj_b = dt("aproj_b", [D])
        self.d_bqkv_w = dt("bqkv_w", [3 * D, D])
        self.d_bg_w = dt("bg_w", [H, D])
        self.d_bg_b = dt("bg_b", [H])
        self.d_bb_w = dt("bb_w", [H, D])
        self.d_bb_b = dt("bb_b", [H])
        self.d_bout_w = dt("bout_w", [D, D])
        self.d_rw = dt("rw", [NRT, D])
        self.d_wg = dt("wg", [NRT, D, HID])
        self.d_wu = dt("wu", [NRT, D, HID])
        self.d_wd = dt("wd", [NRT, HID, D])
        self.d_swg = dt("swg", [1, D, HID])
        self.d_swu = dt("swu", [1, D, HID])
        self.d_swd = dt("swd", [1, HID, D])
        self.d_xres = nc.dram_tensor("x_res", [T, D], F32).ap()

        self.consts = {}
        for nm, shp, dty in CONST_SPECS:
            dap = nc.dram_tensor(nm, shp, dty, kind="ExternalInput").ap()
            sb = self.es.enter_context(nc.sbuf_tensor("c_" + nm, shp, dty))
            self.consts[nm] = sb
            self.prog.dma("qs", (lambda sb=sb, dap=dap: lambda e: e.dma_start(sb[:], dap))(),
                          writes=("c_" + nm,), tag="ldc_" + nm)
        self.d_amask_kq = nc.dram_tensor("amask_kq", [128, 896], BF16, kind="ExternalInput").ap()

        self.dconst = {}
        for nm, shp, dty in PHASE_CONST_SPECS:
            self.dconst[nm] = nc.dram_tensor(nm, shp, dty, kind="ExternalInput").ap()
        self.ident_bf = self.consts["ident_bf"]
        self.ones_bf = self.consts["ones_bf"]
        self.iota_rev = self.consts["iota_rev"]

        self.pf = [self.es.enter_context(nc.psum_tensor(f"pf{i}", [128, 512], F32))
                   for i in range(8)]

        self.ln_xc = self.es.enter_context(nc.sbuf_tensor("ln_xc", [128, D], F32))
        self.ln_st = self.es.enter_context(nc.sbuf_tensor("ln_st", [128, 8], F32))

        self.rwT_hi = self.es.enter_context(nc.sbuf_tensor("rwT_hi", [128, 8, 8], BF16))
        self.rwT_lo = self.es.enter_context(nc.sbuf_tensor("rwT_lo", [128, 8, 8], BF16))

    def dump(self, ap, reads, tag="dbg"):
        if not self.dbg_n or self.dbg_i >= self.dbg_n:
            return
        i = self.dbg_i
        self.dbg_i += 1
        w = ap.shape[-1] if len(ap.shape) == 2 else 1024
        np_parts = ap.shape[0]
        self.prog.dma("qg", (lambda ap=ap, i=i, np_parts=np_parts, w=w: lambda e: e.dma_start(
            self.d_dbg[i, 0:np_parts, 0:w], ap))(),
            reads=tuple(reads), writes=(f"dbg{i}",), tag=f"{tag}{i}")

    def pbank(self, i, lo=0, hi=512):
        return self.pf[i][:, lo:hi]

    def pbank_bf(self, i, lo=0, hi=128):
        return self.pf[i][:].bitcast(BF16)[:, lo:hi]

    def ln(self, xt_ap, xt_res, xn_ap, xn_res, xlo_ap=None, xlo_res=None, tag=""):
        p = self.prog
        st = self.ln_st
        xc = self.ln_xc
        p.step("vector", lambda e: e.tensor_reduce(st[:, 0:1], xt_ap, axis=AX.X, op=ALU.add),
               reads=(xt_res,), writes=("ln_st",), tag=tag + "sum")
        p.step("scalar", lambda e: e.mul(st[:, 1:2], st[:, 0:1], -1.0 / D),
               reads=("ln_st",), writes=("ln_st",), tag=tag + "nm")
        p.step("vector", lambda e: e.tensor_scalar_add(xc[:], xt_ap, st[:, 1:2]),
               reads=(xt_res, "ln_st"), writes=("ln_xc",), tag=tag + "xc",
               sreads=("ln_st",))
        p.step("scalar", lambda e: e.activation(
            xn_ap, xc[:], AF.Square, accum_out=st[:, 2:3]),
            reads=("ln_xc",), writes=(xn_res, "ln_st"), tag=tag + "ss")
        p.step("scalar", lambda e: e.activation(st[:, 3:4], st[:, 2:3], AF.Ln,
                                                scale=1.0 / D,
                                                bias=self.consts["epsv"][:, 0:1]),
               reads=("ln_st", "c_epsv"), writes=("ln_st",), tag=tag + "std")
        p.step("scalar", lambda e: e.activation(st[:, 4:5], st[:, 3:4], AF.Exp, scale=-0.5),
               reads=("ln_st",), writes=("ln_st",), tag=tag + "rstd")
        p.step("vector", lambda e: e.tensor_scalar_mul(xn_ap, xc[:], st[:, 4:5]),
               reads=("ln_xc", "ln_st"), writes=(xn_res,), tag=tag + "xn",
               sreads=("ln_st",))
        if xlo_ap is not None:
            p.step("vector", lambda e: e.scalar_tensor_tensor(
                xlo_ap, xc[:], st[:, 4:5], xn_ap, op0=ALU.mult, op1=ALU.subtract),
                reads=("ln_xc", "ln_st", xn_res), writes=(xlo_res,), tag=tag + "xlo",
                sreads=("ln_st",))

    def tr128(self, src_ap_fn, src_res, dst_ap_fn, dst_res, n, banks, tag=""):
        p = self.prog
        for c in range(n):
            b = banks[c % len(banks)]
            pap = self.pbank_bf(b)
            p.step("tensor", (lambda c=c, pap=pap: lambda e: e.transpose(
                pap, src_ap_fn(c), self.ident_bf[:]))(),
                reads=(src_res, "c_ident_bf"), writes=(f"pf{b}",), tag=f"{tag}T{c}")
            p.step("scalar", (lambda c=c, pap=pap: lambda e: e.copy(dst_ap_fn(c), pap))(),
                   reads=(f"pf{b}",), writes=(dst_res,), tag=f"{tag}E{c}")

    def prelude_router(self):
        nc, p = self.nc, self.prog
        with contextlib.ExitStack() as ps:
            rwf = ps.enter_context(nc.sbuf_tensor("pr_rwf", [128, D], F32))
            rwh = ps.enter_context(nc.sbuf_tensor("pr_rwh", [128, D], BF16))
            rwl = ps.enter_context(nc.sbuf_tensor("pr_rwl", [128, D], BF16))
            p.step("vector", lambda e: e.memset(rwf[:], 0.0), writes=("pr_rwf",), tag="rw0")
            p.step("vector", lambda e: e.memset(self.rwT_hi[:].rearrange("p a b -> p (a b)"), 0.0),
                   writes=("rwT_hi",), tag="rwh0")
            p.step("vector", lambda e: e.memset(self.rwT_lo[:].rearrange("p a b -> p (a b)"), 0.0),
                   writes=("rwT_lo",), tag="rwl0")
            p.dma("qs", lambda e: e.dma_start(rwf[0:NRT, :], self.d_rw),
                  writes=("pr_rwf",), tag="rwld")
            p.step("vector", lambda e: e.tensor_copy(rwh[:], rwf[:]),
                   reads=("pr_rwf",), writes=("pr_rwh",), tag="rwh")
            p.step("vector", lambda e: e.tensor_tensor(rwl[:], rwf[:], rwh[:], op=ALU.subtract),
                   reads=("pr_rwf", "pr_rwh"), writes=("pr_rwl",), tag="rwl")
            for (src, sres, dst, dres) in [(rwh, "pr_rwh", self.rwT_hi, "rwT_hi"),
                                           (rwl, "pr_rwl", self.rwT_lo, "rwT_lo")]:
                for c in range(8):
                    b = c % 2
                    pap = self.pbank_bf(b)
                    p.step("tensor", (lambda src=src, c=c, pap=pap: lambda e: e.transpose(
                        pap, src[:, c * 128:(c + 1) * 128], self.ident_bf[:]))(),
                        reads=(sres, "c_ident_bf"), writes=(f"pf{b}",), tag=f"rwT{c}")
                    p.step("scalar", (lambda dst=dst, c=c, pap=pap: lambda e: e.copy(
                        dst[:, c, 0:NRT], pap[:, 0:NRT]))(),
                        reads=(f"pf{b}",), writes=(dres,), tag=f"rwTe{c}")
            p.barrier("prelude_router_end")

    def phase_attn_a(self, src_dram, dst_dram):
        nc, p = self.nc, self.prog
        ps = contextlib.ExitStack()
        amk = ps.enter_context(nc.sbuf_tensor("a_amk", [128, 896], BF16))
        p.dma("qs", lambda e: e.dma_start(amk[:], self.d_amask_kq), writes=("a_amk",), tag="a_amk")

        yT = ps.enter_context(nc.sbuf_tensor("a_yT", [128, 8, T], BF16))
        kT = [ps.enter_context(nc.sbuf_tensor(f"a_kT{i}", [65, T], BF16)) for i in range(2)]
        qT = [ps.enter_context(nc.sbuf_tensor(f"a_qT{i}", [65, T], BF16)) for i in range(2)]
        vsb = ps.enter_context(nc.sbuf_tensor("a_v", [128, 32, 192], BF16))
        _xt1 = ps.enter_context(nc.sbuf_tensor("a_xt0", [128, D], F32))
        xt = [_xt1, _xt1]
        xn = ps.enter_context(nc.sbuf_tensor("a_xn", [128, D], BF16))
        wgr = ps.enter_context(nc.sbuf_tensor("a_wgr", [128, 8, 384], BF16))
        bqc = ps.enter_context(nc.sbuf_tensor("a_bqc", [64, 48], F32))
        bqr = ps.enter_context(nc.sbuf_tensor("a_bqr", [1, 128], BF16))
        ptile = [ps.enter_context(nc.sbuf_tensor(f"a_pt{i}", [128, 512], BF16)) for i in range(4)]
        sqq = [ps.enter_context(nc.sbuf_tensor(f"a_sqq{i}", [64, 512], BF16)) for i in range(2)]
        nbr = [ps.enter_context(nc.sbuf_tensor(f"a_nbr{i}", [1, 512], BF16)) for i in range(2)]
        mk1 = [ps.enter_context(nc.sbuf_tensor(f"a_mk1{i}", [1, 2], F32)) for i in range(2)]
        _mrb = [ps.enter_context(nc.sbuf_tensor(f"a_mrb{i}", [1, 512], BF16)) for i in range(2)]
        mrb = [_mrb[0], _mrb[0], _mrb[1], _mrb[1]]
        linv = [ps.enter_context(nc.sbuf_tensor(f"a_linv{i}", [1, 512], F32)) for i in range(2)]
        ottmp = ps.enter_context(nc.sbuf_tensor("a_ottmp", [128, 512], BF16))
        xnt_stack = contextlib.ExitStack()
        xnt = xnt_stack.enter_context(nc.sbuf_tensor("a_xnt", [128, 8, T], BF16))

        p.dma("qs", lambda e: e.dma_start(bqc[:], self.d_aqkv_b.rearrange("(c p) -> p c", p=64),
                                          allow_slow_non_contiguous=True),
              writes=("a_bqc",), tag="a_bqc")

        p.step("vector", lambda e: e.memset(vsb[:].rearrange("p a b -> p (a b)"), 0.0),
               writes=("a_v",), tag="a_v0")
        p.step("vector", lambda e: e.memset(vsb[:, :, 96:97], 1.0), writes=("a_v",), tag="a_v1a")
        for hs in range(2):
            p.step("vector", (lambda hs=hs: lambda e: e.memset(kT[hs][64:65, :], 1.0))(),
                   writes=(f"a_kT{hs}r64",), tag=f"a_k1_{hs}")

        for i in range(32):
            b = i % 2
            p.dma("qs", (lambda i=i: lambda e: e.dma_start(
                xt[0][:], src_dram[i * 128:(i + 1) * 128, :]))(),
                writes=("a_xt0",), tag=f"a0ld{i}")
            self.ln(xt[0][:], "a_xt0", xn[:], "a_xn", tag=f"a0ln{i}_")
            self.tr128(lambda c: xn[:, c * 128:(c + 1) * 128], "a_xn",
                       (lambda c, i=i: xnt[:, c, i * 128:(i + 1) * 128]),
                       "a_xnt", 8, [6, 7], tag=f"a0tr{i}_")

        for g in range(8):
            for mat in range(3):
                col0 = mat * D + g * 128
                p.dma("qg", (lambda mat=mat, col0=col0: lambda e: e.dma_start(
                    wgr[:, :, mat * 128:(mat + 1) * 128],
                    self.d_aqkv_w.rearrange("(c p) n -> p c n", p=128)[:, :, col0:col0 + 128]))(),
                    writes=("a_wgr",), tag=f"a1w{g}_{mat}")
            p.dma("qg", (lambda g=g: lambda e: e.dma_start(
                bqr[:], self.d_aqkv_b[2 * D + g * 128:2 * D + (g + 1) * 128].unsqueeze(0)))(),
                writes=("a_bqr",), tag=f"a1bv{g}")
            for tb in range(8):
                t0 = tb * 512
                for mat in range(2):
                    for hs in range(2):
                        bk = (tb * 4 + mat * 2 + hs) % 4
                        for c in range(8):
                            p.step("tensor", (lambda mat=mat, hs=hs, c=c, t0=t0, bk=bk:
                                   lambda e: e.matmul(
                                       self.pbank(bk)[0:64, :],
                                       wgr[:, c, mat * 128 + hs * 64:mat * 128 + (hs + 1) * 64],
                                       xnt[:, c, t0:t0 + 512],
                                       start=(c == 0), stop=(c == 7)))(),
                                   reads=("a_wgr", "a_xnt"), writes=(f"pf{bk}",),
                                   tag=f"a1qk{g}_{tb}_{mat}_{hs}")
                        dst = qT if mat == 0 else kT
                        dres = "a_qT" if mat == 0 else "a_kT"
                        bcol = (mat * 1024 + g * 128 + hs * 64) // 64
                        p.step("scalar", (lambda dst=dst, hs=hs, t0=t0, bk=bk, bcol=bcol:
                               lambda e: e.activation(
                                   dst[hs][0:64, t0:t0 + 512],
                                   self.pf[bk][0:64, 0:512],
                                   AF.Identity, bias=bqc[0:64, bcol:bcol + 1]))(),
                               reads=(f"pf{bk}", "a_bqc"), writes=(f"{dres}{hs}",),
                               tag=f"a1qke{g}_{tb}_{mat}_{hs}", sreads=("a_bqc",))
                for ts in range(4):
                    bk = 4 + ts % 2
                    p.step("tensor", (lambda g=g, bk=bk: lambda e: e.matmul(
                        self.pbank(bk, 0, 128), self.ones_bf[0:1, 0:128],
                        bqr[0:1, 0:128],
                        start=True, stop=False))(),
                        reads=("a_bqr", "c_ones_bf"), writes=(f"pf{bk}",),
                        tag=f"a1vb{g}_{tb}_{ts}")
                    for c in range(8):
                        p.step("tensor", (lambda c=c, t0=t0, ts=ts, bk=bk: lambda e:
                               e.matmul(self.pbank(bk, 0, 128),
                                        xnt[:, c, t0 + ts * 128:t0 + (ts + 1) * 128],
                                        wgr[:, c, 256:384],
                                        start=False, stop=(c == 7)))(),
                               reads=("a_wgr", "a_xnt"), writes=(f"pf{bk}",),
                               tag=f"a1v{g}_{tb}_{ts}_{c}")
                    for hs in range(2):
                        c0 = 0 if hs == 0 else 128
                        p.step("scalar", (lambda tb=tb, ts=ts, hs=hs, bk=bk, c0=c0: lambda e:
                               e.copy(vsb[:, tb * 4 + ts, c0:c0 + 64],
                                      self.pf[bk][:, hs * 64:(hs + 1) * 64]))(),
                            reads=(f"pf{bk}",), writes=("a_v",), tag=f"a1ve{g}_{tb}_{ts}_{hs}")
            def head_plan(hs):
                S = []

                def st(eng, emit, reads=(), writes=(), sreads=()):
                    S.append((eng, emit, tuple(reads), tuple(writes), tuple(sreads)))

                h = 2 * g + hs
                for mat in (1, 0):
                    srcT = qT[hs] if mat == 0 else kT[hs]
                    sres = f"a_qT{hs}" if mat == 0 else f"a_kT{hs}"
                    for j in range(8):
                        bk = 2 * hs + j % 2
                        st("scalar", (lambda srcT=srcT, j=j: lambda e: e.activation(
                            sqq[hs][:], srcT[0:64, j * 512:(j + 1) * 512], AF.Square))(),
                            (sres,), (f"a_sqq{hs}",))
                        st("tensor", (lambda bk=bk: lambda e: e.matmul(
                            self.pf[bk][0:1, 0:512], self.consts["ones_col"][0:64, :],
                            sqq[hs][:], start=True, stop=True))(),
                            (f"a_sqq{hs}", "c_ones_col"), (f"pf{bk}",))
                        st("scalar", (lambda bk=bk: lambda e:
                               e.activation(nbr[hs][:], self.pf[bk][0:1, 0:512], AF.Ln))(),
                               (f"pf{bk}",), (f"a_nbr{hs}",))
                        if mat == 1:
                            st("vector", (lambda: lambda e: e.reduce_max(
                                mk1[hs][:, 1:2], nbr[hs][:], axis=AX.X))(),
                                ((f"a_nbr{hs}"),), (f"a_mk1b{hs}",))
                            if j == 0:
                                st("vector", (lambda: lambda e: e.tensor_copy(
                                    mk1[hs][:, 0:1], mk1[hs][:, 1:2]))(),
                                    (f"a_mk1b{hs}",), (f"a_mk1{hs}",))
                            else:
                                st("vector", (lambda: lambda e: e.tensor_tensor(
                                    mk1[hs][:, 0:1], mk1[hs][:, 0:1], mk1[hs][:, 1:2],
                                    op=ALU.max))(),
                                    (f"a_mk1b{hs}", f"a_mk1{hs}"), (f"a_mk1{hs}",))
                        else:
                            mb = 2 * hs + j % 2
                            st("vector", (lambda mb=mb: lambda e: e.tensor_scalar_add(
                                mrb[mb][:], nbr[hs][:], mk1[hs][:, 0:1]))(),
                                ((f"a_nbr{hs}"), f"a_mk1{hs}"), (f"a_mrb{mb // 2}",),
                                (f"a_mk1{hs}",))
                            st("scalar", (lambda mb=mb: lambda e: e.activation(
                                mrb[mb][:], mrb[mb][:], AF.Exp, scale=0.5))(),
                                (f"a_mrb{mb // 2}",), (f"a_mrb{mb // 2}",))
                            st("vector", (lambda mb=mb: lambda e: e.tensor_scalar_mul(
                                mrb[mb][:], mrb[mb][:], -1.0))(),
                                (f"a_mrb{mb // 2}",), (f"a_mrb{mb // 2}",))
                            st("qsdma", (lambda hs=hs, j=j, mb=mb: lambda e: e.dma_start(
                                qT[hs][64:65, j * 512:(j + 1) * 512], mrb[mb][:]))(),
                                (f"a_mrb{mb // 2}",), (f"a_qT{hs}r64",))
                for qb in range(8):
                    q0 = qb * 512
                    nkt = 4 * (qb + 1)
                    lrow = 96 if hs == 0 else 32
                    vlo = 0 if hs == 0 else 64
                    ob = 4 + hs
                    lb = 6 + hs
                    for kt in range(nkt):
                        bk = 2 * hs + kt % 2
                        pt = 2 * hs + kt % 2
                        st("tensor", (lambda hs=hs, q0=q0, kt=kt, bk=bk: lambda e: e.matmul(
                            self.pbank(bk), kT[hs][0:65, kt * 128:(kt + 1) * 128],
                            qT[hs][0:65, q0:q0 + 512], start=True, stop=True))(),
                            (f"a_qT{hs}", f"a_qT{hs}r64", f"a_kT{hs}", f"a_kT{hs}r64"),
                            (f"pf{bk}",))
                        if kt >= 4 * qb:
                            j = kt - 4 * qb
                            st("vector", (lambda j=j, bk=bk: lambda e: e.tensor_tensor(
                                self.pbank(bk), self.pbank(bk),
                                amk[:, 384 - 128 * j:896 - 128 * j], op=ALU.add))(),
                                (f"pf{bk}", "a_amk"), (f"pf{bk}",))
                        st("scalar", (lambda pt=pt, bk=bk: lambda e: e.activation(
                            ptile[pt][:], self.pbank(bk), AF.Exp, scale=0.125))(),
                            (f"pf{bk}",), (f"a_pt{pt}",))
                        st("tensor", (lambda kt=kt, pt=pt, nkt=nkt, vlo=vlo, ob=ob: lambda e:
                               e.matmul(
                                   self.pf[ob][:, 0:512],
                                   vsb[:, kt, vlo:vlo + 128],
                                   ptile[pt][:],
                                   start=(kt == 0), stop=(kt == nkt - 1)))(),
                            ("a_v", f"a_pt{pt}"), (f"pf{ob}",))
                    st("vector", (lambda lrow=lrow, hs=hs, ob=ob: lambda e: e.reciprocal(
                        linv[hs][:], self.pf[ob][lrow:lrow + 1, 0:512]))(),
                        (f"pf{ob}",), (f"a_linv{hs}",))
                    st("tensor", (lambda hs=hs, lb=lb: lambda e: e.matmul(
                        self.pf[lb][:, 0:512],
                        self.consts["ones_f32"][0:1, 0:128],
                        linv[hs][:], start=True, stop=True))(),
                        ((f"a_linv{hs}"), "c_ones_f32"), (f"pf{lb}",))
                    st("scalar", (lambda hs=hs, ob=ob: lambda e: e.copy(
                        ottmp[hs * 64:(hs + 1) * 64, :],
                        self.pf[ob][hs * 64:(hs + 1) * 64, 0:512]))(),
                        (f"pf{ob}",), (f"a_ottmp{hs}",))
                    st("vector", (lambda hs=hs, g=g, q0=q0, lb=lb: lambda e: e.tensor_tensor(
                        yT[hs * 64:(hs + 1) * 64, g, q0:q0 + 512],
                        ottmp[hs * 64:(hs + 1) * 64, :],
                        self.pf[lb][hs * 64:(hs + 1) * 64, 0:512], op=ALU.mult))(),
                        (f"a_ottmp{hs}", f"pf{lb}"), ("a_yT",))
                return S

            plans = [head_plan(0), head_plan(1)]
            mx = max(len(pl) for pl in plans)
            for i in range(mx):
                for pl in plans:
                    if i < len(pl):
                        eng, emit, reads, writes, sreads = pl[i]
                        if eng == "qsdma":
                            p.dma("qs", emit, reads=reads, writes=writes,
                                  tag=f"aat{g}_{i}")
                        else:
                            p.step(eng, emit, reads=reads, writes=writes,
                                   sreads=sreads, tag=f"aat{g}_{i}")

        p.barrier("a_proj_b", clear=False)
        xnt_stack.close()
        wpj = ps.enter_context(nc.sbuf_tensor("a_wpj", [128, 8, D], BF16))
        bpr = ps.enter_context(nc.sbuf_tensor("a_bpr", [1, D], BF16))
        xo = [ps.enter_context(nc.sbuf_tensor(f"a_xo{i}", [128, D], F32)) for i in range(2)]
        p.dma("qg", lambda e: e.dma_start(
            wpj[:], self.d_aproj_w.rearrange("(c p) n -> p c n", p=128)),
            writes=("a_wpj",), tag="a_wpj")
        p.dma("qg", lambda e: e.dma_start(bpr[:], self.d_aproj_b.unsqueeze(0)),
              writes=("a_bpr",), tag="a_bpr")
        for ti in range(32):
            b = ti % 2
            p.dma("qs", (lambda ti=ti: lambda e: e.dma_start(
                xt[0][:], src_dram[ti * 128:(ti + 1) * 128, :]))(),
                writes=("a_xt0",), tag=f"apld{ti}")
            for half in range(2):
                bk = 2 * b + half
                p.step("tensor", (lambda half=half, bk=bk: lambda e: e.matmul(
                    self.pbank(bk), self.ones_bf[0:1, 0:128],
                    bpr[0:1, half * 512:(half + 1) * 512], start=True, stop=False))(),
                    reads=("a_bpr", "c_ones_bf"), writes=(f"pf{bk}",), tag=f"apb{ti}_{half}")
                for c in range(8):
                    p.step("tensor", (lambda ti=ti, half=half, c=c, bk=bk: lambda e: e.matmul(
                        self.pbank(bk), yT[:, c, ti * 128:(ti + 1) * 128],
                        wpj[:, c, half * 512:(half + 1) * 512],
                        start=False, stop=(c == 7)))(),
                        reads=("a_yT", "a_wpj"), writes=(f"pf{bk}",), tag=f"ap{ti}_{half}_{c}")
                p.step("vector", (lambda ti=ti, b=b, half=half, bk=bk: lambda e: e.tensor_tensor(
                    xo[b][:, half * 512:(half + 1) * 512], xt[0][:, half * 512:(half + 1) * 512],
                    self.pbank(bk), op=ALU.add))(),
                    reads=("a_xt0", f"pf{bk}"), writes=(f"a_xo{b}",), tag=f"apadd{ti}_{half}")
            p.dma("qs", (lambda ti=ti, b=b: lambda e: e.dma_start(
                dst_dram[ti * 128:(ti + 1) * 128, :], xo[b][:]))(),
                reads=(f"a_xo{b}",), writes=(f"dst{ti}",), tag=f"apst{ti}")
        p.barrier("a_end")
        ps.close()

    def phase_ffn(self, src_dram, dst_dram, pfx):
        nc, p = self.nc, self.prog
        ps = contextlib.ExitStack()
        NTB = 4
        TBS = 1024
        xacc = ps.enter_context(nc.sbuf_tensor(pfx + "xacc", [128, 8, 1024], F32))
        xnt = ps.enter_context(nc.sbuf_tensor(pfx + "xnt", [128, 8, 1024], BF16))
        xlt = ps.enter_context(nc.sbuf_tensor(pfx + "xlt", [128, 8, 128], BF16))
        xnb = ps.enter_context(nc.sbuf_tensor(pfx + "xnb", [128, D], BF16))
        xlb = ps.enter_context(nc.sbuf_tensor(pfx + "xlb", [128, D], BF16))
        hsb = ps.enter_context(nc.sbuf_tensor(pfx + "h", [128, 32, 1024], BF16))
        wdsb = ps.enter_context(nc.sbuf_tensor(pfx + "wd", [128, 32, 1024], BF16))
        wgu = [ps.enter_context(nc.sbuf_tensor(f"{pfx}wgu{i}", [128, 8, 512], BF16))
               for i in range(2)]
        sg = [ps.enter_context(nc.sbuf_tensor(f"{pfx}sg{i}", [128, 512], BF16))
              for i in range(4)]
        logits = ps.enter_context(nc.sbuf_tensor(pfx + "lg", [128, 8], F32))
        cmb = ps.enter_context(nc.sbuf_tensor(pfx + "cmb", [128, 8, 8], F32))
        rsc = ps.enter_context(nc.sbuf_tensor(pfx + "rsc", [128, 8], F32))

        for tb in range(NTB):
            r0 = tb * TBS
            p.dma("qs", (lambda r0=r0: lambda e: e.dma_start(
                xacc[:],
                src_dram[r0:r0 + TBS, :].rearrange("(i p) n -> p i n", p=128)))(),
                writes=(pfx + "xacc",), tag=f"{pfx}xld{tb}")
            for ti in range(8):
                self.ln(xacc[:, ti, :], pfx + "xacc", xnb[:], pfx + "xnb",
                        xlb[:], pfx + "xlb", tag=f"{pfx}ln{tb}_{ti}_")
                if tb == 0 and ti == 0:
                    self.dump(xnb[:], (pfx + "xnb",), "d_xnb")
                    self.dump(self.ln_st[:], ("ln_st",), "d_st")
                self.tr128(lambda c: xnb[:, c * 128:(c + 1) * 128], pfx + "xnb",
                           (lambda c, ti=ti: xnt[:, c, ti * 128:(ti + 1) * 128]),
                           pfx + "xnt", 8, [0, 1], tag=f"{pfx}trh{tb}_{ti}_")
                self.tr128(lambda c: xlb[:, c * 128:(c + 1) * 128], pfx + "xlb",
                           (lambda c: xlt[:, c, :]),
                           pfx + "xlt", 8, [0, 1], tag=f"{pfx}trl{tb}_{ti}_")
                bk = 2 + ti % 2
                terms = []
                for (lt, lres, rt, rres) in [(xnt, pfx + "xnt", self.rwT_hi, "rwT_hi"),
                                             (xlt, pfx + "xlt", self.rwT_hi, "rwT_hi"),
                                             (xnt, pfx + "xnt", self.rwT_lo, "rwT_lo")]:
                    for c in range(8):
                        lap = (lt[:, c, ti * 128:(ti + 1) * 128] if lt is xnt
                               else lt[:, c, :])
                        terms.append((lap, lres, rt, rres, c))
                for k, (lap, lres, rt, rres, c) in enumerate(terms):
                    p.step("tensor", (lambda lap=lap, rt=rt, c=c, bk=bk,
                           st=(k == 0), sp=(k == len(terms) - 1):
                           lambda e: e.matmul(self.pbank(bk, 0, 8), lap, rt[:, c, 0:8],
                                              start=st, stop=sp))(),
                           reads=(lres, rres), writes=(f"pf{bk}",),
                           tag=f"{pfx}rt{tb}_{ti}_{k}")
                p.step("scalar", (lambda bk=bk: lambda e: e.copy(logits[:], self.pf[bk][:, 0:8]))(),
                       reads=(f"pf{bk}",), writes=(pfx + "lg",), tag=f"{pfx}rte{tb}_{ti}")
                if tb == 0 and ti == 0:
                    self.dump(logits[:], (pfx + "lg",), "d_lg")
                self._router_select(pfx, tb, ti, logits, rsc, cmb)
                if tb == 0 and ti == 0:
                    self.dump(cmb[:].rearrange("p a b -> p (a b)"), (pfx + "cmb",), "d_cmb")
            for ei in range(8):
                self._expert(pfx, tb, ei, xnt, hsb, wdsb, wgu, sg, xacc, cmb)
                if tb == 0 and ei in (0, 7):
                    self.dump(hsb[:, 0, :], (pfx + "h",), "d_h")
                    self.dump(xacc[:, 0, :], (pfx + "xacc",), "d_xacc")
            p.dma("qs", (lambda r0=r0: lambda e: e.dma_start(
                dst_dram[r0:r0 + TBS, :].rearrange("(i p) n -> p i n", p=128), xacc[:]))(),
                reads=(pfx + "xacc",), writes=(f"{pfx}st{tb}",), tag=f"{pfx}xst{tb}")
        p.barrier(pfx + "end")
        ps.close()

    def _router_select(self, pfx, tb, ti, logits, rsc, cmb):
        p = self.prog
        LG, RS = pfx + "lg", pfx + "rsc"
        p.step("vector", lambda e: e.reduce_max(rsc[:, 0:1], logits[:, 0:NRT], axis=AX.X),
               reads=(LG,), writes=(RS,), tag=f"{pfx}s1{tb}{ti}")
        p.step("vector", lambda e: e.tensor_scalar(
            self._rtmp[:, 0:NRT], logits[:, 0:NRT], rsc[:, 0:1], None, op0=ALU.is_equal),
            reads=(LG, RS), writes=("rtmp",), tag=f"{pfx}s2{tb}{ti}", sreads=(RS,))
        p.step("vector", lambda e: e.tensor_tensor(
            self._rtmp[:, 8:8 + NRT], self._rtmp[:, 0:NRT], self.iota_rev[:, 0:NRT], op=ALU.mult),
            reads=("rtmp", "c_iota_rev"), writes=("rtmp",), tag=f"{pfx}s3{tb}{ti}")
        p.step("vector", lambda e: e.reduce_max(rsc[:, 1:2], self._rtmp[:, 8:8 + NRT], axis=AX.X),
               reads=("rtmp",), writes=(RS,), tag=f"{pfx}s4{tb}{ti}")
        p.step("vector", lambda e: e.tensor_scalar(
            self._rtmp[:, 0:NRT], self.iota_rev[:, 0:NRT], rsc[:, 1:2], None, op0=ALU.is_equal),
            reads=("c_iota_rev", RS), writes=("rtmp",), tag=f"{pfx}s5{tb}{ti}",
            sreads=(RS,))
        p.step("vector", lambda e: e.scalar_tensor_tensor(
            self._rtmp[:, 16:16 + NRT], self._rtmp[:, 0:NRT], NEG, logits[:, 0:NRT],
            op0=ALU.mult, op1=ALU.add),
            reads=("rtmp", LG), writes=("rtmp",), tag=f"{pfx}s6{tb}{ti}")
        p.step("vector", lambda e: e.reduce_max(rsc[:, 2:3], self._rtmp[:, 16:16 + NRT], axis=AX.X),
               reads=("rtmp",), writes=(RS,), tag=f"{pfx}s7{tb}{ti}")
        p.step("vector", lambda e: e.tensor_scalar(
            self._rtmp[:, 24:24 + NRT], self._rtmp[:, 16:16 + NRT], rsc[:, 2:3], None,
            op0=ALU.is_equal),
            reads=("rtmp", RS), writes=("rtmp",), tag=f"{pfx}s8{tb}{ti}",
            sreads=(RS,))
        p.step("vector", lambda e: e.tensor_tensor(
            rsc[:, 3:4], rsc[:, 0:1], rsc[:, 2:3], op=ALU.subtract),
            reads=(RS,), writes=(RS,), tag=f"{pfx}s9{tb}{ti}")
        p.step("scalar", lambda e: e.activation(rsc[:, 4:5], rsc[:, 3:4], AF.Sigmoid),
               reads=(RS,), writes=(RS,), tag=f"{pfx}sa{tb}{ti}")
        p.step("scalar", lambda e: e.activation(rsc[:, 5:6], rsc[:, 3:4], AF.Sigmoid, scale=-1.0),
               reads=(RS,), writes=(RS,), tag=f"{pfx}sb{tb}{ti}")
        p.step("vector", lambda e: e.tensor_scalar_mul(
            self._rtmp[:, 0:NRT], self._rtmp[:, 0:NRT], rsc[:, 4:5]),
            reads=("rtmp", RS), writes=("rtmp",), tag=f"{pfx}sc{tb}{ti}", sreads=(RS,))
        p.step("vector", (lambda ti=ti: lambda e: e.scalar_tensor_tensor(
            cmb[:, ti, 0:NRT], self._rtmp[:, 24:24 + NRT], rsc[:, 5:6], self._rtmp[:, 0:NRT],
            op0=ALU.mult, op1=ALU.add))(),
            reads=("rtmp", RS), writes=(pfx + "cmb",), tag=f"{pfx}sd{tb}{ti}",
            sreads=(RS,))

    def _expert(self, pfx, tb, ei, xnt, hsb, wdsb, wgu, sg, xacc, cmb):
        nc, p = self.nc, self.prog
        shared = (ei == NRT)
        dwg = self.d_swg if shared else self.d_wg
        dwu = self.d_swu if shared else self.d_wu
        dwd = self.d_swd if shared else self.d_wd
        eidx = 0 if shared else ei
        for m in range(32):
            if 4 <= m < 20 and m % 2 == 0:
                mg = (m - 4) // 2
                p.dma("qg", (lambda dwd=dwd, eidx=eidx, mg=mg: lambda e: e.dma_start(
                    wdsb[:, mg * 4:(mg + 1) * 4, :],
                    dwd[eidx].rearrange("(m p) n -> p m n", p=128)[:, mg * 4:(mg + 1) * 4, :]))(),
                    writes=(f"{pfx}wd",), tag=f"{pfx}wd{tb}_{ei}_{mg}")
            sl = (m // 2) % 2
            mo = m % 2
            if mo == 0:
                p.dma("qg", (lambda dwg=dwg, eidx=eidx, m=m, sl=sl: lambda e: e.dma_start(
                    wgu[sl][:, :, 0:256],
                    dwg[eidx].rearrange("(c p) n -> p c n", p=128)[:, :, m * 128:(m + 2) * 128]))(),
                    writes=(f"{pfx}wgu{sl}g",), tag=f"{pfx}lg{tb}_{ei}_{m}")
                p.dma("qg", (lambda dwu=dwu, eidx=eidx, m=m, sl=sl: lambda e: e.dma_start(
                    wgu[sl][:, :, 256:512],
                    dwu[eidx].rearrange("(c p) n -> p c n", p=128)[:, :, m * 128:(m + 2) * 128]))(),
                    writes=(f"{pfx}wgu{sl}u",), tag=f"{pfx}lu{tb}_{ei}_{m}")
            for th in range(2):
                bg = (m % 2) * 4 + th
                bu = (m % 2) * 4 + 2 + th
                for c in range(8):
                    p.step("tensor", (lambda mo=mo, sl=sl, th=th, c=c, bg=bg: lambda e:
                           e.matmul(
                               self.pbank(bg), wgu[sl][:, c, mo * 128:(mo + 1) * 128],
                               xnt[:, c, th * 512:(th + 1) * 512],
                               start=(c == 0), stop=(c == 7)))(),
                        reads=(f"{pfx}wgu{sl}g", pfx + "xnt"), writes=(f"pf{bg}",),
                        tag=f"{pfx}g{tb}_{ei}_{m}_{th}")
                for c in range(8):
                    p.step("tensor", (lambda mo=mo, sl=sl, th=th, c=c, bu=bu: lambda e:
                           e.matmul(
                               self.pbank(bu), wgu[sl][:, c, 256 + mo * 128:256 + (mo + 1) * 128],
                               xnt[:, c, th * 512:(th + 1) * 512],
                               start=(c == 0), stop=(c == 7)))(),
                        reads=(f"{pfx}wgu{sl}u", pfx + "xnt"), writes=(f"pf{bu}",),
                        tag=f"{pfx}u{tb}_{ei}_{m}_{th}")
                si = (m % 2) * 2 + th
                p.step("scalar", (lambda si=si, bg=bg: lambda e: e.activation(
                    sg[si][:], self.pbank(bg), AF.Silu))(),
                    reads=(f"pf{bg}",), writes=(f"{pfx}sg{si}",),
                    tag=f"{pfx}si{tb}_{ei}_{m}_{th}")
                p.step("vector", (lambda m=m, th=th, si=si, bu=bu: lambda e: e.tensor_tensor(
                    hsb[:, m, th * 512:(th + 1) * 512], sg[si][:], self.pbank(bu),
                    op=ALU.mult))(),
                    reads=(f"{pfx}sg{si}", f"pf{bu}"), writes=(pfx + "h",),
                    tag=f"{pfx}hm{tb}_{ei}_{m}_{th}")
        for tg in range(4):
            for m in range(32):
                for tt in range(2):
                    for dmh in range(2):
                        bk = (tg % 2) * 4 + tt * 2 + dmh
                        p.step("tensor", (lambda m=m, tg=tg, tt=tt, dmh=dmh, bk=bk:
                               lambda e: e.matmul(
                                   self.pbank(bk),
                                   hsb[:, m, tg * 256 + tt * 128:tg * 256 + (tt + 1) * 128],
                                   wdsb[:, m, dmh * 512:(dmh + 1) * 512],
                                   start=(m == 0), stop=(m == 31)))(),
                               reads=(pfx + "h", f"{pfx}wd"), writes=(f"pf{bk}",),
                               tag=f"{pfx}d{tb}_{ei}_{tg}_{m}_{tt}_{dmh}")
            for tt in range(2):
                ti = tg * 2 + tt
                for dmh in range(2):
                    bk = (tg % 2) * 4 + tt * 2 + dmh
                    if shared:
                        p.step("vector", (lambda ti=ti, dmh=dmh, bk=bk: lambda e: e.tensor_tensor(
                            xacc[:, ti, dmh * 512:(dmh + 1) * 512],
                            xacc[:, ti, dmh * 512:(dmh + 1) * 512],
                            self.pbank(bk), op=ALU.add))(),
                            reads=(f"pf{bk}",), writes=(pfx + "xacc",),
                            tag=f"{pfx}mx{tb}_{ei}_{ti}_{dmh}")
                    else:
                        p.step("vector", (lambda ti=ti, dmh=dmh, bk=bk, ei=ei: lambda e:
                               e.scalar_tensor_tensor(
                                   xacc[:, ti, dmh * 512:(dmh + 1) * 512],
                                   self.pbank(bk), cmb[:, ti, ei:ei + 1],
                                   xacc[:, ti, dmh * 512:(dmh + 1) * 512],
                                   op0=ALU.mult, op1=ALU.add))(),
                               reads=(f"pf{bk}", pfx + "cmb"), writes=(pfx + "xacc",),
                               tag=f"{pfx}mx{tb}_{ei}_{ti}_{dmh}", sreads=(pfx + "cmb",))

    def phase_attn_b(self, src_dram, dst_dram):
        nc, p = self.nc, self.prog
        ps = contextlib.ExitStack()
        tri = ps.enter_context(nc.sbuf_tensor("b_tri", [128, 384], BF16))
        allon = ps.enter_context(nc.sbuf_tensor("b_allones", [128, 128], BF16))
        p.dma("qs", lambda e: e.dma_start(tri[:], self.dconst["tri_bf"]),
              writes=("c_tri_bf",), tag="b_tri")
        p.dma("qs", lambda e: e.dma_start(allon[:], self.dconst["allones_bf"]),
              writes=("c_allones_bf",), tag="b_allon")
        self.UO = tri[:, 0:128]
        self.MSL = tri[:, 128:256]
        self.MLI = tri[:, 256:384]
        self.allones = allon[:]
        wqkvT = ps.enter_context(nc.sbuf_tensor("b_wqkvT", [128, 8, 3 * D], BF16))
        woutT = ps.enter_context(nc.sbuf_tensor("b_woutT", [128, 8, D], BF16))
        wgbT = ps.enter_context(nc.sbuf_tensor("b_wgbT", [128, 8, 32], BF16))
        gbr = ps.enter_context(nc.sbuf_tensor("b_gbr", [1, 32], BF16))
        OT = ps.enter_context(nc.sbuf_tensor("b_OT", [128, 8, T], BF16))
        stg = ps.enter_context(nc.sbuf_tensor("b_stg", [128, D], BF16))
        stgf = ps.enter_context(nc.sbuf_tensor("b_stgf", [128, D], F32))
        xt = ps.enter_context(nc.sbuf_tensor("b_xt", [128, D], F32))
        xn = ps.enter_context(nc.sbuf_tensor("b_xn", [128, D], BF16))
        xnt = ps.enter_context(nc.sbuf_tensor("b_xnt", [128, 8, 128], BF16))
        qkv = ps.enter_context(nc.sbuf_tensor("b_qkv", [128, 3 * D], F32))
        qkn = ps.enter_context(nc.sbuf_tensor("b_qkn", [128, 2048], BF16))
        qtn = ps.enter_context(nc.sbuf_tensor("b_qtn", [128, D], BF16))
        kqT = ps.enter_context(nc.sbuf_tensor("b_kqT", [128, 16, 128], BF16))
        qtT = ps.enter_context(nc.sbuf_tensor("b_qtT", [128, 8, 128], BF16))
        gst = ps.enter_context(nc.sbuf_tensor("b_gst", [128, 16, 16], F32))
        spb = ps.enter_context(nc.sbuf_tensor("b_spb", [128, 16], BF16))
        sqs = ps.enter_context(nc.sbuf_tensor("b_sqs", [128, 2048], F32))
        ssn = ps.enter_context(nc.sbuf_tensor("b_ssn", [128, 32], F32))
        Zf = [ps.enter_context(nc.sbuf_tensor(f"b_Zf{i}", [128, 8, 128], F32)) for i in range(2)]
        Zb = ps.enter_context(nc.sbuf_tensor("b_Zb", [128, 8, 128], BF16))
        U = [dict(
            gma=ps.enter_context(nc.sbuf_tensor(f"b_gma{i}", [128, 128], BF16)),
            gstr=ps.enter_context(nc.sbuf_tensor(f"b_gstr{i}", [128, 128], BF16)),
            gmm=ps.enter_context(nc.sbuf_tensor(f"b_gmm{i}", [128, 128], BF16)),
            gmt=ps.enter_context(nc.sbuf_tensor(f"b_gmt{i}", [128, 128], BF16)),
            dt=ps.enter_context(nc.sbuf_tensor(f"b_dt{i}", [128, 128], BF16)),
            lu=[ps.enter_context(nc.sbuf_tensor(f"b_lu{i}_{j}", [128, 128], BF16)) for j in range(2)],
            lut=[ps.enter_context(nc.sbuf_tensor(f"b_lut{i}_{j}", [128, 128], BF16)) for j in range(2)],
            xs=[ps.enter_context(nc.sbuf_tensor(f"b_xs{i}_{j}", [128, 128], BF16)) for j in range(2)],
            w2=ps.enter_context(nc.sbuf_tensor(f"b_w2{i}", [128, 128], BF16)),
            w3=ps.enter_context(nc.sbuf_tensor(f"b_w3{i}", [128, 64], BF16)),
        ) for i in range(4)]

        p.step("vector", lambda e: e.memset(Zf[0][:].rearrange("p a b -> p (a b)"), 0.0),
               writes=tuple(f"b_Zf0_{h}" for h in range(16)), tag="bZ0")
        p.step("vector", lambda e: e.memset(Zf[1][:].rearrange("p a b -> p (a b)"), 0.0),
               writes=tuple(f"b_Zf1_{h}" for h in range(16)), tag="bZ1")
        p.step("vector", lambda e: e.memset(Zb[:].rearrange("p a b -> p (a b)"), 0.0),
               writes=tuple(f"b_Zb_{h}" for h in range(16)), tag="bZb0")
        p.dma("qg", lambda e: e.dma_start(gbr[0:1, 0:16], self.d_bg_b.unsqueeze(0)),
              writes=("b_gbr",), tag="b_gb1")
        p.dma("qg", lambda e: e.dma_start(gbr[0:1, 16:32], self.d_bb_b.unsqueeze(0)),
              writes=("b_gbr",), tag="b_gb2")

        def load_transposed(dsrc, rows, dst, dres, coloff, tag):
            nrt = (rows + 127) // 128
            for rt in range(nrt):
                r0 = rt * 128
                nr = min(128, rows - r0)
                p.dma("qg", (lambda r0=r0, nr=nr: lambda e: e.dma_start(
                    stg[0:nr, :], dsrc[r0:r0 + nr, :]))(),
                    writes=("b_stg",), tag=f"{tag}ld{rt}")
                for c in range(8):
                    bk = c % 2
                    p.step("tensor", (lambda c=c, bk=bk: lambda e: e.transpose(
                        self.pbank_bf(bk), stg[:, c * 128:(c + 1) * 128], self.ident_bf[:]))(),
                        reads=("b_stg", "c_ident_bf"), writes=(f"pf{bk}",), tag=f"{tag}T{rt}_{c}")
                    p.step("scalar", (lambda c=c, bk=bk, r0=r0, nr=nr, coloff=coloff: lambda e:
                           e.copy(dst[:, c, coloff + r0:coloff + r0 + nr],
                                  self.pbank_bf(bk)[:, 0:nr]))(),
                           reads=(f"pf{bk}",), writes=(dres,), tag=f"{tag}E{rt}_{c}")

        p.step("vector", lambda e: e.memset(stg[:], 0.0), writes=("b_stg",), tag="bstg0")
        load_transposed(self.d_bqkv_w, 3 * D, wqkvT, "b_wqkvT", 0, "bwq")
        load_transposed(self.d_bout_w, D, woutT, "b_woutT", 0, "bwo")
        p.step("vector", lambda e: e.memset(stgf[:], 0.0), writes=("b_stgf",), tag="bstgf0")
        p.dma("qg", lambda e: e.dma_start(stg[0:16, :], self.d_bg_w), writes=("b_stg",), tag="bgw")
        p.dma("qg", lambda e: e.dma_start(stg[16:32, :], self.d_bb_w), writes=("b_stg",), tag="bbw")
        for c in range(8):
            bk = c % 2
            p.step("tensor", (lambda c=c, bk=bk: lambda e: e.transpose(
                self.pbank_bf(bk), stg[:, c * 128:(c + 1) * 128], self.ident_bf[:]))(),
                reads=("b_stg", "c_ident_bf"), writes=(f"pf{bk}",), tag=f"bgT{c}")
            p.step("scalar", (lambda c=c, bk=bk: lambda e: e.copy(
                wgbT[:, c, 0:32], self.pbank_bf(bk)[:, 0:32]))(),
                reads=(f"pf{bk}",), writes=("b_wgbT",), tag=f"bgE{c}")
        p.barrier("b_prelude_end")

        for c in range(32):
            t0 = c * 128
            par = c % 2
            u = U[par]
            zf_in, zf_out = Zf[par], Zf[1 - par]
            p.dma("qs", (lambda t0=t0: lambda e: e.dma_start(
                xt[:], src_dram[t0:t0 + 128, :]))(),
                writes=("b_xt",), tag=f"bld{c}")
            self.ln(xt[:], "b_xt", xn[:], "b_xn", tag=f"bln{c}_")
            self.tr128(lambda cc: xn[:, cc * 128:(cc + 1) * 128], "b_xn",
                       (lambda cc: xnt[:, cc, :]), "b_xnt", 8, [4, 5], tag=f"btr{c}_")
            for j in range(6):
                bk = 4 + j % 3
                for cc in range(8):
                    p.step("tensor", (lambda j=j, cc=cc, bk=bk: lambda e: e.matmul(
                        self.pbank(bk), xnt[:, cc, :], wqkvT[:, cc, j * 512:(j + 1) * 512],
                        start=(cc == 0), stop=(cc == 7)))(),
                        reads=("b_xnt", "b_wqkvT"), writes=(f"pf{bk}",), tag=f"bqkv{c}_{j}")
                p.step("scalar", (lambda j=j, bk=bk: lambda e: e.copy(
                    qkv[:, j * 512:(j + 1) * 512], self.pbank(bk)))(),
                    reads=(f"pf{bk}",), writes=("b_qkv",), tag=f"bqkve{c}_{j}")
            p.step("tensor", lambda e: e.matmul(
                self.pbank(7, 0, 32), self.ones_bf[0:1, 0:128], gbr[0:1, 0:32],
                start=True, stop=False),
                reads=("b_gbr", "c_ones_bf"), writes=("pf7",), tag=f"bgb{c}")
            for cc in range(8):
                p.step("tensor", (lambda cc=cc: lambda e: e.matmul(
                    self.pbank(7, 0, 32), xnt[:, cc, :], wgbT[:, cc, 0:32],
                    start=False, stop=(cc == 7)))(),
                    reads=("b_xnt", "b_wgbT"), writes=("pf7",), tag=f"bg{c}_{cc}")
            p.step("scalar", lambda e: e.activation(
                gst[:, 11:13, :].rearrange("p a b -> p (a b)"), self.pf[7][:, 0:32],
                AF.Exp, scale=-1.0),
                reads=("pf7",), writes=("b_gex",), tag=f"bge{c}")
            p.step("vector", lambda e: e.tensor_scalar_add(
                gst[:, 13:15, :].rearrange("p a b -> p (a b)"),
                gst[:, 11:13, :].rearrange("p a b -> p (a b)"), 1.0),
                reads=("b_gex",), writes=("b_g1p",), tag=f"bg1{c}")
            p.step("scalar", lambda e: e.activation(gst[:, 0, :], gst[:, 13, :], AF.Ln),
                   reads=("b_g1p",), writes=("b_gsp",), tag=f"bsp{c}")
            p.step("vector", lambda e: e.reciprocal(gst[:, 1, :], gst[:, 14, :]),
                   reads=("b_g1p",), writes=("b_gbs",), tag=f"bbs{c}")
            p.step("scalar", lambda e: e.copy(spb[:], gst[:, 0, :]),
                   reads=("b_gsp",), writes=("b_spb",), tag=f"bspb{c}")
            p.step("tensor", lambda e: e.matmul(
                self.pbank(6, 0, 16), self.UO, spb[:], start=True, stop=True),
                reads=("b_spb", "c_tri_bf"), writes=("pf6",), tag=f"bcs{c}")
            p.step("scalar", lambda e: e.copy(gst[:, 2, :], self.pbank(6, 0, 16)),
                   reads=("pf6",), writes=("b_gcs",), tag=f"bcse{c}")
            p.step("scalar", lambda e: e.activation(gst[:, 3, :], gst[:, 2, :], AF.Exp, scale=-1.0),
                   reads=("b_gcs",), writes=("b_ggam",), tag=f"bgam{c}")
            p.step("vector", lambda e: e.tensor_tensor(
                gst[:, 5, :], gst[:, 1, :], gst[:, 3, :], op=ALU.mult),
                reads=("b_gbs", "b_ggam"), writes=("b_gbg",), tag=f"bbg{c}")
            p.step("vector", lambda e: e.tensor_tensor(
                sqs[:], qkv[:, 0:2048], qkv[:, 0:2048], op=ALU.mult),
                reads=("b_qkv",), writes=("b_sqs",), tag=f"bsq{c}")
            p.step("vector", lambda e: e.tensor_reduce(
                ssn[:].unsqueeze(-1), sqs[:].rearrange("p (a b) -> p a b", b=64),
                axis=AX.X, op=ALU.add),
                reads=("b_sqs",), writes=("b_ssn",), tag=f"bssn{c}")
            p.step("scalar", lambda e: e.activation(ssn[:], ssn[:], AF.Ln,
                                                    bias=self.consts["epsv"][:, 1:2]),
                   reads=("b_ssn", "c_epsv"), writes=("b_ssn",), tag=f"bsrt{c}")
            p.step("scalar", lambda e: e.activation(ssn[:], ssn[:], AF.Exp, scale=-0.5),
                   reads=("b_ssn",), writes=("b_ssn",), tag=f"brcp{c}")
            p.step("vector", lambda e: e.tensor_tensor(
                gst[:, 10, :], ssn[:, 0:16], gst[:, 3, :], op=ALU.mult),
                reads=("b_ssn", "b_ggam"), writes=("b_grq",), tag=f"brq{c}")
            p.step("vector", lambda e: e.tensor_tensor(
                qkn[:, 0:1024].rearrange("p (a b) -> p a b", b=64),
                qkv[:, 0:1024].rearrange("p (a b) -> p a b", b=64),
                ssn[:, 0:16].unsqueeze(-1).broadcast_to([128, 16, 64]), op=ALU.mult),
                reads=("b_qkv", "b_ssn"), writes=("b_qkn",), tag=f"bqh{c}",
                sreads=("b_ssn",))
            p.step("vector", lambda e: e.tensor_tensor(
                qkn[:, 1024:2048].rearrange("p (a b) -> p a b", b=64),
                qkv[:, 1024:2048].rearrange("p (a b) -> p a b", b=64),
                ssn[:, 16:32].unsqueeze(-1).broadcast_to([128, 16, 64]), op=ALU.mult),
                reads=("b_qkv", "b_ssn"), writes=("b_qkn",), tag=f"bkh{c}",
                sreads=("b_ssn",))
            p.step("vector", lambda e: e.tensor_tensor(
                qtn[:].rearrange("p (a b) -> p a b", b=64),
                qkv[:, 0:1024].rearrange("p (a b) -> p a b", b=64),
                gst[:, 10, :].unsqueeze(-1).broadcast_to([128, 16, 64]), op=ALU.mult),
                reads=("b_qkv", "b_grq"), writes=("b_qtn",), tag=f"bqt{c}",
                sreads=("b_grq",))
            self.tr128(lambda cc: qkn[:, cc * 128:(cc + 1) * 128], "b_qkn",
                       (lambda cc: kqT[:, cc, :]), "b_kqT", 16, [6, 7], tag=f"bkqT{c}_")
            self.tr128(lambda cc: qtn[:, cc * 128:(cc + 1) * 128], "b_qtn",
                       (lambda cc: qtT[:, cc, :]), "b_qtT", 8, [6, 7], tag=f"bqtT{c}_")

            plans = [self._unit_plan(c, h, h % 4, U[h % 4], zf_in, zf_out,
                                     Zb, qkv, qkn, kqT, qtT, gst, spb, OT)
                     for h in range(16)]
            nstep = len(plans[0])
            order = []
            for w in range(4):
                grp = plans[w * 4:(w + 1) * 4]
                mx = max(len(pl) for pl in grp)
                for i in range(mx):
                    for pl in grp:
                        if i < len(pl):
                            order.append(pl[i])
            for i, (eng, emit, reads, writes, sreads) in enumerate(order):
                p.step(eng, emit, reads=reads, writes=writes,
                       sreads=sreads, tag=f"bu{c}_{i}")

        p.barrier("b_wout")
        xo = [qkv[:, 0:D], qkv[:, D:2 * D]]
        xti = [xt[:], stgf[:]]
        for ti in range(32):
            b = ti % 2
            p.dma("qs", (lambda ti=ti, b=b: lambda e: e.dma_start(
                xti[b], src_dram[ti * 128:(ti + 1) * 128, :]))(),
                writes=(f"b_xti{b}",), tag=f"bwld{ti}")
            for half in range(2):
                bk = 2 * b + half
                for vc in range(8):
                    p.step("tensor", (lambda ti=ti, half=half, vc=vc, bk=bk: lambda e: e.matmul(
                        self.pbank(bk), OT[:, vc, ti * 128:(ti + 1) * 128],
                        woutT[:, vc, half * 512:(half + 1) * 512],
                        start=(vc == 0), stop=(vc == 7)))(),
                        reads=("b_OT", "b_woutT"), writes=(f"pf{bk}",),
                        tag=f"bw{ti}_{half}_{vc}")
                p.step("vector", (lambda ti=ti, b=b, half=half, bk=bk: lambda e: e.tensor_tensor(
                    xo[b][:, half * 512:(half + 1) * 512],
                    xti[b][:, half * 512:(half + 1) * 512],
                    self.pbank(bk), op=ALU.add))(),
                    reads=(f"b_xti{b}", f"pf{bk}"), writes=(f"b_xo{b}",),
                    tag=f"bwa{ti}_{half}")
            p.dma("qs", (lambda ti=ti, b=b: lambda e: e.dma_start(
                dst_dram[ti * 128:(ti + 1) * 128, :], xo[b]))(),
                reads=(f"b_xo{b}",), writes=(f"bdst{ti}",), tag=f"bwst{ti}")
        p.barrier("b_end")
        ps.close()

    def _unit_plan(self, c, h, s, u, zf_in, zf_out, Zb, qkv, qkn, kqT, qtT,
                   gst, spb, OT):
        S = []

        def st(eng, emit, reads=(), writes=(), sreads=()):
            S.append((eng, emit, tuple(reads), tuple(writes), tuple(sreads)))

        bk = s
        pA = self.pf[bk][:, 0:128]
        pX = self.pf[bk][:, 128:256]
        pO = self.pf[bk][:, 256:384]
        pR = self.pf[bk][:, 384:448]
        pZ = self.pf[bk][:, 448:512]
        PK = f"pf{bk}"
        hs, hc = h % 2, h // 2
        lo, hi = hs * 64, (hs + 1) * 64
        kT_h = kqT[lo:hi, 8 + hc, :]
        qT_h = kqT[lo:hi, hc, :]
        qtT_h = qtT[lo:hi, hc, :]
        Zb_h = Zb[lo:hi, hc, lo:hi]
        un = f"u{s}"
        par = c % 2

        st("vector", (lambda: lambda e: e.tensor_scalar_mul(
            u["gstr"][:], self.UO, gst[:, 0, h:h + 1]))(),
            ("b_gsp", "c_tri_bf"), (f"{un}gstr",), ("b_gsp",))
        st("tensor", (lambda: lambda e: e.matmul(
            pA, self.allones[:], u["gstr"][:], start=True, stop=True))(),
            (f"{un}gstr", "c_allones_bf"), (PK,))
        st("vector", (lambda: lambda e: e.tensor_scalar(
            u["gma"][:], pA, gst[:, 2, h:h + 1], 0.0,
            op0=ALU.subtract, op1=ALU.min))(),
            (PK, "b_gcs"), (f"{un}gma",), ("b_gcs",))
        st("vector", (lambda: lambda e: e.tensor_tensor(
            gst[:, 6, h:h + 1], gst[:, 2, h:h + 1], self.pf[bk][:, 127:128],
            op=ALU.subtract))(),
            (PK, "b_gcs"), (f"b_gecv_{h}",))
        st("scalar", (lambda: lambda e: e.activation(
            gst[:, 6, h:h + 1], gst[:, 6, h:h + 1], AF.Exp))(),
            (f"b_gecv_{h}",), (f"b_gecv_{h}",))
        st("scalar", (lambda: lambda e: e.activation(
            gst[:, 7, h:h + 1], self.pf[bk][:, 127:128], AF.Exp, scale=-1.0))(),
            (PK,), (f"b_ggcb_{h}",))
        st("scalar", (lambda: lambda e: e.activation(
            u["gma"][:], u["gma"][:], AF.Exp))(),
            (f"{un}gma",), (f"{un}gma",))
        st("tensor", (lambda: lambda e: e.matmul(
            pA, kT_h, kT_h, start=True, stop=True))(),
            ("b_kqT",), (PK,))
        st("vector", (lambda: lambda e: e.tensor_tensor(
            u["gstr"][:], u["gma"][:], self.MSL, op=ALU.mult))(),
            (f"{un}gma", "c_tri_bf"), (f"{un}gstr",))
        st("vector", (lambda: lambda e: e.scalar_tensor_tensor(
            u["lu"][0][:], pA, gst[:, 1, h:h + 1], u["gstr"][:],
            op0=ALU.mult, op1=ALU.mult))(),
            (PK, "b_gbs", f"{un}gstr"), (f"{un}lu0",), ("b_gbs",))
        st("tensor", (lambda: lambda e: e.transpose(
            self.pf[bk][:].bitcast(BF16)[:, 0:128], u["lu"][0][:], self.ident_bf[:]))(),
            (f"{un}lu0", "c_ident_bf"), (PK,))
        st("scalar", (lambda: lambda e: e.copy(
            u["lut"][0][:], self.pf[bk][:].bitcast(BF16)[:, 0:128]))(),
            (PK,), (f"{un}lut0",))
        st("vector", (lambda: lambda e: e.tensor_tensor(
            u["gmm"][:], u["gma"][:], self.MLI, op=ALU.mult))(),
            (f"{un}gma", "c_tri_bf"), (f"{un}gmm",))
        st("tensor", (lambda: lambda e: e.transpose(
            self.pf[bk][:].bitcast(BF16)[:, 0:128], u["gmm"][:], self.ident_bf[:]))(),
            (f"{un}gmm", "c_ident_bf"), (PK,))
        st("scalar", (lambda: lambda e: e.copy(
            u["gmt"][:], self.pf[bk][:].bitcast(BF16)[:, 0:128]))(),
            (PK,), (f"{un}gmt",))
        st("tensor", (lambda: lambda e: e.matmul(
            pA, kT_h, qT_h, start=True, stop=True))(),
            ("b_kqT",), (PK,))
        st("vector", (lambda: lambda e: e.tensor_tensor(
            u["dt"][:], pA, u["gmt"][:], op=ALU.mult))(),
            (PK, f"{un}gmt"), (f"{un}dt",))
        st("tensor", (lambda: lambda e: e.matmul(
            pR, kT_h, Zb_h, start=True, stop=True))(),
            ("b_kqT", f"b_Zb_{h}"), (PK,))
        st("vector", (lambda: lambda e: e.tensor_scalar_mul(
            u["xs"][0][:, 0:64], pR, gst[:, 5, h:h + 1]))(),
            (PK, "b_gbg"), (f"{un}xs0",), ("b_gbg",))
        st("vector", (lambda: lambda e: e.tensor_scalar_mul(
            u["xs"][0][:, 64:128], qkv[:, 2048 + h * 64:2048 + (h + 1) * 64],
            gst[:, 1, h:h + 1]))(),
            ("b_qkv", "b_gbs"), (f"{un}xs0",), ("b_gbs",))
        for lev in range(7):
            cur, nxt = lev % 2, (lev + 1) % 2
            st("tensor", (lambda cur=cur: lambda e: e.matmul(
                pX, u["lut"][cur][:], u["xs"][cur][:], start=True, stop=True))(),
                (f"{un}lut{cur}", f"{un}xs{cur}"), (PK,))
            st("vector", (lambda cur=cur, nxt=nxt: lambda e: e.tensor_tensor(
                u["xs"][nxt][:], u["xs"][cur][:], pX, op=ALU.subtract))(),
                (f"{un}xs{cur}", PK), (f"{un}xs{nxt}",))
            if lev < 6:
                st("tensor", (lambda cur=cur: lambda e: e.matmul(
                    pA, u["lu"][cur][:], u["lut"][cur][:], start=True, stop=True))(),
                    (f"{un}lu{cur}", f"{un}lut{cur}"), (PK,))
                st("scalar", (lambda nxt=nxt: lambda e: e.copy(u["lut"][nxt][:], pA))(),
                    (PK,), (f"{un}lut{nxt}",))
            if lev < 5:
                st("tensor", (lambda cur=cur: lambda e: e.matmul(
                    pA, u["lut"][cur][:], u["lu"][cur][:], start=True, stop=True))(),
                    (f"{un}lu{cur}", f"{un}lut{cur}"), (PK,))
                st("vector", (lambda nxt=nxt: lambda e: e.tensor_copy(u["lu"][nxt][:], pA))(),
                    (PK,), (f"{un}lu{nxt}",))
        fin = 1
        st("vector", (lambda: lambda e: e.tensor_tensor(
            u["w2"][:, lo:hi], u["xs"][fin][:, 64:128], u["xs"][fin][:, 0:64],
            op=ALU.subtract))(),
            (f"{un}xs{fin}",), (f"{un}w2",))
        st("vector", (lambda: lambda e: e.tensor_scalar_mul(
            u["w3"][:], u["w2"][:, lo:hi], gst[:, 6, h:h + 1]))(),
            (f"{un}w2", f"b_gecv_{h}"), (f"{un}w3",), (f"b_gecv_{h}",))
        st("tensor", (lambda: lambda e: e.matmul(
            pO, Zb[lo:hi, hc, :], qtT_h, start=True, stop=False))(),
            (f"b_Zb_{h}", "b_qtT"), (PK,))
        st("tensor", (lambda: lambda e: e.matmul(
            pO, u["w2"][:], u["dt"][:], start=False, stop=True))(),
            (f"{un}w2", f"{un}dt"), (PK,))
        st("scalar", (lambda: lambda e: e.copy(
            OT[lo:hi, hc, c * 128:(c + 1) * 128], pO[lo:hi, :]))(),
            (PK,), ("b_OT",))
        st("tensor", (lambda: lambda e: e.matmul(
            pZ, qkn[:, 1024 + hc * 128:1024 + (hc + 1) * 128],
            u["w3"][:], start=True, stop=True))(),
            ("b_qkn", f"{un}w3"), (PK,))
        st("vector", (lambda: lambda e: e.scalar_tensor_tensor(
            zf_out[lo:hi, hc, lo:hi], zf_in[lo:hi, hc, lo:hi], gst[lo:hi, 7, h:h + 1],
            pZ[lo:hi, :], op0=ALU.mult, op1=ALU.add))(),
            (f"b_Zf{par}_{h}", f"b_ggcb_{h}", PK), (f"b_Zf{1 - par}_{h}",),
            (f"b_ggcb_{h}",))
        st("scalar", (lambda: lambda e: e.copy(
            Zb[lo:hi, hc, lo:hi], zf_out[lo:hi, hc, lo:hi]))(),
            (f"b_Zf{1 - par}_{h}",), (f"b_Zb_{h}",))
        return S

    def build(self, phases=("A", "F0", "B", "F1")):
        self._rtmp = self.es.enter_context(self.nc.sbuf_tensor("rtmp", [128, 32], F32))
        self.prelude_router()
        chain = list(phases)
        prev = self.dx
        for i, ph in enumerate(chain):
            dst = self.dy if i == len(chain) - 1 else self.d_xres
            if ph == "A":
                self.phase_attn_a(prev, dst)
            elif ph == "B":
                self.phase_attn_b(prev, dst)
            elif ph in ("F0", "F1"):
                self.phase_ffn(prev, dst, "f0_" if ph == "F0" else "f1_")
            prev = self.d_xres
        self.prog.barrier("final")
        sems = {}
        for d in DOMAINS + ("bar", "epo"):
            sems[d] = self.es.enter_context(self.nc.semaphore("sem_" + d))
        self.prog.resolve()
        with self.nc.Block() as block:
            self.prog.emit_all(sems, block)
        self.es.close()
        return self.nc


def _in_maps(inputs, consts):
    x = np.ascontiguousarray(np.asarray(inputs["L0_attn_a_x"], dtype=np.float32))
    base = dict(
        aqkv_w=np.ascontiguousarray(inputs["L0_attn_a_w_qkv"], dtype=np.float32),
        aqkv_b=np.ascontiguousarray(inputs["L0_attn_a_b_qkv"], dtype=np.float32),
        aproj_w=np.ascontiguousarray(inputs["L0_attn_a_w_proj"], dtype=np.float32),
        aproj_b=np.ascontiguousarray(inputs["L0_attn_a_b_proj"], dtype=np.float32),
        bqkv_w=np.ascontiguousarray(inputs["L0_attn_b_Wqkv"], dtype=np.float32),
        bg_w=np.ascontiguousarray(inputs["L0_attn_b_Wg"], dtype=np.float32),
        bg_b=np.ascontiguousarray(inputs["L0_attn_b_bg"], dtype=np.float32),
        bb_w=np.ascontiguousarray(inputs["L0_attn_b_Wb"], dtype=np.float32),
        bb_b=np.ascontiguousarray(inputs["L0_attn_b_bb"], dtype=np.float32),
        bout_w=np.ascontiguousarray(inputs["L0_attn_b_Wout"], dtype=np.float32),
        rw=np.ascontiguousarray(inputs["L0_ffn_router_w"], dtype=np.float32),
        wg=np.ascontiguousarray(inputs["L0_ffn_w_gate"], dtype=np.float32),
        wu=np.ascontiguousarray(inputs["L0_ffn_w_up"], dtype=np.float32),
        wd=np.ascontiguousarray(inputs["L0_ffn_w_down"], dtype=np.float32),
        swg=np.ascontiguousarray(inputs["L0_ffn_sw_gate"], dtype=np.float32),
        swu=np.ascontiguousarray(inputs["L0_ffn_sw_up"], dtype=np.float32),
        swd=np.ascontiguousarray(inputs["L0_ffn_sw_down"], dtype=np.float32),
        **consts,
    )
    return [dict(base, x=np.ascontiguousarray(x[i])) for i in range(N_CORES)]


def kernel(**inputs):
    consts = make_consts()
    mk = MK()
    nc = mk.build()
    maps = _in_maps(inputs, consts)
    res = run_bass_kernel_spmd(nc, maps, core_ids=list(range(N_CORES)))
    out = np.stack([np.asarray(r["y"], dtype=np.float32) for r in res.results], axis=0)
    return out
